# version 02
# baseline (speedup 1.0000x reference)
import contextlib
import numpy as np
import concourse.bass as bass
import concourse.mybir as mybir
from concourse.bass_utils import run_bass_kernel_spmd

F32 = mybir.dt.float32
BF16 = mybir.dt.bfloat16
I32 = mybir.dt.int32
AF = mybir.ActivationFunctionType
ALU = mybir.AluOpType
AX = mybir.AxisListType

T = 4096
DM = 1024
NH, HD = 16, 64
NCORE = 8
NEG = -1.0e38

TENSORS = {
    "x0":    ([T, DM], F32, "inp"),
    "qkvw":  ([DM, NH * 192], F32, "inp"),
    "qkvb":  ([NH, 192], F32, "inp"),
    "outw":  ([DM, DM], F32, "inp"),
    "outb":  ([DM], F32, "inp"),
    "rw":    ([8, DM], F32, "inp"),
    "eb":    ([8], F32, "inp"),
    "w1c":   ([9, DM, 2048], F32, "inp"),
    "w3c":   ([9, DM, 2048], F32, "inp"),
    "w2c":   ([9, 2048, DM], F32, "inp"),
    "wq":    ([HD, HD], F32, "inp"),
    "wk":    ([HD, HD], F32, "inp"),
    "wv":    ([HD, HD], F32, "inp"),
    "wu":    ([DM, DM], F32, "inp"),
    "bu":    ([DM], F32, "inp"),
    "wg":    ([3072, DM], I32, "inp"),
    "wup":   ([3072, DM], I32, "inp"),
    "wd":    ([DM, 3072], I32, "inp"),
    "qT":    ([NH, HD, T], BF16, "scr"),
    "kT":    ([NH, HD, T], BF16, "scr"),
    "vv":    ([NH, T, HD], BF16, "scr"),
    "attnT": ([DM, T], BF16, "scr"),
    "x1":    ([T, DM], F32, "scr"),
    "x2":    ([T, DM], F32, "scr"),
    "x3":    ([T, DM], F32, "scr"),
    "w1b":   ([9, DM, 2048], BF16, "scr"),
    "w3b":   ([9, DM, 2048], BF16, "scr"),
    "w2b":   ([9, 2048, DM], BF16, "scr"),
    "wgT":   ([DM, 3072], BF16, "scr"),
    "wuT":   ([DM, 3072], BF16, "scr"),
    "wdT":   ([3072, DM], BF16, "scr"),
    "y":     ([T, DM], F32, "out"),
}

PHASE_IO = {
    "qkv0":    (["x0", "qkvw", "qkvb"], ["qT", "kT", "vv"]),
    "attn0":   (["qT", "kT", "vv"], ["attnT"]),
    "oproj0":  (["attnT", "outw", "outb", "x0"], ["x1"]),
    "moeprep": (["w1c", "w3c", "w2c"], ["w1b", "w3b", "w2b"]),
    "moe":     (["x1", "rw", "eb", "w1b", "w3b", "w2b"], ["x2"]),
    "qkv1":    (["x2", "wq", "wk", "wv"], ["qT", "kT", "vv"]),
    "attn1":   (["qT", "kT", "vv"], ["attnT"]),
    "oproj1":  (["attnT", "wu", "bu", "x2"], ["x3"]),
    "ffnprep": (["wg", "wup", "wd"], ["wgT", "wuT", "wdT"]),
    "ffn":     (["x3", "wgT", "wuT", "wdT"], ["y"]),
}
ALL_PHASES = list(PHASE_IO)


class Ctx:
    def __init__(self, nc):
        self.nc = nc
        self.stack = contextlib.ExitStack()

    def sems(self, tag, *names):
        return [self.nc.alloc_semaphore(tag + "_" + n) for n in names]


def ph_qkv(ctx, d, layer):
    nc = ctx.nc
    tg = "q%d" % layer
    xin = d["x0"] if layer == 0 else d["x2"]
    es = contextlib.ExitStack()
    E = es.enter_context
    idf = E(nc.sbuf_tensor(tg + "idf", [128, 128], F32))
    epst = E(nc.sbuf_tensor(tg + "epst", [128, 1], F32))
    idb = E(nc.sbuf_tensor(tg + "idb", [128, 128], BF16))
    pbias = E(nc.psum_tensor(tg + "pbias", [128, 512], F32))
    if layer == 0:
        qkwb = E(nc.sbuf_tensor(tg + "qkwb", [128, 8, NH, 128], BF16))
        vwb = E(nc.sbuf_tensor(tg + "vwb", [128, 8, DM], BF16))
        qkb = E(nc.sbuf_tensor(tg + "qkb", [128, NH], F32))
        vb = E(nc.sbuf_tensor(tg + "vb", [128, DM], F32))
        wf = E(nc.sbuf_tensor(tg + "wf", [128, 2, 3072], F32))
        bsrc = E(nc.sbuf_tensor(tg + "bsrc", [16, 192], F32))
        vrow = E(nc.sbuf_tensor(tg + "vrow", [1, DM], F32))
        ones1 = E(nc.sbuf_tensor(tg + "ones1", [1, 128], F32))
    else:
        wsrc = E(nc.sbuf_tensor(tg + "wsrc", [64, 192], F32))
        wqkT = E(nc.sbuf_tensor(tg + "wqkT", [128, 128], BF16))
        wvT = E(nc.sbuf_tensor(tg + "wvT", [128, 64], BF16))
        vwb = E(nc.sbuf_tensor(tg + "vwb", [128, 8, DM], BF16))

    pl, pg, pa, pp, pbs, pvr, pdup, pvw = ctx.sems(tg, "pl", "pg", "pa", "pp", "pbs", "pvr", "pdup", "pvw")
    pw = ctx.sems(tg, "pw0", "pw1")

    with nc.Block(tg + "prep") as b:
        @b.gpsimd
        def _(g):
            g.memset(epst[:], 1.0e-5)
            g.memset(idf[:], 0.0).then_inc(pg, 1)
            g.wait_ge(pg, 1)
            g.affine_select(out=idf[:], in_=idf[:], compare_op=ALU.not_equal,
                            fill=1.0, base=0, pattern=[[-1, 128]],
                            channel_multiplier=1)
            g.memset(idb[:], 0.0).then_inc(pg, 1)
            g.wait_ge(pg, 2)
            g.affine_select(out=idb[:], in_=idb[:], compare_op=ALU.not_equal,
                            fill=1.0, base=0, pattern=[[-1, 128]],
                            channel_multiplier=1).then_inc(pg, 1)
            if layer == 0:
                g.memset(ones1[:], 1.0).then_inc(pg, 1)


        if layer == 0:
            @b.sync
            def _(s):
                s.dma_start(out=bsrc[:], in_=d["qkvb"]).then_inc(pbs, 16)
                s.dma_start(out=vrow[0:1, :].rearrange("o (h dd) -> o h dd", h=NH),
                            in_=d["qkvb"][None, :, 128:192]).then_inc(pvr, 16)
                qw = d["qkvw"].rearrange("(c p) n -> c p n", p=128)
                for k in range(8):
                    if k >= 2:
                        s.wait_ge(pa, k - 1)
                    s.dma_start(out=wf[:, k % 2, :], in_=qw[k]).then_inc(pw[k % 2], 16)

            @b.scalar
            def _(s):
                for k in range(8):
                    s.wait_ge(pw[k % 2], 16 * (k // 2 + 1))
                    wv_ = wf[:, k % 2, :].rearrange("p (h s) -> p h s", h=NH)
                    s.activation(out=qkwb[:, k, :, 0:64], in_=wv_[:, :, 0:64],
                                 func=AF.Copy, scale=0.125)
                    s.activation(out=qkwb[:, k, :, 64:128], in_=wv_[:, :, 64:128],
                                 func=AF.Copy)
                    s.activation(out=vwb[:, k, :].rearrange("p (h s) -> p h s", h=NH),
                                 in_=wv_[:, :, 128:192], func=AF.Copy).then_inc(pa, 1)
                s.wait_ge(pp, 1)
                s.activation(out=qkb[0:64, :], in_=pbias[0:64, 0:NH], func=AF.Copy,
                             scale=0.125)
                s.activation(out=qkb[64:128, :], in_=pbias[64:128, 0:NH],
                             func=AF.Copy).then_inc(pa, 1)

            @b.tensor
            def _(t):
                t.wait_ge(pbs, 16)
                t.wait_ge(pg, 3)
                t.transpose(pbias[:, 0:16], bsrc[:, 0:128], idf[0:16, 0:16]).then_inc(pp, 1)
        else:
            @b.sync
            def _(s):
                s.dma_start(out=wsrc[:, 0:64], in_=d["wq"]).then_inc(pl, 16)
                s.dma_start(out=wsrc[:, 64:128], in_=d["wk"]).then_inc(pl, 16)
                s.dma_start(out=wsrc[:, 128:192], in_=d["wv"]).then_inc(pl, 16)

            @b.tensor
            def _(t):
                t.wait_ge(pl, 48)
                t.wait_ge(pg, 3)
                t.transpose(pbias[0:64, 0:64], wsrc[:, 0:64], idf[0:64, 0:64])
                t.transpose(pbias[0:64, 64:128], wsrc[:, 64:128], idf[0:64, 0:64])
                t.transpose(pbias[0:64, 128:192], wsrc[:, 128:192], idf[0:64, 0:64]).then_inc(pp, 1)

            @b.scalar
            def _(s):
                s.wait_ge(pp, 1)
                sc = float(DM) ** (-0.25)
                s.activation(out=wqkT[0:64, 0:64], in_=pbias[0:64, 0:64], func=AF.Copy, scale=sc)
                s.activation(out=wqkT[0:64, 64:128], in_=pbias[0:64, 64:128], func=AF.Copy, scale=sc)
                s.activation(out=wvT[0:64, :], in_=pbias[0:64, 128:192], func=AF.Copy).then_inc(pa, 1)

            @b.gpsimd
            def _(g):
                g.memset(vwb[:], 0.0).then_inc(pvw, 1)
                g.wait_ge(pa, 1)
                g.dma_start(out=wqkT[64:128, :], in_=wqkT[0:64, :]).then_inc(pdup, 16)
                g.dma_start(out=wvT[64:128, :], in_=wvT[0:64, :]).then_inc(pdup, 16)

            @b.vector
            def _(v):
                v.wait_ge(pvw, 1)
                v.wait_ge(pdup, 32)
                for k in range(8):
                    v.tensor_copy(out=vwb[0:64, k, k * 128:k * 128 + 64],
                                  in_=wvT[0:64, :])
                    ii = v.tensor_copy(out=vwb[64:128, k, k * 128 + 64:k * 128 + 128],
                                       in_=wvT[64:128, :])
                ii.then_inc(pvw, 1)

    if layer == 0:
        with nc.Block(tg + "prep2") as b:
            pv2, = ctx.sems(tg, "pv2")

            @b.tensor
            def _(t):
                t.wait_ge(pvr, 16)
                t.matmul(pbias[:, 0:512], ones1[:], vrow[:, 0:512], start=True, stop=True).then_inc(pv2, 1)
                t.wait_ge(pv2, 2)
                t.matmul(pbias[:, 0:512], ones1[:], vrow[:, 512:1024], start=True, stop=True).then_inc(pv2, 1)

            @b.vector
            def _(v):
                v.wait_ge(pv2, 1)
                v.tensor_copy(out=vb[:, 0:512], in_=pbias[:, 0:512]).then_inc(pv2, 1)
                v.wait_ge(pv2, 3)
                v.tensor_copy(out=vb[:, 512:1024], in_=pbias[:, 0:512]).then_inc(pv2, 1)

    xt = E(nc.sbuf_tensor(tg + "xt", [128, 2, 4, DM], F32))
    bn6 = E(nc.sbuf_tensor(tg + "bn6", [128, 48], F32))
    bno = E(nc.sbuf_tensor(tg + "bno", [128, 8], F32))
    sd = E(nc.sbuf_tensor(tg + "sd", [128, 4], F32))
    rstd = E(nc.sbuf_tensor(tg + "rstd", [128, 4], F32))
    htok = E(nc.sbuf_tensor(tg + "htok", [128, 4, DM], BF16))
    hf = E(nc.sbuf_tensor(tg + "hf", [128, 8, 512], BF16))
    qksb = E(nc.sbuf_tensor(tg + "qksb", [128, 2, 512], BF16))
    vsb = E(nc.sbuf_tensor(tg + "vsb", [128, 2, NH, HD], BF16))
    pst = E(nc.psum_tensor(tg + "pst", [128, 512], BF16))
    pqk = E(nc.psum_tensor(tg + "pqk", [128, 2, 512], F32))
    pvv = E(nc.psum_tensor(tg + "pvv", [128, 2, 512], F32))

    sv1, sv2, sa1, sv3, sv4, sp1, sv5, sp2, sa2, sp3, sv6 = ctx.sems(
        tg, "sv1", "sv2", "sa1", "sv3", "sv4", "sp1", "sv5", "sp2", "sa2",
        "sp3", "sv6")
    sx = ctx.sems(tg, "sx0", "sx1")
    sqs = ctx.sems(tg, "sqs0", "sqs1")
    svs = ctx.sems(tg, "svs0", "svs1")

    NT = 8

    with nc.Block(tg + "main") as b:
        @b.sync
        def _(s):
            xr = xin.rearrange("(n c p) d -> n p c d", p=128, c=4)
            for i in range(NT):
                if i >= 2:
                    s.wait_ge(sv4, i - 1)
                s.dma_start(out=xt[:, i % 2], in_=xr[i]).then_inc(sx[i % 2], 16)

        @b.vector
        def _(v):
            for i in range(NT):
                v.wait_ge(sx[i % 2], 16 * (i // 2 + 1))
                for c in range(4):
                    ii = v.bn_stats(bn6[:, c * 12:c * 12 + 6], xt[:, i % 2, c, 0:512])
                    ii = v.bn_stats(bn6[:, c * 12 + 6:c * 12 + 12], xt[:, i % 2, c, 512:1024])
                ii.then_inc(sv1, 1)
                v.wait_ge(sv1, i + 1)
                for c in range(4):
                    ii = v.bn_aggr(bno[:, 2 * c:2 * c + 2], bn6[:, c * 12:c * 12 + 12])
                ii.then_inc(sv2, 1)
                v.wait_ge(sa1, i + 1)
                v.reciprocal(rstd[:], sd[:]).then_inc(sv3, 1)
                v.wait_ge(sv3, i + 1)
                for c in range(4):
                    ii = v.tensor_scalar(htok[:, c, :], xt[:, i % 2, c, :],
                                         bno[:, 2 * c:2 * c + 1], rstd[:, c:c + 1],
                                         op0=ALU.subtract, op1=ALU.mult)
                ii.then_inc(sv4, 1)
                for k in range(8):
                    v.wait_ge(sp1, 8 * i + k + 1)
                    v.tensor_copy(out=hf[:, k, :], in_=pst[:]).then_inc(sv5, 1)
                for h in range(NH):
                    v.wait_ge(sp2, 16 * i + h + 1)
                    if 16 * i + h >= 2:
                        v.wait_ge(sqs[h % 2], 32 * ((16 * i + h) // 2))
                    if layer == 0:
                        v.tensor_scalar(qksb[:, h % 2, :], pqk[:, h % 2, :],
                                        qkb[:, h:h + 1], None,
                                        op0=ALU.add).then_inc(sa2, 1)
                    else:
                        v.tensor_copy(out=qksb[:, h % 2, :],
                                      in_=pqk[:, h % 2, :]).then_inc(sa2, 1)
                for c in range(4):
                    if 4 * i + c >= 2:
                        v.wait_ge(svs[c % 2], 16 * ((4 * i + c) // 2))
                    for hh in range(2):
                        v.wait_ge(sp3, 8 * i + 2 * c + hh + 1)
                        dst = vsb[:, c % 2, hh * 8:(hh + 1) * 8, :].rearrange("p h dd -> p (h dd)")
                        if layer == 0:
                            ii = v.tensor_tensor(out=dst, in0=pvv[:, (2 * c + hh) % 2, :],
                                                 in1=vb[:, hh * 512:(hh + 1) * 512], op=ALU.add)
                        else:
                            ii = v.tensor_copy(out=dst, in_=pvv[:, (2 * c + hh) % 2, :])
                    ii.then_inc(sv6, 1)

        @b.scalar
        def _(s):
            vcol = bno[:].rearrange("p (c two) -> p c two", two=2)[:, :, 1:2].rearrange("p c o -> p (c o)")
            for i in range(NT):
                s.wait_ge(sv2, i + 1)
                s.activation(out=sd[:], in_=vcol, func=AF.Sqrt, bias=epst[:, 0:1]).then_inc(sa1, 1)

        @b.tensor
        def _(t):
            t.wait_ge(pa, 9 if layer == 0 else 1)
            if layer == 1:
                t.wait_ge(pdup, 32)
                t.wait_ge(pvw, 2)
            for i in range(NT):
                t.wait_ge(sv4, i + 1)
                for k in range(8):
                    if 8 * i + k >= 1:
                        t.wait_ge(sv5, 8 * i + k)
                    for c in range(4):
                        ii = t.transpose(pst[:, c * 128:(c + 1) * 128],
                                         htok[:, c, k * 128:(k + 1) * 128], idb[:])
                    ii.then_inc(sp1, 1)
                t.wait_ge(sv5, 8 * (i + 1))
                for h in range(NH):
                    if 16 * i + h >= 2:
                        t.wait_ge(sa2, 16 * i + h - 1)
                    if layer == 0:
                        for k in range(8):
                            ii = t.matmul(pqk[:, h % 2, :], qkwb[:, k, h, :], hf[:, k, :],
                                          start=(k == 0), stop=(k == 7))
                    else:
                        hr = (h % 2) * 64
                        ii = t.matmul(pqk[:, h % 2, :], wqkT[hr:hr + 64, :],
                                      hf[hr:hr + 64, h // 2, :], start=True, stop=True)
                    ii.then_inc(sp2, 1)
                for c in range(4):
                    for hh in range(2):
                        j = 8 * i + 2 * c + hh
                        if j >= 2:
                            t.wait_ge(sv6, 4 * i + c)
                        if layer == 0:
                            for k in range(8):
                                ii = t.matmul(pvv[:, (2 * c + hh) % 2, :],
                                              hf[:, k, c * 128:(c + 1) * 128],
                                              vwb[:, k, hh * 512:(hh + 1) * 512],
                                              start=(k == 0), stop=(k == 7))
                        else:
                            for k in range(8):
                                ii = t.matmul(pvv[:, (2 * c + hh) % 2, :],
                                              hf[:, k, c * 128:(c + 1) * 128],
                                              vwb[:, k, hh * 512:(hh + 1) * 512],
                                              start=(k == 0), stop=(k == 7))
                        ii.then_inc(sp3, 1)

        @b.gpsimd
        def _(g):
            vvr = d["vv"].rearrange("h (n p) dd -> n p h dd", p=128)
            for i in range(NT):
                for h in range(NH):
                    g.wait_ge(sa2, 16 * i + h + 1)
                    g.dma_start(out=d["qT"][h][:, i * 512:(i + 1) * 512],
                                in_=qksb[0:64, h % 2, :]).then_inc(sqs[h % 2], 16)
                    g.dma_start(out=d["kT"][h][:, i * 512:(i + 1) * 512],
                                in_=qksb[64:128, h % 2, :]).then_inc(sqs[h % 2], 16)
                for c in range(4):
                    g.wait_ge(sv6, 4 * i + c + 1)
                    g.dma_start(out=vvr[4 * i + c], in_=vsb[:, c % 2]).then_inc(svs[c % 2], 16)
            g.wait_ge(sqs[0], 32 * NT * NH // 2)
            g.wait_ge(sqs[1], 32 * NT * NH // 2)
            g.wait_ge(svs[0], 16 * 2 * NT)
            g.wait_ge(svs[1], 16 * 2 * NT)
    es.close()



def ph_attn(ctx, d, layer):
    nc = ctx.nc
    tg = "a%d" % layer
    es = contextlib.ExitStack()
    E = es.enter_context
    kbuf = E(nc.sbuf_tensor(tg + "kbuf", [65, 2, T], BF16))
    qbuf = E(nc.sbuf_tensor(tg + "qbuf", [64, 2, T], BF16))
    vab = E(nc.sbuf_tensor(tg + "vab", [128, 2, 32, 65], BF16))
    qa = E(nc.sbuf_tensor(tg + "qa", [65, 2, 512], BF16))
    cmax = E(nc.sbuf_tensor(tg + "cmax", [128, 512], F32))
    cmaxb = E(nc.sbuf_tensor(tg + "cmaxb", [128, 512], BF16))
    negq = E(nc.sbuf_tensor(tg + "negq", [128, 4], BF16))
    PT = E(nc.sbuf_tensor(tg + "PT", [128, 32, 512], BF16))
    zrow = E(nc.sbuf_tensor(tg + "zrow", [1, 512], F32))
    zbsb = E(nc.sbuf_tensor(tg + "zbsb", [64, 512], F32))
    aosb = E(nc.sbuf_tensor(tg + "aosb", [64, 2, 512], BF16))
    binm = E(nc.sbuf_tensor(tg + "binm", [128, 4, 512], BF16))
    onesf = E(nc.sbuf_tensor(tg + "onesf", [1, 64], F32))
    idb = E(nc.sbuf_tensor(tg + "aidb", [128, 128], BF16))
    ps1 = E(nc.psum_tensor(tg + "ps1", [128, 2, 512], F32))
    ps2 = E(nc.psum_tensor(tg + "ps2", [128, 2, 512], F32))
    pstat = E(nc.psum_tensor(tg + "pstat", [128, 1024], BF16))
    po = E(nc.psum_tensor(tg + "po", [128, 512], F32))
    pzb = E(nc.psum_tensor(tg + "pzb", [64, 512], F32))

    pg, = ctx.sems(tg, "pg")
    with nc.Block(tg + "prep") as b:
        @b.gpsimd
        def _(g):
            g.memset(onesf[:], 1.0)
            g.memset(kbuf[64:65, :, :], 1.0)
            g.memset(vab[:], 1.0)
            g.memset(idb[:], 0.0)
            g.memset(binm[:], 1.0).then_inc(pg, 1)
            g.wait_ge(pg, 1)
            g.affine_select(out=idb[:], in_=idb[:], compare_op=ALU.not_equal,
                            fill=1.0, base=0, pattern=[[-1, 128]],
                            channel_multiplier=1)
            for j in range(4):
                g.affine_select(out=binm[:, j, :], in_=binm[:, j, :],
                                compare_op=ALU.is_ge, fill=0.0, base=-128 * j,
                                pattern=[[1, 512]], channel_multiplier=-1)

    units = [(h, g) for h in range(NH) for g in range(8)]
    NU = len(units)

    def K_of(g):
        return 4 * g + 4

    P1 = [0]
    for (h, g) in units:
        P1.append(P1[-1] + K_of(g))

    (mm1, vm, ac1, pt, vr, vq, mm2, ae, vd, spo, vz, pz, vz2, va) = ctx.sems(
        tg, "mm1", "vm", "ac1", "pt", "vr", "vq", "mm2", "ae", "vd",
        "spo", "vz", "pz", "vz2", "va")
    ld = ctx.sems(tg, "ld0", "ld1")
    gs = ctx.sems(tg, "gs0", "gs1")

    with nc.Block(tg + "main") as b:
        @b.sync
        def _(s):
            for h in range(NH):
                if h >= 2:
                    s.wait_ge(spo, 8 * (h - 1))
                s.dma_start(out=kbuf[0:64, h % 2, :], in_=d["kT"][h]).then_inc(ld[h % 2], 16)
                s.dma_start(out=qbuf[:, h % 2, :], in_=d["qT"][h]).then_inc(ld[h % 2], 16)
                s.dma_start(out=vab[:, h % 2, :, 0:64],
                            in_=d["vv"][h].rearrange("(kt p) dd -> p kt dd", p=128)).then_inc(ld[h % 2], 16)

        @b.tensor
        def _(t):
            t.wait_ge(pg, 1)
            for u, (h, g) in enumerate(units):
                K = K_of(g)
                if g == 0:
                    t.wait_ge(ld[h % 2], 48 * (h // 2 + 1))
                for kt in range(K):
                    if P1[u] + kt >= 2:
                        t.wait_ge(vm, P1[u] + kt - 1)
                    t.matmul(ps1[:, kt % 2, :], kbuf[0:64, h % 2, kt * 128:(kt + 1) * 128],
                             qbuf[:, h % 2, g * 512:(g + 1) * 512],
                             start=True, stop=True).then_inc(mm1, 1)
                t.wait_ge(ac1, u + 1)
                if u >= 1:
                    t.wait_ge(vq, u)
                for c in range(4):
                    ii = t.transpose(pstat[:, c * 128:(c + 1) * 128],
                                     cmaxb[:, c * 128:(c + 1) * 128], idb[:])
                ii.then_inc(pt, 1)
                t.wait_ge(vr, u + 1)
                for c in range(4):
                    ii = t.transpose(pstat[0:1, 512 + c * 128:512 + (c + 1) * 128],
                                     negq[:, c:c + 1], idb[:])
                ii.then_inc(pt, 1)
                t.wait_ge(vq, u + 1)
                for kt in range(K):
                    if P1[u] + kt >= 2:
                        t.wait_ge(ae, P1[u] + kt - 1)
                    t.matmul(ps2[:, kt % 2, :], kbuf[0:65, h % 2, kt * 128:(kt + 1) * 128],
                             qa[:, u % 2, :], start=True, stop=True).then_inc(mm2, 1)
                t.wait_ge(ae, P1[u] + K)
                t.wait_ge(vd, 4 * (u + 1))
                if u >= 1:
                    t.wait_ge(va, u)
                for kt in range(K):
                    ii = t.matmul(po[0:65, :], vab[:, h % 2, kt, :],
                                  PT[:, kt, :], start=(kt == 0), stop=(kt == K - 1))
                ii.then_inc(spo, 1)
                t.wait_ge(vz, u + 1)
                if u >= 1:
                    t.wait_ge(vz2, u)
                t.matmul(pzb[:], onesf[:], zrow[:], start=True, stop=True).then_inc(pz, 1)

        @b.vector
        def _(v):
            for u, (h, g) in enumerate(units):
                K = K_of(g)
                if g == 0:
                    v.wait_ge(ld[h % 2], 48 * (h // 2 + 1))
                if u >= 2:
                    v.wait_ge(mm2, P1[u - 1])
                v.tensor_copy(out=qa[0:64, u % 2, :], in_=qbuf[:, h % 2, g * 512:(g + 1) * 512])
                if u >= 1:
                    v.wait_ge(ac1, u)
                for kt in range(K):
                    v.wait_ge(mm1, P1[u] + kt + 1)
                    if kt == 0:
                        v.tensor_copy(out=cmax[:], in_=ps1[:, 0, :]).then_inc(vm, 1)
                    else:
                        v.wait_ge(vm, P1[u] + kt)
                        v.tensor_tensor(out=cmax[:], in0=ps1[:, kt % 2, :], in1=cmax[:],
                                        op=ALU.max).then_inc(vm, 1)
                v.wait_ge(pt, 2 * u + 1)
                for c in range(4):
                    ii = v.reduce_max(negq[:, c:c + 1], pstat[:, c * 128:(c + 1) * 128],
                                      axis=AX.X, negate=True)
                ii.then_inc(vr, 1)
                v.wait_ge(pt, 2 * u + 2)
                v.tensor_copy(out=qa[64:65, u % 2, :], in_=pstat[0:1, 512:1024]).then_inc(vq, 1)
                for j in range(4):
                    kt = K - 4 + j
                    v.wait_ge(ae, P1[u] + kt + 1)
                    v.tensor_tensor(out=PT[:, kt, :], in0=PT[:, kt, :],
                                    in1=binm[:, j, :], op=ALU.mult).then_inc(vd, 1)
                v.wait_ge(spo, u + 1)
                if u >= 1:
                    v.wait_ge(pz, u)
                v.reciprocal(zrow[:], po[64:65, :]).then_inc(vz, 1)
                v.wait_ge(pz, u + 1)
                v.tensor_copy(out=zbsb[:], in_=pzb[0:64, :]).then_inc(vz2, 1)
                v.wait_ge(vz2, u + 1)
                if u >= 2:
                    v.wait_ge(gs[u % 2], 16 * (u // 2))
                v.tensor_tensor(out=aosb[:, u % 2, :], in0=po[0:64, :], in1=zbsb[:],
                                op=ALU.mult).then_inc(va, 1)

        @b.scalar
        def _(s):
            for u, (h, g) in enumerate(units):
                K = K_of(g)
                s.wait_ge(vm, P1[u] + K)
                if u >= 1:
                    s.wait_ge(pt, 2 * u - 1)
                s.activation(out=cmaxb[:], in_=cmax[:], func=AF.Copy).then_inc(ac1, 1)
                for kt in range(K):
                    s.wait_ge(mm2, P1[u] + kt + 1)
                    s.activation(out=PT[:, kt, :], in_=ps2[:, kt % 2, :],
                                 func=AF.Exp).then_inc(ae, 1)

        @b.gpsimd
        def _(g_):
            for u, (h, g) in enumerate(units):
                g_.wait_ge(va, u + 1)
                g_.dma_start(out=d["attnT"][h * 64:(h + 1) * 64, g * 512:(g + 1) * 512],
                             in_=aosb[:, u % 2, :]).then_inc(gs[u % 2], 16)
            g_.wait_ge(gs[0], 16 * NU // 2)
            g_.wait_ge(gs[1], 16 * NU // 2)
    es.close()


def ph_oproj(ctx, d, layer):
    nc = ctx.nc
    tg = "o%d" % layer
    es = contextlib.ExitStack()
    E = es.enter_context
    xin = d["x0"] if layer == 0 else d["x2"]
    xout = d["x1"] if layer == 0 else d["x3"]
    wsrc_name = "outw" if layer == 0 else "wu"
    bsrc_name = "outb" if layer == 0 else "bu"

    idb = E(nc.sbuf_tensor(tg + "idb", [128, 128], BF16))
    idf = E(nc.sbuf_tensor(tg + "idf", [128, 128], F32))
    owb = E(nc.sbuf_tensor(tg + "owb", [128, 8, DM], BF16))
    obt = E(nc.sbuf_tensor(tg + "obt", [128, 8], F32))
    wfo = E(nc.sbuf_tensor(tg + "wfo", [128, 2, DM], F32))
    bro = E(nc.sbuf_tensor(tg + "bro", [8, 128], F32))
    pprep = E(nc.psum_tensor(tg + "pprep", [128, 512], F32))
    pprepb = E(nc.psum_tensor(tg + "pprepb", [128, 512], BF16))

    pl, pg, pa, pp, pv, pp2, pv2, pbs = ctx.sems(tg, "pl", "pg", "pa", "pp", "pv", "pp2", "pv2", "pbs")
    pw = ctx.sems(tg, "pw0", "pw1")

    with nc.Block(tg + "prep") as b:
        @b.gpsimd
        def _(g):
            g.memset(idb[:], 0.0)
            g.memset(idf[:], 0.0).then_inc(pg, 1)
            g.wait_ge(pg, 1)
            g.affine_select(out=idb[:], in_=idb[:], compare_op=ALU.not_equal,
                            fill=1.0, base=0, pattern=[[-1, 128]],
                            channel_multiplier=1)
            g.affine_select(out=idf[:], in_=idf[:], compare_op=ALU.not_equal,
                            fill=1.0, base=0, pattern=[[-1, 128]],
                            channel_multiplier=1).then_inc(pg, 1)

        @b.sync
        def _(s):
            s.dma_start(out=bro[:], in_=d[bsrc_name].rearrange("(c k) -> c k", k=128)).then_inc(pbs, 16)
            wr = d[wsrc_name].rearrange("(c p) n -> c p n", p=128)
            for k in range(8):
                if k >= 2:
                    s.wait_ge(pa, k - 1)
                s.dma_start(out=wfo[:, k % 2, :], in_=wr[k]).then_inc(pw[k % 2], 16)

        if layer == 0:
            @b.scalar
            def _(s):
                for k in range(8):
                    s.wait_ge(pw[k % 2], 16 * (k // 2 + 1))
                    s.activation(out=owb[:, k, :], in_=wfo[:, k % 2, :],
                                 func=AF.Copy).then_inc(pa, 1)
        else:
            wtb = E(nc.sbuf_tensor(tg + "wtb", [128, 2, DM], BF16))

            @b.scalar
            def _(s):
                for k in range(8):
                    s.wait_ge(pw[k % 2], 16 * (k // 2 + 1))
                    if k >= 2:
                        s.wait_ge(pp, 8 * (k - 2) + 8)
                    s.activation(out=wtb[:, k % 2, :], in_=wfo[:, k % 2, :],
                                 func=AF.Copy).then_inc(pa, 1)

            @b.tensor
            def _(t):
                t.wait_ge(pg, 2)
                for k in range(8):
                    t.wait_ge(pa, k + 1)
                    for c in range(8):
                        if (8 * k + c) >= 2:
                            t.wait_ge(pv, 8 * k + c - 1)
                        t.transpose(pprepb[:, (c % 4) * 128:(c % 4 + 1) * 128],
                                    wtb[:, k % 2, c * 128:(c + 1) * 128],
                                    idb[:]).then_inc(pp, 1)

            @b.vector
            def _(v):
                for k in range(8):
                    for c in range(8):
                        v.wait_ge(pp, 8 * k + c + 1)
                        v.tensor_copy(out=owb[:, c, k * 128:(k + 1) * 128],
                                      in_=pprepb[:, (c % 4) * 128:(c % 4 + 1) * 128]).then_inc(pv, 1)

        @b.tensor
        def _(t):
            t.wait_ge(pbs, 16)
            t.wait_ge(pg, 2)
            t.transpose(pprep[:, 0:8], bro[:], idf[0:8, 0:8]).then_inc(pp2, 1)

        @b.vector
        def _(v):
            v.wait_ge(pp2, 1)
            v.tensor_copy(out=obt[:], in_=pprep[:, 0:8]).then_inc(pv2, 1)

    at = E(nc.sbuf_tensor(tg + "at", [128, 2, 8, 512], BF16))
    xti = E(nc.sbuf_tensor(tg + "xti", [128, 2, 4, DM], F32))
    xto = E(nc.sbuf_tensor(tg + "xto", [128, 2, 4, DM], F32))
    obsb = E(nc.sbuf_tensor(tg + "obsb", [128, 8, 512], BF16))
    ppj = E(nc.psum_tensor(tg + "ppj", [128, 2, 512], F32))
    ptr = E(nc.psum_tensor(tg + "ptr", [128, 512], BF16))

    pm, ve, pt2, vadd = ctx.sems(tg, "pm", "ve", "pt2", "vadd")
    lx = ctx.sems(tg, "lx0", "lx1")
    gso = ctx.sems(tg, "gso0", "gso1")
    NT = 8

    with nc.Block(tg + "main") as b:
        @b.sync
        def _(s):
            atr = d["attnT"].rearrange("(k p) t -> p k t", p=128)
            xr = xin.rearrange("(n c p) dd -> n p c dd", p=128, c=4)
            for i in range(NT):
                if i >= 2:
                    s.wait_ge(pm, 8 * (i - 1))
                s.dma_start(out=at[:, i % 2], in_=atr[:, :, i * 512:(i + 1) * 512]).then_inc(lx[i % 2], 16)
                if i >= 2:
                    s.wait_ge(vadd, 8 * (i - 1))
                s.dma_start(out=xti[:, i % 2], in_=xr[i]).then_inc(lx[i % 2], 16)

        @b.tensor
        def _(t):
            t.wait_ge(pa, 8 if layer == 0 else 0)
            if layer == 1:
                t.wait_ge(pv, 64)
            for i in range(NT):
                t.wait_ge(lx[i % 2], 32 * (i // 2 + 1))
                for mo in range(8):
                    j = 8 * i + mo
                    if j >= 2:
                        t.wait_ge(ve, j - 1)
                    for k in range(8):
                        ii = t.matmul(ppj[:, mo % 2, :], owb[:, k, mo * 128:(mo + 1) * 128],
                                      at[:, i % 2, k, :], start=(k == 0), stop=(k == 7))
                    ii.then_inc(pm, 1)
                for mo in range(8):
                    j = 8 * i + mo
                    t.wait_ge(ve, j + 1)
                    if j >= 1:
                        t.wait_ge(vadd, j)
                    for c in range(4):
                        ii = t.transpose(ptr[:, c * 128:(c + 1) * 128],
                                         obsb[:, mo, c * 128:(c + 1) * 128], idb[:])
                    ii.then_inc(pt2, 1)

        @b.vector
        def _(v):
            v.wait_ge(pv2, 1)
            for i in range(NT):
                for mo in range(8):
                    j = 8 * i + mo
                    v.wait_ge(pm, j + 1)
                    if i >= 1:
                        v.wait_ge(pt2, 8 * (i - 1) + mo + 1)
                    v.tensor_scalar(obsb[:, mo, :], ppj[:, mo % 2, :],
                                    obt[:, mo:mo + 1], None, op0=ALU.add).then_inc(ve, 1)
                v.wait_ge(lx[i % 2], 32 * (i // 2 + 1))
                if i >= 2:
                    v.wait_ge(gso[i % 2], 16 * (i // 2))
                for mo in range(8):
                    j = 8 * i + mo
                    v.wait_ge(pt2, j + 1)
                    for c in range(4):
                        ii = v.tensor_tensor(out=xto[:, i % 2, c, mo * 128:(mo + 1) * 128],
                                             in0=ptr[:, c * 128:(c + 1) * 128],
                                             in1=xti[:, i % 2, c, mo * 128:(mo + 1) * 128],
                                             op=ALU.add)
                    ii.then_inc(vadd, 1)

        @b.gpsimd
        def _(g):
            xo = xout.rearrange("(n c p) dd -> n p c dd", p=128, c=4)
            for i in range(NT):
                g.wait_ge(vadd, 8 * (i + 1))
                g.dma_start(out=xo[i], in_=xto[:, i % 2]).then_inc(gso[i % 2], 16)
            g.wait_ge(gso[0], 16 * NT // 2)
            g.wait_ge(gso[1], 16 * NT // 2)
    es.close()


def ph_moeprep(ctx, d):
    nc = ctx.nc
    tg = "mp"
    es = contextlib.ExitStack()
    E = es.enter_context
    wfm = E(nc.sbuf_tensor(tg + "wfm", [128, 2, 2048], F32))
    wbm = E(nc.sbuf_tensor(tg + "wbm", [128, 2, 2048], BF16))
    mc, = ctx.sems(tg, "mc")
    ml = ctx.sems(tg, "ml0", "ml1")
    ms = ctx.sems(tg, "ms0", "ms1")

    units = []
    for name_s, name_d, nchunk, width in (("w1c", "w1b", 8, 2048),
                                          ("w3c", "w3b", 8, 2048),
                                          ("w2c", "w2b", 16, 1024)):
        for e in range(9):
            sv = d[name_s][e].rearrange("(c p) n -> c p n", p=128)
            dv = d[name_d][e].rearrange("(c p) n -> c p n", p=128)
            for rc in range(nchunk):
                units.append((sv[rc], dv[rc], width))
    NU = len(units)

    with nc.Block(tg + "main") as b:
        @b.sync
        def _(s):
            for j, (sv, dv, w) in enumerate(units):
                if j >= 2:
                    s.wait_ge(mc, j - 1)
                s.dma_start(out=wfm[:, j % 2, 0:w], in_=sv).then_inc(ml[j % 2], 16)

        @b.scalar
        def _(s):
            for j, (sv, dv, w) in enumerate(units):
                s.wait_ge(ml[j % 2], 16 * (j // 2 + 1))
                if j >= 2:
                    s.wait_ge(ms[j % 2], 16 * (j // 2))
                s.activation(out=wbm[:, j % 2, 0:w], in_=wfm[:, j % 2, 0:w],
                             func=AF.Copy).then_inc(mc, 1)

        @b.gpsimd
        def _(g):
            for j, (sv, dv, w) in enumerate(units):
                g.wait_ge(mc, j + 1)
                g.dma_start(out=dv, in_=wbm[:, j % 2, 0:w]).then_inc(ms[j % 2], 16)
            g.wait_ge(ms[0], 16 * ((NU + 1) // 2))
            g.wait_ge(ms[1], 16 * (NU // 2))
    es.close()


def ph_moe(ctx, d):
    nc = ctx.nc
    tg = "m"
    es = contextlib.ExitStack()
    E = es.enter_context
    idf = E(nc.sbuf_tensor(tg + "idf", [128, 128], F32))
    idb = E(nc.sbuf_tensor(tg + "idb", [128, 128], BF16))
    epst = E(nc.sbuf_tensor(tg + "epst", [128, 1], F32))
    ones1 = E(nc.sbuf_tensor(tg + "ones1", [1, 128], F32))
    rsrc = E(nc.sbuf_tensor(tg + "rsrc", [8, DM], F32))
    ebr = E(nc.sbuf_tensor(tg + "ebr", [1, 8], F32))
    rwT = E(nc.sbuf_tensor(tg + "rwT", [128, 8, 8], F32))
    ebb = E(nc.sbuf_tensor(tg + "ebb", [128, 8], F32))
    xt = E(nc.sbuf_tensor(tg + "xt", [128, 2, 4, DM], F32))
    bn6 = E(nc.sbuf_tensor(tg + "bn6", [128, 48], F32))
    bno = E(nc.sbuf_tensor(tg + "bno", [128, 8], F32))
    sd = E(nc.sbuf_tensor(tg + "sd", [128, 4], F32))
    rstd = E(nc.sbuf_tensor(tg + "rstd", [128, 4], F32))
    htf = E(nc.sbuf_tensor(tg + "htf", [128, 4, DM], F32))
    hf32 = E(nc.sbuf_tensor(tg + "hf32", [128, 8, 512], F32))
    hbf = E(nc.sbuf_tensor(tg + "hbf", [128, 8, 512], BF16))
    lgs = E(nc.sbuf_tensor(tg + "lgs", [128, 32], F32))
    m1 = E(nc.sbuf_tensor(tg + "m1", [128, 4], F32))
    m1n = E(nc.sbuf_tensor(tg + "m1n", [128, 4], F32))
    m2 = E(nc.sbuf_tensor(tg + "m2", [128, 4], F32))
    sel = E(nc.sbuf_tensor(tg + "sel", [128, 32], F32))
    msk = E(nc.sbuf_tensor(tg + "msk", [128, 32], F32))
    e1t = E(nc.sbuf_tensor(tg + "e1t", [128, 32], F32))
    wun = E(nc.sbuf_tensor(tg + "wun", [128, 32], F32))
    zz = E(nc.sbuf_tensor(tg + "zz", [128, 4], F32))
    zrr = E(nc.sbuf_tensor(tg + "zrr", [128, 4], F32))
    cwT = E(nc.sbuf_tensor(tg + "cwT", [128, 4, 8], F32))
    cwr = E(nc.sbuf_tensor(tg + "cwr", [8, 512], F32))
    cw0 = E(nc.sbuf_tensor(tg + "cw0", [1, 2, 512], F32))
    cwb = E(nc.sbuf_tensor(tg + "cwb", [128, 8, 512], BF16))
    w1s = E(nc.sbuf_tensor(tg + "w1s", [128, 2, 8, 512], BF16))
    w3s = E(nc.sbuf_tensor(tg + "w3s", [128, 2, 8, 512], BF16))
    w2s = E(nc.sbuf_tensor(tg + "w2s", [128, 16, 1024], BF16))
    sg = E(nc.sbuf_tensor(tg + "sg", [128, 2, 512], F32))
    eo1 = E(nc.sbuf_tensor(tg + "eo1", [128, 2, 512], BF16))
    eo = E(nc.sbuf_tensor(tg + "eo", [128, 16, 512], BF16))
    acc = E(nc.sbuf_tensor(tg + "acc", [128, 8, 512], F32))
    accb = E(nc.sbuf_tensor(tg + "accb", [128, 8, 512], BF16))
    pmisc = E(nc.psum_tensor(tg + "pmisc", [128, 512], F32))
    pg_ = E(nc.psum_tensor(tg + "pg", [128, 512], F32))
    pu_ = E(nc.psum_tensor(tg + "pu", [128, 512], F32))
    pp2 = E(nc.psum_tensor(tg + "pp2", [128, 2, 512], F32))
    pcwt = E(nc.psum_tensor(tg + "pcwt", [8, 512], F32))
    pcb = E(nc.psum_tensor(tg + "pcb", [128, 512], F32))
    pepi = E(nc.psum_tensor(tg + "pepi", [128, 512], BF16))

    EO = [8, 0, 1, 2, 3, 4, 5, 6, 7]

    (pg, mrw, meb, mpp, mvv, mv1, ma1, mv4, mp1, mv5, mhb, mpr, mvm, mae,
     mcw1, mpc, mvr, mpb, macw, mpg, mpu, mas, mve, mps, mpw, mva, mab,
     mpe, mvp, mself) = ctx.sems(
        tg, "pg", "mrw", "meb", "mpp", "mvv", "mv1", "ma1", "mv4", "mp1",
        "mv5", "mhb", "mpr", "mvm", "mae", "mcw1", "mpc", "mvr", "mpb",
        "macw", "mpg", "mpu", "mas", "mve", "mps", "mpw", "mva", "mab",
        "mpe", "mvp", "mself")
    mx = ctx.sems(tg, "mx0", "mx1")
    mw1 = ctx.sems(tg, "mw10", "mw11")
    mw3 = ctx.sems(tg, "mw30", "mw31")
    mw2, = ctx.sems(tg, "mw2")
    msx = ctx.sems(tg, "msx0", "msx1")
    mcw = ctx.sems(tg, "mcw0", "mcw1s")

    with nc.Block(tg + "prep") as b:
        @b.gpsimd
        def _(g):
            g.memset(epst[:], 1.0e-5)
            g.memset(ones1[:], 1.0)
            g.memset(idf[:], 0.0)
            g.memset(idb[:], 0.0).then_inc(pg, 1)
            g.wait_ge(pg, 1)
            g.affine_select(out=idf[:], in_=idf[:], compare_op=ALU.not_equal,
                            fill=1.0, base=0, pattern=[[-1, 128]],
                            channel_multiplier=1)
            g.affine_select(out=idb[:], in_=idb[:], compare_op=ALU.not_equal,
                            fill=1.0, base=0, pattern=[[-1, 128]],
                            channel_multiplier=1).then_inc(pg, 1)

        @b.sync
        def _(s):
            s.dma_start(out=rsrc[:], in_=d["rw"]).then_inc(mrw, 16)
            s.dma_start(out=ebr[:], in_=d["eb"][None, :]).then_inc(meb, 16)

        @b.tensor
        def _(t):
            t.wait_ge(mrw, 16)
            t.wait_ge(pg, 2)
            for k in range(8):
                if k >= 1:
                    t.wait_ge(mvv, k)
                t.transpose(pmisc[:, 0:8], rsrc[:, k * 128:(k + 1) * 128],
                            idf[0:8, 0:8]).then_inc(mpp, 1)
            t.wait_ge(meb, 16)
            t.wait_ge(mvv, 8)
            t.matmul(pmisc[:, 8:16], ones1[:], ebr[:], start=True,
                     stop=True).then_inc(mpp, 1)

        @b.vector
        def _(v):
            for k in range(8):
                v.wait_ge(mpp, k + 1)
                v.tensor_copy(out=rwT[:, k, :], in_=pmisc[:, 0:8]).then_inc(mvv, 1)
            v.wait_ge(mpp, 9)
            v.tensor_copy(out=ebb[:], in_=pmisc[:, 8:16]).then_inc(mvv, 1)

    NT = 8

    def IDX(tt, ei, mh):
        return 144 * tt + 16 * ei + mh

    def WIDX(tt, ei, mo):
        return 72 * tt + 8 * ei + mo

    with nc.Block(tg + "main") as b:
        @b.sync
        def _(s):
            xr = d["x1"].rearrange("(n c p) dd -> n p c dd", p=128, c=4)
            for tt in range(NT):
                if tt >= 2:
                    s.wait_ge(msx[tt % 2], 16 * (tt // 2))
                s.dma_start(out=xt[:, tt % 2], in_=xr[tt]).then_inc(mx[tt % 2], 16)
                for ei, e in enumerate(EO):
                    w1v = d["w1b"][e].rearrange("(c p) n -> p c n", p=128)
                    w3v = d["w3b"][e].rearrange("(c p) n -> p c n", p=128)
                    w2v = d["w2b"][e].rearrange("(c p) n -> p c n", p=128)
                    for q in range(4):
                        gate = 144 * tt + 16 * ei + 4 * q - 4
                        if gate >= 1:
                            s.wait_ge(mpg, gate)
                        s.dma_start(out=w1s[:, q % 2],
                                    in_=w1v[:, :, q * 512:(q + 1) * 512]).then_inc(mw1[q % 2], 16)
                        if gate >= 1:
                            s.wait_ge(mpu, gate)
                        s.dma_start(out=w3s[:, q % 2],
                                    in_=w3v[:, :, q * 512:(q + 1) * 512]).then_inc(mw3[q % 2], 16)
                    if 9 * tt + ei >= 1:
                        s.wait_ge(mpw, 72 * tt + 8 * ei)
                    s.dma_start(out=w2s[:], in_=w2v).then_inc(mw2, 16)

        @b.vector
        def _(v):
            ns = 0

            def selfw(ii):
                nonlocal ns
                ii.then_inc(mself, 1)
                ns += 1
                v.wait_ge(mself, ns)

            for tt in range(NT):
                v.wait_ge(mx[tt % 2], 16 * (tt // 2 + 1))
                for c in range(4):
                    ii = v.bn_stats(bn6[:, c * 12:c * 12 + 6], xt[:, tt % 2, c, 0:512])
                    ii = v.bn_stats(bn6[:, c * 12 + 6:c * 12 + 12], xt[:, tt % 2, c, 512:1024])
                selfw(ii)
                for c in range(4):
                    ii = v.bn_aggr(bno[:, 2 * c:2 * c + 2], bn6[:, c * 12:c * 12 + 12])
                ii.then_inc(mv1, 1)
                v.wait_ge(ma1, tt + 1)
                ii = v.reciprocal(rstd[:], sd[:])
                selfw(ii)
                if tt >= 1:
                    v.wait_ge(mp1, 8 * tt)
                for c in range(4):
                    ii = v.tensor_scalar(htf[:, c, :], xt[:, tt % 2, c, :],
                                         bno[:, 2 * c:2 * c + 1], rstd[:, c:c + 1],
                                         op0=ALU.subtract, op1=ALU.mult)
                ii.then_inc(mv4, 1)
                for k in range(8):
                    v.wait_ge(mp1, 8 * tt + k + 1)
                    v.tensor_copy(out=hf32[:, k, :], in_=pmisc[:]).then_inc(mv5, 1)
                for c in range(4):
                    v.wait_ge(mpr, 4 * tt + 4)
                    c8 = slice(c * 8, (c + 1) * 8)
                    ii = v.tensor_tensor(out=lgs[:, c8], in0=pmisc[:, 448 + c * 8:448 + (c + 1) * 8],
                                         in1=ebb[:], op=ALU.add)
                    selfw(ii)
                    ii = v.reduce_max(m1[:, c:c + 1], lgs[:, c8], axis=AX.X)
                    selfw(ii)
                    ii = v.tensor_scalar(sel[:, c8], lgs[:, c8], m1[:, c:c + 1], None,
                                         op0=ALU.is_ge)
                    selfw(ii)
                    ii = v.scalar_tensor_tensor(msk[:, c8], sel[:, c8], -1.0e30,
                                                lgs[:, c8], op0=ALU.mult, op1=ALU.add)
                    selfw(ii)
                    ii = v.reduce_max(m2[:, c:c + 1], msk[:, c8], axis=AX.X)
                    selfw(ii)
                    ii = v.tensor_scalar(sel[:, c8], lgs[:, c8], m2[:, c:c + 1], None,
                                         op0=ALU.is_ge)
                    selfw(ii)
                    v.tensor_scalar(m1n[:, c:c + 1], m1[:, c:c + 1], -1.0, None,
                                    op0=ALU.mult).then_inc(mvm, 1)
                    v.wait_ge(mae, 4 * tt + c + 1)
                    ii = v.tensor_tensor(out=wun[:, c8], in0=sel[:, c8], in1=e1t[:, c8],
                                         op=ALU.mult)
                    selfw(ii)
                    ii = v.reduce_sum(zz[:, c:c + 1], wun[:, c8], axis=AX.X)
                    selfw(ii)
                    ii = v.reciprocal(zrr[:, c:c + 1], zz[:, c:c + 1])
                    selfw(ii)
                    v.tensor_scalar(cwT[:, c, :], wun[:, c8], zrr[:, c:c + 1], None,
                                    op0=ALU.mult).then_inc(mcw1, 1)
                v.wait_ge(mpc, tt + 1)
                v.tensor_copy(out=cwr[:], in_=pcwt[:]).then_inc(mvr, 1)
                for ei, e in enumerate(EO):
                    for mh in range(16):
                        ix = IDX(tt, ei, mh)
                        if ei == 0:
                            v.wait_ge(mas, ix + 1)
                            v.wait_ge(mpu, ix + 1)
                            if tt >= 1:
                                v.wait_ge(mpw, 72 * tt)
                            v.tensor_tensor(out=eo[:, mh, :], in0=pu_[:],
                                            in1=sg[:, mh % 2, :], op=ALU.mult).then_inc(mve, 1)
                        else:
                            v.wait_ge(mas, ix + 1)
                            v.wait_ge(mpu, ix + 1)
                            six = 128 * tt + 16 * (ei - 1) + mh
                            if six >= 2:
                                v.wait_ge(mps, six - 1)
                            v.tensor_tensor(out=eo1[:, mh % 2, :], in0=pu_[:],
                                            in1=sg[:, mh % 2, :], op=ALU.mult).then_inc(mve, 1)
                    for mo in range(8):
                        wx = WIDX(tt, ei, mo)
                        v.wait_ge(mpw, wx + 1)
                        if ei == 0:
                            if tt >= 1:
                                v.wait_ge(mab, tt)
                            v.tensor_copy(out=acc[:, mo, :], in_=pp2[:, mo % 2, :]).then_inc(mva, 1)
                        else:
                            v.wait_ge(mva, WIDX(tt, ei - 1, mo) + 1)
                            v.tensor_tensor(out=acc[:, mo, :], in0=pp2[:, mo % 2, :],
                                            in1=acc[:, mo, :], op=ALU.add).then_inc(mva, 1)
                for mo in range(8):
                    v.wait_ge(mpe, 8 * tt + mo + 1)
                    for c in range(4):
                        ii = v.tensor_tensor(out=xt[:, tt % 2, c, mo * 128:(mo + 1) * 128],
                                             in0=pepi[:, c * 128:(c + 1) * 128],
                                             in1=xt[:, tt % 2, c, mo * 128:(mo + 1) * 128],
                                             op=ALU.add)
                    ii.then_inc(mvp, 1)

        @b.scalar
        def _(s):
            vcol = bno[:].rearrange("p (c two) -> p c two", two=2)[:, :, 1:2].rearrange("p c o -> p (c o)")
            for tt in range(NT):
                s.wait_ge(mv1, tt + 1)
                s.activation(out=sd[:], in_=vcol, func=AF.Sqrt, bias=epst[:, 0:1]).then_inc(ma1, 1)
                s.wait_ge(mv5, 8 * (tt + 1))
                s.activation(out=hbf[:], in_=hf32[:], func=AF.Copy).then_inc(mhb, 1)
                for c in range(4):
                    s.wait_ge(mvm, 4 * tt + c + 1)
                    s.activation(out=e1t[:, c * 8:(c + 1) * 8], in_=lgs[:, c * 8:(c + 1) * 8],
                                 func=AF.Exp, bias=m1n[:, c:c + 1]).then_inc(mae, 1)
                for e in range(8):
                    s.wait_ge(mpb, 8 * tt + e + 1)
                    if tt >= 1:
                        s.wait_ge(mps, 128 * (tt - 1) + 16 * (e + 1))
                    s.activation(out=cwb[:, e, :], in_=pcb[:], func=AF.Copy).then_inc(macw, 1)
                for ei, e in enumerate(EO):
                    for mh in range(16):
                        ix = IDX(tt, ei, mh)
                        s.wait_ge(mpg, ix + 1)
                        if ix >= 2:
                            s.wait_ge(mve, ix - 1)
                        s.activation(out=sg[:, mh % 2, :], in_=pg_[:],
                                     func=AF.Silu).then_inc(mas, 1)
                s.wait_ge(mva, 72 * (tt + 1))
                if tt >= 1:
                    s.wait_ge(mpe, 8 * tt)
                s.activation(out=accb[:], in_=acc[:], func=AF.Copy).then_inc(mab, 1)

        @b.tensor
        def _(t):
            t.wait_ge(mvv, 9)
            t.wait_ge(pg, 2)
            for tt in range(NT):
                t.wait_ge(mv4, tt + 1)
                if tt >= 1:
                    t.wait_ge(mcw1, 4 * tt)
                for k in range(8):
                    if 8 * tt + k >= 1:
                        t.wait_ge(mv5, 8 * tt + k)
                    for c in range(4):
                        ii = t.transpose(pmisc[:, c * 128:(c + 1) * 128],
                                         htf[:, c, k * 128:(k + 1) * 128], idf[:])
                    ii.then_inc(mp1, 1)
                t.wait_ge(mv5, 8 * (tt + 1))
                for c in range(4):
                    for k in range(8):
                        ii = t.matmul(pmisc[:, 448 + c * 8:448 + (c + 1) * 8],
                                      hf32[:, k, c * 128:(c + 1) * 128], rwT[:, k, :],
                                      start=(k == 0), stop=(k == 7))
                    ii.then_inc(mpr, 1)
                t.wait_ge(mcw1, 4 * (tt + 1))
                if tt >= 1:
                    t.wait_ge(mvr, tt)
                for c in range(4):
                    ii = t.transpose(pcwt[:, c * 128:(c + 1) * 128], cwT[:, c, :],
                                     idf[0:128, 0:128])
                ii.then_inc(mpc, 1)
                for e in range(8):
                    t.wait_ge(mcw[e % 2], 16 * ((8 * tt + e) // 2 + 1))
                    if 8 * tt + e >= 1:
                        t.wait_ge(macw, 8 * tt + e)
                    t.matmul(pcb[:], ones1[:], cw0[0:1, e % 2, :], start=True,
                             stop=True).then_inc(mpb, 1)
                t.wait_ge(mhb, tt + 1)
                for ei, e in enumerate(EO):
                    for q in range(4):
                        t.wait_ge(mw1[q % 2], 16 * (2 * (9 * tt + ei) + q // 2 + 1))
                        t.wait_ge(mw3[q % 2], 16 * (2 * (9 * tt + ei) + q // 2 + 1))
                        for mhl in range(4):
                            mh = 4 * q + mhl
                            ix = IDX(tt, ei, mh)
                            if ix >= 1:
                                t.wait_ge(mas, ix)
                            for k in range(8):
                                ii = t.matmul(pg_[:], w1s[:, q % 2, k, mhl * 128:(mhl + 1) * 128],
                                              hbf[:, k, :], start=(k == 0), stop=(k == 7))
                            ii.then_inc(mpg, 1)
                            if ix >= 1:
                                t.wait_ge(mve, ix)
                            for k in range(8):
                                ii = t.matmul(pu_[:], w3s[:, q % 2, k, mhl * 128:(mhl + 1) * 128],
                                              hbf[:, k, :], start=(k == 0), stop=(k == 7))
                            ii.then_inc(mpu, 1)
                    t.wait_ge(mw2, 16 * (9 * tt + ei + 1))
                    if ei == 0:
                        t.wait_ge(mve, 144 * tt + 16)
                    else:
                        t.wait_ge(mps, 128 * tt + 16 * ei)
                    for mo in range(8):
                        wx = WIDX(tt, ei, mo)
                        if wx >= 2:
                            t.wait_ge(mva, wx - 1)
                        for mh in range(16):
                            ii = t.matmul(pp2[:, mo % 2, :], w2s[:, mh, mo * 128:(mo + 1) * 128],
                                          eo[:, mh, :], start=(mh == 0), stop=(mh == 15))
                        ii.then_inc(mpw, 1)
                t.wait_ge(mab, tt + 1)
                for mo in range(8):
                    if 8 * tt + mo >= 1:
                        t.wait_ge(mvp, 8 * tt + mo)
                    for c in range(4):
                        ii = t.transpose(pepi[:, c * 128:(c + 1) * 128],
                                         accb[:, mo, c * 128:(c + 1) * 128], idb[:])
                    ii.then_inc(mpe, 1)

        @b.gpsimd
        def _(g):
            xo = d["x2"].rearrange("(n c p) dd -> n p c dd", p=128, c=4)
            for tt in range(NT):
                g.wait_ge(mvr, tt + 1)
                for e in range(8):
                    if 8 * tt + e >= 2:
                        g.wait_ge(mpb, 8 * tt + e - 1)
                    g.dma_start(out=cw0[0:1, e % 2, :], in_=cwr[e:e + 1, :]).then_inc(mcw[e % 2], 16)
                for ei, e in enumerate(EO):
                    if ei == 0:
                        continue
                    for mh in range(16):
                        ix = IDX(tt, ei, mh)
                        six = 128 * tt + 16 * (ei - 1) + mh
                        g.wait_ge(mve, ix + 1)
                        g.wait_ge(macw, 8 * tt + ei)
                        g.wait_ge(mpw, 72 * tt + 8 * ei)
                        g.tensor_tensor(out=eo[:, mh, :], in0=eo1[:, mh % 2, :],
                                        in1=cwb[:, e, :], op=ALU.mult).then_inc(mps, 1)
                g.wait_ge(mvp, 8 * (tt + 1))
                g.dma_start(out=xo[tt], in_=xt[:, tt % 2]).then_inc(msx[tt % 2], 16)
            g.wait_ge(msx[0], 16 * (NT // 2))
            g.wait_ge(msx[1], 16 * (NT // 2))
    es.close()


def ph_ffnprep(ctx, d):
    nc = ctx.nc
    tg = "fp"
    es = contextlib.ExitStack()
    E = es.enter_context
    idb = E(nc.sbuf_tensor(tg + "idb", [128, 128], BF16))
    wim = E(nc.sbuf_tensor(tg + "wim", [128, 2, 3072], I32))
    wbb = E(nc.sbuf_tensor(tg + "wbb", [128, 2, 3072], BF16))
    tst = E(nc.sbuf_tensor(tg + "tst", [128, 4, 128], BF16))
    ptb = E(nc.psum_tensor(tg + "ptb", [128, 4, 128], BF16))

    pg, fc, fp, fv = ctx.sems(tg, "pg", "fc", "fp", "fv")
    fl = ctx.sems(tg, "fl0", "fl1")
    fs = ctx.sems(tg, "fs0", "fs1", "fs2", "fs3")

    units = []
    for name_s, name_d, nrc, W in (("wg", "wgT", 24, 1024),
                                   ("wup", "wuT", 24, 1024),
                                   ("wd", "wdT", 8, 3072)):
        sv = d[name_s].rearrange("(c p) n -> c p n", p=128)
        for rc in range(nrc):
            units.append((sv[rc], d[name_d], rc, W // 128, W))

    blocks_before = [0]
    for u in units:
        blocks_before.append(blocks_before[-1] + u[3])
    NB = blocks_before[-1]

    groups_before = [x // 4 for x in blocks_before]
    NG = NB // 4
    fsg = fs[0]

    with nc.Block(tg + "main") as b:
        @b.gpsimd
        def _(g):
            g.memset(idb[:], 0.0).then_inc(pg, 1)
            g.wait_ge(pg, 1)
            g.affine_select(out=idb[:], in_=idb[:], compare_op=ALU.not_equal,
                            fill=1.0, base=0, pattern=[[-1, 128]],
                            channel_multiplier=1).then_inc(pg, 1)
            for j, (sv, dvt, rc, ncb, W) in enumerate(units):
                for gg in range(ncb // 4):
                    gi = groups_before[j] + gg
                    g.wait_ge(fv, gi + 1)
                    for q in range(4):
                        cb = 4 * gg + q
                        g.dma_start(out=dvt[cb * 128:(cb + 1) * 128, rc * 128:(rc + 1) * 128],
                                    in_=tst[:, q, :]).then_inc(fsg, 16)
            g.wait_ge(fsg, 64 * NG)

        @b.sync
        def _(s):
            for j, (sv, dvt, rc, ncb, W) in enumerate(units):
                if j >= 2:
                    s.wait_ge(fc, j - 1)
                s.dma_start(out=wim[:, j % 2, 0:W], in_=sv).then_inc(fl[j % 2], 16)

        @b.scalar
        def _(s):
            for j, (sv, dvt, rc, ncb, W) in enumerate(units):
                s.wait_ge(fl[j % 2], 16 * (j // 2 + 1))
                if j >= 2:
                    s.wait_ge(fp, groups_before[j - 1])
                s.activation(out=wbb[:, j % 2, 0:W], in_=wim[:, j % 2, 0:W],
                             func=AF.Copy, scale=1.0 / 127.0).then_inc(fc, 1)

        @b.tensor
        def _(t):
            t.wait_ge(pg, 2)
            for j, (sv, dvt, rc, ncb, W) in enumerate(units):
                t.wait_ge(fc, j + 1)
                for gg in range(ncb // 4):
                    gi = groups_before[j] + gg
                    if gi >= 1:
                        t.wait_ge(fv, gi)
                    for q in range(4):
                        cb = 4 * gg + q
                        ii = t.transpose(ptb[:, q, :], wbb[:, j % 2, cb * 128:(cb + 1) * 128],
                                         idb[:])
                    ii.then_inc(fp, 1)

        @b.vector
        def _(v):
            for j, (sv, dvt, rc, ncb, W) in enumerate(units):
                for gg in range(ncb // 4):
                    gi = groups_before[j] + gg
                    v.wait_ge(fp, gi + 1)
                    if gi >= 1:
                        v.wait_ge(fsg, 64 * gi)
                    v.tensor_copy(out=tst[:], in_=ptb[:]).then_inc(fv, 1)
    es.close()


def ph_ffn(ctx, d):
    nc = ctx.nc
    tg = "f"
    es = contextlib.ExitStack()
    E = es.enter_context
    idb = E(nc.sbuf_tensor(tg + "idb", [128, 128], BF16))
    epst = E(nc.sbuf_tensor(tg + "epst", [128, 1], F32))
    wgs = E(nc.sbuf_tensor(tg + "wgs", [128, 8, 3072], BF16))
    wus = E(nc.sbuf_tensor(tg + "wus", [128, 8, 3072], BF16))
    wds = E(nc.sbuf_tensor(tg + "wds", [128, 24, 1024], BF16))
    xt = E(nc.sbuf_tensor(tg + "xt", [128, 4, DM], F32))
    bn6 = E(nc.sbuf_tensor(tg + "bn6", [128, 48], F32))
    bno = E(nc.sbuf_tensor(tg + "bno", [128, 8], F32))
    sd = E(nc.sbuf_tensor(tg + "sd", [128, 4], F32))
    rstd = E(nc.sbuf_tensor(tg + "rstd", [128, 4], F32))
    htok = E(nc.sbuf_tensor(tg + "htok", [128, 4, DM], BF16))
    hbf = E(nc.sbuf_tensor(tg + "hbf", [128, 8, 512], BF16))
    sg = E(nc.sbuf_tensor(tg + "sg", [128, 1, 512], F32))
    h3 = E(nc.sbuf_tensor(tg + "h3", [128, 24, 512], BF16))
    obsb = E(nc.sbuf_tensor(tg + "obsb", [128, 4, 512], BF16))
    pst = E(nc.psum_tensor(tg + "pst", [128, 512], BF16))
    pg_ = E(nc.psum_tensor(tg + "pg", [128, 512], F32))
    pu_ = E(nc.psum_tensor(tg + "pu", [128, 512], F32))
    pp2 = E(nc.psum_tensor(tg + "pp2", [128, 2, 512], F32))
    pepi = E(nc.psum_tensor(tg + "pepi", [128, 512], BF16))

    (pg, fv1, fa1, fv4, fp1, fv5, fpg, fpu, fas, fve, fpw, fae, fpe, fvp,
     fself) = ctx.sems(tg, "pg", "fv1", "fa1", "fv4", "fp1", "fv5", "fpg",
                       "fpu", "fas", "fve", "fpw", "fae", "fpe", "fvp", "fself")
    fw, = ctx.sems(tg, "fw")
    fx = ctx.sems(tg, "fx0", "fx1")
    fsx = ctx.sems(tg, "fsx0", "fsx1")

    with nc.Block(tg + "prep") as b:
        @b.gpsimd
        def _(g):
            g.memset(epst[:], 1.0e-5)
            g.memset(idb[:], 0.0).then_inc(pg, 1)
            g.wait_ge(pg, 1)
            g.affine_select(out=idb[:], in_=idb[:], compare_op=ALU.not_equal,
                            fill=1.0, base=0, pattern=[[-1, 128]],
                            channel_multiplier=1).then_inc(pg, 1)

        @b.sync
        def _(s):
            s.dma_start(out=wgs[:], in_=d["wgT"].rearrange("(c p) n -> p c n", p=128)).then_inc(fw, 16)
            s.dma_start(out=wus[:], in_=d["wuT"].rearrange("(c p) n -> p c n", p=128)).then_inc(fw, 16)
            s.dma_start(out=wds[:], in_=d["wdT"].rearrange("(c p) n -> p c n", p=128)).then_inc(fw, 16)

    NT = 8
    with nc.Block(tg + "main") as b:
        @b.sync
        def _(s):
            xr = d["x3"].rearrange("(n c p) dd -> n p c dd", p=128, c=4)
            for tt in range(NT):
                if tt >= 1:
                    s.wait_ge(fsx[(tt - 1) % 2], 16 * ((tt - 1) // 2 + 1))
                s.dma_start(out=xt[:], in_=xr[tt]).then_inc(fx[tt % 2], 16)

        @b.vector
        def _(v):
            ns = 0

            def selfw(ii):
                nonlocal ns
                ii.then_inc(fself, 1)
                ns += 1
                v.wait_ge(fself, ns)

            for tt in range(NT):
                v.wait_ge(fx[tt % 2], 16 * (tt // 2 + 1))
                if tt >= 1:
                    v.wait_ge(fvp, 8 * tt)
                for c in range(4):
                    ii = v.bn_stats(bn6[:, c * 12:c * 12 + 6], xt[:, c, 0:512])
                    ii = v.bn_stats(bn6[:, c * 12 + 6:c * 12 + 12], xt[:, c, 512:1024])
                selfw(ii)
                for c in range(4):
                    ii = v.bn_aggr(bno[:, 2 * c:2 * c + 2], bn6[:, c * 12:c * 12 + 12])
                ii.then_inc(fv1, 1)
                v.wait_ge(fa1, tt + 1)
                ii = v.reciprocal(rstd[:], sd[:])
                selfw(ii)
                if tt >= 1:
                    v.wait_ge(fp1, 8 * tt)
                for c in range(4):
                    ii = v.tensor_scalar(htok[:, c, :], xt[:, c, :],
                                         bno[:, 2 * c:2 * c + 1], rstd[:, c:c + 1],
                                         op0=ALU.subtract, op1=ALU.mult)
                ii.then_inc(fv4, 1)
                for k in range(8):
                    v.wait_ge(fp1, 8 * tt + k + 1)
                    v.tensor_copy(out=hbf[:, k, :], in_=pst[:]).then_inc(fv5, 1)
                for mh in range(24):
                    ix = 24 * tt + mh
                    v.wait_ge(fas, ix + 1)
                    v.wait_ge(fpu, ix + 1)
                    if tt >= 1:
                        v.wait_ge(fpw, 8 * tt)
                    v.tensor_tensor(out=h3[:, mh, :], in0=pu_[:], in1=sg[:, 0, :],
                                    op=ALU.mult).then_inc(fve, 1)
                for mo in range(8):
                    v.wait_ge(fpe, 8 * tt + mo + 1)
                    for c in range(4):
                        ii = v.tensor_tensor(out=xt[:, c, mo * 128:(mo + 1) * 128],
                                             in0=pepi[:, c * 128:(c + 1) * 128],
                                             in1=xt[:, c, mo * 128:(mo + 1) * 128],
                                             op=ALU.add)
                    ii.then_inc(fvp, 1)

        @b.scalar
        def _(s):
            vcol = bno[:].rearrange("p (c two) -> p c two", two=2)[:, :, 1:2].rearrange("p c o -> p (c o)")
            for tt in range(NT):
                s.wait_ge(fv1, tt + 1)
                s.activation(out=sd[:], in_=vcol, func=AF.Sqrt, bias=epst[:, 0:1]).then_inc(fa1, 1)
                for mh in range(24):
                    ix = 24 * tt + mh
                    s.wait_ge(fpg, ix + 1)
                    if ix >= 1:
                        s.wait_ge(fve, ix)
                    s.activation(out=sg[:, 0, :], in_=pg_[:],
                                 func=AF.Silu).then_inc(fas, 1)
                for mo in range(8):
                    j = 8 * tt + mo
                    s.wait_ge(fpw, j + 1)
                    if j >= 4:
                        s.wait_ge(fpe, j - 3)
                    s.activation(out=obsb[:, mo % 4, :], in_=pp2[:, mo % 2, :],
                                 func=AF.Copy).then_inc(fae, 1)

        @b.tensor
        def _(t):
            t.wait_ge(pg, 2)
            t.wait_ge(fw, 48)
            for tt in range(NT):
                t.wait_ge(fv4, tt + 1)
                for k in range(8):
                    if 8 * tt + k >= 1:
                        t.wait_ge(fv5, 8 * tt + k)
                    for c in range(4):
                        ii = t.transpose(pst[:, c * 128:(c + 1) * 128],
                                         htok[:, c, k * 128:(k + 1) * 128], idb[:])
                    ii.then_inc(fp1, 1)
                t.wait_ge(fv5, 8 * (tt + 1))
                for mh in range(24):
                    ix = 24 * tt + mh
                    if ix >= 1:
                        t.wait_ge(fas, ix)
                    for k in range(8):
                        ii = t.matmul(pg_[:], wgs[:, k, mh * 128:(mh + 1) * 128],
                                      hbf[:, k, :], start=(k == 0), stop=(k == 7))
                    ii.then_inc(fpg, 1)
                    if ix >= 1:
                        t.wait_ge(fve, ix)
                    for k in range(8):
                        ii = t.matmul(pu_[:], wus[:, k, mh * 128:(mh + 1) * 128],
                                      hbf[:, k, :], start=(k == 0), stop=(k == 7))
                    ii.then_inc(fpu, 1)
                t.wait_ge(fve, 24 * (tt + 1))
                for mo in range(9):
                    if mo < 8:
                        j = 8 * tt + mo
                        if j >= 2:
                            t.wait_ge(fae, j - 1)
                        for mh in range(24):
                            ii = t.matmul(pp2[:, mo % 2, :], wds[:, mh, mo * 128:(mo + 1) * 128],
                                          h3[:, mh, :], start=(mh == 0), stop=(mh == 23))
                        ii.then_inc(fpw, 1)
                    if mo >= 1:
                        mt = mo - 1
                        t.wait_ge(fae, 8 * tt + mt + 1)
                        if 8 * tt + mt >= 1:
                            t.wait_ge(fvp, 8 * tt + mt)
                        for c in range(4):
                            ii = t.transpose(pepi[:, c * 128:(c + 1) * 128],
                                             obsb[:, mt % 4, c * 128:(c + 1) * 128], idb[:])
                        ii.then_inc(fpe, 1)

        @b.gpsimd
        def _(g):
            yo = d["y"].rearrange("(n c p) dd -> n p c dd", p=128, c=4)
            for tt in range(NT):
                g.wait_ge(fvp, 8 * (tt + 1))
                g.dma_start(out=yo[tt], in_=xt[:]).then_inc(fsx[tt % 2], 16)
            g.wait_ge(fsx[0], 16 * (NT // 2))
            g.wait_ge(fsx[1], 16 * (NT // 2))
    es.close()

EMIT = {
    "qkv0": lambda c, d: ph_qkv(c, d, 0),
    "qkv1": lambda c, d: ph_qkv(c, d, 1),
    "attn0": lambda c, d: ph_attn(c, d, 0),
    "attn1": lambda c, d: ph_attn(c, d, 1),
    "oproj0": lambda c, d: ph_oproj(c, d, 0),
    "oproj1": lambda c, d: ph_oproj(c, d, 1),
    "moeprep": lambda c, d: ph_moeprep(c, d),
    "moe": lambda c, d: ph_moe(c, d),
    "ffnprep": lambda c, d: ph_ffnprep(c, d),
    "ffn": lambda c, d: ph_ffn(c, d),
}

GROUPS = [["qkv0", "attn0", "oproj0"], ["moeprep", "moe"],
          ["qkv1", "attn1", "oproj1"], ["ffnprep", "ffn"]]


def build_group(phases):
    nc = bass.Bass("TRN2", target_bir_lowering=False)
    ctx = Ctx(nc)
    reads = set()
    writes = set()
    for p in phases:
        r, w = PHASE_IO[p]
        reads.update(x for x in r if x not in writes)
        writes.update(w)
    gi = [ALL_PHASES.index(p) for p in phases]
    later_reads = set()
    for p in ALL_PHASES:
        if ALL_PHASES.index(p) > max(gi) and p not in phases:
            later_reads.update(PHASE_IO[p][0])
    kinds = {}
    for name in set(reads) | set(writes):
        if name in writes:
            kinds[name] = ("ExternalOutput"
                           if (name in later_reads or TENSORS[name][2] == "out")
                           else "Internal")
        else:
            kinds[name] = "ExternalInput"
    d = {}
    for name, k in kinds.items():
        shape, dt, _ = TENSORS[name]
        d[name] = nc.dram_tensor(name, shape, dt, kind=k).ap()
    with ctx.stack:
        for p in phases:
            EMIT[p](ctx, d)
    ext_in = [n for n, k in kinds.items() if k == "ExternalInput"]
    ext_out = [n for n, k in kinds.items() if k == "ExternalOutput"]
    return nc, ext_in, ext_out


def prep_inputs(inputs):
    common = {
        "qkvw": np.ascontiguousarray(inputs["L0_attn_qkv_w"].reshape(DM, NH * 192)),
        "qkvb": np.ascontiguousarray(inputs["L0_attn_qkv_b"]),
        "outw": np.ascontiguousarray(inputs["L0_attn_out_w"].reshape(DM, DM)),
        "outb": np.ascontiguousarray(inputs["L0_attn_out_b"]),
        "rw": np.ascontiguousarray(inputs["L0_ffn_router_w"]),
        "eb": np.ascontiguousarray(inputs["L0_ffn_experts_bias"]),
        "w1c": np.ascontiguousarray(np.concatenate([inputs["L0_ffn_w1"], inputs["L0_ffn_sw1"]], axis=0)),
        "w3c": np.ascontiguousarray(np.concatenate([inputs["L0_ffn_w3"], inputs["L0_ffn_sw3"]], axis=0)),
        "w2c": np.ascontiguousarray(np.concatenate([inputs["L0_ffn_w2"], inputs["L0_ffn_sw2"]], axis=0)),
        "wq": np.ascontiguousarray(inputs["L1_attn_Wq"]),
        "wk": np.ascontiguousarray(inputs["L1_attn_Wk"]),
        "wv": np.ascontiguousarray(inputs["L1_attn_Wv"]),
        "wu": np.ascontiguousarray(inputs["L1_attn_Wu"]),
        "bu": np.ascontiguousarray(inputs["L1_attn_bu"]),
        "wg": np.ascontiguousarray(inputs["L1_ffn_w_gate"]),
        "wup": np.ascontiguousarray(inputs["L1_ffn_w_up"]),
        "wd": np.ascontiguousarray(inputs["L1_ffn_w_down"]),
    }
    pools = []
    for c in range(NCORE):
        p = dict(common)
        p["x0"] = np.ascontiguousarray(np.asarray(inputs["L0_attn_x"])[c])
        pools.append(p)
    return pools


_BUILD_CACHE = {}


def kernel(**inputs):
    pools = prep_inputs(inputs)
    for phases in GROUPS:
        key = tuple(phases)
        if key not in _BUILD_CACHE:
            _BUILD_CACHE[key] = build_group(phases)
        nc, ext_in, ext_out = _BUILD_CACHE[key]
        in_maps = [{n: pools[c][n] for n in ext_in} for c in range(NCORE)]
        res = run_bass_kernel_spmd(nc, in_maps, core_ids=list(range(NCORE)))
        for c in range(NCORE):
            for n in ext_out:
                pools[c][n] = res.results[c][n]
    return np.stack([pools[c]["y"] for c in range(NCORE)])


# version 05
# speedup vs baseline: 1.1840x; 1.1840x over previous
import contextlib
import numpy as np
import concourse.bass as bass
import concourse.mybir as mybir
from concourse.bass_utils import run_bass_kernel_spmd

F32 = mybir.dt.float32
BF16 = mybir.dt.bfloat16
I32 = mybir.dt.int32
AF = mybir.ActivationFunctionType
ALU = mybir.AluOpType
AX = mybir.AxisListType

T = 4096
DM = 1024
NH, HD = 16, 64
NCORE = 8
NEG = -1.0e38

TENSORS = {
    "x0":    ([T, DM], F32, "inp"),
    "qkvw":  ([DM, NH * 192], F32, "inp"),
    "qkvb":  ([NH, 192], F32, "inp"),
    "outw":  ([DM, DM], F32, "inp"),
    "outb":  ([DM], F32, "inp"),
    "rw":    ([8, DM], F32, "inp"),
    "eb":    ([8], F32, "inp"),
    "w1c":   ([9, DM, 2048], F32, "inp"),
    "w3c":   ([9, DM, 2048], F32, "inp"),
    "w2c":   ([9, 2048, DM], F32, "inp"),
    "wq":    ([HD, HD], F32, "inp"),
    "wk":    ([HD, HD], F32, "inp"),
    "wv":    ([HD, HD], F32, "inp"),
    "wu":    ([DM, DM], F32, "inp"),
    "bu":    ([DM], F32, "inp"),
    "wg":    ([3072, DM], I32, "inp"),
    "wup":   ([3072, DM], I32, "inp"),
    "wd":    ([DM, 3072], I32, "inp"),
    "qT":    ([NH, HD, T], BF16, "scr"),
    "kT":    ([NH, HD, T], BF16, "scr"),
    "vv":    ([NH, T, HD], BF16, "scr"),
    "attnT": ([DM, T], BF16, "scr"),
    "x1":    ([T, DM], F32, "scr"),
    "x2":    ([T, DM], F32, "scr"),
    "x3":    ([T, DM], F32, "scr"),
    "w1b":   ([9, DM, 2048], BF16, "scr"),
    "w3b":   ([9, DM, 2048], BF16, "scr"),
    "w2b":   ([9, 2048, DM], BF16, "scr"),
    "wgT":   ([DM, 3072], BF16, "scr"),
    "wuT":   ([DM, 3072], BF16, "scr"),
    "wdT":   ([3072, DM], BF16, "scr"),
    "y":     ([T, DM], F32, "out"),
}

PHASE_IO = {
    "qkv0":    (["x0", "qkvw", "qkvb"], ["qT", "kT", "vv"]),
    "attn0":   (["qT", "kT", "vv"], ["attnT"]),
    "oproj0":  (["attnT", "outw", "outb", "x0"], ["x1"]),
    "moeprep": (["w1c", "w3c", "w2c"], ["w1b", "w3b", "w2b"]),
    "moe":     (["x1", "rw", "eb", "w1b", "w3b", "w2b"], ["x2"]),
    "qkv1":    (["x2", "wq", "wk", "wv"], ["qT", "kT", "vv"]),
    "attn1":   (["qT", "kT", "vv"], ["attnT"]),
    "oproj1":  (["attnT", "wu", "bu", "x2"], ["x3"]),
    "ffnprep": (["wg", "wup", "wd"], ["wgT", "wuT", "wdT"]),
    "ffn":     (["x3", "wgT", "wuT", "wdT"], ["y"]),
}
ALL_PHASES = list(PHASE_IO)


class Ctx:
    def __init__(self, nc):
        self.nc = nc
        self.stack = contextlib.ExitStack()

    def sems(self, tag, *names):
        return [self.nc.alloc_semaphore(tag + "_" + n) for n in names]


def ph_qkv(ctx, d, layer):
    nc = ctx.nc
    tg = "q%d" % layer
    xin = d["x0"] if layer == 0 else d["x2"]
    es = contextlib.ExitStack()
    E = es.enter_context
    idf = E(nc.sbuf_tensor(tg + "idf", [128, 128], F32))
    epst = E(nc.sbuf_tensor(tg + "epst", [128, 1], F32))
    idb = E(nc.sbuf_tensor(tg + "idb", [128, 128], BF16))
    pbias = E(nc.psum_tensor(tg + "pbias", [128, 512], F32))
    if layer == 0:
        qkwb = E(nc.sbuf_tensor(tg + "qkwb", [128, 8, NH, 128], BF16))
        vwb = E(nc.sbuf_tensor(tg + "vwb", [128, 8, DM], BF16))
        qkb = E(nc.sbuf_tensor(tg + "qkb", [128, NH], F32))
        vb = E(nc.sbuf_tensor(tg + "vb", [128, DM], F32))
        wf = E(nc.sbuf_tensor(tg + "wf", [128, 2, 3072], F32))
        bsrc = E(nc.sbuf_tensor(tg + "bsrc", [16, 192], F32))
        vrow = E(nc.sbuf_tensor(tg + "vrow", [1, DM], F32))
        ones1 = E(nc.sbuf_tensor(tg + "ones1", [1, 128], F32))
    else:
        wsrc = E(nc.sbuf_tensor(tg + "wsrc", [64, 192], F32))
        wqkT = E(nc.sbuf_tensor(tg + "wqkT", [128, 128], BF16))
        wvT = E(nc.sbuf_tensor(tg + "wvT", [128, 64], BF16))
        vwb = E(nc.sbuf_tensor(tg + "vwb", [128, 8, DM], BF16))

    pl, pg, pa, pp, pbs, pvr, pdup, pvw = ctx.sems(tg, "pl", "pg", "pa", "pp", "pbs", "pvr", "pdup", "pvw")
    pw = ctx.sems(tg, "pw0", "pw1")

    with nc.Block(tg + "prep") as b:
        @b.gpsimd
        def _(g):
            g.memset(epst[:], 1.0e-5)
            g.memset(idf[:], 0.0).then_inc(pg, 1)
            g.wait_ge(pg, 1)
            g.affine_select(out=idf[:], in_=idf[:], compare_op=ALU.not_equal,
                            fill=1.0, base=0, pattern=[[-1, 128]],
                            channel_multiplier=1)
            g.memset(idb[:], 0.0).then_inc(pg, 1)
            g.wait_ge(pg, 2)
            g.affine_select(out=idb[:], in_=idb[:], compare_op=ALU.not_equal,
                            fill=1.0, base=0, pattern=[[-1, 128]],
                            channel_multiplier=1).then_inc(pg, 1)
            if layer == 0:
                g.memset(ones1[:], 1.0).then_inc(pg, 1)


        if layer == 0:
            @b.sync
            def _(s):
                s.dma_start(out=bsrc[:], in_=d["qkvb"]).then_inc(pbs, 16)
                s.dma_start(out=vrow[0:1, :].rearrange("o (h dd) -> o h dd", h=NH),
                            in_=d["qkvb"][None, :, 128:192]).then_inc(pvr, 16)
                qw = d["qkvw"].rearrange("(c p) n -> c p n", p=128)
                for k in range(8):
                    if k >= 2:
                        s.wait_ge(pa, k - 1)
                    s.dma_start(out=wf[:, k % 2, :], in_=qw[k]).then_inc(pw[k % 2], 16)

            @b.scalar
            def _(s):
                for k in range(8):
                    s.wait_ge(pw[k % 2], 16 * (k // 2 + 1))
                    wv_ = wf[:, k % 2, :].rearrange("p (h s) -> p h s", h=NH)
                    s.activation(out=qkwb[:, k, :, 0:64], in_=wv_[:, :, 0:64],
                                 func=AF.Copy, scale=0.125)
                    s.activation(out=qkwb[:, k, :, 64:128], in_=wv_[:, :, 64:128],
                                 func=AF.Copy)
                    s.activation(out=vwb[:, k, :].rearrange("p (h s) -> p h s", h=NH),
                                 in_=wv_[:, :, 128:192], func=AF.Copy).then_inc(pa, 1)
                s.wait_ge(pp, 1)
                s.activation(out=qkb[0:64, :], in_=pbias[0:64, 0:NH], func=AF.Copy,
                             scale=0.125)
                s.activation(out=qkb[64:128, :], in_=pbias[64:128, 0:NH],
                             func=AF.Copy).then_inc(pa, 1)

            @b.tensor
            def _(t):
                t.wait_ge(pbs, 16)
                t.wait_ge(pg, 3)
                t.transpose(pbias[:, 0:16], bsrc[:, 0:128], idf[0:16, 0:16]).then_inc(pp, 1)
        else:
            @b.sync
            def _(s):
                s.dma_start(out=wsrc[:, 0:64], in_=d["wq"]).then_inc(pl, 16)
                s.dma_start(out=wsrc[:, 64:128], in_=d["wk"]).then_inc(pl, 16)
                s.dma_start(out=wsrc[:, 128:192], in_=d["wv"]).then_inc(pl, 16)

            @b.tensor
            def _(t):
                t.wait_ge(pl, 48)
                t.wait_ge(pg, 3)
                t.transpose(pbias[0:64, 0:64], wsrc[:, 0:64], idf[0:64, 0:64])
                t.transpose(pbias[0:64, 64:128], wsrc[:, 64:128], idf[0:64, 0:64])
                t.transpose(pbias[0:64, 128:192], wsrc[:, 128:192], idf[0:64, 0:64]).then_inc(pp, 1)

            @b.scalar
            def _(s):
                s.wait_ge(pp, 1)
                sc = float(DM) ** (-0.25)
                s.activation(out=wqkT[0:64, 0:64], in_=pbias[0:64, 0:64], func=AF.Copy, scale=sc)
                s.activation(out=wqkT[0:64, 64:128], in_=pbias[0:64, 64:128], func=AF.Copy, scale=sc)
                s.activation(out=wvT[0:64, :], in_=pbias[0:64, 128:192], func=AF.Copy).then_inc(pa, 1)

            @b.gpsimd
            def _(g):
                g.memset(vwb[:], 0.0).then_inc(pvw, 1)
                g.wait_ge(pa, 1)
                g.dma_start(out=wqkT[64:128, :], in_=wqkT[0:64, :]).then_inc(pdup, 16)
                g.dma_start(out=wvT[64:128, :], in_=wvT[0:64, :]).then_inc(pdup, 16)

            @b.vector
            def _(v):
                v.wait_ge(pvw, 1)
                v.wait_ge(pdup, 32)
                for k in range(8):
                    v.tensor_copy(out=vwb[0:64, k, k * 128:k * 128 + 64],
                                  in_=wvT[0:64, :])
                    ii = v.tensor_copy(out=vwb[64:128, k, k * 128 + 64:k * 128 + 128],
                                       in_=wvT[64:128, :])
                ii.then_inc(pvw, 1)

    if layer == 0:
        with nc.Block(tg + "prep2") as b:
            pv2, = ctx.sems(tg, "pv2")

            @b.tensor
            def _(t):
                t.wait_ge(pvr, 16)
                t.matmul(pbias[:, 0:512], ones1[:], vrow[:, 0:512], start=True, stop=True).then_inc(pv2, 1)
                t.wait_ge(pv2, 2)
                t.matmul(pbias[:, 0:512], ones1[:], vrow[:, 512:1024], start=True, stop=True).then_inc(pv2, 1)

            @b.vector
            def _(v):
                v.wait_ge(pv2, 1)
                v.tensor_copy(out=vb[:, 0:512], in_=pbias[:, 0:512]).then_inc(pv2, 1)
                v.wait_ge(pv2, 3)
                v.tensor_copy(out=vb[:, 512:1024], in_=pbias[:, 0:512]).then_inc(pv2, 1)

    xt = E(nc.sbuf_tensor(tg + "xt", [128, 2, 4, DM], F32))
    bn6 = E(nc.sbuf_tensor(tg + "bn6", [128, 48], F32))
    bno = E(nc.sbuf_tensor(tg + "bno", [128, 8], F32))
    sd = E(nc.sbuf_tensor(tg + "sd", [128, 4], F32))
    rstd = E(nc.sbuf_tensor(tg + "rstd", [128, 4], F32))
    htok = E(nc.sbuf_tensor(tg + "htok", [128, 4, DM], BF16))
    hf = E(nc.sbuf_tensor(tg + "hf", [128, 8, 512], BF16))
    qksb = E(nc.sbuf_tensor(tg + "qksb", [128, 2, 512], BF16))
    vsb = E(nc.sbuf_tensor(tg + "vsb", [128, 2, NH, HD], BF16))
    pst = E(nc.psum_tensor(tg + "pst", [128, 512], BF16))
    pqk = E(nc.psum_tensor(tg + "pqk", [128, 2, 512], F32))
    pvv = E(nc.psum_tensor(tg + "pvv", [128, 2, 512], F32))

    sv1, sv2, sa1, sv3, sv4, sp1, sv5, sp2, sa2, sp3, sv6 = ctx.sems(
        tg, "sv1", "sv2", "sa1", "sv3", "sv4", "sp1", "sv5", "sp2", "sa2",
        "sp3", "sv6")
    sx = ctx.sems(tg, "sx0", "sx1")
    sqs = ctx.sems(tg, "sqs0", "sqs1")
    svs = ctx.sems(tg, "svs0", "svs1")

    NT = 8

    with nc.Block(tg + "main") as b:
        @b.sync
        def _(s):
            xr = xin.rearrange("(n c p) d -> n p c d", p=128, c=4)
            for i in range(NT):
                if i >= 2:
                    s.wait_ge(sv4, i - 1)
                s.dma_start(out=xt[:, i % 2], in_=xr[i]).then_inc(sx[i % 2], 16)

        @b.vector
        def _(v):
            for i in range(NT):
                v.wait_ge(sx[i % 2], 16 * (i // 2 + 1))
                for c in range(4):
                    ii = v.bn_stats(bn6[:, c * 12:c * 12 + 6], xt[:, i % 2, c, 0:512])
                    ii = v.bn_stats(bn6[:, c * 12 + 6:c * 12 + 12], xt[:, i % 2, c, 512:1024])
                ii.then_inc(sv1, 1)
                v.wait_ge(sv1, i + 1)
                for c in range(4):
                    ii = v.bn_aggr(bno[:, 2 * c:2 * c + 2], bn6[:, c * 12:c * 12 + 12])
                ii.then_inc(sv2, 1)
                v.wait_ge(sa1, i + 1)
                v.reciprocal(rstd[:], sd[:]).then_inc(sv3, 1)
                v.wait_ge(sv3, i + 1)
                for c in range(4):
                    ii = v.tensor_scalar(htok[:, c, :], xt[:, i % 2, c, :],
                                         bno[:, 2 * c:2 * c + 1], rstd[:, c:c + 1],
                                         op0=ALU.subtract, op1=ALU.mult)
                ii.then_inc(sv4, 1)
                for k in range(8):
                    v.wait_ge(sp1, 8 * i + k + 1)
                    v.tensor_copy(out=hf[:, k, :], in_=pst[:]).then_inc(sv5, 1)
                for h in range(NH):
                    v.wait_ge(sp2, 16 * i + h + 1)
                    if 16 * i + h >= 2:
                        v.wait_ge(sqs[h % 2], 32 * ((16 * i + h) // 2))
                    if layer == 0:
                        v.tensor_scalar(qksb[:, h % 2, :], pqk[:, h % 2, :],
                                        qkb[:, h:h + 1], None,
                                        op0=ALU.add).then_inc(sa2, 1)
                    else:
                        v.tensor_copy(out=qksb[:, h % 2, :],
                                      in_=pqk[:, h % 2, :]).then_inc(sa2, 1)
                for c in range(4):
                    if 4 * i + c >= 2:
                        v.wait_ge(svs[c % 2], 16 * ((4 * i + c) // 2))
                    for hh in range(2):
                        v.wait_ge(sp3, 8 * i + 2 * c + hh + 1)
                        dst = vsb[:, c % 2, hh * 8:(hh + 1) * 8, :].rearrange("p h dd -> p (h dd)")
                        if layer == 0:
                            ii = v.tensor_tensor(out=dst, in0=pvv[:, (2 * c + hh) % 2, :],
                                                 in1=vb[:, hh * 512:(hh + 1) * 512], op=ALU.add)
                        else:
                            ii = v.tensor_copy(out=dst, in_=pvv[:, (2 * c + hh) % 2, :])
                    ii.then_inc(sv6, 1)

        @b.scalar
        def _(s):
            vcol = bno[:].rearrange("p (c two) -> p c two", two=2)[:, :, 1:2].rearrange("p c o -> p (c o)")
            for i in range(NT):
                s.wait_ge(sv2, i + 1)
                s.activation(out=sd[:], in_=vcol, func=AF.Sqrt, bias=epst[:, 0:1]).then_inc(sa1, 1)

        @b.tensor
        def _(t):
            t.wait_ge(pa, 9 if layer == 0 else 1)
            if layer == 1:
                t.wait_ge(pdup, 32)
                t.wait_ge(pvw, 2)
            for i in range(NT):
                t.wait_ge(sv4, i + 1)
                for k in range(8):
                    if 8 * i + k >= 1:
                        t.wait_ge(sv5, 8 * i + k)
                    for c in range(4):
                        ii = t.transpose(pst[:, c * 128:(c + 1) * 128],
                                         htok[:, c, k * 128:(k + 1) * 128], idb[:])
                    ii.then_inc(sp1, 1)
                t.wait_ge(sv5, 8 * (i + 1))
                for h in range(NH):
                    if 16 * i + h >= 2:
                        t.wait_ge(sa2, 16 * i + h - 1)
                    if layer == 0:
                        for k in range(8):
                            ii = t.matmul(pqk[:, h % 2, :], qkwb[:, k, h, :], hf[:, k, :],
                                          start=(k == 0), stop=(k == 7))
                    else:
                        hr = (h % 2) * 64
                        ii = t.matmul(pqk[:, h % 2, :], wqkT[hr:hr + 64, :],
                                      hf[hr:hr + 64, h // 2, :], start=True, stop=True)
                    ii.then_inc(sp2, 1)
                for c in range(4):
                    for hh in range(2):
                        j = 8 * i + 2 * c + hh
                        if j >= 2:
                            t.wait_ge(sv6, 4 * i + c)
                        if layer == 0:
                            for k in range(8):
                                ii = t.matmul(pvv[:, (2 * c + hh) % 2, :],
                                              hf[:, k, c * 128:(c + 1) * 128],
                                              vwb[:, k, hh * 512:(hh + 1) * 512],
                                              start=(k == 0), stop=(k == 7))
                        else:
                            for k in range(8):
                                ii = t.matmul(pvv[:, (2 * c + hh) % 2, :],
                                              hf[:, k, c * 128:(c + 1) * 128],
                                              vwb[:, k, hh * 512:(hh + 1) * 512],
                                              start=(k == 0), stop=(k == 7))
                        ii.then_inc(sp3, 1)

        @b.gpsimd
        def _(g):
            vvr = d["vv"].rearrange("h (n p) dd -> n p h dd", p=128)
            for i in range(NT):
                for h in range(NH):
                    g.wait_ge(sa2, 16 * i + h + 1)
                    g.dma_start(out=d["qT"][h][:, i * 512:(i + 1) * 512],
                                in_=qksb[0:64, h % 2, :]).then_inc(sqs[h % 2], 16)
                    g.dma_start(out=d["kT"][h][:, i * 512:(i + 1) * 512],
                                in_=qksb[64:128, h % 2, :]).then_inc(sqs[h % 2], 16)
                for c in range(4):
                    g.wait_ge(sv6, 4 * i + c + 1)
                    g.dma_start(out=vvr[4 * i + c], in_=vsb[:, c % 2]).then_inc(svs[c % 2], 16)
            g.wait_ge(sqs[0], 32 * NT * NH // 2)
            g.wait_ge(sqs[1], 32 * NT * NH // 2)
            g.wait_ge(svs[0], 16 * 2 * NT)
            g.wait_ge(svs[1], 16 * 2 * NT)
    es.close()



def ph_attn(ctx, d, layer):
    nc = ctx.nc
    tg = "a%d" % layer
    es = contextlib.ExitStack()
    E = es.enter_context
    kbuf = E(nc.sbuf_tensor(tg + "kbuf", [65, 2, T], BF16))
    qbuf = E(nc.sbuf_tensor(tg + "qbuf", [65, 2, T], BF16))
    vab = E(nc.sbuf_tensor(tg + "vab", [128, 2, 32, 65], BF16))
    sq = E(nc.sbuf_tensor(tg + "sq", [64, T], BF16))
    nrm = E(nc.sbuf_tensor(tg + "nrm", [1, 2, T], F32))
    kn2 = E(nc.sbuf_tensor(tg + "kn2", [1, T], F32))
    km2 = E(nc.sbuf_tensor(tg + "km2", [1, 1], F32))
    kms = E(nc.sbuf_tensor(tg + "kms", [1, 1], F32))
    kmn = E(nc.sbuf_tensor(tg + "kmn", [1, 2, 1], F32))
    ones64 = E(nc.sbuf_tensor(tg + "ones64", [64, 1], BF16))
    PT = E(nc.sbuf_tensor(tg + "PT", [128, 32, 512], BF16))
    zrow = E(nc.sbuf_tensor(tg + "zrow", [1, 512], F32))
    zbsb = E(nc.sbuf_tensor(tg + "zbsb", [64, 512], F32))
    aosb = E(nc.sbuf_tensor(tg + "aosb", [64, 2, 512], BF16))
    binm = E(nc.sbuf_tensor(tg + "binm", [128, 4, 512], BF16))
    onesf = E(nc.sbuf_tensor(tg + "onesf", [1, 64], F32))
    ps = E(nc.psum_tensor(tg + "ps", [128, 2, 512], F32))
    po = E(nc.psum_tensor(tg + "po", [128, 512], F32))
    pzb = E(nc.psum_tensor(tg + "pzb", [64, 512], F32))
    pn = E(nc.psum_tensor(tg + "pn", [1, 2, 512], F32))

    pg, = ctx.sems(tg, "pg")
    with nc.Block(tg + "prep") as b:
        @b.gpsimd
        def _(g):
            g.memset(onesf[:], 1.0)
            g.memset(ones64[:], 1.0)
            g.memset(kbuf[64:65, :, :], 1.0)
            g.memset(vab[:], 1.0)
            g.memset(binm[:], 1.0).then_inc(pg, 1)
            g.wait_ge(pg, 1)
            for j in range(4):
                g.affine_select(out=binm[:, j, :], in_=binm[:, j, :],
                                compare_op=ALU.is_ge, fill=0.0, base=-128 * j,
                                pattern=[[1, 512]], channel_multiplier=-1)

    units = [(h, g) for h in range(NH) for g in range(8)]
    NU = len(units)

    def K_of(g):
        return 4 * g + 4

    P1 = [0]
    for (h, g) in units:
        P1.append(P1[-1] + K_of(g))

    (an, pnm, vn, vq, mm, ae, vd, spo, vz, pz, vz2, va) = ctx.sems(
        tg, "an", "pnm", "vn", "vq", "mm", "ae", "vd", "spo", "vz", "pz",
        "vz2", "va")
    ld = ctx.sems(tg, "ld0", "ld1")
    gs = ctx.sems(tg, "gs0", "gs1")

    with nc.Block(tg + "main") as b:
        @b.sync
        def _(s):
            for h in range(NH):
                if h >= 2:
                    s.wait_ge(spo, 8 * (h - 1))
                s.dma_start(out=kbuf[0:64, h % 2, :], in_=d["kT"][h]).then_inc(ld[h % 2], 16)
                s.dma_start(out=qbuf[0:64, h % 2, :], in_=d["qT"][h]).then_inc(ld[h % 2], 16)
                s.dma_start(out=vab[:, h % 2, :, 0:64],
                            in_=d["vv"][h].rearrange("(kt p) dd -> p kt dd", p=128)).then_inc(ld[h % 2], 16)

        def norm_scalar(s, h):
            s.wait_ge(ld[h % 2], 48 * (h // 2 + 1))
            if h >= 1:
                s.wait_ge(pnm, 16 * h)
            s.activation(out=sq[:], in_=kbuf[0:64, h % 2, :],
                         func=AF.Square).then_inc(an, 1)
            s.wait_ge(vn, 18 * h + 9)
            s.activation(out=kms[:], in_=km2[:], func=AF.Sqrt).then_inc(an, 1)
            s.wait_ge(pnm, 16 * h + 8)
            s.activation(out=sq[:], in_=qbuf[0:64, h % 2, :],
                         func=AF.Square).then_inc(an, 1)
            s.wait_ge(vn, 18 * h + 18)
            if h >= 2:
                s.wait_ge(vq, 8 * (h - 1))
            s.activation(out=nrm[0:1, h % 2, :], in_=kn2[0:1, :],
                         func=AF.Sqrt).then_inc(an, 1)

        def norm_pe(t, h):
            t.wait_ge(an, 4 * h + 1)
            for c in range(8):
                if c >= 2:
                    t.wait_ge(vn, 18 * h + c - 1)
                elif h >= 1:
                    t.wait_ge(vn, 18 * h)
                t.matmul(pn[0:1, c % 2, :], ones64[:],
                         sq[:, c * 512:(c + 1) * 512], start=True,
                         stop=True).then_inc(pnm, 1)
            t.wait_ge(an, 4 * h + 3)
            for c in range(8):
                t.wait_ge(vn, 18 * h + c + 9 if c >= 2 else 18 * h + c + 7)
                t.matmul(pn[0:1, c % 2, :], ones64[:],
                         sq[:, c * 512:(c + 1) * 512], start=True,
                         stop=True).then_inc(pnm, 1)

        def norm_vector(v, h):
            if h >= 1:
                v.wait_ge(an, 4 * h)
            for c in range(8):
                v.wait_ge(pnm, 16 * h + c + 1)
                v.tensor_copy(out=kn2[0:1, c * 512:(c + 1) * 512],
                              in_=pn[0:1, c % 2, :]).then_inc(vn, 1)
            v.reduce_max(km2[:], kn2[:], axis=AX.X).then_inc(vn, 1)
            v.wait_ge(an, 4 * h + 2)
            v.tensor_scalar(kmn[0:1, h % 2, :], kms[:], -1.0, None,
                            op0=ALU.mult).then_inc(vn, 1)
            for c in range(8):
                v.wait_ge(pnm, 16 * h + 8 + c + 1)
                v.tensor_copy(out=kn2[0:1, c * 512:(c + 1) * 512],
                              in_=pn[0:1, c % 2, :]).then_inc(vn, 1)

        @b.tensor
        def _(t):
            norm_pe(t, 0)
            for u, (h, g) in enumerate(units):
                K = K_of(g)
                t.wait_ge(vq, u + 1)
                for kt in range(K):
                    if P1[u] + kt >= 2:
                        t.wait_ge(ae, P1[u] + kt - 1)
                    t.matmul(ps[:, kt % 2, :], kbuf[0:65, h % 2, kt * 128:(kt + 1) * 128],
                             qbuf[0:65, h % 2, g * 512:(g + 1) * 512],
                             start=True, stop=True).then_inc(mm, 1)
                t.wait_ge(ae, P1[u] + K)
                t.wait_ge(vd, 4 * (u + 1))
                if u >= 1:
                    t.wait_ge(va, u)
                for kt in range(K):
                    ii = t.matmul(po[0:65, :], vab[:, h % 2, kt, :],
                                  PT[:, kt, :], start=(kt == 0), stop=(kt == K - 1))
                ii.then_inc(spo, 1)
                t.wait_ge(vz, u + 1)
                if u >= 1:
                    t.wait_ge(vz2, u)
                t.matmul(pzb[:], onesf[:], zrow[:], start=True, stop=True).then_inc(pz, 1)
                if g == 0 and h + 1 < NH:
                    norm_pe(t, h + 1)

        @b.vector
        def _(v):
            norm_vector(v, 0)
            for u, (h, g) in enumerate(units):
                K = K_of(g)
                v.wait_ge(an, 4 * h + 4)
                if u >= 16:
                    v.wait_ge(spo, u - 15)
                v.tensor_scalar(qbuf[64:65, h % 2, g * 512:(g + 1) * 512],
                                nrm[0:1, h % 2, g * 512:(g + 1) * 512],
                                kmn[0:1, h % 2, 0:1], None,
                                op0=ALU.mult).then_inc(vq, 1)
                for j in range(4):
                    kt = K - 4 + j
                    v.wait_ge(ae, P1[u] + kt + 1)
                    v.tensor_tensor(out=PT[:, kt, :], in0=PT[:, kt, :],
                                    in1=binm[:, j, :], op=ALU.mult).then_inc(vd, 1)
                v.wait_ge(spo, u + 1)
                if u >= 1:
                    v.wait_ge(pz, u)
                v.reciprocal(zrow[:], po[64:65, :]).then_inc(vz, 1)
                v.wait_ge(pz, u + 1)
                v.tensor_copy(out=zbsb[:], in_=pzb[0:64, :]).then_inc(vz2, 1)
                if u >= 2:
                    v.wait_ge(gs[u % 2], 16 * (u // 2))
                v.tensor_tensor(out=aosb[:, u % 2, :], in0=po[0:64, :], in1=zbsb[:],
                                op=ALU.mult).then_inc(va, 1)
                if g == 0 and h + 1 < NH:
                    norm_vector(v, h + 1)

        @b.scalar
        def _(s):
            norm_scalar(s, 0)
            for u, (h, g) in enumerate(units):
                K = K_of(g)
                for kt in range(K):
                    s.wait_ge(mm, P1[u] + kt + 1)
                    s.activation(out=PT[:, kt, :], in_=ps[:, kt % 2, :],
                                 func=AF.Exp).then_inc(ae, 1)
                if g == 0 and h + 1 < NH:
                    norm_scalar(s, h + 1)

        @b.gpsimd
        def _(g_):
            for u, (h, g) in enumerate(units):
                g_.wait_ge(va, u + 1)
                g_.dma_start(out=d["attnT"][h * 64:(h + 1) * 64, g * 512:(g + 1) * 512],
                             in_=aosb[:, u % 2, :]).then_inc(gs[u % 2], 16)
            g_.wait_ge(gs[0], 16 * NU // 2)
            g_.wait_ge(gs[1], 16 * NU // 2)
    es.close()


def ph_oproj(ctx, d, layer):
    nc = ctx.nc
    tg = "o%d" % layer
    es = contextlib.ExitStack()
    E = es.enter_context
    xin = d["x0"] if layer == 0 else d["x2"]
    xout = d["x1"] if layer == 0 else d["x3"]
    wsrc_name = "outw" if layer == 0 else "wu"
    bsrc_name = "outb" if layer == 0 else "bu"

    idb = E(nc.sbuf_tensor(tg + "idb", [128, 128], BF16))
    idf = E(nc.sbuf_tensor(tg + "idf", [128, 128], F32))
    owb = E(nc.sbuf_tensor(tg + "owb", [128, 8, DM], BF16))
    obt = E(nc.sbuf_tensor(tg + "obt", [128, 8], F32))
    wfo = E(nc.sbuf_tensor(tg + "wfo", [128, 2, DM], F32))
    bro = E(nc.sbuf_tensor(tg + "bro", [8, 128], F32))
    pprep = E(nc.psum_tensor(tg + "pprep", [128, 512], F32))
    pprepb = E(nc.psum_tensor(tg + "pprepb", [128, 512], BF16))

    pl, pg, pa, pp, pv, pp2, pv2, pbs = ctx.sems(tg, "pl", "pg", "pa", "pp", "pv", "pp2", "pv2", "pbs")
    pw = ctx.sems(tg, "pw0", "pw1")

    with nc.Block(tg + "prep") as b:
        @b.gpsimd
        def _(g):
            g.memset(idb[:], 0.0)
            g.memset(idf[:], 0.0).then_inc(pg, 1)
            g.wait_ge(pg, 1)
            g.affine_select(out=idb[:], in_=idb[:], compare_op=ALU.not_equal,
                            fill=1.0, base=0, pattern=[[-1, 128]],
                            channel_multiplier=1)
            g.affine_select(out=idf[:], in_=idf[:], compare_op=ALU.not_equal,
                            fill=1.0, base=0, pattern=[[-1, 128]],
                            channel_multiplier=1).then_inc(pg, 1)

        @b.sync
        def _(s):
            s.dma_start(out=bro[:], in_=d[bsrc_name].rearrange("(c k) -> c k", k=128)).then_inc(pbs, 16)
            wr = d[wsrc_name].rearrange("(c p) n -> c p n", p=128)
            for k in range(8):
                if k >= 2:
                    s.wait_ge(pa, k - 1)
                s.dma_start(out=wfo[:, k % 2, :], in_=wr[k]).then_inc(pw[k % 2], 16)

        if layer == 0:
            @b.scalar
            def _(s):
                for k in range(8):
                    s.wait_ge(pw[k % 2], 16 * (k // 2 + 1))
                    s.activation(out=owb[:, k, :], in_=wfo[:, k % 2, :],
                                 func=AF.Copy).then_inc(pa, 1)
        else:
            wtb = E(nc.sbuf_tensor(tg + "wtb", [128, 2, DM], BF16))

            @b.scalar
            def _(s):
                for k in range(8):
                    s.wait_ge(pw[k % 2], 16 * (k // 2 + 1))
                    if k >= 2:
                        s.wait_ge(pp, 8 * (k - 2) + 8)
                    s.activation(out=wtb[:, k % 2, :], in_=wfo[:, k % 2, :],
                                 func=AF.Copy).then_inc(pa, 1)

            @b.tensor
            def _(t):
                t.wait_ge(pg, 2)
                for k in range(8):
                    t.wait_ge(pa, k + 1)
                    for c in range(8):
                        if (8 * k + c) >= 2:
                            t.wait_ge(pv, 8 * k + c - 1)
                        t.transpose(pprepb[:, (c % 4) * 128:(c % 4 + 1) * 128],
                                    wtb[:, k % 2, c * 128:(c + 1) * 128],
                                    idb[:]).then_inc(pp, 1)

            @b.vector
            def _(v):
                for k in range(8):
                    for c in range(8):
                        v.wait_ge(pp, 8 * k + c + 1)
                        v.tensor_copy(out=owb[:, c, k * 128:(k + 1) * 128],
                                      in_=pprepb[:, (c % 4) * 128:(c % 4 + 1) * 128]).then_inc(pv, 1)

        @b.tensor
        def _(t):
            t.wait_ge(pbs, 16)
            t.wait_ge(pg, 2)
            t.transpose(pprep[:, 0:8], bro[:], idf[0:8, 0:8]).then_inc(pp2, 1)

        @b.vector
        def _(v):
            v.wait_ge(pp2, 1)
            v.tensor_copy(out=obt[:], in_=pprep[:, 0:8]).then_inc(pv2, 1)

    at = E(nc.sbuf_tensor(tg + "at", [128, 2, 8, 512], BF16))
    xti = E(nc.sbuf_tensor(tg + "xti", [128, 2, 4, DM], F32))
    xto = E(nc.sbuf_tensor(tg + "xto", [128, 2, 4, DM], F32))
    obsb = E(nc.sbuf_tensor(tg + "obsb", [128, 8, 512], BF16))
    ppj = E(nc.psum_tensor(tg + "ppj", [128, 2, 512], F32))
    ptr = E(nc.psum_tensor(tg + "ptr", [128, 512], BF16))

    pm, ve, pt2, vadd = ctx.sems(tg, "pm", "ve", "pt2", "vadd")
    lx = ctx.sems(tg, "lx0", "lx1")
    gso = ctx.sems(tg, "gso0", "gso1")
    NT = 8

    with nc.Block(tg + "main") as b:
        @b.sync
        def _(s):
            atr = d["attnT"].rearrange("(k p) t -> p k t", p=128)
            xr = xin.rearrange("(n c p) dd -> n p c dd", p=128, c=4)
            for i in range(NT):
                if i >= 2:
                    s.wait_ge(pm, 8 * (i - 1))
                s.dma_start(out=at[:, i % 2], in_=atr[:, :, i * 512:(i + 1) * 512]).then_inc(lx[i % 2], 16)
                if i >= 2:
                    s.wait_ge(vadd, 8 * (i - 1))
                s.dma_start(out=xti[:, i % 2], in_=xr[i]).then_inc(lx[i % 2], 16)

        @b.tensor
        def _(t):
            t.wait_ge(pa, 8 if layer == 0 else 0)
            if layer == 1:
                t.wait_ge(pv, 64)
            for i in range(NT):
                t.wait_ge(lx[i % 2], 32 * (i // 2 + 1))
                for mo in range(8):
                    j = 8 * i + mo
                    if j >= 2:
                        t.wait_ge(ve, j - 1)
                    for k in range(8):
                        ii = t.matmul(ppj[:, mo % 2, :], owb[:, k, mo * 128:(mo + 1) * 128],
                                      at[:, i % 2, k, :], start=(k == 0), stop=(k == 7))
                    ii.then_inc(pm, 1)
                for mo in range(8):
                    j = 8 * i + mo
                    t.wait_ge(ve, j + 1)
                    if j >= 1:
                        t.wait_ge(vadd, j)
                    for c in range(4):
                        ii = t.transpose(ptr[:, c * 128:(c + 1) * 128],
                                         obsb[:, mo, c * 128:(c + 1) * 128], idb[:])
                    ii.then_inc(pt2, 1)

        @b.vector
        def _(v):
            v.wait_ge(pv2, 1)
            for i in range(NT):
                for mo in range(8):
                    j = 8 * i + mo
                    v.wait_ge(pm, j + 1)
                    if i >= 1:
                        v.wait_ge(pt2, 8 * (i - 1) + mo + 1)
                    v.tensor_scalar(obsb[:, mo, :], ppj[:, mo % 2, :],
                                    obt[:, mo:mo + 1], None, op0=ALU.add).then_inc(ve, 1)
                v.wait_ge(lx[i % 2], 32 * (i // 2 + 1))
                if i >= 2:
                    v.wait_ge(gso[i % 2], 16 * (i // 2))
                for mo in range(8):
                    j = 8 * i + mo
                    v.wait_ge(pt2, j + 1)
                    for c in range(4):
                        ii = v.tensor_tensor(out=xto[:, i % 2, c, mo * 128:(mo + 1) * 128],
                                             in0=ptr[:, c * 128:(c + 1) * 128],
                                             in1=xti[:, i % 2, c, mo * 128:(mo + 1) * 128],
                                             op=ALU.add)
                    ii.then_inc(vadd, 1)

        @b.gpsimd
        def _(g):
            xo = xout.rearrange("(n c p) dd -> n p c dd", p=128, c=4)
            for i in range(NT):
                g.wait_ge(vadd, 8 * (i + 1))
                g.dma_start(out=xo[i], in_=xto[:, i % 2]).then_inc(gso[i % 2], 16)
            g.wait_ge(gso[0], 16 * NT // 2)
            g.wait_ge(gso[1], 16 * NT // 2)
    es.close()


def ph_moeprep(ctx, d):
    nc = ctx.nc
    tg = "mp"
    es = contextlib.ExitStack()
    E = es.enter_context
    wfm = E(nc.sbuf_tensor(tg + "wfm", [128, 2, 2048], F32))
    wbm = E(nc.sbuf_tensor(tg + "wbm", [128, 2, 2048], BF16))
    mc, = ctx.sems(tg, "mc")
    ml = ctx.sems(tg, "ml0", "ml1")
    ms = ctx.sems(tg, "ms0", "ms1")

    units = []
    for name_s, name_d, nchunk, width in (("w1c", "w1b", 8, 2048),
                                          ("w3c", "w3b", 8, 2048),
                                          ("w2c", "w2b", 16, 1024)):
        for e in range(9):
            sv = d[name_s][e].rearrange("(c p) n -> c p n", p=128)
            dv = d[name_d][e].rearrange("(c p) n -> c p n", p=128)
            for rc in range(nchunk):
                units.append((sv[rc], dv[rc], width))
    NU = len(units)

    with nc.Block(tg + "main") as b:
        @b.sync
        def _(s):
            for j, (sv, dv, w) in enumerate(units):
                if j >= 2:
                    s.wait_ge(mc, j - 1)
                s.dma_start(out=wfm[:, j % 2, 0:w], in_=sv).then_inc(ml[j % 2], 16)

        @b.scalar
        def _(s):
            for j, (sv, dv, w) in enumerate(units):
                s.wait_ge(ml[j % 2], 16 * (j // 2 + 1))
                if j >= 2:
                    s.wait_ge(ms[j % 2], 16 * (j // 2))
                s.activation(out=wbm[:, j % 2, 0:w], in_=wfm[:, j % 2, 0:w],
                             func=AF.Copy).then_inc(mc, 1)

        @b.gpsimd
        def _(g):
            for j, (sv, dv, w) in enumerate(units):
                g.wait_ge(mc, j + 1)
                g.dma_start(out=dv, in_=wbm[:, j % 2, 0:w]).then_inc(ms[j % 2], 16)
            g.wait_ge(ms[0], 16 * ((NU + 1) // 2))
            g.wait_ge(ms[1], 16 * (NU // 2))
    es.close()


def ph_moe(ctx, d):
    nc = ctx.nc
    tg = "m"
    es = contextlib.ExitStack()
    E = es.enter_context
    idf = E(nc.sbuf_tensor(tg + "idf", [128, 128], F32))
    idb = E(nc.sbuf_tensor(tg + "idb", [128, 128], BF16))
    epst = E(nc.sbuf_tensor(tg + "epst", [128, 1], F32))
    ones1 = E(nc.sbuf_tensor(tg + "ones1", [1, 128], F32))
    rsrc = E(nc.sbuf_tensor(tg + "rsrc", [8, DM], F32))
    ebr = E(nc.sbuf_tensor(tg + "ebr", [1, 8], F32))
    rwT = E(nc.sbuf_tensor(tg + "rwT", [128, 8, 8], F32))
    ebb = E(nc.sbuf_tensor(tg + "ebb", [128, 8], F32))
    xt = E(nc.sbuf_tensor(tg + "xt", [128, 2, 4, DM], F32))
    bn6 = E(nc.sbuf_tensor(tg + "bn6", [128, 48], F32))
    bno = E(nc.sbuf_tensor(tg + "bno", [128, 8], F32))
    sd = E(nc.sbuf_tensor(tg + "sd", [128, 4], F32))
    rstd = E(nc.sbuf_tensor(tg + "rstd", [128, 4], F32))
    htf = E(nc.sbuf_tensor(tg + "htf", [128, 4, DM], F32))
    hf32 = E(nc.sbuf_tensor(tg + "hf32", [128, 8, 512], F32))
    hbf = E(nc.sbuf_tensor(tg + "hbf", [128, 8, 512], BF16))
    lgs = E(nc.sbuf_tensor(tg + "lgs", [128, 32], F32))
    m1 = E(nc.sbuf_tensor(tg + "m1", [128, 4], F32))
    m1n = E(nc.sbuf_tensor(tg + "m1n", [128, 4], F32))
    m2 = E(nc.sbuf_tensor(tg + "m2", [128, 4], F32))
    sel = E(nc.sbuf_tensor(tg + "sel", [128, 32], F32))
    msk = E(nc.sbuf_tensor(tg + "msk", [128, 32], F32))
    e1t = E(nc.sbuf_tensor(tg + "e1t", [128, 32], F32))
    wun = E(nc.sbuf_tensor(tg + "wun", [128, 32], F32))
    zz = E(nc.sbuf_tensor(tg + "zz", [128, 4], F32))
    zrr = E(nc.sbuf_tensor(tg + "zrr", [128, 4], F32))
    cwT = E(nc.sbuf_tensor(tg + "cwT", [128, 4, 8], F32))
    cwr = E(nc.sbuf_tensor(tg + "cwr", [8, 512], F32))
    cw0 = E(nc.sbuf_tensor(tg + "cw0", [1, 2, 512], F32))
    cwb = E(nc.sbuf_tensor(tg + "cwb", [128, 8, 512], BF16))
    w1s = E(nc.sbuf_tensor(tg + "w1s", [128, 2, 8, 512], BF16))
    w3s = E(nc.sbuf_tensor(tg + "w3s", [128, 2, 8, 512], BF16))
    w2s = E(nc.sbuf_tensor(tg + "w2s", [128, 16, 1024], BF16))
    sg = E(nc.sbuf_tensor(tg + "sg", [128, 2, 512], F32))
    eo1 = E(nc.sbuf_tensor(tg + "eo1", [128, 2, 512], BF16))
    eo = E(nc.sbuf_tensor(tg + "eo", [128, 16, 512], BF16))
    acc = E(nc.sbuf_tensor(tg + "acc", [128, 8, 512], F32))
    accb = E(nc.sbuf_tensor(tg + "accb", [128, 8, 512], BF16))
    pmisc = E(nc.psum_tensor(tg + "pmisc", [128, 512], F32))
    pg_ = E(nc.psum_tensor(tg + "pg", [128, 512], F32))
    pu_ = E(nc.psum_tensor(tg + "pu", [128, 512], F32))
    pp2 = E(nc.psum_tensor(tg + "pp2", [128, 2, 512], F32))
    pcwt = E(nc.psum_tensor(tg + "pcwt", [8, 512], F32))
    pcb = E(nc.psum_tensor(tg + "pcb", [128, 512], F32))
    pepi = E(nc.psum_tensor(tg + "pepi", [128, 512], BF16))

    EO = [8, 0, 1, 2, 3, 4, 5, 6, 7]

    (pg, mrw, meb, mpp, mvv, mv1, ma1, mv4, mp1, mv5, mhb, mpr, mvm, mae,
     mcw1, mpc, mvr, mpb, macw, mpg, mpu, mas, mve, mps, mpw, mva, mab,
     mpe, mvp, mself) = ctx.sems(
        tg, "pg", "mrw", "meb", "mpp", "mvv", "mv1", "ma1", "mv4", "mp1",
        "mv5", "mhb", "mpr", "mvm", "mae", "mcw1", "mpc", "mvr", "mpb",
        "macw", "mpg", "mpu", "mas", "mve", "mps", "mpw", "mva", "mab",
        "mpe", "mvp", "mself")
    mx = ctx.sems(tg, "mx0", "mx1")
    mw1 = ctx.sems(tg, "mw10", "mw11")
    mw3 = ctx.sems(tg, "mw30", "mw31")
    mw2, = ctx.sems(tg, "mw2")
    msx = ctx.sems(tg, "msx0", "msx1")
    mcw = ctx.sems(tg, "mcw0", "mcw1s")

    with nc.Block(tg + "prep") as b:
        @b.gpsimd
        def _(g):
            g.memset(epst[:], 1.0e-5)
            g.memset(ones1[:], 1.0)
            g.memset(idf[:], 0.0)
            g.memset(idb[:], 0.0).then_inc(pg, 1)
            g.wait_ge(pg, 1)
            g.affine_select(out=idf[:], in_=idf[:], compare_op=ALU.not_equal,
                            fill=1.0, base=0, pattern=[[-1, 128]],
                            channel_multiplier=1)
            g.affine_select(out=idb[:], in_=idb[:], compare_op=ALU.not_equal,
                            fill=1.0, base=0, pattern=[[-1, 128]],
                            channel_multiplier=1).then_inc(pg, 1)

        @b.sync
        def _(s):
            s.dma_start(out=rsrc[:], in_=d["rw"]).then_inc(mrw, 16)
            s.dma_start(out=ebr[:], in_=d["eb"][None, :]).then_inc(meb, 16)

        @b.tensor
        def _(t):
            t.wait_ge(mrw, 16)
            t.wait_ge(pg, 2)
            for k in range(8):
                if k >= 1:
                    t.wait_ge(mvv, k)
                t.transpose(pmisc[:, 0:8], rsrc[:, k * 128:(k + 1) * 128],
                            idf[0:8, 0:8]).then_inc(mpp, 1)
            t.wait_ge(meb, 16)
            t.wait_ge(mvv, 8)
            t.matmul(pmisc[:, 8:16], ones1[:], ebr[:], start=True,
                     stop=True).then_inc(mpp, 1)

        @b.vector
        def _(v):
            for k in range(8):
                v.wait_ge(mpp, k + 1)
                v.tensor_copy(out=rwT[:, k, :], in_=pmisc[:, 0:8]).then_inc(mvv, 1)
            v.wait_ge(mpp, 9)
            v.tensor_copy(out=ebb[:], in_=pmisc[:, 8:16]).then_inc(mvv, 1)

    NT = 8

    def IDX(tt, ei, mh):
        return 144 * tt + 16 * ei + mh

    def WIDX(tt, ei, mo):
        return 72 * tt + 8 * ei + mo

    with nc.Block(tg + "main") as b:
        @b.sync
        def _(s):
            xr = d["x1"].rearrange("(n c p) dd -> n p c dd", p=128, c=4)
            for tt in range(NT):
                if tt >= 2:
                    s.wait_ge(msx[tt % 2], 16 * (tt // 2))
                s.dma_start(out=xt[:, tt % 2], in_=xr[tt]).then_inc(mx[tt % 2], 16)
                for ei, e in enumerate(EO):
                    w1v = d["w1b"][e].rearrange("(c p) n -> p c n", p=128)
                    w3v = d["w3b"][e].rearrange("(c p) n -> p c n", p=128)
                    w2v = d["w2b"][e].rearrange("(c p) n -> p c n", p=128)
                    for q in range(4):
                        gate = 144 * tt + 16 * ei + 4 * q - 4
                        if gate >= 1:
                            s.wait_ge(mpg, gate)
                        s.dma_start(out=w1s[:, q % 2],
                                    in_=w1v[:, :, q * 512:(q + 1) * 512]).then_inc(mw1[q % 2], 16)
                        if gate >= 1:
                            s.wait_ge(mpu, gate)
                        s.dma_start(out=w3s[:, q % 2],
                                    in_=w3v[:, :, q * 512:(q + 1) * 512]).then_inc(mw3[q % 2], 16)
                    if 9 * tt + ei >= 1:
                        s.wait_ge(mpw, 72 * tt + 8 * ei)
                    s.dma_start(out=w2s[:], in_=w2v).then_inc(mw2, 16)

        @b.vector
        def _(v):
            ns = 0

            def selfw(ii):
                nonlocal ns
                ii.then_inc(mself, 1)
                ns += 1
                v.wait_ge(mself, ns)

            for tt in range(NT):
                v.wait_ge(mx[tt % 2], 16 * (tt // 2 + 1))
                for c in range(4):
                    ii = v.bn_stats(bn6[:, c * 12:c * 12 + 6], xt[:, tt % 2, c, 0:512])
                    ii = v.bn_stats(bn6[:, c * 12 + 6:c * 12 + 12], xt[:, tt % 2, c, 512:1024])
                selfw(ii)
                for c in range(4):
                    ii = v.bn_aggr(bno[:, 2 * c:2 * c + 2], bn6[:, c * 12:c * 12 + 12])
                ii.then_inc(mv1, 1)
                v.wait_ge(ma1, tt + 1)
                ii = v.reciprocal(rstd[:], sd[:])
                selfw(ii)
                if tt >= 1:
                    v.wait_ge(mp1, 8 * tt)
                for c in range(4):
                    ii = v.tensor_scalar(htf[:, c, :], xt[:, tt % 2, c, :],
                                         bno[:, 2 * c:2 * c + 1], rstd[:, c:c + 1],
                                         op0=ALU.subtract, op1=ALU.mult)
                ii.then_inc(mv4, 1)
                for k in range(8):
                    v.wait_ge(mp1, 8 * tt + k + 1)
                    v.tensor_copy(out=hf32[:, k, :], in_=pmisc[:]).then_inc(mv5, 1)
                for c in range(4):
                    v.wait_ge(mpr, 4 * tt + 4)
                    c8 = slice(c * 8, (c + 1) * 8)
                    ii = v.tensor_tensor(out=lgs[:, c8], in0=pmisc[:, 448 + c * 8:448 + (c + 1) * 8],
                                         in1=ebb[:], op=ALU.add)
                    selfw(ii)
                    ii = v.reduce_max(m1[:, c:c + 1], lgs[:, c8], axis=AX.X)
                    selfw(ii)
                    ii = v.tensor_scalar(sel[:, c8], lgs[:, c8], m1[:, c:c + 1], None,
                                         op0=ALU.is_ge)
                    selfw(ii)
                    ii = v.scalar_tensor_tensor(msk[:, c8], sel[:, c8], -1.0e30,
                                                lgs[:, c8], op0=ALU.mult, op1=ALU.add)
                    selfw(ii)
                    ii = v.reduce_max(m2[:, c:c + 1], msk[:, c8], axis=AX.X)
                    selfw(ii)
                    ii = v.tensor_scalar(sel[:, c8], lgs[:, c8], m2[:, c:c + 1], None,
                                         op0=ALU.is_ge)
                    selfw(ii)
                    v.tensor_scalar(m1n[:, c:c + 1], m1[:, c:c + 1], -1.0, None,
                                    op0=ALU.mult).then_inc(mvm, 1)
                    v.wait_ge(mae, 4 * tt + c + 1)
                    ii = v.tensor_tensor(out=wun[:, c8], in0=sel[:, c8], in1=e1t[:, c8],
                                         op=ALU.mult)
                    selfw(ii)
                    ii = v.reduce_sum(zz[:, c:c + 1], wun[:, c8], axis=AX.X)
                    selfw(ii)
                    ii = v.reciprocal(zrr[:, c:c + 1], zz[:, c:c + 1])
                    selfw(ii)
                    v.tensor_scalar(cwT[:, c, :], wun[:, c8], zrr[:, c:c + 1], None,
                                    op0=ALU.mult).then_inc(mcw1, 1)
                v.wait_ge(mpc, tt + 1)
                v.tensor_copy(out=cwr[:], in_=pcwt[:]).then_inc(mvr, 1)
                for ei, e in enumerate(EO):
                    for mh in range(16):
                        ix = IDX(tt, ei, mh)
                        if ei == 0:
                            v.wait_ge(mas, ix + 1)
                            v.wait_ge(mpu, ix + 1)
                            if tt >= 1:
                                v.wait_ge(mpw, 72 * tt)
                            v.tensor_tensor(out=eo[:, mh, :], in0=pu_[:],
                                            in1=sg[:, mh % 2, :], op=ALU.mult).then_inc(mve, 1)
                        else:
                            v.wait_ge(mas, ix + 1)
                            v.wait_ge(mpu, ix + 1)
                            six = 128 * tt + 16 * (ei - 1) + mh
                            if six >= 2:
                                v.wait_ge(mps, six - 1)
                            v.tensor_tensor(out=eo1[:, mh % 2, :], in0=pu_[:],
                                            in1=sg[:, mh % 2, :], op=ALU.mult).then_inc(mve, 1)
                    for mo in range(8):
                        wx = WIDX(tt, ei, mo)
                        v.wait_ge(mpw, wx + 1)
                        if ei == 0:
                            if tt >= 1:
                                v.wait_ge(mab, tt)
                            v.tensor_copy(out=acc[:, mo, :], in_=pp2[:, mo % 2, :]).then_inc(mva, 1)
                        else:
                            v.wait_ge(mva, WIDX(tt, ei - 1, mo) + 1)
                            v.tensor_tensor(out=acc[:, mo, :], in0=pp2[:, mo % 2, :],
                                            in1=acc[:, mo, :], op=ALU.add).then_inc(mva, 1)
                for mo in range(8):
                    v.wait_ge(mpe, 8 * tt + mo + 1)
                    for c in range(4):
                        ii = v.tensor_tensor(out=xt[:, tt % 2, c, mo * 128:(mo + 1) * 128],
                                             in0=pepi[:, c * 128:(c + 1) * 128],
                                             in1=xt[:, tt % 2, c, mo * 128:(mo + 1) * 128],
                                             op=ALU.add)
                    ii.then_inc(mvp, 1)

        @b.scalar
        def _(s):
            vcol = bno[:].rearrange("p (c two) -> p c two", two=2)[:, :, 1:2].rearrange("p c o -> p (c o)")
            for tt in range(NT):
                s.wait_ge(mv1, tt + 1)
                s.activation(out=sd[:], in_=vcol, func=AF.Sqrt, bias=epst[:, 0:1]).then_inc(ma1, 1)
                s.wait_ge(mv5, 8 * (tt + 1))
                s.activation(out=hbf[:], in_=hf32[:], func=AF.Copy).then_inc(mhb, 1)
                for c in range(4):
                    s.wait_ge(mvm, 4 * tt + c + 1)
                    s.activation(out=e1t[:, c * 8:(c + 1) * 8], in_=lgs[:, c * 8:(c + 1) * 8],
                                 func=AF.Exp, bias=m1n[:, c:c + 1]).then_inc(mae, 1)
                for e in range(8):
                    s.wait_ge(mpb, 8 * tt + e + 1)
                    if tt >= 1:
                        s.wait_ge(mps, 128 * (tt - 1) + 16 * (e + 1))
                    s.activation(out=cwb[:, e, :], in_=pcb[:], func=AF.Copy).then_inc(macw, 1)
                for ei, e in enumerate(EO):
                    for mh in range(16):
                        ix = IDX(tt, ei, mh)
                        s.wait_ge(mpg, ix + 1)
                        if ix >= 2:
                            s.wait_ge(mve, ix - 1)
                        s.activation(out=sg[:, mh % 2, :], in_=pg_[:],
                                     func=AF.Silu).then_inc(mas, 1)
                s.wait_ge(mva, 72 * (tt + 1))
                if tt >= 1:
                    s.wait_ge(mpe, 8 * tt)
                s.activation(out=accb[:], in_=acc[:], func=AF.Copy).then_inc(mab, 1)

        @b.tensor
        def _(t):
            t.wait_ge(mvv, 9)
            t.wait_ge(pg, 2)
            for tt in range(NT):
                t.wait_ge(mv4, tt + 1)
                if tt >= 1:
                    t.wait_ge(mcw1, 4 * tt)
                for k in range(8):
                    if 8 * tt + k >= 1:
                        t.wait_ge(mv5, 8 * tt + k)
                    for c in range(4):
                        ii = t.transpose(pmisc[:, c * 128:(c + 1) * 128],
                                         htf[:, c, k * 128:(k + 1) * 128], idf[:])
                    ii.then_inc(mp1, 1)
                t.wait_ge(mv5, 8 * (tt + 1))
                for c in range(4):
                    for k in range(8):
                        ii = t.matmul(pmisc[:, 448 + c * 8:448 + (c + 1) * 8],
                                      hf32[:, k, c * 128:(c + 1) * 128], rwT[:, k, :],
                                      start=(k == 0), stop=(k == 7))
                    ii.then_inc(mpr, 1)
                t.wait_ge(mcw1, 4 * (tt + 1))
                if tt >= 1:
                    t.wait_ge(mvr, tt)
                for c in range(4):
                    ii = t.transpose(pcwt[:, c * 128:(c + 1) * 128], cwT[:, c, :],
                                     idf[0:128, 0:128])
                ii.then_inc(mpc, 1)
                for e in range(8):
                    t.wait_ge(mcw[e % 2], 16 * ((8 * tt + e) // 2 + 1))
                    if 8 * tt + e >= 1:
                        t.wait_ge(macw, 8 * tt + e)
                    t.matmul(pcb[:], ones1[:], cw0[0:1, e % 2, :], start=True,
                             stop=True).then_inc(mpb, 1)
                t.wait_ge(mhb, tt + 1)
                for ei, e in enumerate(EO):
                    for q in range(4):
                        t.wait_ge(mw1[q % 2], 16 * (2 * (9 * tt + ei) + q // 2 + 1))
                        t.wait_ge(mw3[q % 2], 16 * (2 * (9 * tt + ei) + q // 2 + 1))
                        for mhl in range(4):
                            mh = 4 * q + mhl
                            ix = IDX(tt, ei, mh)
                            if ix >= 1:
                                t.wait_ge(mas, ix)
                            for k in range(8):
                                ii = t.matmul(pg_[:], w1s[:, q % 2, k, mhl * 128:(mhl + 1) * 128],
                                              hbf[:, k, :], start=(k == 0), stop=(k == 7))
                            ii.then_inc(mpg, 1)
                            if ix >= 1:
                                t.wait_ge(mve, ix)
                            for k in range(8):
                                ii = t.matmul(pu_[:], w3s[:, q % 2, k, mhl * 128:(mhl + 1) * 128],
                                              hbf[:, k, :], start=(k == 0), stop=(k == 7))
                            ii.then_inc(mpu, 1)
                    t.wait_ge(mw2, 16 * (9 * tt + ei + 1))
                    if ei == 0:
                        t.wait_ge(mve, 144 * tt + 16)
                    else:
                        t.wait_ge(mps, 128 * tt + 16 * ei)
                    for mo in range(8):
                        wx = WIDX(tt, ei, mo)
                        if wx >= 2:
                            t.wait_ge(mva, wx - 1)
                        for mh in range(16):
                            ii = t.matmul(pp2[:, mo % 2, :], w2s[:, mh, mo * 128:(mo + 1) * 128],
                                          eo[:, mh, :], start=(mh == 0), stop=(mh == 15))
                        ii.then_inc(mpw, 1)
                t.wait_ge(mab, tt + 1)
                for mo in range(8):
                    if 8 * tt + mo >= 1:
                        t.wait_ge(mvp, 8 * tt + mo)
                    for c in range(4):
                        ii = t.transpose(pepi[:, c * 128:(c + 1) * 128],
                                         accb[:, mo, c * 128:(c + 1) * 128], idb[:])
                    ii.then_inc(mpe, 1)

        @b.gpsimd
        def _(g):
            xo = d["x2"].rearrange("(n c p) dd -> n p c dd", p=128, c=4)
            for tt in range(NT):
                g.wait_ge(mvr, tt + 1)
                for e in range(8):
                    if 8 * tt + e >= 2:
                        g.wait_ge(mpb, 8 * tt + e - 1)
                    g.dma_start(out=cw0[0:1, e % 2, :], in_=cwr[e:e + 1, :]).then_inc(mcw[e % 2], 16)
                for ei, e in enumerate(EO):
                    if ei == 0:
                        continue
                    for mh in range(16):
                        ix = IDX(tt, ei, mh)
                        six = 128 * tt + 16 * (ei - 1) + mh
                        g.wait_ge(mve, ix + 1)
                        g.wait_ge(macw, 8 * tt + ei)
                        g.wait_ge(mpw, 72 * tt + 8 * ei)
                        g.tensor_tensor(out=eo[:, mh, :], in0=eo1[:, mh % 2, :],
                                        in1=cwb[:, e, :], op=ALU.mult).then_inc(mps, 1)
                g.wait_ge(mvp, 8 * (tt + 1))
                g.dma_start(out=xo[tt], in_=xt[:, tt % 2]).then_inc(msx[tt % 2], 16)
            g.wait_ge(msx[0], 16 * (NT // 2))
            g.wait_ge(msx[1], 16 * (NT // 2))
    es.close()


def ph_ffnprep(ctx, d):
    nc = ctx.nc
    tg = "fp"
    es = contextlib.ExitStack()
    E = es.enter_context
    idb = E(nc.sbuf_tensor(tg + "idb", [128, 128], BF16))
    wim = E(nc.sbuf_tensor(tg + "wim", [128, 2, 3072], I32))
    wbb = E(nc.sbuf_tensor(tg + "wbb", [128, 2, 3072], BF16))
    tst = E(nc.sbuf_tensor(tg + "tst", [128, 4, 128], BF16))
    ptb = E(nc.psum_tensor(tg + "ptb", [128, 4, 128], BF16))

    pg, fc, fp, fv = ctx.sems(tg, "pg", "fc", "fp", "fv")
    fl = ctx.sems(tg, "fl0", "fl1")
    fs = ctx.sems(tg, "fs0", "fs1", "fs2", "fs3")

    units = []
    for name_s, name_d, nrc, W in (("wg", "wgT", 24, 1024),
                                   ("wup", "wuT", 24, 1024),
                                   ("wd", "wdT", 8, 3072)):
        sv = d[name_s].rearrange("(c p) n -> c p n", p=128)
        for rc in range(nrc):
            units.append((sv[rc], d[name_d], rc, W // 128, W))

    blocks_before = [0]
    for u in units:
        blocks_before.append(blocks_before[-1] + u[3])
    NB = blocks_before[-1]

    groups_before = [x // 4 for x in blocks_before]
    NG = NB // 4
    fsg = fs[0]

    with nc.Block(tg + "main") as b:
        @b.gpsimd
        def _(g):
            g.memset(idb[:], 0.0).then_inc(pg, 1)
            g.wait_ge(pg, 1)
            g.affine_select(out=idb[:], in_=idb[:], compare_op=ALU.not_equal,
                            fill=1.0, base=0, pattern=[[-1, 128]],
                            channel_multiplier=1).then_inc(pg, 1)
            for j, (sv, dvt, rc, ncb, W) in enumerate(units):
                for gg in range(ncb // 4):
                    gi = groups_before[j] + gg
                    g.wait_ge(fv, gi + 1)
                    for q in range(4):
                        cb = 4 * gg + q
                        g.dma_start(out=dvt[cb * 128:(cb + 1) * 128, rc * 128:(rc + 1) * 128],
                                    in_=tst[:, q, :]).then_inc(fsg, 16)
            g.wait_ge(fsg, 64 * NG)

        @b.sync
        def _(s):
            for j, (sv, dvt, rc, ncb, W) in enumerate(units):
                if j >= 2:
                    s.wait_ge(fc, j - 1)
                s.dma_start(out=wim[:, j % 2, 0:W], in_=sv).then_inc(fl[j % 2], 16)

        @b.scalar
        def _(s):
            for j, (sv, dvt, rc, ncb, W) in enumerate(units):
                s.wait_ge(fl[j % 2], 16 * (j // 2 + 1))
                if j >= 2:
                    s.wait_ge(fp, groups_before[j - 1])
                s.activation(out=wbb[:, j % 2, 0:W], in_=wim[:, j % 2, 0:W],
                             func=AF.Copy, scale=1.0 / 127.0).then_inc(fc, 1)

        @b.tensor
        def _(t):
            t.wait_ge(pg, 2)
            for j, (sv, dvt, rc, ncb, W) in enumerate(units):
                t.wait_ge(fc, j + 1)
                for gg in range(ncb // 4):
                    gi = groups_before[j] + gg
                    if gi >= 1:
                        t.wait_ge(fv, gi)
                    for q in range(4):
                        cb = 4 * gg + q
                        ii = t.transpose(ptb[:, q, :], wbb[:, j % 2, cb * 128:(cb + 1) * 128],
                                         idb[:])
                    ii.then_inc(fp, 1)

        @b.vector
        def _(v):
            for j, (sv, dvt, rc, ncb, W) in enumerate(units):
                for gg in range(ncb // 4):
                    gi = groups_before[j] + gg
                    v.wait_ge(fp, gi + 1)
                    if gi >= 1:
                        v.wait_ge(fsg, 64 * gi)
                    v.tensor_copy(out=tst[:], in_=ptb[:]).then_inc(fv, 1)
    es.close()


def ph_ffn(ctx, d):
    nc = ctx.nc
    tg = "f"
    es = contextlib.ExitStack()
    E = es.enter_context
    idb = E(nc.sbuf_tensor(tg + "idb", [128, 128], BF16))
    epst = E(nc.sbuf_tensor(tg + "epst", [128, 1], F32))
    wgs = E(nc.sbuf_tensor(tg + "wgs", [128, 8, 3072], BF16))
    wus = E(nc.sbuf_tensor(tg + "wus", [128, 8, 3072], BF16))
    wds = E(nc.sbuf_tensor(tg + "wds", [128, 24, 1024], BF16))
    xt = E(nc.sbuf_tensor(tg + "xt", [128, 4, DM], F32))
    bn6 = E(nc.sbuf_tensor(tg + "bn6", [128, 48], F32))
    bno = E(nc.sbuf_tensor(tg + "bno", [128, 8], F32))
    sd = E(nc.sbuf_tensor(tg + "sd", [128, 4], F32))
    rstd = E(nc.sbuf_tensor(tg + "rstd", [128, 4], F32))
    htok = E(nc.sbuf_tensor(tg + "htok", [128, 4, DM], BF16))
    hbf = E(nc.sbuf_tensor(tg + "hbf", [128, 8, 512], BF16))
    sg = E(nc.sbuf_tensor(tg + "sg", [128, 1, 512], F32))
    h3 = E(nc.sbuf_tensor(tg + "h3", [128, 24, 512], BF16))
    obsb = E(nc.sbuf_tensor(tg + "obsb", [128, 4, 512], BF16))
    pst = E(nc.psum_tensor(tg + "pst", [128, 512], BF16))
    pg_ = E(nc.psum_tensor(tg + "pg", [128, 512], F32))
    pu_ = E(nc.psum_tensor(tg + "pu", [128, 512], F32))
    pp2 = E(nc.psum_tensor(tg + "pp2", [128, 2, 512], F32))
    pepi = E(nc.psum_tensor(tg + "pepi", [128, 512], BF16))

    (pg, fv1, fa1, fv4, fp1, fv5, fpg, fpu, fas, fve, fpw, fae, fpe, fvp,
     fself) = ctx.sems(tg, "pg", "fv1", "fa1", "fv4", "fp1", "fv5", "fpg",
                       "fpu", "fas", "fve", "fpw", "fae", "fpe", "fvp", "fself")
    fw, = ctx.sems(tg, "fw")
    fx = ctx.sems(tg, "fx0", "fx1")
    fsx = ctx.sems(tg, "fsx0", "fsx1")

    with nc.Block(tg + "prep") as b:
        @b.gpsimd
        def _(g):
            g.memset(epst[:], 1.0e-5)
            g.memset(idb[:], 0.0).then_inc(pg, 1)
            g.wait_ge(pg, 1)
            g.affine_select(out=idb[:], in_=idb[:], compare_op=ALU.not_equal,
                            fill=1.0, base=0, pattern=[[-1, 128]],
                            channel_multiplier=1).then_inc(pg, 1)

        @b.sync
        def _(s):
            s.dma_start(out=wgs[:], in_=d["wgT"].rearrange("(c p) n -> p c n", p=128)).then_inc(fw, 16)
            s.dma_start(out=wus[:], in_=d["wuT"].rearrange("(c p) n -> p c n", p=128)).then_inc(fw, 16)
            s.dma_start(out=wds[:], in_=d["wdT"].rearrange("(c p) n -> p c n", p=128)).then_inc(fw, 16)

    NT = 8
    with nc.Block(tg + "main") as b:
        @b.sync
        def _(s):
            xr = d["x3"].rearrange("(n c p) dd -> n p c dd", p=128, c=4)
            for tt in range(NT):
                if tt >= 1:
                    s.wait_ge(fsx[(tt - 1) % 2], 16 * ((tt - 1) // 2 + 1))
                s.dma_start(out=xt[:], in_=xr[tt]).then_inc(fx[tt % 2], 16)

        @b.vector
        def _(v):
            ns = 0

            def selfw(ii):
                nonlocal ns
                ii.then_inc(fself, 1)
                ns += 1
                v.wait_ge(fself, ns)

            for tt in range(NT):
                v.wait_ge(fx[tt % 2], 16 * (tt // 2 + 1))
                if tt >= 1:
                    v.wait_ge(fvp, 8 * tt)
                for c in range(4):
                    ii = v.bn_stats(bn6[:, c * 12:c * 12 + 6], xt[:, c, 0:512])
                    ii = v.bn_stats(bn6[:, c * 12 + 6:c * 12 + 12], xt[:, c, 512:1024])
                selfw(ii)
                for c in range(4):
                    ii = v.bn_aggr(bno[:, 2 * c:2 * c + 2], bn6[:, c * 12:c * 12 + 12])
                ii.then_inc(fv1, 1)
                v.wait_ge(fa1, tt + 1)
                ii = v.reciprocal(rstd[:], sd[:])
                selfw(ii)
                if tt >= 1:
                    v.wait_ge(fp1, 8 * tt)
                for c in range(4):
                    ii = v.tensor_scalar(htok[:, c, :], xt[:, c, :],
                                         bno[:, 2 * c:2 * c + 1], rstd[:, c:c + 1],
                                         op0=ALU.subtract, op1=ALU.mult)
                ii.then_inc(fv4, 1)
                for k in range(8):
                    v.wait_ge(fp1, 8 * tt + k + 1)
                    v.tensor_copy(out=hbf[:, k, :], in_=pst[:]).then_inc(fv5, 1)
                for mh in range(24):
                    ix = 24 * tt + mh
                    v.wait_ge(fas, ix + 1)
                    v.wait_ge(fpu, ix + 1)
                    if tt >= 1:
                        v.wait_ge(fpw, 8 * tt)
                    v.tensor_tensor(out=h3[:, mh, :], in0=pu_[:], in1=sg[:, 0, :],
                                    op=ALU.mult).then_inc(fve, 1)
                for mo in range(8):
                    v.wait_ge(fpe, 8 * tt + mo + 1)
                    for c in range(4):
                        ii = v.tensor_tensor(out=xt[:, c, mo * 128:(mo + 1) * 128],
                                             in0=pepi[:, c * 128:(c + 1) * 128],
                                             in1=xt[:, c, mo * 128:(mo + 1) * 128],
                                             op=ALU.add)
                    ii.then_inc(fvp, 1)

        @b.scalar
        def _(s):
            vcol = bno[:].rearrange("p (c two) -> p c two", two=2)[:, :, 1:2].rearrange("p c o -> p (c o)")
            for tt in range(NT):
                s.wait_ge(fv1, tt + 1)
                s.activation(out=sd[:], in_=vcol, func=AF.Sqrt, bias=epst[:, 0:1]).then_inc(fa1, 1)
                for mh in range(24):
                    ix = 24 * tt + mh
                    s.wait_ge(fpg, ix + 1)
                    if ix >= 1:
                        s.wait_ge(fve, ix)
                    s.activation(out=sg[:, 0, :], in_=pg_[:],
                                 func=AF.Silu).then_inc(fas, 1)
                for mo in range(8):
                    j = 8 * tt + mo
                    s.wait_ge(fpw, j + 1)
                    if j >= 4:
                        s.wait_ge(fpe, j - 3)
                    s.activation(out=obsb[:, mo % 4, :], in_=pp2[:, mo % 2, :],
                                 func=AF.Copy).then_inc(fae, 1)

        @b.tensor
        def _(t):
            t.wait_ge(pg, 2)
            t.wait_ge(fw, 48)
            for tt in range(NT):
                t.wait_ge(fv4, tt + 1)
                for k in range(8):
                    if 8 * tt + k >= 1:
                        t.wait_ge(fv5, 8 * tt + k)
                    for c in range(4):
                        ii = t.transpose(pst[:, c * 128:(c + 1) * 128],
                                         htok[:, c, k * 128:(k + 1) * 128], idb[:])
                    ii.then_inc(fp1, 1)
                t.wait_ge(fv5, 8 * (tt + 1))
                for mh in range(24):
                    ix = 24 * tt + mh
                    if ix >= 1:
                        t.wait_ge(fas, ix)
                    for k in range(8):
                        ii = t.matmul(pg_[:], wgs[:, k, mh * 128:(mh + 1) * 128],
                                      hbf[:, k, :], start=(k == 0), stop=(k == 7))
                    ii.then_inc(fpg, 1)
                    if ix >= 1:
                        t.wait_ge(fve, ix)
                    for k in range(8):
                        ii = t.matmul(pu_[:], wus[:, k, mh * 128:(mh + 1) * 128],
                                      hbf[:, k, :], start=(k == 0), stop=(k == 7))
                    ii.then_inc(fpu, 1)
                t.wait_ge(fve, 24 * (tt + 1))
                for mo in range(9):
                    if mo < 8:
                        j = 8 * tt + mo
                        if j >= 2:
                            t.wait_ge(fae, j - 1)
                        for mh in range(24):
                            ii = t.matmul(pp2[:, mo % 2, :], wds[:, mh, mo * 128:(mo + 1) * 128],
                                          h3[:, mh, :], start=(mh == 0), stop=(mh == 23))
                        ii.then_inc(fpw, 1)
                    if mo >= 1:
                        mt = mo - 1
                        t.wait_ge(fae, 8 * tt + mt + 1)
                        if 8 * tt + mt >= 1:
                            t.wait_ge(fvp, 8 * tt + mt)
                        for c in range(4):
                            ii = t.transpose(pepi[:, c * 128:(c + 1) * 128],
                                             obsb[:, mt % 4, c * 128:(c + 1) * 128], idb[:])
                        ii.then_inc(fpe, 1)

        @b.gpsimd
        def _(g):
            yo = d["y"].rearrange("(n c p) dd -> n p c dd", p=128, c=4)
            for tt in range(NT):
                g.wait_ge(fvp, 8 * (tt + 1))
                g.dma_start(out=yo[tt], in_=xt[:]).then_inc(fsx[tt % 2], 16)
            g.wait_ge(fsx[0], 16 * (NT // 2))
            g.wait_ge(fsx[1], 16 * (NT // 2))
    es.close()

EMIT = {
    "qkv0": lambda c, d: ph_qkv(c, d, 0),
    "qkv1": lambda c, d: ph_qkv(c, d, 1),
    "attn0": lambda c, d: ph_attn(c, d, 0),
    "attn1": lambda c, d: ph_attn(c, d, 1),
    "oproj0": lambda c, d: ph_oproj(c, d, 0),
    "oproj1": lambda c, d: ph_oproj(c, d, 1),
    "moeprep": lambda c, d: ph_moeprep(c, d),
    "moe": lambda c, d: ph_moe(c, d),
    "ffnprep": lambda c, d: ph_ffnprep(c, d),
    "ffn": lambda c, d: ph_ffn(c, d),
}

GROUPS = [ALL_PHASES]


def build_group(phases):
    nc = bass.Bass("TRN2", target_bir_lowering=False)
    ctx = Ctx(nc)
    reads = set()
    writes = set()
    for p in phases:
        r, w = PHASE_IO[p]
        reads.update(x for x in r if x not in writes)
        writes.update(w)
    gi = [ALL_PHASES.index(p) for p in phases]
    later_reads = set()
    for p in ALL_PHASES:
        if ALL_PHASES.index(p) > max(gi) and p not in phases:
            later_reads.update(PHASE_IO[p][0])
    kinds = {}
    for name in set(reads) | set(writes):
        if name in writes:
            kinds[name] = ("ExternalOutput"
                           if (name in later_reads or TENSORS[name][2] == "out")
                           else "Internal")
        else:
            kinds[name] = "ExternalInput"
    d = {}
    for name in sorted(kinds):
        k = kinds[name]
        shape, dt, _ = TENSORS[name]
        d[name] = nc.dram_tensor(name, shape, dt, kind=k).ap()
    with ctx.stack:
        for p in phases:
            with nc.cleanup_on_exit():
                EMIT[p](ctx, d)
    ext_in = sorted(n for n, k in kinds.items() if k == "ExternalInput")
    ext_out = sorted(n for n, k in kinds.items() if k == "ExternalOutput")
    return nc, ext_in, ext_out


def prep_inputs(inputs):
    common = {
        "qkvw": np.ascontiguousarray(inputs["L0_attn_qkv_w"].reshape(DM, NH * 192)),
        "qkvb": np.ascontiguousarray(inputs["L0_attn_qkv_b"]),
        "outw": np.ascontiguousarray(inputs["L0_attn_out_w"].reshape(DM, DM)),
        "outb": np.ascontiguousarray(inputs["L0_attn_out_b"]),
        "rw": np.ascontiguousarray(inputs["L0_ffn_router_w"]),
        "eb": np.ascontiguousarray(inputs["L0_ffn_experts_bias"]),
        "w1c": np.ascontiguousarray(np.concatenate([inputs["L0_ffn_w1"], inputs["L0_ffn_sw1"]], axis=0)),
        "w3c": np.ascontiguousarray(np.concatenate([inputs["L0_ffn_w3"], inputs["L0_ffn_sw3"]], axis=0)),
        "w2c": np.ascontiguousarray(np.concatenate([inputs["L0_ffn_w2"], inputs["L0_ffn_sw2"]], axis=0)),
        "wq": np.ascontiguousarray(inputs["L1_attn_Wq"]),
        "wk": np.ascontiguousarray(inputs["L1_attn_Wk"]),
        "wv": np.ascontiguousarray(inputs["L1_attn_Wv"]),
        "wu": np.ascontiguousarray(inputs["L1_attn_Wu"]),
        "bu": np.ascontiguousarray(inputs["L1_attn_bu"]),
        "wg": np.ascontiguousarray(inputs["L1_ffn_w_gate"]),
        "wup": np.ascontiguousarray(inputs["L1_ffn_w_up"]),
        "wd": np.ascontiguousarray(inputs["L1_ffn_w_down"]),
    }
    pools = []
    for c in range(NCORE):
        p = dict(common)
        p["x0"] = np.ascontiguousarray(np.asarray(inputs["L0_attn_x"])[c])
        pools.append(p)
    return pools


_BUILD_CACHE = {}


def kernel(**inputs):
    pools = prep_inputs(inputs)
    for phases in GROUPS:
        key = tuple(phases)
        if key not in _BUILD_CACHE:
            _BUILD_CACHE[key] = build_group(phases)
        nc, ext_in, ext_out = _BUILD_CACHE[key]
        in_maps = [{n: pools[c][n] for n in ext_in} for c in range(NCORE)]
        res = run_bass_kernel_spmd(nc, in_maps, core_ids=list(range(NCORE)))
        for c in range(NCORE):
            for n in ext_out:
                pools[c][n] = res.results[c][n]
    return np.stack([pools[c]["y"] for c in range(NCORE)])


# version 07
# speedup vs baseline: 1.2910x; 1.0903x over previous
import contextlib
import numpy as np
import concourse.bass as bass
import concourse.mybir as mybir
from concourse.bass_utils import run_bass_kernel_spmd

F32 = mybir.dt.float32
BF16 = mybir.dt.bfloat16
I32 = mybir.dt.int32
AF = mybir.ActivationFunctionType
ALU = mybir.AluOpType
AX = mybir.AxisListType

T = 4096
DM = 1024
NH, HD = 16, 64
NCORE = 8
NEG = -1.0e38

TENSORS = {
    "x0":    ([T, DM], F32, "inp"),
    "qkvw":  ([DM, NH * 192], F32, "inp"),
    "qkvb":  ([NH, 192], F32, "inp"),
    "outw":  ([DM, DM], F32, "inp"),
    "outb":  ([DM], F32, "inp"),
    "rw":    ([8, DM], F32, "inp"),
    "eb":    ([8], F32, "inp"),
    "w1c":   ([9, DM, 2048], F32, "inp"),
    "w3c":   ([9, DM, 2048], F32, "inp"),
    "w2c":   ([9, 2048, DM], F32, "inp"),
    "wq":    ([HD, HD], F32, "inp"),
    "wk":    ([HD, HD], F32, "inp"),
    "wv":    ([HD, HD], F32, "inp"),
    "wu":    ([DM, DM], F32, "inp"),
    "bu":    ([DM], F32, "inp"),
    "wg":    ([3072, DM], I32, "inp"),
    "wup":   ([3072, DM], I32, "inp"),
    "wd":    ([DM, 3072], I32, "inp"),
    "qT":    ([NH, HD, T], BF16, "scr"),
    "kT":    ([NH, HD, T], BF16, "scr"),
    "vv":    ([NH, T, HD], BF16, "scr"),
    "attnT": ([DM, T], BF16, "scr"),
    "x1":    ([T, DM], F32, "scr"),
    "x2":    ([T, DM], F32, "scr"),
    "x3":    ([T, DM], F32, "scr"),
    "w1b":   ([9, DM, 2048], BF16, "scr"),
    "w3b":   ([9, DM, 2048], BF16, "scr"),
    "w2b":   ([9, 2048, DM], BF16, "scr"),
    "wgT":   ([DM, 3072], BF16, "scr"),
    "wuT":   ([DM, 3072], BF16, "scr"),
    "wdT":   ([3072, DM], BF16, "scr"),
    "y":     ([T, DM], F32, "out"),
}

PHASE_IO = {
    "qkv0":    (["x0", "qkvw", "qkvb"], ["qT", "kT", "vv"]),
    "attn0":   (["qT", "kT", "vv"], ["attnT"]),
    "oproj0":  (["attnT", "outw", "outb", "x0"], ["x1"]),
    "moeprep": (["w1c", "w3c", "w2c"], ["w1b", "w3b", "w2b"]),
    "moe":     (["x1", "rw", "eb", "w1b", "w3b", "w2b"], ["x2"]),
    "qkv1":    (["x2", "wq", "wk", "wv"], ["qT", "kT", "vv"]),
    "attn1":   (["qT", "kT", "vv"], ["attnT"]),
    "oproj1":  (["attnT", "wu", "bu", "x2"], ["x3"]),
    "ffnprep": (["wg", "wup", "wd"], ["wgT", "wuT", "wdT"]),
    "ffn":     (["x3", "wgT", "wuT", "wdT"], ["y"]),
}
ALL_PHASES = list(PHASE_IO)


class Ctx:
    def __init__(self, nc):
        self.nc = nc
        self.stack = contextlib.ExitStack()

    def sems(self, tag, *names):
        return [self.nc.alloc_semaphore(tag + "_" + n) for n in names]


def ph_qkv(ctx, d, layer):
    nc = ctx.nc
    tg = "q%d" % layer
    xin = d["x0"] if layer == 0 else d["x2"]
    es = contextlib.ExitStack()
    E = es.enter_context
    idf = E(nc.sbuf_tensor(tg + "idf", [128, 128], F32))
    epst = E(nc.sbuf_tensor(tg + "epst", [128, 1], F32))
    idb = E(nc.sbuf_tensor(tg + "idb", [128, 128], BF16))
    pbias = E(nc.psum_tensor(tg + "pbias", [128, 512], F32))
    if layer == 0:
        qkwb = E(nc.sbuf_tensor(tg + "qkwb", [128, 8, NH, 128], BF16))
        vwb = E(nc.sbuf_tensor(tg + "vwb", [128, 8, DM], BF16))
        qkb = E(nc.sbuf_tensor(tg + "qkb", [128, NH], F32))
        vb = E(nc.sbuf_tensor(tg + "vb", [128, DM], F32))
        wf = E(nc.sbuf_tensor(tg + "wf", [128, 2, 3072], F32))
        bsrc = E(nc.sbuf_tensor(tg + "bsrc", [16, 192], F32))
        vrow = E(nc.sbuf_tensor(tg + "vrow", [1, DM], F32))
        ones1 = E(nc.sbuf_tensor(tg + "ones1", [1, 128], F32))
    else:
        wsrc = E(nc.sbuf_tensor(tg + "wsrc", [64, 192], F32))
        wqkT = E(nc.sbuf_tensor(tg + "wqkT", [128, 128], BF16))
        wvT = E(nc.sbuf_tensor(tg + "wvT", [128, 64], BF16))
        vwb = E(nc.sbuf_tensor(tg + "vwb", [128, 8, DM], BF16))

    pl, pg, pa, pp, pbs, pvr, pdup, pvw = ctx.sems(tg, "pl", "pg", "pa", "pp", "pbs", "pvr", "pdup", "pvw")
    pw = ctx.sems(tg, "pw0", "pw1")

    with nc.Block(tg + "prep") as b:
        @b.gpsimd
        def _(g):
            g.memset(epst[:], 1.0e-5)
            g.memset(idf[:], 0.0).then_inc(pg, 1)
            g.wait_ge(pg, 1)
            g.affine_select(out=idf[:], in_=idf[:], compare_op=ALU.not_equal,
                            fill=1.0, base=0, pattern=[[-1, 128]],
                            channel_multiplier=1)
            g.memset(idb[:], 0.0).then_inc(pg, 1)
            g.wait_ge(pg, 2)
            g.affine_select(out=idb[:], in_=idb[:], compare_op=ALU.not_equal,
                            fill=1.0, base=0, pattern=[[-1, 128]],
                            channel_multiplier=1).then_inc(pg, 1)
            if layer == 0:
                g.memset(ones1[:], 1.0).then_inc(pg, 1)


        if layer == 0:
            @b.sync
            def _(s):
                s.dma_start(out=bsrc[:], in_=d["qkvb"]).then_inc(pbs, 16)
                s.dma_start(out=vrow[0:1, :].rearrange("o (h dd) -> o h dd", h=NH),
                            in_=d["qkvb"][None, :, 128:192]).then_inc(pvr, 16)
                qw = d["qkvw"].rearrange("(c p) n -> c p n", p=128)
                for k in range(8):
                    if k >= 2:
                        s.wait_ge(pa, k - 1)
                    s.dma_start(out=wf[:, k % 2, :], in_=qw[k]).then_inc(pw[k % 2], 16)

            @b.scalar
            def _(s):
                for k in range(8):
                    s.wait_ge(pw[k % 2], 16 * (k // 2 + 1))
                    wv_ = wf[:, k % 2, :].rearrange("p (h s) -> p h s", h=NH)
                    s.activation(out=qkwb[:, k, :, 0:64], in_=wv_[:, :, 0:64],
                                 func=AF.Copy, scale=0.125)
                    s.activation(out=qkwb[:, k, :, 64:128], in_=wv_[:, :, 64:128],
                                 func=AF.Copy)
                    s.activation(out=vwb[:, k, :].rearrange("p (h s) -> p h s", h=NH),
                                 in_=wv_[:, :, 128:192], func=AF.Copy).then_inc(pa, 1)
                s.wait_ge(pp, 1)
                s.activation(out=qkb[0:64, :], in_=pbias[0:64, 0:NH], func=AF.Copy,
                             scale=0.125)
                s.activation(out=qkb[64:128, :], in_=pbias[64:128, 0:NH],
                             func=AF.Copy).then_inc(pa, 1)

            @b.tensor
            def _(t):
                t.wait_ge(pbs, 16)
                t.wait_ge(pg, 3)
                t.transpose(pbias[:, 0:16], bsrc[:, 0:128], idf[0:16, 0:16]).then_inc(pp, 1)
        else:
            @b.sync
            def _(s):
                s.dma_start(out=wsrc[:, 0:64], in_=d["wq"]).then_inc(pl, 16)
                s.dma_start(out=wsrc[:, 64:128], in_=d["wk"]).then_inc(pl, 16)
                s.dma_start(out=wsrc[:, 128:192], in_=d["wv"]).then_inc(pl, 16)

            @b.tensor
            def _(t):
                t.wait_ge(pl, 48)
                t.wait_ge(pg, 3)
                t.transpose(pbias[0:64, 0:64], wsrc[:, 0:64], idf[0:64, 0:64])
                t.transpose(pbias[0:64, 64:128], wsrc[:, 64:128], idf[0:64, 0:64])
                t.transpose(pbias[0:64, 128:192], wsrc[:, 128:192], idf[0:64, 0:64]).then_inc(pp, 1)

            @b.scalar
            def _(s):
                s.wait_ge(pp, 1)
                sc = float(DM) ** (-0.25)
                s.activation(out=wqkT[0:64, 0:64], in_=pbias[0:64, 0:64], func=AF.Copy, scale=sc)
                s.activation(out=wqkT[0:64, 64:128], in_=pbias[0:64, 64:128], func=AF.Copy, scale=sc)
                s.activation(out=wvT[0:64, :], in_=pbias[0:64, 128:192], func=AF.Copy).then_inc(pa, 1)

            @b.gpsimd
            def _(g):
                g.memset(vwb[:], 0.0).then_inc(pvw, 1)
                g.wait_ge(pa, 1)
                g.dma_start(out=wqkT[64:128, :], in_=wqkT[0:64, :]).then_inc(pdup, 16)
                g.dma_start(out=wvT[64:128, :], in_=wvT[0:64, :]).then_inc(pdup, 16)

            @b.vector
            def _(v):
                v.wait_ge(pvw, 1)
                v.wait_ge(pdup, 32)
                for k in range(8):
                    v.tensor_copy(out=vwb[0:64, k, k * 128:k * 128 + 64],
                                  in_=wvT[0:64, :])
                    ii = v.tensor_copy(out=vwb[64:128, k, k * 128 + 64:k * 128 + 128],
                                       in_=wvT[64:128, :])
                ii.then_inc(pvw, 1)

    if layer == 0:
        with nc.Block(tg + "prep2") as b:
            pv2, = ctx.sems(tg, "pv2")

            @b.tensor
            def _(t):
                t.wait_ge(pvr, 16)
                t.matmul(pbias[:, 0:512], ones1[:], vrow[:, 0:512], start=True, stop=True).then_inc(pv2, 1)
                t.wait_ge(pv2, 2)
                t.matmul(pbias[:, 0:512], ones1[:], vrow[:, 512:1024], start=True, stop=True).then_inc(pv2, 1)

            @b.vector
            def _(v):
                v.wait_ge(pv2, 1)
                v.tensor_copy(out=vb[:, 0:512], in_=pbias[:, 0:512]).then_inc(pv2, 1)
                v.wait_ge(pv2, 3)
                v.tensor_copy(out=vb[:, 512:1024], in_=pbias[:, 0:512]).then_inc(pv2, 1)

    xt = E(nc.sbuf_tensor(tg + "xt", [128, 2, 4, DM], F32))
    bn6 = E(nc.sbuf_tensor(tg + "bn6", [128, 48], F32))
    bno = E(nc.sbuf_tensor(tg + "bno", [128, 8], F32))
    sd = E(nc.sbuf_tensor(tg + "sd", [128, 4], F32))
    rstd = E(nc.sbuf_tensor(tg + "rstd", [128, 4], F32))
    htok = E(nc.sbuf_tensor(tg + "htok", [128, 4, DM], BF16))
    hf = E(nc.sbuf_tensor(tg + "hf", [128, 8, 512], BF16))
    qksb = E(nc.sbuf_tensor(tg + "qksb", [128, 2, 512], BF16))
    vsb = E(nc.sbuf_tensor(tg + "vsb", [128, 2, NH, HD], BF16))
    pst = E(nc.psum_tensor(tg + "pst", [128, 512], BF16))
    pqk = E(nc.psum_tensor(tg + "pqk", [128, 2, 512], F32))
    pvv = E(nc.psum_tensor(tg + "pvv", [128, 2, 512], F32))

    sv1, sv2, sa1, sv3, sv4, sp1, sv5, sp2, sa2, sp3, sv6 = ctx.sems(
        tg, "sv1", "sv2", "sa1", "sv3", "sv4", "sp1", "sv5", "sp2", "sa2",
        "sp3", "sv6")
    sx = ctx.sems(tg, "sx0", "sx1")
    sqs = ctx.sems(tg, "sqs0", "sqs1")
    svs = ctx.sems(tg, "svs0", "svs1")

    NT = 8

    with nc.Block(tg + "main") as b:
        @b.sync
        def _(s):
            xr = xin.rearrange("(n c p) d -> n p c d", p=128, c=4)
            for i in range(NT):
                if i >= 2:
                    s.wait_ge(sv4, i - 1)
                s.dma_start(out=xt[:, i % 2], in_=xr[i]).then_inc(sx[i % 2], 16)

        @b.vector
        def _(v):
            for i in range(NT):
                v.wait_ge(sx[i % 2], 16 * (i // 2 + 1))
                for c in range(4):
                    ii = v.bn_stats(bn6[:, c * 12:c * 12 + 6], xt[:, i % 2, c, 0:512])
                    ii = v.bn_stats(bn6[:, c * 12 + 6:c * 12 + 12], xt[:, i % 2, c, 512:1024])
                ii.then_inc(sv1, 1)
                v.wait_ge(sv1, i + 1)
                for c in range(4):
                    ii = v.bn_aggr(bno[:, 2 * c:2 * c + 2], bn6[:, c * 12:c * 12 + 12])
                ii.then_inc(sv2, 1)
                v.wait_ge(sa1, i + 1)
                v.reciprocal(rstd[:], sd[:]).then_inc(sv3, 1)
                v.wait_ge(sv3, i + 1)
                for c in range(4):
                    ii = v.tensor_scalar(htok[:, c, :], xt[:, i % 2, c, :],
                                         bno[:, 2 * c:2 * c + 1], rstd[:, c:c + 1],
                                         op0=ALU.subtract, op1=ALU.mult)
                ii.then_inc(sv4, 1)
                for k in range(8):
                    v.wait_ge(sp1, 8 * i + k + 1)
                    v.tensor_copy(out=hf[:, k, :], in_=pst[:]).then_inc(sv5, 1)
                for h in range(NH):
                    v.wait_ge(sp2, 16 * i + h + 1)
                    if 16 * i + h >= 2:
                        v.wait_ge(sqs[h % 2], 32 * ((16 * i + h) // 2))
                    if layer == 0:
                        v.tensor_scalar(qksb[:, h % 2, :], pqk[:, h % 2, :],
                                        qkb[:, h:h + 1], None,
                                        op0=ALU.add).then_inc(sa2, 1)
                    else:
                        v.tensor_copy(out=qksb[:, h % 2, :],
                                      in_=pqk[:, h % 2, :]).then_inc(sa2, 1)
                for c in range(4):
                    if 4 * i + c >= 2:
                        v.wait_ge(svs[c % 2], 16 * ((4 * i + c) // 2))
                    for hh in range(2):
                        v.wait_ge(sp3, 8 * i + 2 * c + hh + 1)
                        dst = vsb[:, c % 2, hh * 8:(hh + 1) * 8, :].rearrange("p h dd -> p (h dd)")
                        if layer == 0:
                            ii = v.tensor_tensor(out=dst, in0=pvv[:, (2 * c + hh) % 2, :],
                                                 in1=vb[:, hh * 512:(hh + 1) * 512], op=ALU.add)
                        else:
                            ii = v.tensor_copy(out=dst, in_=pvv[:, (2 * c + hh) % 2, :])
                    ii.then_inc(sv6, 1)

        @b.scalar
        def _(s):
            vcol = bno[:].rearrange("p (c two) -> p c two", two=2)[:, :, 1:2].rearrange("p c o -> p (c o)")
            for i in range(NT):
                s.wait_ge(sv2, i + 1)
                s.activation(out=sd[:], in_=vcol, func=AF.Sqrt, bias=epst[:, 0:1]).then_inc(sa1, 1)

        @b.tensor
        def _(t):
            t.wait_ge(pa, 9 if layer == 0 else 1)
            if layer == 1:
                t.wait_ge(pdup, 32)
                t.wait_ge(pvw, 2)
            for i in range(NT):
                t.wait_ge(sv4, i + 1)
                for k in range(8):
                    if 8 * i + k >= 1:
                        t.wait_ge(sv5, 8 * i + k)
                    for c in range(4):
                        ii = t.transpose(pst[:, c * 128:(c + 1) * 128],
                                         htok[:, c, k * 128:(k + 1) * 128], idb[:])
                    ii.then_inc(sp1, 1)
                t.wait_ge(sv5, 8 * (i + 1))
                for h in range(NH):
                    if 16 * i + h >= 2:
                        t.wait_ge(sa2, 16 * i + h - 1)
                    if layer == 0:
                        for k in range(8):
                            ii = t.matmul(pqk[:, h % 2, :], qkwb[:, k, h, :], hf[:, k, :],
                                          start=(k == 0), stop=(k == 7))
                    else:
                        hr = (h % 2) * 64
                        ii = t.matmul(pqk[:, h % 2, :], wqkT[hr:hr + 64, :],
                                      hf[hr:hr + 64, h // 2, :], start=True, stop=True)
                    ii.then_inc(sp2, 1)
                for c in range(4):
                    for hh in range(2):
                        j = 8 * i + 2 * c + hh
                        if j >= 2:
                            t.wait_ge(sv6, 4 * i + c)
                        if layer == 0:
                            for k in range(8):
                                ii = t.matmul(pvv[:, (2 * c + hh) % 2, :],
                                              hf[:, k, c * 128:(c + 1) * 128],
                                              vwb[:, k, hh * 512:(hh + 1) * 512],
                                              start=(k == 0), stop=(k == 7))
                        else:
                            for k in range(8):
                                ii = t.matmul(pvv[:, (2 * c + hh) % 2, :],
                                              hf[:, k, c * 128:(c + 1) * 128],
                                              vwb[:, k, hh * 512:(hh + 1) * 512],
                                              start=(k == 0), stop=(k == 7))
                        ii.then_inc(sp3, 1)

        @b.gpsimd
        def _(g):
            vvr = d["vv"].rearrange("h (n p) dd -> n p h dd", p=128)
            for i in range(NT):
                for h in range(NH):
                    g.wait_ge(sa2, 16 * i + h + 1)
                    g.dma_start(out=d["qT"][h][:, i * 512:(i + 1) * 512],
                                in_=qksb[0:64, h % 2, :]).then_inc(sqs[h % 2], 16)
                    g.dma_start(out=d["kT"][h][:, i * 512:(i + 1) * 512],
                                in_=qksb[64:128, h % 2, :]).then_inc(sqs[h % 2], 16)
                for c in range(4):
                    g.wait_ge(sv6, 4 * i + c + 1)
                    g.dma_start(out=vvr[4 * i + c], in_=vsb[:, c % 2]).then_inc(svs[c % 2], 16)
            g.wait_ge(sqs[0], 32 * NT * NH // 2)
            g.wait_ge(sqs[1], 32 * NT * NH // 2)
            g.wait_ge(svs[0], 16 * 2 * NT)
            g.wait_ge(svs[1], 16 * 2 * NT)
    es.close()



def ph_attn(ctx, d, layer):
    nc = ctx.nc
    tg = "a%d" % layer
    es = contextlib.ExitStack()
    E = es.enter_context
    kbuf = E(nc.sbuf_tensor(tg + "kbuf", [65, 2, T], BF16))
    qbuf = E(nc.sbuf_tensor(tg + "qbuf", [65, 2, T], BF16))
    vab = E(nc.sbuf_tensor(tg + "vab", [128, 2, 32, 65], BF16))
    sq = E(nc.sbuf_tensor(tg + "sq", [64, T], BF16))
    nrm = E(nc.sbuf_tensor(tg + "nrm", [1, 2, T], F32))
    kn2 = E(nc.sbuf_tensor(tg + "kn2", [1, T], F32))
    km2 = E(nc.sbuf_tensor(tg + "km2", [1, 1], F32))
    kms = E(nc.sbuf_tensor(tg + "kms", [1, 1], F32))
    kmn = E(nc.sbuf_tensor(tg + "kmn", [1, 2, 1], F32))
    ones64 = E(nc.sbuf_tensor(tg + "ones64", [64, 1], BF16))
    PT = E(nc.sbuf_tensor(tg + "PT", [128, 32, 512], BF16))
    zrow = E(nc.sbuf_tensor(tg + "zrow", [1, 512], F32))
    zbsb = E(nc.sbuf_tensor(tg + "zbsb", [64, 512], F32))
    aosb = E(nc.sbuf_tensor(tg + "aosb", [64, 2, 512], BF16))
    binm = E(nc.sbuf_tensor(tg + "binm", [128, 4, 512], BF16))
    onesf = E(nc.sbuf_tensor(tg + "onesf", [1, 64], F32))
    ps = E(nc.psum_tensor(tg + "ps", [128, 2, 512], F32))
    po = E(nc.psum_tensor(tg + "po", [128, 512], F32))
    pzb = E(nc.psum_tensor(tg + "pzb", [64, 512], F32))
    pn = E(nc.psum_tensor(tg + "pn", [1, 2, 512], F32))

    pg, = ctx.sems(tg, "pg")
    with nc.Block(tg + "prep") as b:
        @b.gpsimd
        def _(g):
            g.memset(onesf[:], 1.0)
            g.memset(ones64[:], 1.0)
            g.memset(kbuf[64:65, :, :], 1.0)
            g.memset(vab[:], 1.0)
            g.memset(binm[:], 1.0).then_inc(pg, 1)
            g.wait_ge(pg, 1)
            for j in range(4):
                g.affine_select(out=binm[:, j, :], in_=binm[:, j, :],
                                compare_op=ALU.is_ge, fill=0.0, base=-128 * j,
                                pattern=[[1, 512]], channel_multiplier=-1)

    units = [(h, g) for h in range(NH) for g in range(8)]
    NU = len(units)

    def K_of(g):
        return 4 * g + 4

    P1 = [0]
    for (h, g) in units:
        P1.append(P1[-1] + K_of(g))

    (an, pnm, vn, vq, mm, ae, vd, spo, vz, pz, vz2, va) = ctx.sems(
        tg, "an", "pnm", "vn", "vq", "mm", "ae", "vd", "spo", "vz", "pz",
        "vz2", "va")
    ld = ctx.sems(tg, "ld0", "ld1")
    gs = ctx.sems(tg, "gs0", "gs1")

    with nc.Block(tg + "main") as b:
        @b.sync
        def _(s):
            for h in range(NH):
                if h >= 2:
                    s.wait_ge(spo, 8 * (h - 1))
                s.dma_start(out=kbuf[0:64, h % 2, :], in_=d["kT"][h]).then_inc(ld[h % 2], 16)
                s.dma_start(out=qbuf[0:64, h % 2, :], in_=d["qT"][h]).then_inc(ld[h % 2], 16)
                s.dma_start(out=vab[:, h % 2, :, 0:64],
                            in_=d["vv"][h].rearrange("(kt p) dd -> p kt dd", p=128)).then_inc(ld[h % 2], 16)

        def norm_scalar(s, h):
            s.wait_ge(ld[h % 2], 48 * (h // 2 + 1))
            if h >= 1:
                s.wait_ge(pnm, 16 * h)
            s.activation(out=sq[:], in_=kbuf[0:64, h % 2, :],
                         func=AF.Square).then_inc(an, 1)
            s.wait_ge(vn, 18 * h + 9)
            s.activation(out=kms[:], in_=km2[:], func=AF.Sqrt).then_inc(an, 1)
            s.wait_ge(pnm, 16 * h + 8)
            s.activation(out=sq[:], in_=qbuf[0:64, h % 2, :],
                         func=AF.Square).then_inc(an, 1)
            s.wait_ge(vn, 18 * h + 18)
            if h >= 2:
                s.wait_ge(vq, 8 * (h - 1))
            s.activation(out=nrm[0:1, h % 2, :], in_=kn2[0:1, :],
                         func=AF.Sqrt).then_inc(an, 1)

        def norm_pe(t, h):
            t.wait_ge(an, 4 * h + 1)
            for c in range(8):
                if c >= 2:
                    t.wait_ge(vn, 18 * h + c - 1)
                elif h >= 1:
                    t.wait_ge(vn, 18 * h)
                t.matmul(pn[0:1, c % 2, :], ones64[:],
                         sq[:, c * 512:(c + 1) * 512], start=True,
                         stop=True).then_inc(pnm, 1)
            t.wait_ge(an, 4 * h + 3)
            for c in range(8):
                t.wait_ge(vn, 18 * h + c + 9 if c >= 2 else 18 * h + c + 7)
                t.matmul(pn[0:1, c % 2, :], ones64[:],
                         sq[:, c * 512:(c + 1) * 512], start=True,
                         stop=True).then_inc(pnm, 1)

        def norm_vector(v, h):
            if h >= 1:
                v.wait_ge(an, 4 * h)
            for c in range(8):
                v.wait_ge(pnm, 16 * h + c + 1)
                v.tensor_copy(out=kn2[0:1, c * 512:(c + 1) * 512],
                              in_=pn[0:1, c % 2, :]).then_inc(vn, 1)
            v.reduce_max(km2[:], kn2[:], axis=AX.X).then_inc(vn, 1)
            v.wait_ge(an, 4 * h + 2)
            v.tensor_scalar(kmn[0:1, h % 2, :], kms[:], -1.0, None,
                            op0=ALU.mult).then_inc(vn, 1)
            for c in range(8):
                v.wait_ge(pnm, 16 * h + 8 + c + 1)
                v.tensor_copy(out=kn2[0:1, c * 512:(c + 1) * 512],
                              in_=pn[0:1, c % 2, :]).then_inc(vn, 1)

        @b.tensor
        def _(t):
            norm_pe(t, 0)
            for u, (h, g) in enumerate(units):
                K = K_of(g)
                t.wait_ge(vq, u + 1)
                for kt in range(K):
                    if P1[u] + kt >= 2:
                        t.wait_ge(ae, P1[u] + kt - 1)
                    t.matmul(ps[:, kt % 2, :], kbuf[0:65, h % 2, kt * 128:(kt + 1) * 128],
                             qbuf[0:65, h % 2, g * 512:(g + 1) * 512],
                             start=True, stop=True).then_inc(mm, 1)
                if u >= 1:
                    t.wait_ge(vz, u)
                    if u >= 2:
                        t.wait_ge(vz2, u - 1)
                    t.matmul(pzb[:], onesf[:], zrow[:], start=True, stop=True).then_inc(pz, 1)
                if u >= 1:
                    t.wait_ge(va, u)
                for kt in range(K):
                    t.wait_ge(ae, P1[u] + kt + 1)
                    if kt >= K - 4:
                        t.wait_ge(vd, 4 * u + (kt - (K - 4)) + 1)
                    ii = t.matmul(po[0:65, :], vab[:, h % 2, kt, :],
                                  PT[:, kt, :], start=(kt == 0), stop=(kt == K - 1))
                ii.then_inc(spo, 1)
                if g == 0 and h + 1 < NH:
                    norm_pe(t, h + 1)
            t.wait_ge(vz, NU)
            t.wait_ge(vz2, NU - 1)
            t.matmul(pzb[:], onesf[:], zrow[:], start=True, stop=True).then_inc(pz, 1)

        @b.vector
        def _(v):
            norm_vector(v, 0)

            def epilogue(u2):
                v.wait_ge(spo, u2 + 1)
                if u2 >= 1:
                    v.wait_ge(pz, u2)
                v.reciprocal(zrow[:], po[64:65, :]).then_inc(vz, 1)
                v.wait_ge(pz, u2 + 1)
                v.tensor_copy(out=zbsb[:], in_=pzb[0:64, :]).then_inc(vz2, 1)
                if u2 >= 2:
                    v.wait_ge(gs[u2 % 2], 16 * (u2 // 2))
                v.tensor_tensor(out=aosb[:, u2 % 2, :], in0=po[0:64, :], in1=zbsb[:],
                                op=ALU.mult).then_inc(va, 1)

            for u, (h, g) in enumerate(units):
                K = K_of(g)
                v.wait_ge(an, 4 * h + 4)
                if u >= 16:
                    v.wait_ge(spo, u - 15)
                v.tensor_scalar(qbuf[64:65, h % 2, g * 512:(g + 1) * 512],
                                nrm[0:1, h % 2, g * 512:(g + 1) * 512],
                                kmn[0:1, h % 2, 0:1], None,
                                op0=ALU.mult).then_inc(vq, 1)
                if u >= 1:
                    epilogue(u - 1)
                if g == 0 and h + 1 < NH:
                    norm_vector(v, h + 1)
            epilogue(NU - 1)

        @b.scalar
        def _(s):
            norm_scalar(s, 0)
            for u, (h, g) in enumerate(units):
                K = K_of(g)
                for kt in range(K):
                    s.wait_ge(mm, P1[u] + kt + 1)
                    s.activation(out=PT[:, kt, :], in_=ps[:, kt % 2, :],
                                 func=AF.Exp).then_inc(ae, 1)
                if g == 0 and h + 1 < NH:
                    norm_scalar(s, h + 1)

        @b.gpsimd
        def _(g_):
            def store(u2):
                h2, g2 = units[u2]
                g_.wait_ge(va, u2 + 1)
                g_.dma_start(out=d["attnT"][h2 * 64:(h2 + 1) * 64, g2 * 512:(g2 + 1) * 512],
                             in_=aosb[:, u2 % 2, :]).then_inc(gs[u2 % 2], 16)
            for u, (h, g) in enumerate(units):
                for j in range(4):
                    kt = K_of(g) - 4 + j
                    g_.wait_ge(ae, P1[u] + kt + 1)
                    g_.tensor_tensor(out=PT[:, kt, :], in0=PT[:, kt, :],
                                     in1=binm[:, j, :], op=ALU.mult).then_inc(vd, 1)
                if u >= 1:
                    store(u - 1)
            store(NU - 1)
            g_.wait_ge(gs[0], 16 * NU // 2)
            g_.wait_ge(gs[1], 16 * NU // 2)
    es.close()


def ph_oproj(ctx, d, layer):
    nc = ctx.nc
    tg = "o%d" % layer
    es = contextlib.ExitStack()
    E = es.enter_context
    xin = d["x0"] if layer == 0 else d["x2"]
    xout = d["x1"] if layer == 0 else d["x3"]
    wsrc_name = "outw" if layer == 0 else "wu"
    bsrc_name = "outb" if layer == 0 else "bu"

    idb = E(nc.sbuf_tensor(tg + "idb", [128, 128], BF16))
    idf = E(nc.sbuf_tensor(tg + "idf", [128, 128], F32))
    owb = E(nc.sbuf_tensor(tg + "owb", [128, 8, DM], BF16))
    obt = E(nc.sbuf_tensor(tg + "obt", [128, 8], F32))
    wfo = E(nc.sbuf_tensor(tg + "wfo", [128, 2, DM], F32))
    bro = E(nc.sbuf_tensor(tg + "bro", [8, 128], F32))
    pprep = E(nc.psum_tensor(tg + "pprep", [128, 512], F32))
    pprepb = E(nc.psum_tensor(tg + "pprepb", [128, 512], BF16))

    pl, pg, pa, pp, pv, pp2, pv2, pbs = ctx.sems(tg, "pl", "pg", "pa", "pp", "pv", "pp2", "pv2", "pbs")
    pw = ctx.sems(tg, "pw0", "pw1")

    with nc.Block(tg + "prep") as b:
        @b.gpsimd
        def _(g):
            g.memset(idb[:], 0.0)
            g.memset(idf[:], 0.0).then_inc(pg, 1)
            g.wait_ge(pg, 1)
            g.affine_select(out=idb[:], in_=idb[:], compare_op=ALU.not_equal,
                            fill=1.0, base=0, pattern=[[-1, 128]],
                            channel_multiplier=1)
            g.affine_select(out=idf[:], in_=idf[:], compare_op=ALU.not_equal,
                            fill=1.0, base=0, pattern=[[-1, 128]],
                            channel_multiplier=1).then_inc(pg, 1)

        @b.sync
        def _(s):
            s.dma_start(out=bro[:], in_=d[bsrc_name].rearrange("(c k) -> c k", k=128)).then_inc(pbs, 16)
            wr = d[wsrc_name].rearrange("(c p) n -> c p n", p=128)
            for k in range(8):
                if k >= 2:
                    s.wait_ge(pa, k - 1)
                s.dma_start(out=wfo[:, k % 2, :], in_=wr[k]).then_inc(pw[k % 2], 16)

        if layer == 0:
            @b.scalar
            def _(s):
                for k in range(8):
                    s.wait_ge(pw[k % 2], 16 * (k // 2 + 1))
                    s.activation(out=owb[:, k, :], in_=wfo[:, k % 2, :],
                                 func=AF.Copy).then_inc(pa, 1)
        else:
            wtb = E(nc.sbuf_tensor(tg + "wtb", [128, 2, DM], BF16))

            @b.scalar
            def _(s):
                for k in range(8):
                    s.wait_ge(pw[k % 2], 16 * (k // 2 + 1))
                    if k >= 2:
                        s.wait_ge(pp, 2 * (k - 2) + 2)
                    s.activation(out=wtb[:, k % 2, :], in_=wfo[:, k % 2, :],
                                 func=AF.Copy).then_inc(pa, 1)

            @b.tensor
            def _(t):
                t.wait_ge(pg, 2)
                for k in range(8):
                    t.wait_ge(pa, k + 1)
                    for j in range(2):
                        gi = 2 * k + j
                        if gi >= 1:
                            t.wait_ge(pv, gi)
                        for c4 in range(4):
                            ii = t.transpose(pprepb[:, c4 * 128:(c4 + 1) * 128],
                                             wtb[:, k % 2, (4 * j + c4) * 128:(4 * j + c4 + 1) * 128],
                                             idb[:])
                        ii.then_inc(pp, 1)

            @b.vector
            def _(v):
                pr = pprepb[:].rearrange("p (c f) -> p c f", f=128)
                for k in range(8):
                    for j in range(2):
                        gi = 2 * k + j
                        v.wait_ge(pp, gi + 1)
                        v.tensor_copy(out=owb[:, 4 * j:4 * j + 4, k * 128:(k + 1) * 128],
                                      in_=pr).then_inc(pv, 1)

        @b.tensor
        def _(t):
            t.wait_ge(pbs, 16)
            t.wait_ge(pg, 2)
            t.transpose(pprep[:, 0:8], bro[:], idf[0:8, 0:8]).then_inc(pp2, 1)

        @b.vector
        def _(v):
            v.wait_ge(pp2, 1)
            v.tensor_copy(out=obt[:], in_=pprep[:, 0:8]).then_inc(pv2, 1)

    at = E(nc.sbuf_tensor(tg + "at", [128, 2, 8, 512], BF16))
    xti = E(nc.sbuf_tensor(tg + "xti", [128, 2, 4, DM], F32))
    xto = E(nc.sbuf_tensor(tg + "xto", [128, 2, 4, DM], F32))
    obsb = E(nc.sbuf_tensor(tg + "obsb", [128, 8, 512], BF16))
    ppj = E(nc.psum_tensor(tg + "ppj", [128, 2, 512], F32))
    ptr = E(nc.psum_tensor(tg + "ptr", [128, 512], BF16))

    pm, ve, pt2, vadd = ctx.sems(tg, "pm", "ve", "pt2", "vadd")
    lx = ctx.sems(tg, "lx0", "lx1")
    gso = ctx.sems(tg, "gso0", "gso1")
    NT = 8

    with nc.Block(tg + "main") as b:
        @b.sync
        def _(s):
            atr = d["attnT"].rearrange("(k p) t -> p k t", p=128)
            xr = xin.rearrange("(n c p) dd -> n p c dd", p=128, c=4)
            for i in range(NT):
                if i >= 2:
                    s.wait_ge(pm, 8 * (i - 1))
                s.dma_start(out=at[:, i % 2], in_=atr[:, :, i * 512:(i + 1) * 512]).then_inc(lx[i % 2], 16)
                if i >= 2:
                    s.wait_ge(vadd, 8 * (i - 1))
                s.dma_start(out=xti[:, i % 2], in_=xr[i]).then_inc(lx[i % 2], 16)

        @b.tensor
        def _(t):
            t.wait_ge(pa, 8 if layer == 0 else 0)
            if layer == 1:
                t.wait_ge(pv, 16)
            for i in range(NT):
                t.wait_ge(lx[i % 2], 32 * (i // 2 + 1))
                for mo in range(8):
                    j = 8 * i + mo
                    if j >= 2:
                        t.wait_ge(ve, j - 1)
                    for k in range(8):
                        ii = t.matmul(ppj[:, mo % 2, :], owb[:, k, mo * 128:(mo + 1) * 128],
                                      at[:, i % 2, k, :], start=(k == 0), stop=(k == 7))
                    ii.then_inc(pm, 1)
                for mo in range(8):
                    j = 8 * i + mo
                    t.wait_ge(ve, j + 1)
                    if j >= 1:
                        t.wait_ge(vadd, j)
                    for c in range(4):
                        ii = t.transpose(ptr[:, c * 128:(c + 1) * 128],
                                         obsb[:, mo, c * 128:(c + 1) * 128], idb[:])
                    ii.then_inc(pt2, 1)

        @b.vector
        def _(v):
            v.wait_ge(pv2, 1)
            for i in range(NT):
                for mo in range(8):
                    j = 8 * i + mo
                    v.wait_ge(pm, j + 1)
                    if i >= 1:
                        v.wait_ge(pt2, 8 * (i - 1) + mo + 1)
                    v.tensor_scalar(obsb[:, mo, :], ppj[:, mo % 2, :],
                                    obt[:, mo:mo + 1], None, op0=ALU.add).then_inc(ve, 1)
                v.wait_ge(lx[i % 2], 32 * (i // 2 + 1))
                if i >= 2:
                    v.wait_ge(gso[i % 2], 16 * (i // 2))
                for mo in range(8):
                    j = 8 * i + mo
                    v.wait_ge(pt2, j + 1)
                    for c in range(4):
                        ii = v.tensor_tensor(out=xto[:, i % 2, c, mo * 128:(mo + 1) * 128],
                                             in0=ptr[:, c * 128:(c + 1) * 128],
                                             in1=xti[:, i % 2, c, mo * 128:(mo + 1) * 128],
                                             op=ALU.add)
                    ii.then_inc(vadd, 1)

        @b.gpsimd
        def _(g):
            xo = xout.rearrange("(n c p) dd -> n p c dd", p=128, c=4)
            for i in range(NT):
                g.wait_ge(vadd, 8 * (i + 1))
                g.dma_start(out=xo[i], in_=xto[:, i % 2]).then_inc(gso[i % 2], 16)
            g.wait_ge(gso[0], 16 * NT // 2)
            g.wait_ge(gso[1], 16 * NT // 2)
    es.close()


def ph_moeprep(ctx, d):
    nc = ctx.nc
    tg = "mp"
    es = contextlib.ExitStack()
    E = es.enter_context
    wfm = E(nc.sbuf_tensor(tg + "wfm", [128, 2, 2048], F32))
    wbm = E(nc.sbuf_tensor(tg + "wbm", [128, 2, 2048], BF16))
    mc, = ctx.sems(tg, "mc")
    ml = ctx.sems(tg, "ml0", "ml1")
    ms = ctx.sems(tg, "ms0", "ms1")

    units = []
    for name_s, name_d, nchunk, width in (("w1c", "w1b", 8, 2048),
                                          ("w3c", "w3b", 8, 2048),
                                          ("w2c", "w2b", 16, 1024)):
        for e in range(9):
            sv = d[name_s][e].rearrange("(c p) n -> c p n", p=128)
            dv = d[name_d][e].rearrange("(c p) n -> c p n", p=128)
            for rc in range(nchunk):
                units.append((sv[rc], dv[rc], width))
    NU = len(units)

    with nc.Block(tg + "main") as b:
        @b.sync
        def _(s):
            for j, (sv, dv, w) in enumerate(units):
                if j >= 2:
                    s.wait_ge(mc, j - 1)
                s.dma_start(out=wfm[:, j % 2, 0:w], in_=sv).then_inc(ml[j % 2], 16)

        @b.scalar
        def _(s):
            for j, (sv, dv, w) in enumerate(units):
                s.wait_ge(ml[j % 2], 16 * (j // 2 + 1))
                if j >= 2:
                    s.wait_ge(ms[j % 2], 16 * (j // 2))
                s.activation(out=wbm[:, j % 2, 0:w], in_=wfm[:, j % 2, 0:w],
                             func=AF.Copy).then_inc(mc, 1)

        @b.gpsimd
        def _(g):
            for j, (sv, dv, w) in enumerate(units):
                g.wait_ge(mc, j + 1)
                g.dma_start(out=dv, in_=wbm[:, j % 2, 0:w]).then_inc(ms[j % 2], 16)
            g.wait_ge(ms[0], 16 * ((NU + 1) // 2))
            g.wait_ge(ms[1], 16 * (NU // 2))
    es.close()


def ph_moe(ctx, d):
    nc = ctx.nc
    tg = "m"
    es = contextlib.ExitStack()
    E = es.enter_context
    idf = E(nc.sbuf_tensor(tg + "idf", [128, 128], F32))
    idb = E(nc.sbuf_tensor(tg + "idb", [128, 128], BF16))
    epst = E(nc.sbuf_tensor(tg + "epst", [128, 1], F32))
    ones1 = E(nc.sbuf_tensor(tg + "ones1", [1, 128], F32))
    rsrc = E(nc.sbuf_tensor(tg + "rsrc", [8, DM], F32))
    ebr = E(nc.sbuf_tensor(tg + "ebr", [1, 8], F32))
    rwT = E(nc.sbuf_tensor(tg + "rwT", [128, 8, 8], F32))
    ebb = E(nc.sbuf_tensor(tg + "ebb", [128, 8], F32))
    xt = E(nc.sbuf_tensor(tg + "xt", [128, 2, 4, DM], F32))
    bn6 = E(nc.sbuf_tensor(tg + "bn6", [128, 48], F32))
    bno = E(nc.sbuf_tensor(tg + "bno", [128, 8], F32))
    sd = E(nc.sbuf_tensor(tg + "sd", [128, 4], F32))
    rstd = E(nc.sbuf_tensor(tg + "rstd", [128, 4], F32))
    htf = E(nc.sbuf_tensor(tg + "htf", [128, 4, DM], F32))
    hf32 = E(nc.sbuf_tensor(tg + "hf32", [128, 8, 512], F32))
    hbf = E(nc.sbuf_tensor(tg + "hbf", [128, 8, 512], BF16))
    lgs = E(nc.sbuf_tensor(tg + "lgs", [128, 32], F32))
    m1 = E(nc.sbuf_tensor(tg + "m1", [128, 4], F32))
    m1n = E(nc.sbuf_tensor(tg + "m1n", [128, 4], F32))
    m2 = E(nc.sbuf_tensor(tg + "m2", [128, 4], F32))
    sel = E(nc.sbuf_tensor(tg + "sel", [128, 32], F32))
    msk = E(nc.sbuf_tensor(tg + "msk", [128, 32], F32))
    e1t = E(nc.sbuf_tensor(tg + "e1t", [128, 32], F32))
    wun = E(nc.sbuf_tensor(tg + "wun", [128, 32], F32))
    zz = E(nc.sbuf_tensor(tg + "zz", [128, 4], F32))
    zrr = E(nc.sbuf_tensor(tg + "zrr", [128, 4], F32))
    cwT = E(nc.sbuf_tensor(tg + "cwT", [128, 4, 8], F32))
    cwr = E(nc.sbuf_tensor(tg + "cwr", [8, 512], F32))
    cw0 = E(nc.sbuf_tensor(tg + "cw0", [1, 2, 512], F32))
    cwb = E(nc.sbuf_tensor(tg + "cwb", [128, 8, 512], BF16))
    w1s = E(nc.sbuf_tensor(tg + "w1s", [128, 2, 8, 512], BF16))
    w3s = E(nc.sbuf_tensor(tg + "w3s", [128, 2, 8, 512], BF16))
    w2s = E(nc.sbuf_tensor(tg + "w2s", [128, 16, 1024], BF16))
    sg = E(nc.sbuf_tensor(tg + "sg", [128, 2, 512], F32))
    eo1 = E(nc.sbuf_tensor(tg + "eo1", [128, 2, 512], BF16))
    eo = E(nc.sbuf_tensor(tg + "eo", [128, 16, 512], BF16))
    acc = E(nc.sbuf_tensor(tg + "acc", [128, 8, 512], F32))
    accb = E(nc.sbuf_tensor(tg + "accb", [128, 8, 512], BF16))
    pmisc = E(nc.psum_tensor(tg + "pmisc", [128, 512], F32))
    pg_ = E(nc.psum_tensor(tg + "pg", [128, 512], F32))
    pu_ = E(nc.psum_tensor(tg + "pu", [128, 512], F32))
    pp2 = E(nc.psum_tensor(tg + "pp2", [128, 2, 512], F32))
    pcwt = E(nc.psum_tensor(tg + "pcwt", [8, 512], F32))
    pcb = E(nc.psum_tensor(tg + "pcb", [128, 512], F32))
    pepi = E(nc.psum_tensor(tg + "pepi", [128, 512], BF16))

    EO = [8, 0, 1, 2, 3, 4, 5, 6, 7]

    (pg, mrw, meb, mpp, mvv, mv1, ma1, mv4, mp1, mv5, mhb, mpr, mvm, mae,
     mcw1, mpc, mvr, mpb, macw, mpg, mpu, mas, mve, mps, mpw, mva, mab,
     mpe, mvp, mself) = ctx.sems(
        tg, "pg", "mrw", "meb", "mpp", "mvv", "mv1", "ma1", "mv4", "mp1",
        "mv5", "mhb", "mpr", "mvm", "mae", "mcw1", "mpc", "mvr", "mpb",
        "macw", "mpg", "mpu", "mas", "mve", "mps", "mpw", "mva", "mab",
        "mpe", "mvp", "mself")
    mx = ctx.sems(tg, "mx0", "mx1")
    mw1 = ctx.sems(tg, "mw10", "mw11")
    mw3 = ctx.sems(tg, "mw30", "mw31")
    mw2, = ctx.sems(tg, "mw2")
    msx = ctx.sems(tg, "msx0", "msx1")
    mcw = ctx.sems(tg, "mcw0", "mcw1s")

    with nc.Block(tg + "prep") as b:
        @b.gpsimd
        def _(g):
            g.memset(epst[:], 1.0e-5)
            g.memset(ones1[:], 1.0)
            g.memset(idf[:], 0.0)
            g.memset(idb[:], 0.0).then_inc(pg, 1)
            g.wait_ge(pg, 1)
            g.affine_select(out=idf[:], in_=idf[:], compare_op=ALU.not_equal,
                            fill=1.0, base=0, pattern=[[-1, 128]],
                            channel_multiplier=1)
            g.affine_select(out=idb[:], in_=idb[:], compare_op=ALU.not_equal,
                            fill=1.0, base=0, pattern=[[-1, 128]],
                            channel_multiplier=1).then_inc(pg, 1)

        @b.sync
        def _(s):
            s.dma_start(out=rsrc[:], in_=d["rw"]).then_inc(mrw, 16)
            s.dma_start(out=ebr[:], in_=d["eb"][None, :]).then_inc(meb, 16)

        @b.tensor
        def _(t):
            t.wait_ge(mrw, 16)
            t.wait_ge(pg, 2)
            for k in range(8):
                if k >= 1:
                    t.wait_ge(mvv, k)
                t.transpose(pmisc[:, 0:8], rsrc[:, k * 128:(k + 1) * 128],
                            idf[0:8, 0:8]).then_inc(mpp, 1)
            t.wait_ge(meb, 16)
            t.wait_ge(mvv, 8)
            t.matmul(pmisc[:, 8:16], ones1[:], ebr[:], start=True,
                     stop=True).then_inc(mpp, 1)

        @b.vector
        def _(v):
            for k in range(8):
                v.wait_ge(mpp, k + 1)
                v.tensor_copy(out=rwT[:, k, :], in_=pmisc[:, 0:8]).then_inc(mvv, 1)
            v.wait_ge(mpp, 9)
            v.tensor_copy(out=ebb[:], in_=pmisc[:, 8:16]).then_inc(mvv, 1)

    NT = 8

    def IDX(tt, ei, mh):
        return 144 * tt + 16 * ei + mh

    def WIDX(tt, ei, mo):
        return 72 * tt + 8 * ei + mo

    with nc.Block(tg + "main") as b:
        @b.sync
        def _(s):
            xr = d["x1"].rearrange("(n c p) dd -> n p c dd", p=128, c=4)
            for tt in range(NT):
                if tt >= 2:
                    s.wait_ge(msx[tt % 2], 16 * (tt // 2))
                s.dma_start(out=xt[:, tt % 2], in_=xr[tt]).then_inc(mx[tt % 2], 16)
                for ei, e in enumerate(EO):
                    w1v = d["w1b"][e].rearrange("(c p) n -> p c n", p=128)
                    w3v = d["w3b"][e].rearrange("(c p) n -> p c n", p=128)
                    w2v = d["w2b"][e].rearrange("(c p) n -> p c n", p=128)
                    for q in range(4):
                        gate = 144 * tt + 16 * ei + 4 * q - 4
                        if gate >= 1:
                            s.wait_ge(mpg, gate)
                        s.dma_start(out=w1s[:, q % 2],
                                    in_=w1v[:, :, q * 512:(q + 1) * 512]).then_inc(mw1[q % 2], 16)
                        if gate >= 1:
                            s.wait_ge(mpu, gate)
                        s.dma_start(out=w3s[:, q % 2],
                                    in_=w3v[:, :, q * 512:(q + 1) * 512]).then_inc(mw3[q % 2], 16)
                    if 9 * tt + ei >= 1:
                        s.wait_ge(mpw, 72 * tt + 8 * ei)
                    s.dma_start(out=w2s[:], in_=w2v).then_inc(mw2, 16)

        @b.vector
        def _(v):
            ns = 0

            def selfw(ii):
                nonlocal ns
                ii.then_inc(mself, 1)
                ns += 1
                v.wait_ge(mself, ns)

            for tt in range(NT):
                v.wait_ge(mx[tt % 2], 16 * (tt // 2 + 1))
                for c in range(4):
                    ii = v.bn_stats(bn6[:, c * 12:c * 12 + 6], xt[:, tt % 2, c, 0:512])
                    ii = v.bn_stats(bn6[:, c * 12 + 6:c * 12 + 12], xt[:, tt % 2, c, 512:1024])
                selfw(ii)
                for c in range(4):
                    ii = v.bn_aggr(bno[:, 2 * c:2 * c + 2], bn6[:, c * 12:c * 12 + 12])
                ii.then_inc(mv1, 1)
                v.wait_ge(ma1, tt + 1)
                ii = v.reciprocal(rstd[:], sd[:])
                selfw(ii)
                if tt >= 1:
                    v.wait_ge(mp1, 8 * tt)
                for c in range(4):
                    ii = v.tensor_scalar(htf[:, c, :], xt[:, tt % 2, c, :],
                                         bno[:, 2 * c:2 * c + 1], rstd[:, c:c + 1],
                                         op0=ALU.subtract, op1=ALU.mult)
                ii.then_inc(mv4, 1)
                for k in range(8):
                    v.wait_ge(mp1, 8 * tt + k + 1)
                    v.tensor_copy(out=hf32[:, k, :], in_=pmisc[:]).then_inc(mv5, 1)
                for c in range(4):
                    v.wait_ge(mpr, 4 * tt + 4)
                    c8 = slice(c * 8, (c + 1) * 8)
                    ii = v.tensor_tensor(out=lgs[:, c8], in0=pmisc[:, 448 + c * 8:448 + (c + 1) * 8],
                                         in1=ebb[:], op=ALU.add)
                    selfw(ii)
                    ii = v.reduce_max(m1[:, c:c + 1], lgs[:, c8], axis=AX.X)
                    selfw(ii)
                    ii = v.tensor_scalar(sel[:, c8], lgs[:, c8], m1[:, c:c + 1], None,
                                         op0=ALU.is_ge)
                    selfw(ii)
                    ii = v.scalar_tensor_tensor(msk[:, c8], sel[:, c8], -1.0e30,
                                                lgs[:, c8], op0=ALU.mult, op1=ALU.add)
                    selfw(ii)
                    ii = v.reduce_max(m2[:, c:c + 1], msk[:, c8], axis=AX.X)
                    selfw(ii)
                    ii = v.tensor_scalar(sel[:, c8], lgs[:, c8], m2[:, c:c + 1], None,
                                         op0=ALU.is_ge)
                    selfw(ii)
                    v.tensor_scalar(m1n[:, c:c + 1], m1[:, c:c + 1], -1.0, None,
                                    op0=ALU.mult).then_inc(mvm, 1)
                    v.wait_ge(mae, 4 * tt + c + 1)
                    ii = v.tensor_tensor(out=wun[:, c8], in0=sel[:, c8], in1=e1t[:, c8],
                                         op=ALU.mult)
                    selfw(ii)
                    ii = v.reduce_sum(zz[:, c:c + 1], wun[:, c8], axis=AX.X)
                    selfw(ii)
                    ii = v.reciprocal(zrr[:, c:c + 1], zz[:, c:c + 1])
                    selfw(ii)
                    v.tensor_scalar(cwT[:, c, :], wun[:, c8], zrr[:, c:c + 1], None,
                                    op0=ALU.mult).then_inc(mcw1, 1)
                v.wait_ge(mpc, tt + 1)
                v.tensor_copy(out=cwr[:], in_=pcwt[:]).then_inc(mvr, 1)
                for ei, e in enumerate(EO):
                    for mh in range(16):
                        ix = IDX(tt, ei, mh)
                        if ei == 0:
                            v.wait_ge(mas, ix + 1)
                            v.wait_ge(mpu, ix + 1)
                            if tt >= 1:
                                v.wait_ge(mpw, 72 * tt)
                            v.tensor_tensor(out=eo[:, mh, :], in0=pu_[:],
                                            in1=sg[:, mh % 2, :], op=ALU.mult).then_inc(mve, 1)
                        else:
                            v.wait_ge(mas, ix + 1)
                            v.wait_ge(mpu, ix + 1)
                            six = 128 * tt + 16 * (ei - 1) + mh
                            if six >= 2:
                                v.wait_ge(mps, six - 1)
                            v.tensor_tensor(out=eo1[:, mh % 2, :], in0=pu_[:],
                                            in1=sg[:, mh % 2, :], op=ALU.mult).then_inc(mve, 1)
                    for mo in range(8):
                        wx = WIDX(tt, ei, mo)
                        v.wait_ge(mpw, wx + 1)
                        if ei == 0:
                            if tt >= 1:
                                v.wait_ge(mab, tt)
                            v.tensor_copy(out=acc[:, mo, :], in_=pp2[:, mo % 2, :]).then_inc(mva, 1)
                        else:
                            v.wait_ge(mva, WIDX(tt, ei - 1, mo) + 1)
                            v.tensor_tensor(out=acc[:, mo, :], in0=pp2[:, mo % 2, :],
                                            in1=acc[:, mo, :], op=ALU.add).then_inc(mva, 1)
                for mo in range(8):
                    v.wait_ge(mpe, 8 * tt + mo + 1)
                    for c in range(4):
                        ii = v.tensor_tensor(out=xt[:, tt % 2, c, mo * 128:(mo + 1) * 128],
                                             in0=pepi[:, c * 128:(c + 1) * 128],
                                             in1=xt[:, tt % 2, c, mo * 128:(mo + 1) * 128],
                                             op=ALU.add)
                    ii.then_inc(mvp, 1)

        @b.scalar
        def _(s):
            vcol = bno[:].rearrange("p (c two) -> p c two", two=2)[:, :, 1:2].rearrange("p c o -> p (c o)")
            for tt in range(NT):
                s.wait_ge(mv1, tt + 1)
                s.activation(out=sd[:], in_=vcol, func=AF.Sqrt, bias=epst[:, 0:1]).then_inc(ma1, 1)
                s.wait_ge(mv5, 8 * (tt + 1))
                s.activation(out=hbf[:], in_=hf32[:], func=AF.Copy).then_inc(mhb, 1)
                for c in range(4):
                    s.wait_ge(mvm, 4 * tt + c + 1)
                    s.activation(out=e1t[:, c * 8:(c + 1) * 8], in_=lgs[:, c * 8:(c + 1) * 8],
                                 func=AF.Exp, bias=m1n[:, c:c + 1]).then_inc(mae, 1)
                for e in range(8):
                    s.wait_ge(mpb, 8 * tt + e + 1)
                    if tt >= 1:
                        s.wait_ge(mps, 128 * (tt - 1) + 16 * (e + 1))
                    s.activation(out=cwb[:, e, :], in_=pcb[:], func=AF.Copy).then_inc(macw, 1)
                for ei, e in enumerate(EO):
                    for mh in range(16):
                        ix = IDX(tt, ei, mh)
                        s.wait_ge(mpg, ix + 1)
                        if ix >= 2:
                            s.wait_ge(mve, ix - 1)
                        s.activation(out=sg[:, mh % 2, :], in_=pg_[:],
                                     func=AF.Silu).then_inc(mas, 1)
                s.wait_ge(mva, 72 * (tt + 1))
                if tt >= 1:
                    s.wait_ge(mpe, 8 * tt)
                s.activation(out=accb[:], in_=acc[:], func=AF.Copy).then_inc(mab, 1)

        @b.tensor
        def _(t):
            t.wait_ge(mvv, 9)
            t.wait_ge(pg, 2)
            for tt in range(NT):
                t.wait_ge(mv4, tt + 1)
                if tt >= 1:
                    t.wait_ge(mcw1, 4 * tt)
                for k in range(8):
                    if 8 * tt + k >= 1:
                        t.wait_ge(mv5, 8 * tt + k)
                    for c in range(4):
                        ii = t.transpose(pmisc[:, c * 128:(c + 1) * 128],
                                         htf[:, c, k * 128:(k + 1) * 128], idf[:])
                    ii.then_inc(mp1, 1)
                t.wait_ge(mv5, 8 * (tt + 1))
                for c in range(4):
                    for k in range(8):
                        ii = t.matmul(pmisc[:, 448 + c * 8:448 + (c + 1) * 8],
                                      hf32[:, k, c * 128:(c + 1) * 128], rwT[:, k, :],
                                      start=(k == 0), stop=(k == 7))
                    ii.then_inc(mpr, 1)
                t.wait_ge(mcw1, 4 * (tt + 1))
                if tt >= 1:
                    t.wait_ge(mvr, tt)
                for c in range(4):
                    ii = t.transpose(pcwt[:, c * 128:(c + 1) * 128], cwT[:, c, :],
                                     idf[0:128, 0:128])
                ii.then_inc(mpc, 1)
                for e in range(8):
                    t.wait_ge(mcw[e % 2], 16 * ((8 * tt + e) // 2 + 1))
                    if 8 * tt + e >= 1:
                        t.wait_ge(macw, 8 * tt + e)
                    t.matmul(pcb[:], ones1[:], cw0[0:1, e % 2, :], start=True,
                             stop=True).then_inc(mpb, 1)
                t.wait_ge(mhb, tt + 1)
                for ei, e in enumerate(EO):
                    for q in range(4):
                        t.wait_ge(mw1[q % 2], 16 * (2 * (9 * tt + ei) + q // 2 + 1))
                        t.wait_ge(mw3[q % 2], 16 * (2 * (9 * tt + ei) + q // 2 + 1))
                        for mhl in range(4):
                            mh = 4 * q + mhl
                            ix = IDX(tt, ei, mh)
                            if ix >= 1:
                                t.wait_ge(mas, ix)
                            for k in range(8):
                                ii = t.matmul(pg_[:], w1s[:, q % 2, k, mhl * 128:(mhl + 1) * 128],
                                              hbf[:, k, :], start=(k == 0), stop=(k == 7))
                            ii.then_inc(mpg, 1)
                            if ix >= 1:
                                t.wait_ge(mve, ix)
                            for k in range(8):
                                ii = t.matmul(pu_[:], w3s[:, q % 2, k, mhl * 128:(mhl + 1) * 128],
                                              hbf[:, k, :], start=(k == 0), stop=(k == 7))
                            ii.then_inc(mpu, 1)
                    t.wait_ge(mw2, 16 * (9 * tt + ei + 1))
                    if ei == 0:
                        t.wait_ge(mve, 144 * tt + 16)
                    else:
                        t.wait_ge(mps, 128 * tt + 16 * ei)
                    for mo in range(8):
                        wx = WIDX(tt, ei, mo)
                        if wx >= 2:
                            t.wait_ge(mva, wx - 1)
                        for mh in range(16):
                            ii = t.matmul(pp2[:, mo % 2, :], w2s[:, mh, mo * 128:(mo + 1) * 128],
                                          eo[:, mh, :], start=(mh == 0), stop=(mh == 15))
                        ii.then_inc(mpw, 1)
                t.wait_ge(mab, tt + 1)
                for mo in range(8):
                    if 8 * tt + mo >= 1:
                        t.wait_ge(mvp, 8 * tt + mo)
                    for c in range(4):
                        ii = t.transpose(pepi[:, c * 128:(c + 1) * 128],
                                         accb[:, mo, c * 128:(c + 1) * 128], idb[:])
                    ii.then_inc(mpe, 1)

        @b.gpsimd
        def _(g):
            xo = d["x2"].rearrange("(n c p) dd -> n p c dd", p=128, c=4)
            for tt in range(NT):
                g.wait_ge(mvr, tt + 1)
                for e in range(8):
                    if 8 * tt + e >= 2:
                        g.wait_ge(mpb, 8 * tt + e - 1)
                    g.dma_start(out=cw0[0:1, e % 2, :], in_=cwr[e:e + 1, :]).then_inc(mcw[e % 2], 16)
                for ei, e in enumerate(EO):
                    if ei == 0:
                        continue
                    for mh in range(16):
                        ix = IDX(tt, ei, mh)
                        six = 128 * tt + 16 * (ei - 1) + mh
                        g.wait_ge(mve, ix + 1)
                        g.wait_ge(macw, 8 * tt + ei)
                        g.wait_ge(mpw, 72 * tt + 8 * ei)
                        g.tensor_tensor(out=eo[:, mh, :], in0=eo1[:, mh % 2, :],
                                        in1=cwb[:, e, :], op=ALU.mult).then_inc(mps, 1)
                g.wait_ge(mvp, 8 * (tt + 1))
                g.dma_start(out=xo[tt], in_=xt[:, tt % 2]).then_inc(msx[tt % 2], 16)
            g.wait_ge(msx[0], 16 * (NT // 2))
            g.wait_ge(msx[1], 16 * (NT // 2))
    es.close()


def ph_ffnprep(ctx, d):
    nc = ctx.nc
    tg = "fp"
    es = contextlib.ExitStack()
    E = es.enter_context
    idb = E(nc.sbuf_tensor(tg + "idb", [128, 128], BF16))
    wim = E(nc.sbuf_tensor(tg + "wim", [128, 2, 3072], I32))
    wbb = E(nc.sbuf_tensor(tg + "wbb", [128, 2, 3072], BF16))
    tst = E(nc.sbuf_tensor(tg + "tst", [128, 4, 128], BF16))
    ptb = E(nc.psum_tensor(tg + "ptb", [128, 4, 128], BF16))

    pg, fc, fp, fv = ctx.sems(tg, "pg", "fc", "fp", "fv")
    fl = ctx.sems(tg, "fl0", "fl1")
    fs = ctx.sems(tg, "fs0", "fs1", "fs2", "fs3")

    units = []
    for name_s, name_d, nrc, W in (("wg", "wgT", 24, 1024),
                                   ("wup", "wuT", 24, 1024),
                                   ("wd", "wdT", 8, 3072)):
        sv = d[name_s].rearrange("(c p) n -> c p n", p=128)
        for rc in range(nrc):
            units.append((sv[rc], d[name_d], rc, W // 128, W))

    blocks_before = [0]
    for u in units:
        blocks_before.append(blocks_before[-1] + u[3])
    NB = blocks_before[-1]

    groups_before = [x // 4 for x in blocks_before]
    NG = NB // 4
    fsg = fs[0]

    with nc.Block(tg + "main") as b:
        @b.gpsimd
        def _(g):
            g.memset(idb[:], 0.0).then_inc(pg, 1)
            g.wait_ge(pg, 1)
            g.affine_select(out=idb[:], in_=idb[:], compare_op=ALU.not_equal,
                            fill=1.0, base=0, pattern=[[-1, 128]],
                            channel_multiplier=1).then_inc(pg, 1)
            for j, (sv, dvt, rc, ncb, W) in enumerate(units):
                for gg in range(ncb // 4):
                    gi = groups_before[j] + gg
                    g.wait_ge(fv, gi + 1)
                    for q in range(4):
                        cb = 4 * gg + q
                        g.dma_start(out=dvt[cb * 128:(cb + 1) * 128, rc * 128:(rc + 1) * 128],
                                    in_=tst[:, q, :]).then_inc(fsg, 16)
            g.wait_ge(fsg, 64 * NG)

        @b.sync
        def _(s):
            for j, (sv, dvt, rc, ncb, W) in enumerate(units):
                if j >= 2:
                    s.wait_ge(fc, j - 1)
                s.dma_start(out=wim[:, j % 2, 0:W], in_=sv).then_inc(fl[j % 2], 16)

        @b.scalar
        def _(s):
            for j, (sv, dvt, rc, ncb, W) in enumerate(units):
                s.wait_ge(fl[j % 2], 16 * (j // 2 + 1))
                if j >= 2:
                    s.wait_ge(fp, groups_before[j - 1])
                s.activation(out=wbb[:, j % 2, 0:W], in_=wim[:, j % 2, 0:W],
                             func=AF.Copy, scale=1.0 / 127.0).then_inc(fc, 1)

        @b.tensor
        def _(t):
            t.wait_ge(pg, 2)
            for j, (sv, dvt, rc, ncb, W) in enumerate(units):
                t.wait_ge(fc, j + 1)
                for gg in range(ncb // 4):
                    gi = groups_before[j] + gg
                    if gi >= 1:
                        t.wait_ge(fv, gi)
                    for q in range(4):
                        cb = 4 * gg + q
                        ii = t.transpose(ptb[:, q, :], wbb[:, j % 2, cb * 128:(cb + 1) * 128],
                                         idb[:])
                    ii.then_inc(fp, 1)

        @b.vector
        def _(v):
            for j, (sv, dvt, rc, ncb, W) in enumerate(units):
                for gg in range(ncb // 4):
                    gi = groups_before[j] + gg
                    v.wait_ge(fp, gi + 1)
                    if gi >= 1:
                        v.wait_ge(fsg, 64 * gi)
                    v.tensor_copy(out=tst[:], in_=ptb[:]).then_inc(fv, 1)
    es.close()


def ph_ffn(ctx, d):
    nc = ctx.nc
    tg = "f"
    es = contextlib.ExitStack()
    E = es.enter_context
    idb = E(nc.sbuf_tensor(tg + "idb", [128, 128], BF16))
    epst = E(nc.sbuf_tensor(tg + "epst", [128, 1], F32))
    wgs = E(nc.sbuf_tensor(tg + "wgs", [128, 8, 3072], BF16))
    wus = E(nc.sbuf_tensor(tg + "wus", [128, 8, 3072], BF16))
    wds = E(nc.sbuf_tensor(tg + "wds", [128, 24, 1024], BF16))
    xt = E(nc.sbuf_tensor(tg + "xt", [128, 4, DM], F32))
    bn6 = E(nc.sbuf_tensor(tg + "bn6", [128, 48], F32))
    bno = E(nc.sbuf_tensor(tg + "bno", [128, 8], F32))
    sd = E(nc.sbuf_tensor(tg + "sd", [128, 4], F32))
    rstd = E(nc.sbuf_tensor(tg + "rstd", [128, 4], F32))
    htok = E(nc.sbuf_tensor(tg + "htok", [128, 4, DM], BF16))
    hbf = E(nc.sbuf_tensor(tg + "hbf", [128, 8, 512], BF16))
    sg = E(nc.sbuf_tensor(tg + "sg", [128, 1, 512], F32))
    h3 = E(nc.sbuf_tensor(tg + "h3", [128, 24, 512], BF16))
    obsb = E(nc.sbuf_tensor(tg + "obsb", [128, 4, 512], BF16))
    pst = E(nc.psum_tensor(tg + "pst", [128, 512], BF16))
    pg_ = E(nc.psum_tensor(tg + "pg", [128, 512], F32))
    pu_ = E(nc.psum_tensor(tg + "pu", [128, 512], F32))
    pp2 = E(nc.psum_tensor(tg + "pp2", [128, 2, 512], F32))
    pepi = E(nc.psum_tensor(tg + "pepi", [128, 512], BF16))

    (pg, fv1, fa1, fv4, fp1, fv5, fpg, fpu, fas, fve, fpw, fae, fpe, fvp,
     fself) = ctx.sems(tg, "pg", "fv1", "fa1", "fv4", "fp1", "fv5", "fpg",
                       "fpu", "fas", "fve", "fpw", "fae", "fpe", "fvp", "fself")
    fw, = ctx.sems(tg, "fw")
    fx = ctx.sems(tg, "fx0", "fx1")
    fsx = ctx.sems(tg, "fsx0", "fsx1")

    with nc.Block(tg + "prep") as b:
        @b.gpsimd
        def _(g):
            g.memset(epst[:], 1.0e-5)
            g.memset(idb[:], 0.0).then_inc(pg, 1)
            g.wait_ge(pg, 1)
            g.affine_select(out=idb[:], in_=idb[:], compare_op=ALU.not_equal,
                            fill=1.0, base=0, pattern=[[-1, 128]],
                            channel_multiplier=1).then_inc(pg, 1)

        @b.sync
        def _(s):
            s.dma_start(out=wgs[:], in_=d["wgT"].rearrange("(c p) n -> p c n", p=128)).then_inc(fw, 16)
            s.dma_start(out=wus[:], in_=d["wuT"].rearrange("(c p) n -> p c n", p=128)).then_inc(fw, 16)
            s.dma_start(out=wds[:], in_=d["wdT"].rearrange("(c p) n -> p c n", p=128)).then_inc(fw, 16)

    NT = 8
    with nc.Block(tg + "main") as b:
        @b.sync
        def _(s):
            xr = d["x3"].rearrange("(n c p) dd -> n p c dd", p=128, c=4)
            for tt in range(NT):
                if tt >= 1:
                    s.wait_ge(fsx[(tt - 1) % 2], 16 * ((tt - 1) // 2 + 1))
                s.dma_start(out=xt[:], in_=xr[tt]).then_inc(fx[tt % 2], 16)

        @b.vector
        def _(v):
            ns = 0

            def selfw(ii):
                nonlocal ns
                ii.then_inc(fself, 1)
                ns += 1
                v.wait_ge(fself, ns)

            for tt in range(NT):
                v.wait_ge(fx[tt % 2], 16 * (tt // 2 + 1))
                if tt >= 1:
                    v.wait_ge(fvp, 8 * tt)
                for c in range(4):
                    ii = v.bn_stats(bn6[:, c * 12:c * 12 + 6], xt[:, c, 0:512])
                    ii = v.bn_stats(bn6[:, c * 12 + 6:c * 12 + 12], xt[:, c, 512:1024])
                selfw(ii)
                for c in range(4):
                    ii = v.bn_aggr(bno[:, 2 * c:2 * c + 2], bn6[:, c * 12:c * 12 + 12])
                ii.then_inc(fv1, 1)
                v.wait_ge(fa1, tt + 1)
                ii = v.reciprocal(rstd[:], sd[:])
                selfw(ii)
                if tt >= 1:
                    v.wait_ge(fp1, 8 * tt)
                for c in range(4):
                    ii = v.tensor_scalar(htok[:, c, :], xt[:, c, :],
                                         bno[:, 2 * c:2 * c + 1], rstd[:, c:c + 1],
                                         op0=ALU.subtract, op1=ALU.mult)
                ii.then_inc(fv4, 1)
                for k in range(8):
                    v.wait_ge(fp1, 8 * tt + k + 1)
                    v.tensor_copy(out=hbf[:, k, :], in_=pst[:]).then_inc(fv5, 1)
                for mh in range(24):
                    ix = 24 * tt + mh
                    v.wait_ge(fas, ix + 1)
                    v.wait_ge(fpu, ix + 1)
                    if tt >= 1:
                        v.wait_ge(fpw, 8 * tt)
                    v.tensor_tensor(out=h3[:, mh, :], in0=pu_[:], in1=sg[:, 0, :],
                                    op=ALU.mult).then_inc(fve, 1)
                for mo in range(8):
                    v.wait_ge(fpe, 8 * tt + mo + 1)
                    for c in range(4):
                        ii = v.tensor_tensor(out=xt[:, c, mo * 128:(mo + 1) * 128],
                                             in0=pepi[:, c * 128:(c + 1) * 128],
                                             in1=xt[:, c, mo * 128:(mo + 1) * 128],
                                             op=ALU.add)
                    ii.then_inc(fvp, 1)

        @b.scalar
        def _(s):
            vcol = bno[:].rearrange("p (c two) -> p c two", two=2)[:, :, 1:2].rearrange("p c o -> p (c o)")
            for tt in range(NT):
                s.wait_ge(fv1, tt + 1)
                s.activation(out=sd[:], in_=vcol, func=AF.Sqrt, bias=epst[:, 0:1]).then_inc(fa1, 1)
                for mh in range(24):
                    ix = 24 * tt + mh
                    s.wait_ge(fpg, ix + 1)
                    if ix >= 1:
                        s.wait_ge(fve, ix)
                    s.activation(out=sg[:, 0, :], in_=pg_[:],
                                 func=AF.Silu).then_inc(fas, 1)
                for mo in range(8):
                    j = 8 * tt + mo
                    s.wait_ge(fpw, j + 1)
                    if j >= 4:
                        s.wait_ge(fpe, j - 3)
                    s.activation(out=obsb[:, mo % 4, :], in_=pp2[:, mo % 2, :],
                                 func=AF.Copy).then_inc(fae, 1)

        @b.tensor
        def _(t):
            t.wait_ge(pg, 2)
            t.wait_ge(fw, 48)
            for tt in range(NT):
                t.wait_ge(fv4, tt + 1)
                for k in range(8):
                    if 8 * tt + k >= 1:
                        t.wait_ge(fv5, 8 * tt + k)
                    for c in range(4):
                        ii = t.transpose(pst[:, c * 128:(c + 1) * 128],
                                         htok[:, c, k * 128:(k + 1) * 128], idb[:])
                    ii.then_inc(fp1, 1)
                t.wait_ge(fv5, 8 * (tt + 1))
                for mh in range(24):
                    ix = 24 * tt + mh
                    if ix >= 1:
                        t.wait_ge(fas, ix)
                    for k in range(8):
                        ii = t.matmul(pg_[:], wgs[:, k, mh * 128:(mh + 1) * 128],
                                      hbf[:, k, :], start=(k == 0), stop=(k == 7))
                    ii.then_inc(fpg, 1)
                    if ix >= 1:
                        t.wait_ge(fve, ix)
                    for k in range(8):
                        ii = t.matmul(pu_[:], wus[:, k, mh * 128:(mh + 1) * 128],
                                      hbf[:, k, :], start=(k == 0), stop=(k == 7))
                    ii.then_inc(fpu, 1)
                t.wait_ge(fve, 24 * (tt + 1))
                for mo in range(9):
                    if mo < 8:
                        j = 8 * tt + mo
                        if j >= 2:
                            t.wait_ge(fae, j - 1)
                        for mh in range(24):
                            ii = t.matmul(pp2[:, mo % 2, :], wds[:, mh, mo * 128:(mo + 1) * 128],
                                          h3[:, mh, :], start=(mh == 0), stop=(mh == 23))
                        ii.then_inc(fpw, 1)
                    if mo >= 1:
                        mt = mo - 1
                        t.wait_ge(fae, 8 * tt + mt + 1)
                        if 8 * tt + mt >= 1:
                            t.wait_ge(fvp, 8 * tt + mt)
                        for c in range(4):
                            ii = t.transpose(pepi[:, c * 128:(c + 1) * 128],
                                             obsb[:, mt % 4, c * 128:(c + 1) * 128], idb[:])
                        ii.then_inc(fpe, 1)

        @b.gpsimd
        def _(g):
            yo = d["y"].rearrange("(n c p) dd -> n p c dd", p=128, c=4)
            for tt in range(NT):
                g.wait_ge(fvp, 8 * (tt + 1))
                g.dma_start(out=yo[tt], in_=xt[:]).then_inc(fsx[tt % 2], 16)
            g.wait_ge(fsx[0], 16 * (NT // 2))
            g.wait_ge(fsx[1], 16 * (NT // 2))
    es.close()

EMIT = {
    "qkv0": lambda c, d: ph_qkv(c, d, 0),
    "qkv1": lambda c, d: ph_qkv(c, d, 1),
    "attn0": lambda c, d: ph_attn(c, d, 0),
    "attn1": lambda c, d: ph_attn(c, d, 1),
    "oproj0": lambda c, d: ph_oproj(c, d, 0),
    "oproj1": lambda c, d: ph_oproj(c, d, 1),
    "moeprep": lambda c, d: ph_moeprep(c, d),
    "moe": lambda c, d: ph_moe(c, d),
    "ffnprep": lambda c, d: ph_ffnprep(c, d),
    "ffn": lambda c, d: ph_ffn(c, d),
}

GROUPS = [ALL_PHASES]


def build_group(phases):
    nc = bass.Bass("TRN2", target_bir_lowering=False)
    ctx = Ctx(nc)
    reads = set()
    writes = set()
    for p in phases:
        r, w = PHASE_IO[p]
        reads.update(x for x in r if x not in writes)
        writes.update(w)
    gi = [ALL_PHASES.index(p) for p in phases]
    later_reads = set()
    for p in ALL_PHASES:
        if ALL_PHASES.index(p) > max(gi) and p not in phases:
            later_reads.update(PHASE_IO[p][0])
    kinds = {}
    for name in set(reads) | set(writes):
        if name in writes:
            kinds[name] = ("ExternalOutput"
                           if (name in later_reads or TENSORS[name][2] == "out")
                           else "Internal")
        else:
            kinds[name] = "ExternalInput"
    d = {}
    for name in sorted(kinds):
        k = kinds[name]
        shape, dt, _ = TENSORS[name]
        d[name] = nc.dram_tensor(name, shape, dt, kind=k).ap()
    with ctx.stack:
        for p in phases:
            with nc.cleanup_on_exit():
                EMIT[p](ctx, d)
    ext_in = sorted(n for n, k in kinds.items() if k == "ExternalInput")
    ext_out = sorted(n for n, k in kinds.items() if k == "ExternalOutput")
    return nc, ext_in, ext_out


def prep_inputs(inputs):
    common = {
        "qkvw": np.ascontiguousarray(inputs["L0_attn_qkv_w"].reshape(DM, NH * 192)),
        "qkvb": np.ascontiguousarray(inputs["L0_attn_qkv_b"]),
        "outw": np.ascontiguousarray(inputs["L0_attn_out_w"].reshape(DM, DM)),
        "outb": np.ascontiguousarray(inputs["L0_attn_out_b"]),
        "rw": np.ascontiguousarray(inputs["L0_ffn_router_w"]),
        "eb": np.ascontiguousarray(inputs["L0_ffn_experts_bias"]),
        "w1c": np.ascontiguousarray(np.concatenate([inputs["L0_ffn_w1"], inputs["L0_ffn_sw1"]], axis=0)),
        "w3c": np.ascontiguousarray(np.concatenate([inputs["L0_ffn_w3"], inputs["L0_ffn_sw3"]], axis=0)),
        "w2c": np.ascontiguousarray(np.concatenate([inputs["L0_ffn_w2"], inputs["L0_ffn_sw2"]], axis=0)),
        "wq": np.ascontiguousarray(inputs["L1_attn_Wq"]),
        "wk": np.ascontiguousarray(inputs["L1_attn_Wk"]),
        "wv": np.ascontiguousarray(inputs["L1_attn_Wv"]),
        "wu": np.ascontiguousarray(inputs["L1_attn_Wu"]),
        "bu": np.ascontiguousarray(inputs["L1_attn_bu"]),
        "wg": np.ascontiguousarray(inputs["L1_ffn_w_gate"]),
        "wup": np.ascontiguousarray(inputs["L1_ffn_w_up"]),
        "wd": np.ascontiguousarray(inputs["L1_ffn_w_down"]),
    }
    pools = []
    for c in range(NCORE):
        p = dict(common)
        p["x0"] = np.ascontiguousarray(np.asarray(inputs["L0_attn_x"])[c])
        pools.append(p)
    return pools


_BUILD_CACHE = {}


def kernel(**inputs):
    pools = prep_inputs(inputs)
    for phases in GROUPS:
        key = tuple(phases)
        if key not in _BUILD_CACHE:
            _BUILD_CACHE[key] = build_group(phases)
        nc, ext_in, ext_out = _BUILD_CACHE[key]
        in_maps = [{n: pools[c][n] for n in ext_in} for c in range(NCORE)]
        res = run_bass_kernel_spmd(nc, in_maps, core_ids=list(range(NCORE)))
        for c in range(NCORE):
            for n in ext_out:
                pools[c][n] = res.results[c][n]
    return np.stack([pools[c]["y"] for c in range(NCORE)])


# version 08
# speedup vs baseline: 1.3503x; 1.0459x over previous
import contextlib
import numpy as np
import concourse.bass as bass
import concourse.mybir as mybir
from concourse.bass_utils import run_bass_kernel_spmd

F32 = mybir.dt.float32
BF16 = mybir.dt.bfloat16
I32 = mybir.dt.int32
AF = mybir.ActivationFunctionType
ALU = mybir.AluOpType
AX = mybir.AxisListType

T = 4096
DM = 1024
NH, HD = 16, 64
NCORE = 8
NEG = -1.0e38

TENSORS = {
    "x0":    ([T, DM], F32, "inp"),
    "qkvw":  ([DM, NH * 192], F32, "inp"),
    "qkvb":  ([NH, 192], F32, "inp"),
    "outw":  ([DM, DM], F32, "inp"),
    "outb":  ([DM], F32, "inp"),
    "rw":    ([8, DM], F32, "inp"),
    "eb":    ([8], F32, "inp"),
    "w1c":   ([9, DM, 2048], F32, "inp"),
    "w3c":   ([9, DM, 2048], F32, "inp"),
    "w2c":   ([9, 2048, DM], F32, "inp"),
    "wq":    ([HD, HD], F32, "inp"),
    "wk":    ([HD, HD], F32, "inp"),
    "wv":    ([HD, HD], F32, "inp"),
    "wu":    ([DM, DM], F32, "inp"),
    "bu":    ([DM], F32, "inp"),
    "wg":    ([3072, DM], I32, "inp"),
    "wup":   ([3072, DM], I32, "inp"),
    "wd":    ([DM, 3072], I32, "inp"),
    "qT":    ([NH, HD, T], BF16, "scr"),
    "kT":    ([NH, HD, T], BF16, "scr"),
    "vv":    ([NH, T, HD], BF16, "scr"),
    "attnT": ([DM, T], BF16, "scr"),
    "x1":    ([T, DM], F32, "scr"),
    "x2":    ([T, DM], F32, "scr"),
    "x3":    ([T, DM], F32, "scr"),
    "w1b":   ([9, DM, 2048], BF16, "scr"),
    "w3b":   ([9, DM, 2048], BF16, "scr"),
    "w2b":   ([9, 2048, DM], BF16, "scr"),
    "wgT":   ([DM, 3072], BF16, "scr"),
    "wuT":   ([DM, 3072], BF16, "scr"),
    "wdT":   ([3072, DM], BF16, "scr"),
    "y":     ([T, DM], F32, "out"),
}

PHASE_IO = {
    "qkv0":    (["x0", "qkvw", "qkvb"], ["qT", "kT", "vv"]),
    "attn0":   (["qT", "kT", "vv"], ["attnT"]),
    "oproj0":  (["attnT", "outw", "outb", "x0"], ["x1"]),
    "moeprep": (["w1c", "w3c", "w2c"], ["w1b", "w3b", "w2b"]),
    "moe":     (["x1", "rw", "eb", "w1b", "w3b", "w2b"], ["x2"]),
    "qkv1":    (["x2", "wq", "wk", "wv"], ["qT", "kT", "vv"]),
    "attn1":   (["qT", "kT", "vv"], ["attnT"]),
    "oproj1":  (["attnT", "wu", "bu", "x2"], ["x3"]),
    "ffnprep": (["wg", "wup", "wd"], ["wgT", "wuT", "wdT"]),
    "ffn":     (["x3", "wgT", "wuT", "wdT"], ["y"]),
}
ALL_PHASES = list(PHASE_IO)


class Ctx:
    def __init__(self, nc):
        self.nc = nc
        self.stack = contextlib.ExitStack()

    def sems(self, tag, *names):
        return [self.nc.alloc_semaphore(tag + "_" + n) for n in names]


def ph_qkv(ctx, d, layer):
    nc = ctx.nc
    tg = "q%d" % layer
    xin = d["x0"] if layer == 0 else d["x2"]
    es = contextlib.ExitStack()
    E = es.enter_context
    idf = E(nc.sbuf_tensor(tg + "idf", [128, 128], F32))
    epst = E(nc.sbuf_tensor(tg + "epst", [128, 1], F32))
    idb = E(nc.sbuf_tensor(tg + "idb", [128, 128], BF16))
    pbias = E(nc.psum_tensor(tg + "pbias", [128, 512], F32))
    if layer == 0:
        qkwb = E(nc.sbuf_tensor(tg + "qkwb", [128, 8, NH, 128], BF16))
        vwb = E(nc.sbuf_tensor(tg + "vwb", [128, 8, DM], BF16))
        qkb = E(nc.sbuf_tensor(tg + "qkb", [128, NH], F32))
        vb = E(nc.sbuf_tensor(tg + "vb", [128, DM], F32))
        wf = E(nc.sbuf_tensor(tg + "wf", [128, 2, 3072], F32))
        bsrc = E(nc.sbuf_tensor(tg + "bsrc", [16, 192], F32))
        vrow = E(nc.sbuf_tensor(tg + "vrow", [1, DM], F32))
        ones1 = E(nc.sbuf_tensor(tg + "ones1", [1, 128], F32))
    else:
        wsrc = E(nc.sbuf_tensor(tg + "wsrc", [64, 192], F32))
        wqkT = E(nc.sbuf_tensor(tg + "wqkT", [128, 128], BF16))
        wvT = E(nc.sbuf_tensor(tg + "wvT", [128, 64], BF16))
        vwb = E(nc.sbuf_tensor(tg + "vwb", [128, 8, DM], BF16))

    pl, pg, pa, pp, pbs, pvr, pdup, pvw = ctx.sems(tg, "pl", "pg", "pa", "pp", "pbs", "pvr", "pdup", "pvw")
    pw = ctx.sems(tg, "pw0", "pw1")

    with nc.Block(tg + "prep") as b:
        @b.gpsimd
        def _(g):
            g.memset(epst[:], 1.0e-5)
            g.memset(idf[:], 0.0).then_inc(pg, 1)
            g.wait_ge(pg, 1)
            g.affine_select(out=idf[:], in_=idf[:], compare_op=ALU.not_equal,
                            fill=1.0, base=0, pattern=[[-1, 128]],
                            channel_multiplier=1)
            g.memset(idb[:], 0.0).then_inc(pg, 1)
            g.wait_ge(pg, 2)
            g.affine_select(out=idb[:], in_=idb[:], compare_op=ALU.not_equal,
                            fill=1.0, base=0, pattern=[[-1, 128]],
                            channel_multiplier=1).then_inc(pg, 1)
            if layer == 0:
                g.memset(ones1[:], 1.0).then_inc(pg, 1)


        if layer == 0:
            @b.sync
            def _(s):
                s.dma_start(out=bsrc[:], in_=d["qkvb"]).then_inc(pbs, 16)
                s.dma_start(out=vrow[0:1, :].rearrange("o (h dd) -> o h dd", h=NH),
                            in_=d["qkvb"][None, :, 128:192]).then_inc(pvr, 16)
                qw = d["qkvw"].rearrange("(c p) n -> c p n", p=128)
                for k in range(8):
                    if k >= 2:
                        s.wait_ge(pa, k - 1)
                    s.dma_start(out=wf[:, k % 2, :], in_=qw[k]).then_inc(pw[k % 2], 16)

            @b.scalar
            def _(s):
                for k in range(8):
                    s.wait_ge(pw[k % 2], 16 * (k // 2 + 1))
                    wv_ = wf[:, k % 2, :].rearrange("p (h s) -> p h s", h=NH)
                    s.activation(out=qkwb[:, k, :, 0:64], in_=wv_[:, :, 0:64],
                                 func=AF.Copy, scale=0.125)
                    s.activation(out=qkwb[:, k, :, 64:128], in_=wv_[:, :, 64:128],
                                 func=AF.Copy)
                    s.activation(out=vwb[:, k, :].rearrange("p (h s) -> p h s", h=NH),
                                 in_=wv_[:, :, 128:192], func=AF.Copy).then_inc(pa, 1)
                s.wait_ge(pp, 1)
                s.activation(out=qkb[0:64, :], in_=pbias[0:64, 0:NH], func=AF.Copy,
                             scale=0.125)
                s.activation(out=qkb[64:128, :], in_=pbias[64:128, 0:NH],
                             func=AF.Copy).then_inc(pa, 1)

            @b.tensor
            def _(t):
                t.wait_ge(pbs, 16)
                t.wait_ge(pg, 3)
                t.transpose(pbias[:, 0:16], bsrc[:, 0:128], idf[0:16, 0:16]).then_inc(pp, 1)
        else:
            @b.sync
            def _(s):
                s.dma_start(out=wsrc[:, 0:64], in_=d["wq"]).then_inc(pl, 16)
                s.dma_start(out=wsrc[:, 64:128], in_=d["wk"]).then_inc(pl, 16)
                s.dma_start(out=wsrc[:, 128:192], in_=d["wv"]).then_inc(pl, 16)

            @b.tensor
            def _(t):
                t.wait_ge(pl, 48)
                t.wait_ge(pg, 3)
                t.transpose(pbias[0:64, 0:64], wsrc[:, 0:64], idf[0:64, 0:64])
                t.transpose(pbias[0:64, 64:128], wsrc[:, 64:128], idf[0:64, 0:64])
                t.transpose(pbias[0:64, 128:192], wsrc[:, 128:192], idf[0:64, 0:64]).then_inc(pp, 1)

            @b.scalar
            def _(s):
                s.wait_ge(pp, 1)
                sc = float(DM) ** (-0.25)
                s.activation(out=wqkT[0:64, 0:64], in_=pbias[0:64, 0:64], func=AF.Copy, scale=sc)
                s.activation(out=wqkT[0:64, 64:128], in_=pbias[0:64, 64:128], func=AF.Copy, scale=sc)
                s.activation(out=wvT[0:64, :], in_=pbias[0:64, 128:192], func=AF.Copy).then_inc(pa, 1)

            @b.gpsimd
            def _(g):
                g.memset(vwb[:], 0.0).then_inc(pvw, 1)
                g.wait_ge(pa, 1)
                g.dma_start(out=wqkT[64:128, :], in_=wqkT[0:64, :]).then_inc(pdup, 16)
                g.dma_start(out=wvT[64:128, :], in_=wvT[0:64, :]).then_inc(pdup, 16)

            @b.vector
            def _(v):
                v.wait_ge(pvw, 1)
                v.wait_ge(pdup, 32)
                for k in range(8):
                    v.tensor_copy(out=vwb[0:64, k, k * 128:k * 128 + 64],
                                  in_=wvT[0:64, :])
                    ii = v.tensor_copy(out=vwb[64:128, k, k * 128 + 64:k * 128 + 128],
                                       in_=wvT[64:128, :])
                ii.then_inc(pvw, 1)

    if layer == 0:
        with nc.Block(tg + "prep2") as b:
            pv2, = ctx.sems(tg, "pv2")

            @b.tensor
            def _(t):
                t.wait_ge(pvr, 16)
                t.matmul(pbias[:, 0:512], ones1[:], vrow[:, 0:512], start=True, stop=True).then_inc(pv2, 1)
                t.wait_ge(pv2, 2)
                t.matmul(pbias[:, 0:512], ones1[:], vrow[:, 512:1024], start=True, stop=True).then_inc(pv2, 1)

            @b.vector
            def _(v):
                v.wait_ge(pv2, 1)
                v.tensor_copy(out=vb[:, 0:512], in_=pbias[:, 0:512]).then_inc(pv2, 1)
                v.wait_ge(pv2, 3)
                v.tensor_copy(out=vb[:, 512:1024], in_=pbias[:, 0:512]).then_inc(pv2, 1)

    xt = E(nc.sbuf_tensor(tg + "xt", [128, 2, 4, DM], F32))
    bn6 = E(nc.sbuf_tensor(tg + "bn6", [128, 48], F32))
    bno = E(nc.sbuf_tensor(tg + "bno", [128, 8], F32))
    sd = E(nc.sbuf_tensor(tg + "sd", [128, 4], F32))
    rstd = E(nc.sbuf_tensor(tg + "rstd", [128, 4], F32))
    htok = E(nc.sbuf_tensor(tg + "htok", [128, 4, DM], BF16))
    hf = E(nc.sbuf_tensor(tg + "hf", [128, 8, 512], BF16))
    qksb = E(nc.sbuf_tensor(tg + "qksb", [128, 2, 512], BF16))
    vsb = E(nc.sbuf_tensor(tg + "vsb", [128, 2, NH, HD], BF16))
    pst = E(nc.psum_tensor(tg + "pst", [128, 512], BF16))
    pqk = E(nc.psum_tensor(tg + "pqk", [128, 2, 512], F32))
    pvv = E(nc.psum_tensor(tg + "pvv", [128, 2, 512], F32))

    sv1, sv2, sa1, sv3, sv4, sp1, sv5, sp2, sa2, sp3, sv6 = ctx.sems(
        tg, "sv1", "sv2", "sa1", "sv3", "sv4", "sp1", "sv5", "sp2", "sa2",
        "sp3", "sv6")
    sx = ctx.sems(tg, "sx0", "sx1")
    sqs = ctx.sems(tg, "sqs0", "sqs1")
    svs = ctx.sems(tg, "svs0", "svs1")

    NT = 8

    with nc.Block(tg + "main") as b:
        @b.sync
        def _(s):
            xr = xin.rearrange("(n c p) d -> n p c d", p=128, c=4)
            for i in range(NT):
                if i >= 2:
                    s.wait_ge(sv4, i - 1)
                s.dma_start(out=xt[:, i % 2], in_=xr[i]).then_inc(sx[i % 2], 16)

        @b.vector
        def _(v):
            for i in range(NT):
                v.wait_ge(sx[i % 2], 16 * (i // 2 + 1))
                for c in range(4):
                    ii = v.bn_stats(bn6[:, c * 12:c * 12 + 6], xt[:, i % 2, c, 0:512])
                    ii = v.bn_stats(bn6[:, c * 12 + 6:c * 12 + 12], xt[:, i % 2, c, 512:1024])
                ii.then_inc(sv1, 1)
                v.wait_ge(sv1, i + 1)
                for c in range(4):
                    ii = v.bn_aggr(bno[:, 2 * c:2 * c + 2], bn6[:, c * 12:c * 12 + 12])
                ii.then_inc(sv2, 1)
                v.wait_ge(sa1, i + 1)
                v.reciprocal(rstd[:], sd[:]).then_inc(sv3, 1)
                v.wait_ge(sv3, i + 1)
                for c in range(4):
                    ii = v.tensor_scalar(htok[:, c, :], xt[:, i % 2, c, :],
                                         bno[:, 2 * c:2 * c + 1], rstd[:, c:c + 1],
                                         op0=ALU.subtract, op1=ALU.mult)
                ii.then_inc(sv4, 1)
                for k in range(8):
                    v.wait_ge(sp1, 8 * i + k + 1)
                    v.tensor_copy(out=hf[:, k, :], in_=pst[:]).then_inc(sv5, 1)
                for h in range(NH):
                    v.wait_ge(sp2, 16 * i + h + 1)
                    if 16 * i + h >= 2:
                        v.wait_ge(sqs[h % 2], 32 * ((16 * i + h) // 2))
                    if layer == 0:
                        v.tensor_scalar(qksb[:, h % 2, :], pqk[:, h % 2, :],
                                        qkb[:, h:h + 1], None,
                                        op0=ALU.add).then_inc(sa2, 1)
                    else:
                        v.tensor_copy(out=qksb[:, h % 2, :],
                                      in_=pqk[:, h % 2, :]).then_inc(sa2, 1)
                for c in range(4):
                    if 4 * i + c >= 2:
                        v.wait_ge(svs[c % 2], 16 * ((4 * i + c) // 2))
                    for hh in range(2):
                        v.wait_ge(sp3, 8 * i + 2 * c + hh + 1)
                        dst = vsb[:, c % 2, hh * 8:(hh + 1) * 8, :].rearrange("p h dd -> p (h dd)")
                        if layer == 0:
                            ii = v.tensor_tensor(out=dst, in0=pvv[:, (2 * c + hh) % 2, :],
                                                 in1=vb[:, hh * 512:(hh + 1) * 512], op=ALU.add)
                        else:
                            ii = v.tensor_copy(out=dst, in_=pvv[:, (2 * c + hh) % 2, :])
                    ii.then_inc(sv6, 1)

        @b.scalar
        def _(s):
            vcol = bno[:].rearrange("p (c two) -> p c two", two=2)[:, :, 1:2].rearrange("p c o -> p (c o)")
            for i in range(NT):
                s.wait_ge(sv2, i + 1)
                s.activation(out=sd[:], in_=vcol, func=AF.Sqrt, bias=epst[:, 0:1]).then_inc(sa1, 1)

        @b.tensor
        def _(t):
            t.wait_ge(pa, 9 if layer == 0 else 1)
            if layer == 1:
                t.wait_ge(pdup, 32)
                t.wait_ge(pvw, 2)
            for i in range(NT):
                t.wait_ge(sv4, i + 1)
                for k in range(8):
                    if 8 * i + k >= 1:
                        t.wait_ge(sv5, 8 * i + k)
                    for c in range(4):
                        ii = t.transpose(pst[:, c * 128:(c + 1) * 128],
                                         htok[:, c, k * 128:(k + 1) * 128], idb[:])
                    ii.then_inc(sp1, 1)
                t.wait_ge(sv5, 8 * (i + 1))
                for h in range(NH):
                    if 16 * i + h >= 2:
                        t.wait_ge(sa2, 16 * i + h - 1)
                    if layer == 0:
                        for k in range(8):
                            ii = t.matmul(pqk[:, h % 2, :], qkwb[:, k, h, :], hf[:, k, :],
                                          start=(k == 0), stop=(k == 7))
                    else:
                        hr = (h % 2) * 64
                        ii = t.matmul(pqk[:, h % 2, :], wqkT[hr:hr + 64, :],
                                      hf[hr:hr + 64, h // 2, :], start=True, stop=True)
                    ii.then_inc(sp2, 1)
                for c in range(4):
                    for hh in range(2):
                        j = 8 * i + 2 * c + hh
                        if j >= 2:
                            t.wait_ge(sv6, 4 * i + c)
                        if layer == 0:
                            for k in range(8):
                                ii = t.matmul(pvv[:, (2 * c + hh) % 2, :],
                                              hf[:, k, c * 128:(c + 1) * 128],
                                              vwb[:, k, hh * 512:(hh + 1) * 512],
                                              start=(k == 0), stop=(k == 7))
                        else:
                            for k in range(8):
                                ii = t.matmul(pvv[:, (2 * c + hh) % 2, :],
                                              hf[:, k, c * 128:(c + 1) * 128],
                                              vwb[:, k, hh * 512:(hh + 1) * 512],
                                              start=(k == 0), stop=(k == 7))
                        ii.then_inc(sp3, 1)

        @b.gpsimd
        def _(g):
            vvr = d["vv"].rearrange("h (n p) dd -> n p h dd", p=128)
            for i in range(NT):
                for h in range(NH):
                    g.wait_ge(sa2, 16 * i + h + 1)
                    g.dma_start(out=d["qT"][h][:, i * 512:(i + 1) * 512],
                                in_=qksb[0:64, h % 2, :]).then_inc(sqs[h % 2], 16)
                    g.dma_start(out=d["kT"][h][:, i * 512:(i + 1) * 512],
                                in_=qksb[64:128, h % 2, :]).then_inc(sqs[h % 2], 16)
                for c in range(4):
                    g.wait_ge(sv6, 4 * i + c + 1)
                    g.dma_start(out=vvr[4 * i + c], in_=vsb[:, c % 2]).then_inc(svs[c % 2], 16)
            g.wait_ge(sqs[0], 32 * NT * NH // 2)
            g.wait_ge(sqs[1], 32 * NT * NH // 2)
            g.wait_ge(svs[0], 16 * 2 * NT)
            g.wait_ge(svs[1], 16 * 2 * NT)
    es.close()



def ph_attn(ctx, d, layer):
    nc = ctx.nc
    tg = "a%d" % layer
    es = contextlib.ExitStack()
    E = es.enter_context
    kbuf = E(nc.sbuf_tensor(tg + "kbuf", [65, 2, T], BF16))
    qbuf = E(nc.sbuf_tensor(tg + "qbuf", [65, 2, T], BF16))
    vab = E(nc.sbuf_tensor(tg + "vab", [128, 2, 32, 65], BF16))
    sq = E(nc.sbuf_tensor(tg + "sq", [64, T], BF16))
    nrm = E(nc.sbuf_tensor(tg + "nrm", [1, 2, T], F32))
    kn2 = E(nc.sbuf_tensor(tg + "kn2", [1, T], F32))
    km2 = E(nc.sbuf_tensor(tg + "km2", [1, 1], F32))
    kms = E(nc.sbuf_tensor(tg + "kms", [1, 1], F32))
    kmn = E(nc.sbuf_tensor(tg + "kmn", [1, 2, 1], F32))
    ones64 = E(nc.sbuf_tensor(tg + "ones64", [64, 1], BF16))
    PT = E(nc.sbuf_tensor(tg + "PT", [128, 32, 512], BF16))
    zrow = E(nc.sbuf_tensor(tg + "zrow", [1, 512], F32))
    zbsb = E(nc.sbuf_tensor(tg + "zbsb", [64, 512], F32))
    aosb = E(nc.sbuf_tensor(tg + "aosb", [64, 2, 512], BF16))
    binm = E(nc.sbuf_tensor(tg + "binm", [128, 4, 512], BF16))
    onesf = E(nc.sbuf_tensor(tg + "onesf", [1, 64], F32))
    ps = E(nc.psum_tensor(tg + "ps", [128, 3, 512], F32))
    po = E(nc.psum_tensor(tg + "po", [128, 2, 512], F32))
    pzb = E(nc.psum_tensor(tg + "pzb", [64, 512], F32))
    pn = E(nc.psum_tensor(tg + "pn", [1, 2, 512], F32))

    pg, = ctx.sems(tg, "pg")
    with nc.Block(tg + "prep") as b:
        @b.gpsimd
        def _(g):
            g.memset(onesf[:], 1.0)
            g.memset(ones64[:], 1.0)
            g.memset(kbuf[64:65, :, :], 1.0)
            g.memset(vab[:], 1.0)
            g.memset(binm[:], 1.0).then_inc(pg, 1)
            g.wait_ge(pg, 1)
            for j in range(4):
                g.affine_select(out=binm[:, j, :], in_=binm[:, j, :],
                                compare_op=ALU.is_ge, fill=0.0, base=-128 * j,
                                pattern=[[1, 512]], channel_multiplier=-1)

    units = [(h, g) for h in range(NH) for g in range(8)]
    NU = len(units)

    def K_of(g):
        return 4 * g + 4

    P1 = [0]
    for (h, g) in units:
        P1.append(P1[-1] + K_of(g))

    (an, pnm, vn, vq, mm, ae, vd, spo, vz, pz, vz2, va) = ctx.sems(
        tg, "an", "pnm", "vn", "vq", "mm", "ae", "vd", "spo", "vz", "pz",
        "vz2", "va")
    ld = ctx.sems(tg, "ld0", "ld1")
    gs = ctx.sems(tg, "gs0", "gs1")

    with nc.Block(tg + "main") as b:
        @b.sync
        def _(s):
            for h in range(NH):
                if h >= 2:
                    s.wait_ge(spo, 8 * (h - 1))
                s.dma_start(out=kbuf[0:64, h % 2, :], in_=d["kT"][h]).then_inc(ld[h % 2], 16)
                s.dma_start(out=qbuf[0:64, h % 2, :], in_=d["qT"][h]).then_inc(ld[h % 2], 16)
                s.dma_start(out=vab[:, h % 2, :, 0:64],
                            in_=d["vv"][h].rearrange("(kt p) dd -> p kt dd", p=128)).then_inc(ld[h % 2], 16)

        def norm_scalar_s1(s, h):
            s.wait_ge(ld[h % 2], 48 * (h // 2 + 1))
            if h >= 1:
                s.wait_ge(pnm, 16 * h)
            s.activation(out=sq[:], in_=kbuf[0:64, h % 2, :],
                         func=AF.Square).then_inc(an, 1)

        def norm_scalar_s23(s, h):
            s.wait_ge(vn, 18 * h + 9)
            s.activation(out=kms[:], in_=km2[:], func=AF.Sqrt).then_inc(an, 1)
            s.wait_ge(pnm, 16 * h + 8)
            s.activation(out=sq[:], in_=qbuf[0:64, h % 2, :],
                         func=AF.Square).then_inc(an, 1)

        def norm_scalar_s4(s, h):
            s.wait_ge(vn, 18 * h + 18)
            if h >= 2:
                s.wait_ge(vq, 8 * (h - 1))
            s.activation(out=nrm[0:1, h % 2, :], in_=kn2[0:1, :],
                         func=AF.Sqrt).then_inc(an, 1)

        def norm_pe_k(t, h):
            t.wait_ge(an, 4 * h + 1)
            for c in range(8):
                if c >= 2:
                    t.wait_ge(vn, 18 * h + c - 1)
                elif h >= 1:
                    t.wait_ge(vn, 18 * h)
                t.matmul(pn[0:1, c % 2, :], ones64[:],
                         sq[:, c * 512:(c + 1) * 512], start=True,
                         stop=True).then_inc(pnm, 1)

        def norm_pe_q(t, h):
            t.wait_ge(an, 4 * h + 3)
            for c in range(8):
                t.wait_ge(vn, 18 * h + c + 9 if c >= 2 else 18 * h + c + 7)
                t.matmul(pn[0:1, c % 2, :], ones64[:],
                         sq[:, c * 512:(c + 1) * 512], start=True,
                         stop=True).then_inc(pnm, 1)

        def norm_vector_k(v, h):
            if h >= 1:
                v.wait_ge(an, 4 * h)
            for c in range(8):
                v.wait_ge(pnm, 16 * h + c + 1)
                v.tensor_copy(out=kn2[0:1, c * 512:(c + 1) * 512],
                              in_=pn[0:1, c % 2, :]).then_inc(vn, 1)
            v.reduce_max(km2[:], kn2[:], axis=AX.X).then_inc(vn, 1)

        def norm_vector_kmn(v, h):
            v.wait_ge(an, 4 * h + 2)
            v.tensor_scalar(kmn[0:1, h % 2, :], kms[:], -1.0, None,
                            op0=ALU.mult).then_inc(vn, 1)

        def norm_vector_q(v, h):
            for c in range(8):
                v.wait_ge(pnm, 16 * h + 8 + c + 1)
                v.tensor_copy(out=kn2[0:1, c * 512:(c + 1) * 512],
                              in_=pn[0:1, c % 2, :]).then_inc(vn, 1)

        @b.tensor
        def _(t):
            norm_pe_k(t, 0)
            norm_pe_q(t, 0)
            for u, (h, g) in enumerate(units):
                K = K_of(g)
                t.wait_ge(vq, u + 1)
                for kt in range(K):
                    if P1[u] + kt >= 3:
                        t.wait_ge(ae, P1[u] + kt - 2)
                    t.matmul(ps[:, kt % 3, :], kbuf[0:65, h % 2, kt * 128:(kt + 1) * 128],
                             qbuf[0:65, h % 2, g * 512:(g + 1) * 512],
                             start=True, stop=True).then_inc(mm, 1)
                if u >= 1:
                    t.wait_ge(vz, u)
                    if u >= 2:
                        t.wait_ge(vz2, u - 1)
                    t.matmul(pzb[:], onesf[:], zrow[:], start=True, stop=True).then_inc(pz, 1)
                if u >= 2:
                    t.wait_ge(va, u - 1)
                for kt in range(K):
                    t.wait_ge(ae, P1[u] + kt + 1)
                    if kt >= K - 4:
                        t.wait_ge(vd, 4 * u + (kt - (K - 4)) + 1)
                    ii = t.matmul(po[0:65, u % 2, :], vab[:, h % 2, kt, :],
                                  PT[:, kt, :], start=(kt == 0), stop=(kt == K - 1))
                ii.then_inc(spo, 1)
                if g == 0 and h + 1 < NH:
                    norm_pe_k(t, h + 1)
                if g == 2 and h + 1 < NH:
                    norm_pe_q(t, h + 1)
            t.wait_ge(vz, NU)
            t.wait_ge(vz2, NU - 1)
            t.matmul(pzb[:], onesf[:], zrow[:], start=True, stop=True).then_inc(pz, 1)

        @b.vector
        def _(v):
            norm_vector_k(v, 0)
            norm_vector_kmn(v, 0)
            norm_vector_q(v, 0)

            def epilogue(u2):
                v.wait_ge(spo, u2 + 1)
                if u2 >= 1:
                    v.wait_ge(pz, u2)
                v.reciprocal(zrow[:], po[64:65, u2 % 2, :]).then_inc(vz, 1)
                v.wait_ge(pz, u2 + 1)
                v.tensor_copy(out=zbsb[:], in_=pzb[0:64, :]).then_inc(vz2, 1)
                if u2 >= 2:
                    v.wait_ge(gs[u2 % 2], 16 * (u2 // 2))
                v.tensor_tensor(out=aosb[:, u2 % 2, :], in0=po[0:64, u2 % 2, :], in1=zbsb[:],
                                op=ALU.mult).then_inc(va, 1)

            for u, (h, g) in enumerate(units):
                K = K_of(g)
                v.wait_ge(an, 4 * h + 4)
                if u >= 16:
                    v.wait_ge(spo, u - 15)
                v.tensor_scalar(qbuf[64:65, h % 2, g * 512:(g + 1) * 512],
                                nrm[0:1, h % 2, g * 512:(g + 1) * 512],
                                kmn[0:1, h % 2, 0:1], None,
                                op0=ALU.mult).then_inc(vq, 1)
                if u >= 1:
                    epilogue(u - 1)
                if h + 1 < NH:
                    if g == 0:
                        norm_vector_k(v, h + 1)
                    elif g == 1:
                        norm_vector_kmn(v, h + 1)
                    elif g == 2:
                        norm_vector_q(v, h + 1)
            epilogue(NU - 1)

        @b.scalar
        def _(s):
            norm_scalar_s1(s, 0)
            norm_scalar_s23(s, 0)
            norm_scalar_s4(s, 0)
            for u, (h, g) in enumerate(units):
                K = K_of(g)
                for kt in range(K):
                    s.wait_ge(mm, P1[u] + kt + 1)
                    s.activation(out=PT[:, kt, :], in_=ps[:, kt % 3, :],
                                 func=AF.Exp).then_inc(ae, 1)
                if h + 1 < NH:
                    if g == 0:
                        norm_scalar_s1(s, h + 1)
                    elif g == 1:
                        norm_scalar_s23(s, h + 1)
                    elif g == 3:
                        norm_scalar_s4(s, h + 1)

        @b.gpsimd
        def _(g_):
            def store(u2):
                h2, g2 = units[u2]
                g_.wait_ge(va, u2 + 1)
                g_.dma_start(out=d["attnT"][h2 * 64:(h2 + 1) * 64, g2 * 512:(g2 + 1) * 512],
                             in_=aosb[:, u2 % 2, :]).then_inc(gs[u2 % 2], 16)
            for u, (h, g) in enumerate(units):
                for j in range(4):
                    kt = K_of(g) - 4 + j
                    g_.wait_ge(ae, P1[u] + kt + 1)
                    g_.tensor_tensor(out=PT[:, kt, :], in0=PT[:, kt, :],
                                     in1=binm[:, j, :], op=ALU.mult).then_inc(vd, 1)
                if u >= 1:
                    store(u - 1)
            store(NU - 1)
            g_.wait_ge(gs[0], 16 * NU // 2)
            g_.wait_ge(gs[1], 16 * NU // 2)
    es.close()


def ph_oproj(ctx, d, layer):
    nc = ctx.nc
    tg = "o%d" % layer
    es = contextlib.ExitStack()
    E = es.enter_context
    xin = d["x0"] if layer == 0 else d["x2"]
    xout = d["x1"] if layer == 0 else d["x3"]
    wsrc_name = "outw" if layer == 0 else "wu"
    bsrc_name = "outb" if layer == 0 else "bu"

    idb = E(nc.sbuf_tensor(tg + "idb", [128, 128], BF16))
    idf = E(nc.sbuf_tensor(tg + "idf", [128, 128], F32))
    owb = E(nc.sbuf_tensor(tg + "owb", [128, 8, DM], BF16))
    obt = E(nc.sbuf_tensor(tg + "obt", [128, 8], F32))
    wfo = E(nc.sbuf_tensor(tg + "wfo", [128, 2, DM], F32))
    bro = E(nc.sbuf_tensor(tg + "bro", [8, 128], F32))
    pprep = E(nc.psum_tensor(tg + "pprep", [128, 512], F32))
    pprepb = E(nc.psum_tensor(tg + "pprepb", [128, 512], BF16))

    pl, pg, pa, pp, pv, pp2, pv2, pbs = ctx.sems(tg, "pl", "pg", "pa", "pp", "pv", "pp2", "pv2", "pbs")
    pw = ctx.sems(tg, "pw0", "pw1")

    with nc.Block(tg + "prep") as b:
        @b.gpsimd
        def _(g):
            g.memset(idb[:], 0.0)
            g.memset(idf[:], 0.0).then_inc(pg, 1)
            g.wait_ge(pg, 1)
            g.affine_select(out=idb[:], in_=idb[:], compare_op=ALU.not_equal,
                            fill=1.0, base=0, pattern=[[-1, 128]],
                            channel_multiplier=1)
            g.affine_select(out=idf[:], in_=idf[:], compare_op=ALU.not_equal,
                            fill=1.0, base=0, pattern=[[-1, 128]],
                            channel_multiplier=1).then_inc(pg, 1)

        @b.sync
        def _(s):
            s.dma_start(out=bro[:], in_=d[bsrc_name].rearrange("(c k) -> c k", k=128)).then_inc(pbs, 16)
            wr = d[wsrc_name].rearrange("(c p) n -> c p n", p=128)
            for k in range(8):
                if k >= 2:
                    s.wait_ge(pa, k - 1)
                s.dma_start(out=wfo[:, k % 2, :], in_=wr[k]).then_inc(pw[k % 2], 16)

        if layer == 0:
            @b.scalar
            def _(s):
                for k in range(8):
                    s.wait_ge(pw[k % 2], 16 * (k // 2 + 1))
                    s.activation(out=owb[:, k, :], in_=wfo[:, k % 2, :],
                                 func=AF.Copy).then_inc(pa, 1)
        else:
            wtb = E(nc.sbuf_tensor(tg + "wtb", [128, 2, DM], BF16))

            @b.scalar
            def _(s):
                for k in range(8):
                    s.wait_ge(pw[k % 2], 16 * (k // 2 + 1))
                    if k >= 2:
                        s.wait_ge(pp, 2 * (k - 2) + 2)
                    s.activation(out=wtb[:, k % 2, :], in_=wfo[:, k % 2, :],
                                 func=AF.Copy).then_inc(pa, 1)

            @b.tensor
            def _(t):
                t.wait_ge(pg, 2)
                for k in range(8):
                    t.wait_ge(pa, k + 1)
                    for j in range(2):
                        gi = 2 * k + j
                        if gi >= 1:
                            t.wait_ge(pv, gi)
                        for c4 in range(4):
                            ii = t.transpose(pprepb[:, c4 * 128:(c4 + 1) * 128],
                                             wtb[:, k % 2, (4 * j + c4) * 128:(4 * j + c4 + 1) * 128],
                                             idb[:])
                        ii.then_inc(pp, 1)

            @b.vector
            def _(v):
                pr = pprepb[:].rearrange("p (c f) -> p c f", f=128)
                for k in range(8):
                    for j in range(2):
                        gi = 2 * k + j
                        v.wait_ge(pp, gi + 1)
                        v.tensor_copy(out=owb[:, 4 * j:4 * j + 4, k * 128:(k + 1) * 128],
                                      in_=pr).then_inc(pv, 1)

        @b.tensor
        def _(t):
            t.wait_ge(pbs, 16)
            t.wait_ge(pg, 2)
            t.transpose(pprep[:, 0:8], bro[:], idf[0:8, 0:8]).then_inc(pp2, 1)

        @b.vector
        def _(v):
            v.wait_ge(pp2, 1)
            v.tensor_copy(out=obt[:], in_=pprep[:, 0:8]).then_inc(pv2, 1)

    at = E(nc.sbuf_tensor(tg + "at", [128, 2, 8, 512], BF16))
    xti = E(nc.sbuf_tensor(tg + "xti", [128, 2, 4, DM], F32))
    xto = E(nc.sbuf_tensor(tg + "xto", [128, 2, 4, DM], F32))
    obsb = E(nc.sbuf_tensor(tg + "obsb", [128, 8, 512], BF16))
    ppj = E(nc.psum_tensor(tg + "ppj", [128, 2, 512], F32))
    ptr = E(nc.psum_tensor(tg + "ptr", [128, 512], BF16))

    pm, ve, pt2, vadd = ctx.sems(tg, "pm", "ve", "pt2", "vadd")
    lx = ctx.sems(tg, "lx0", "lx1")
    gso = ctx.sems(tg, "gso0", "gso1")
    NT = 8

    with nc.Block(tg + "main") as b:
        @b.sync
        def _(s):
            atr = d["attnT"].rearrange("(k p) t -> p k t", p=128)
            xr = xin.rearrange("(n c p) dd -> n p c dd", p=128, c=4)
            for i in range(NT):
                if i >= 2:
                    s.wait_ge(pm, 8 * (i - 1))
                s.dma_start(out=at[:, i % 2], in_=atr[:, :, i * 512:(i + 1) * 512]).then_inc(lx[i % 2], 16)
                if i >= 2:
                    s.wait_ge(vadd, 8 * (i - 1))
                s.dma_start(out=xti[:, i % 2], in_=xr[i]).then_inc(lx[i % 2], 16)

        @b.tensor
        def _(t):
            t.wait_ge(pa, 8 if layer == 0 else 0)
            if layer == 1:
                t.wait_ge(pv, 16)
            for i in range(NT):
                t.wait_ge(lx[i % 2], 32 * (i // 2 + 1))
                for mo in range(8):
                    j = 8 * i + mo
                    if j >= 2:
                        t.wait_ge(ve, j - 1)
                    for k in range(8):
                        ii = t.matmul(ppj[:, mo % 2, :], owb[:, k, mo * 128:(mo + 1) * 128],
                                      at[:, i % 2, k, :], start=(k == 0), stop=(k == 7))
                    ii.then_inc(pm, 1)
                for mo in range(8):
                    j = 8 * i + mo
                    t.wait_ge(ve, j + 1)
                    if j >= 1:
                        t.wait_ge(vadd, j)
                    for c in range(4):
                        ii = t.transpose(ptr[:, c * 128:(c + 1) * 128],
                                         obsb[:, mo, c * 128:(c + 1) * 128], idb[:])
                    ii.then_inc(pt2, 1)

        @b.vector
        def _(v):
            v.wait_ge(pv2, 1)
            for i in range(NT):
                for mo in range(8):
                    j = 8 * i + mo
                    v.wait_ge(pm, j + 1)
                    if i >= 1:
                        v.wait_ge(pt2, 8 * (i - 1) + mo + 1)
                    v.tensor_scalar(obsb[:, mo, :], ppj[:, mo % 2, :],
                                    obt[:, mo:mo + 1], None, op0=ALU.add).then_inc(ve, 1)
                v.wait_ge(lx[i % 2], 32 * (i // 2 + 1))
                if i >= 2:
                    v.wait_ge(gso[i % 2], 16 * (i // 2))
                for mo in range(8):
                    j = 8 * i + mo
                    v.wait_ge(pt2, j + 1)
                    for c in range(4):
                        ii = v.tensor_tensor(out=xto[:, i % 2, c, mo * 128:(mo + 1) * 128],
                                             in0=ptr[:, c * 128:(c + 1) * 128],
                                             in1=xti[:, i % 2, c, mo * 128:(mo + 1) * 128],
                                             op=ALU.add)
                    ii.then_inc(vadd, 1)

        @b.gpsimd
        def _(g):
            xo = xout.rearrange("(n c p) dd -> n p c dd", p=128, c=4)
            for i in range(NT):
                g.wait_ge(vadd, 8 * (i + 1))
                g.dma_start(out=xo[i], in_=xto[:, i % 2]).then_inc(gso[i % 2], 16)
            g.wait_ge(gso[0], 16 * NT // 2)
            g.wait_ge(gso[1], 16 * NT // 2)
    es.close()


def ph_moeprep(ctx, d):
    nc = ctx.nc
    tg = "mp"
    es = contextlib.ExitStack()
    E = es.enter_context
    wfm = E(nc.sbuf_tensor(tg + "wfm", [128, 2, 2048], F32))
    wbm = E(nc.sbuf_tensor(tg + "wbm", [128, 2, 2048], BF16))
    mc, = ctx.sems(tg, "mc")
    ml = ctx.sems(tg, "ml0", "ml1")
    ms = ctx.sems(tg, "ms0", "ms1")

    units = []
    for name_s, name_d, nchunk, width in (("w1c", "w1b", 8, 2048),
                                          ("w3c", "w3b", 8, 2048),
                                          ("w2c", "w2b", 16, 1024)):
        for e in range(9):
            sv = d[name_s][e].rearrange("(c p) n -> c p n", p=128)
            dv = d[name_d][e].rearrange("(c p) n -> c p n", p=128)
            for rc in range(nchunk):
                units.append((sv[rc], dv[rc], width))
    NU = len(units)

    with nc.Block(tg + "main") as b:
        @b.sync
        def _(s):
            for j, (sv, dv, w) in enumerate(units):
                if j >= 2:
                    s.wait_ge(mc, j - 1)
                s.dma_start(out=wfm[:, j % 2, 0:w], in_=sv).then_inc(ml[j % 2], 16)

        @b.scalar
        def _(s):
            for j, (sv, dv, w) in enumerate(units):
                s.wait_ge(ml[j % 2], 16 * (j // 2 + 1))
                if j >= 2:
                    s.wait_ge(ms[j % 2], 16 * (j // 2))
                s.activation(out=wbm[:, j % 2, 0:w], in_=wfm[:, j % 2, 0:w],
                             func=AF.Copy).then_inc(mc, 1)

        @b.gpsimd
        def _(g):
            for j, (sv, dv, w) in enumerate(units):
                g.wait_ge(mc, j + 1)
                g.dma_start(out=dv, in_=wbm[:, j % 2, 0:w]).then_inc(ms[j % 2], 16)
            g.wait_ge(ms[0], 16 * ((NU + 1) // 2))
            g.wait_ge(ms[1], 16 * (NU // 2))
    es.close()


def ph_moe(ctx, d):
    nc = ctx.nc
    tg = "m"
    es = contextlib.ExitStack()
    E = es.enter_context
    idf = E(nc.sbuf_tensor(tg + "idf", [128, 128], F32))
    idb = E(nc.sbuf_tensor(tg + "idb", [128, 128], BF16))
    epst = E(nc.sbuf_tensor(tg + "epst", [128, 1], F32))
    ones1 = E(nc.sbuf_tensor(tg + "ones1", [1, 128], F32))
    rsrc = E(nc.sbuf_tensor(tg + "rsrc", [8, DM], F32))
    ebr = E(nc.sbuf_tensor(tg + "ebr", [1, 8], F32))
    rwT = E(nc.sbuf_tensor(tg + "rwT", [128, 8, 8], F32))
    ebb = E(nc.sbuf_tensor(tg + "ebb", [128, 8], F32))
    xt = E(nc.sbuf_tensor(tg + "xt", [128, 2, 4, DM], F32))
    bn6 = E(nc.sbuf_tensor(tg + "bn6", [128, 48], F32))
    bno = E(nc.sbuf_tensor(tg + "bno", [128, 8], F32))
    sd = E(nc.sbuf_tensor(tg + "sd", [128, 4], F32))
    rstd = E(nc.sbuf_tensor(tg + "rstd", [128, 4], F32))
    htf = E(nc.sbuf_tensor(tg + "htf", [128, 4, DM], F32))
    hf32 = E(nc.sbuf_tensor(tg + "hf32", [128, 8, 512], F32))
    hbf = E(nc.sbuf_tensor(tg + "hbf", [128, 8, 512], BF16))
    lgs = E(nc.sbuf_tensor(tg + "lgs", [128, 32], F32))
    m1 = E(nc.sbuf_tensor(tg + "m1", [128, 4], F32))
    m1n = E(nc.sbuf_tensor(tg + "m1n", [128, 4], F32))
    m2 = E(nc.sbuf_tensor(tg + "m2", [128, 4], F32))
    sel = E(nc.sbuf_tensor(tg + "sel", [128, 32], F32))
    msk = E(nc.sbuf_tensor(tg + "msk", [128, 32], F32))
    e1t = E(nc.sbuf_tensor(tg + "e1t", [128, 32], F32))
    wun = E(nc.sbuf_tensor(tg + "wun", [128, 32], F32))
    zz = E(nc.sbuf_tensor(tg + "zz", [128, 4], F32))
    zrr = E(nc.sbuf_tensor(tg + "zrr", [128, 4], F32))
    cwT = E(nc.sbuf_tensor(tg + "cwT", [128, 4, 8], F32))
    cwr = E(nc.sbuf_tensor(tg + "cwr", [8, 512], F32))
    cw0 = E(nc.sbuf_tensor(tg + "cw0", [1, 2, 512], F32))
    cwb = E(nc.sbuf_tensor(tg + "cwb", [128, 8, 512], BF16))
    w1s = E(nc.sbuf_tensor(tg + "w1s", [128, 2, 8, 512], BF16))
    w3s = E(nc.sbuf_tensor(tg + "w3s", [128, 2, 8, 512], BF16))
    w2s = E(nc.sbuf_tensor(tg + "w2s", [128, 16, 1024], BF16))
    sg = E(nc.sbuf_tensor(tg + "sg", [128, 2, 512], F32))
    eo1 = E(nc.sbuf_tensor(tg + "eo1", [128, 2, 512], BF16))
    eo = E(nc.sbuf_tensor(tg + "eo", [128, 16, 512], BF16))
    acc = E(nc.sbuf_tensor(tg + "acc", [128, 8, 512], F32))
    accb = E(nc.sbuf_tensor(tg + "accb", [128, 8, 512], BF16))
    pmisc = E(nc.psum_tensor(tg + "pmisc", [128, 512], F32))
    pg_ = E(nc.psum_tensor(tg + "pg", [128, 512], F32))
    pu_ = E(nc.psum_tensor(tg + "pu", [128, 512], F32))
    pp2 = E(nc.psum_tensor(tg + "pp2", [128, 2, 512], F32))
    pcwt = E(nc.psum_tensor(tg + "pcwt", [8, 512], F32))
    pcb = E(nc.psum_tensor(tg + "pcb", [128, 512], F32))
    pepi = E(nc.psum_tensor(tg + "pepi", [128, 512], BF16))

    EO = [8, 0, 1, 2, 3, 4, 5, 6, 7]

    (pg, mrw, meb, mpp, mvv, mv1, ma1, mv4, mp1, mv5, mhb, mpr, mvm, mae,
     mcw1, mpc, mvr, mpb, macw, mpg, mpu, mas, mve, mps, mpw, mva, mab,
     mpe, mvp, mself) = ctx.sems(
        tg, "pg", "mrw", "meb", "mpp", "mvv", "mv1", "ma1", "mv4", "mp1",
        "mv5", "mhb", "mpr", "mvm", "mae", "mcw1", "mpc", "mvr", "mpb",
        "macw", "mpg", "mpu", "mas", "mve", "mps", "mpw", "mva", "mab",
        "mpe", "mvp", "mself")
    mx = ctx.sems(tg, "mx0", "mx1")
    mw1 = ctx.sems(tg, "mw10", "mw11")
    mw3 = ctx.sems(tg, "mw30", "mw31")
    mw2, = ctx.sems(tg, "mw2")
    msx = ctx.sems(tg, "msx0", "msx1")
    mcw = ctx.sems(tg, "mcw0", "mcw1s")

    with nc.Block(tg + "prep") as b:
        @b.gpsimd
        def _(g):
            g.memset(epst[:], 1.0e-5)
            g.memset(ones1[:], 1.0)
            g.memset(idf[:], 0.0)
            g.memset(idb[:], 0.0).then_inc(pg, 1)
            g.wait_ge(pg, 1)
            g.affine_select(out=idf[:], in_=idf[:], compare_op=ALU.not_equal,
                            fill=1.0, base=0, pattern=[[-1, 128]],
                            channel_multiplier=1)
            g.affine_select(out=idb[:], in_=idb[:], compare_op=ALU.not_equal,
                            fill=1.0, base=0, pattern=[[-1, 128]],
                            channel_multiplier=1).then_inc(pg, 1)

        @b.sync
        def _(s):
            s.dma_start(out=rsrc[:], in_=d["rw"]).then_inc(mrw, 16)
            s.dma_start(out=ebr[:], in_=d["eb"][None, :]).then_inc(meb, 16)

        @b.tensor
        def _(t):
            t.wait_ge(mrw, 16)
            t.wait_ge(pg, 2)
            for k in range(8):
                if k >= 1:
                    t.wait_ge(mvv, k)
                t.transpose(pmisc[:, 0:8], rsrc[:, k * 128:(k + 1) * 128],
                            idf[0:8, 0:8]).then_inc(mpp, 1)
            t.wait_ge(meb, 16)
            t.wait_ge(mvv, 8)
            t.matmul(pmisc[:, 8:16], ones1[:], ebr[:], start=True,
                     stop=True).then_inc(mpp, 1)

        @b.vector
        def _(v):
            for k in range(8):
                v.wait_ge(mpp, k + 1)
                v.tensor_copy(out=rwT[:, k, :], in_=pmisc[:, 0:8]).then_inc(mvv, 1)
            v.wait_ge(mpp, 9)
            v.tensor_copy(out=ebb[:], in_=pmisc[:, 8:16]).then_inc(mvv, 1)

    NT = 8

    def IDX(tt, ei, mh):
        return 144 * tt + 16 * ei + mh

    def WIDX(tt, ei, mo):
        return 72 * tt + 8 * ei + mo

    with nc.Block(tg + "main") as b:
        @b.sync
        def _(s):
            xr = d["x1"].rearrange("(n c p) dd -> n p c dd", p=128, c=4)
            for tt in range(NT):
                if tt >= 2:
                    s.wait_ge(msx[tt % 2], 16 * (tt // 2))
                s.dma_start(out=xt[:, tt % 2], in_=xr[tt]).then_inc(mx[tt % 2], 16)
                for ei, e in enumerate(EO):
                    w1v = d["w1b"][e].rearrange("(c p) n -> p c n", p=128)
                    w3v = d["w3b"][e].rearrange("(c p) n -> p c n", p=128)
                    w2v = d["w2b"][e].rearrange("(c p) n -> p c n", p=128)
                    for q in range(4):
                        gate = 144 * tt + 16 * ei + 4 * q - 4
                        if gate >= 1:
                            s.wait_ge(mpg, gate)
                        s.dma_start(out=w1s[:, q % 2],
                                    in_=w1v[:, :, q * 512:(q + 1) * 512]).then_inc(mw1[q % 2], 16)
                        if gate >= 1:
                            s.wait_ge(mpu, gate)
                        s.dma_start(out=w3s[:, q % 2],
                                    in_=w3v[:, :, q * 512:(q + 1) * 512]).then_inc(mw3[q % 2], 16)
                    if 9 * tt + ei >= 1:
                        s.wait_ge(mpw, 72 * tt + 8 * ei)
                    s.dma_start(out=w2s[:], in_=w2v).then_inc(mw2, 16)

        @b.vector
        def _(v):
            ns = 0

            def selfw(ii):
                nonlocal ns
                ii.then_inc(mself, 1)
                ns += 1
                v.wait_ge(mself, ns)

            for tt in range(NT):
                v.wait_ge(mx[tt % 2], 16 * (tt // 2 + 1))
                for c in range(4):
                    ii = v.bn_stats(bn6[:, c * 12:c * 12 + 6], xt[:, tt % 2, c, 0:512])
                    ii = v.bn_stats(bn6[:, c * 12 + 6:c * 12 + 12], xt[:, tt % 2, c, 512:1024])
                selfw(ii)
                for c in range(4):
                    ii = v.bn_aggr(bno[:, 2 * c:2 * c + 2], bn6[:, c * 12:c * 12 + 12])
                ii.then_inc(mv1, 1)
                v.wait_ge(ma1, tt + 1)
                ii = v.reciprocal(rstd[:], sd[:])
                selfw(ii)
                if tt >= 1:
                    v.wait_ge(mp1, 8 * tt)
                for c in range(4):
                    ii = v.tensor_scalar(htf[:, c, :], xt[:, tt % 2, c, :],
                                         bno[:, 2 * c:2 * c + 1], rstd[:, c:c + 1],
                                         op0=ALU.subtract, op1=ALU.mult)
                ii.then_inc(mv4, 1)
                for k in range(8):
                    v.wait_ge(mp1, 8 * tt + k + 1)
                    v.tensor_copy(out=hf32[:, k, :], in_=pmisc[:]).then_inc(mv5, 1)
                for c in range(4):
                    v.wait_ge(mpr, 4 * tt + 4)
                    c8 = slice(c * 8, (c + 1) * 8)
                    ii = v.tensor_tensor(out=lgs[:, c8], in0=pmisc[:, 448 + c * 8:448 + (c + 1) * 8],
                                         in1=ebb[:], op=ALU.add)
                    selfw(ii)
                    ii = v.reduce_max(m1[:, c:c + 1], lgs[:, c8], axis=AX.X)
                    selfw(ii)
                    ii = v.tensor_scalar(sel[:, c8], lgs[:, c8], m1[:, c:c + 1], None,
                                         op0=ALU.is_ge)
                    selfw(ii)
                    ii = v.scalar_tensor_tensor(msk[:, c8], sel[:, c8], -1.0e30,
                                                lgs[:, c8], op0=ALU.mult, op1=ALU.add)
                    selfw(ii)
                    ii = v.reduce_max(m2[:, c:c + 1], msk[:, c8], axis=AX.X)
                    selfw(ii)
                    ii = v.tensor_scalar(sel[:, c8], lgs[:, c8], m2[:, c:c + 1], None,
                                         op0=ALU.is_ge)
                    selfw(ii)
                    v.tensor_scalar(m1n[:, c:c + 1], m1[:, c:c + 1], -1.0, None,
                                    op0=ALU.mult).then_inc(mvm, 1)
                    v.wait_ge(mae, 4 * tt + c + 1)
                    ii = v.tensor_tensor(out=wun[:, c8], in0=sel[:, c8], in1=e1t[:, c8],
                                         op=ALU.mult)
                    selfw(ii)
                    ii = v.reduce_sum(zz[:, c:c + 1], wun[:, c8], axis=AX.X)
                    selfw(ii)
                    ii = v.reciprocal(zrr[:, c:c + 1], zz[:, c:c + 1])
                    selfw(ii)
                    v.tensor_scalar(cwT[:, c, :], wun[:, c8], zrr[:, c:c + 1], None,
                                    op0=ALU.mult).then_inc(mcw1, 1)
                v.wait_ge(mpc, tt + 1)
                v.tensor_copy(out=cwr[:], in_=pcwt[:]).then_inc(mvr, 1)
                for ei, e in enumerate(EO):
                    for mh in range(16):
                        ix = IDX(tt, ei, mh)
                        if ei == 0:
                            v.wait_ge(mas, ix + 1)
                            v.wait_ge(mpu, ix + 1)
                            if tt >= 1:
                                v.wait_ge(mpw, 72 * tt)
                            v.tensor_tensor(out=eo[:, mh, :], in0=pu_[:],
                                            in1=sg[:, mh % 2, :], op=ALU.mult).then_inc(mve, 1)
                        else:
                            v.wait_ge(mas, ix + 1)
                            v.wait_ge(mpu, ix + 1)
                            six = 128 * tt + 16 * (ei - 1) + mh
                            if six >= 2:
                                v.wait_ge(mps, six - 1)
                            v.tensor_tensor(out=eo1[:, mh % 2, :], in0=pu_[:],
                                            in1=sg[:, mh % 2, :], op=ALU.mult).then_inc(mve, 1)
                    for mo in range(8):
                        wx = WIDX(tt, ei, mo)
                        v.wait_ge(mpw, wx + 1)
                        if ei == 0:
                            if tt >= 1:
                                v.wait_ge(mab, tt)
                            v.tensor_copy(out=acc[:, mo, :], in_=pp2[:, mo % 2, :]).then_inc(mva, 1)
                        else:
                            v.wait_ge(mva, WIDX(tt, ei - 1, mo) + 1)
                            v.tensor_tensor(out=acc[:, mo, :], in0=pp2[:, mo % 2, :],
                                            in1=acc[:, mo, :], op=ALU.add).then_inc(mva, 1)
                for mo in range(8):
                    v.wait_ge(mpe, 8 * tt + mo + 1)
                    for c in range(4):
                        ii = v.tensor_tensor(out=xt[:, tt % 2, c, mo * 128:(mo + 1) * 128],
                                             in0=pepi[:, c * 128:(c + 1) * 128],
                                             in1=xt[:, tt % 2, c, mo * 128:(mo + 1) * 128],
                                             op=ALU.add)
                    ii.then_inc(mvp, 1)

        @b.scalar
        def _(s):
            vcol = bno[:].rearrange("p (c two) -> p c two", two=2)[:, :, 1:2].rearrange("p c o -> p (c o)")
            for tt in range(NT):
                s.wait_ge(mv1, tt + 1)
                s.activation(out=sd[:], in_=vcol, func=AF.Sqrt, bias=epst[:, 0:1]).then_inc(ma1, 1)
                s.wait_ge(mv5, 8 * (tt + 1))
                s.activation(out=hbf[:], in_=hf32[:], func=AF.Copy).then_inc(mhb, 1)
                for c in range(4):
                    s.wait_ge(mvm, 4 * tt + c + 1)
                    s.activation(out=e1t[:, c * 8:(c + 1) * 8], in_=lgs[:, c * 8:(c + 1) * 8],
                                 func=AF.Exp, bias=m1n[:, c:c + 1]).then_inc(mae, 1)
                for e in range(8):
                    s.wait_ge(mpb, 8 * tt + e + 1)
                    if tt >= 1:
                        s.wait_ge(mps, 128 * (tt - 1) + 16 * (e + 1))
                    s.activation(out=cwb[:, e, :], in_=pcb[:], func=AF.Copy).then_inc(macw, 1)
                for ei, e in enumerate(EO):
                    for mh in range(16):
                        ix = IDX(tt, ei, mh)
                        s.wait_ge(mpg, ix + 1)
                        if ix >= 2:
                            s.wait_ge(mve, ix - 1)
                        s.activation(out=sg[:, mh % 2, :], in_=pg_[:],
                                     func=AF.Silu).then_inc(mas, 1)
                s.wait_ge(mva, 72 * (tt + 1))
                if tt >= 1:
                    s.wait_ge(mpe, 8 * tt)
                s.activation(out=accb[:], in_=acc[:], func=AF.Copy).then_inc(mab, 1)

        @b.tensor
        def _(t):
            t.wait_ge(mvv, 9)
            t.wait_ge(pg, 2)
            for tt in range(NT):
                t.wait_ge(mv4, tt + 1)
                if tt >= 1:
                    t.wait_ge(mcw1, 4 * tt)
                for k in range(8):
                    if 8 * tt + k >= 1:
                        t.wait_ge(mv5, 8 * tt + k)
                    for c in range(4):
                        ii = t.transpose(pmisc[:, c * 128:(c + 1) * 128],
                                         htf[:, c, k * 128:(k + 1) * 128], idf[:])
                    ii.then_inc(mp1, 1)
                t.wait_ge(mv5, 8 * (tt + 1))
                for c in range(4):
                    for k in range(8):
                        ii = t.matmul(pmisc[:, 448 + c * 8:448 + (c + 1) * 8],
                                      hf32[:, k, c * 128:(c + 1) * 128], rwT[:, k, :],
                                      start=(k == 0), stop=(k == 7))
                    ii.then_inc(mpr, 1)
                t.wait_ge(mcw1, 4 * (tt + 1))
                if tt >= 1:
                    t.wait_ge(mvr, tt)
                for c in range(4):
                    ii = t.transpose(pcwt[:, c * 128:(c + 1) * 128], cwT[:, c, :],
                                     idf[0:128, 0:128])
                ii.then_inc(mpc, 1)
                for e in range(8):
                    t.wait_ge(mcw[e % 2], 16 * ((8 * tt + e) // 2 + 1))
                    if 8 * tt + e >= 1:
                        t.wait_ge(macw, 8 * tt + e)
                    t.matmul(pcb[:], ones1[:], cw0[0:1, e % 2, :], start=True,
                             stop=True).then_inc(mpb, 1)
                t.wait_ge(mhb, tt + 1)
                for ei, e in enumerate(EO):
                    for q in range(4):
                        t.wait_ge(mw1[q % 2], 16 * (2 * (9 * tt + ei) + q // 2 + 1))
                        t.wait_ge(mw3[q % 2], 16 * (2 * (9 * tt + ei) + q // 2 + 1))
                        for mhl in range(4):
                            mh = 4 * q + mhl
                            ix = IDX(tt, ei, mh)
                            if ix >= 1:
                                t.wait_ge(mas, ix)
                            for k in range(8):
                                ii = t.matmul(pg_[:], w1s[:, q % 2, k, mhl * 128:(mhl + 1) * 128],
                                              hbf[:, k, :], start=(k == 0), stop=(k == 7))
                            ii.then_inc(mpg, 1)
                            if ix >= 1:
                                t.wait_ge(mve, ix)
                            for k in range(8):
                                ii = t.matmul(pu_[:], w3s[:, q % 2, k, mhl * 128:(mhl + 1) * 128],
                                              hbf[:, k, :], start=(k == 0), stop=(k == 7))
                            ii.then_inc(mpu, 1)
                    t.wait_ge(mw2, 16 * (9 * tt + ei + 1))
                    if ei == 0:
                        t.wait_ge(mve, 144 * tt + 16)
                    else:
                        t.wait_ge(mps, 128 * tt + 16 * ei)
                    for mo in range(8):
                        wx = WIDX(tt, ei, mo)
                        if wx >= 2:
                            t.wait_ge(mva, wx - 1)
                        for mh in range(16):
                            ii = t.matmul(pp2[:, mo % 2, :], w2s[:, mh, mo * 128:(mo + 1) * 128],
                                          eo[:, mh, :], start=(mh == 0), stop=(mh == 15))
                        ii.then_inc(mpw, 1)
                t.wait_ge(mab, tt + 1)
                for mo in range(8):
                    if 8 * tt + mo >= 1:
                        t.wait_ge(mvp, 8 * tt + mo)
                    for c in range(4):
                        ii = t.transpose(pepi[:, c * 128:(c + 1) * 128],
                                         accb[:, mo, c * 128:(c + 1) * 128], idb[:])
                    ii.then_inc(mpe, 1)

        @b.gpsimd
        def _(g):
            xo = d["x2"].rearrange("(n c p) dd -> n p c dd", p=128, c=4)
            for tt in range(NT):
                g.wait_ge(mvr, tt + 1)
                for e in range(8):
                    if 8 * tt + e >= 2:
                        g.wait_ge(mpb, 8 * tt + e - 1)
                    g.dma_start(out=cw0[0:1, e % 2, :], in_=cwr[e:e + 1, :]).then_inc(mcw[e % 2], 16)
                for ei, e in enumerate(EO):
                    if ei == 0:
                        continue
                    for mh in range(16):
                        ix = IDX(tt, ei, mh)
                        six = 128 * tt + 16 * (ei - 1) + mh
                        g.wait_ge(mve, ix + 1)
                        g.wait_ge(macw, 8 * tt + ei)
                        g.wait_ge(mpw, 72 * tt + 8 * ei)
                        g.tensor_tensor(out=eo[:, mh, :], in0=eo1[:, mh % 2, :],
                                        in1=cwb[:, e, :], op=ALU.mult).then_inc(mps, 1)
                g.wait_ge(mvp, 8 * (tt + 1))
                g.dma_start(out=xo[tt], in_=xt[:, tt % 2]).then_inc(msx[tt % 2], 16)
            g.wait_ge(msx[0], 16 * (NT // 2))
            g.wait_ge(msx[1], 16 * (NT // 2))
    es.close()


def ph_ffnprep(ctx, d):
    nc = ctx.nc
    tg = "fp"
    es = contextlib.ExitStack()
    E = es.enter_context
    idb = E(nc.sbuf_tensor(tg + "idb", [128, 128], BF16))
    wim = E(nc.sbuf_tensor(tg + "wim", [128, 2, 3072], I32))
    wbb = E(nc.sbuf_tensor(tg + "wbb", [128, 2, 3072], BF16))
    tst = E(nc.sbuf_tensor(tg + "tst", [128, 4, 128], BF16))
    ptb = E(nc.psum_tensor(tg + "ptb", [128, 4, 128], BF16))

    pg, fc, fp, fv = ctx.sems(tg, "pg", "fc", "fp", "fv")
    fl = ctx.sems(tg, "fl0", "fl1")
    fs = ctx.sems(tg, "fs0", "fs1", "fs2", "fs3")

    units = []
    for name_s, name_d, nrc, W in (("wg", "wgT", 24, 1024),
                                   ("wup", "wuT", 24, 1024),
                                   ("wd", "wdT", 8, 3072)):
        sv = d[name_s].rearrange("(c p) n -> c p n", p=128)
        for rc in range(nrc):
            units.append((sv[rc], d[name_d], rc, W // 128, W))

    blocks_before = [0]
    for u in units:
        blocks_before.append(blocks_before[-1] + u[3])
    NB = blocks_before[-1]

    groups_before = [x // 4 for x in blocks_before]
    NG = NB // 4
    fsg = fs[0]

    with nc.Block(tg + "main") as b:
        @b.gpsimd
        def _(g):
            g.memset(idb[:], 0.0).then_inc(pg, 1)
            g.wait_ge(pg, 1)
            g.affine_select(out=idb[:], in_=idb[:], compare_op=ALU.not_equal,
                            fill=1.0, base=0, pattern=[[-1, 128]],
                            channel_multiplier=1).then_inc(pg, 1)
            for j, (sv, dvt, rc, ncb, W) in enumerate(units):
                for gg in range(ncb // 4):
                    gi = groups_before[j] + gg
                    g.wait_ge(fv, gi + 1)
                    for q in range(4):
                        cb = 4 * gg + q
                        g.dma_start(out=dvt[cb * 128:(cb + 1) * 128, rc * 128:(rc + 1) * 128],
                                    in_=tst[:, q, :]).then_inc(fsg, 16)
            g.wait_ge(fsg, 64 * NG)

        @b.sync
        def _(s):
            for j, (sv, dvt, rc, ncb, W) in enumerate(units):
                if j >= 2:
                    s.wait_ge(fc, j - 1)
                s.dma_start(out=wim[:, j % 2, 0:W], in_=sv).then_inc(fl[j % 2], 16)

        @b.scalar
        def _(s):
            for j, (sv, dvt, rc, ncb, W) in enumerate(units):
                s.wait_ge(fl[j % 2], 16 * (j // 2 + 1))
                if j >= 2:
                    s.wait_ge(fp, groups_before[j - 1])
                s.activation(out=wbb[:, j % 2, 0:W], in_=wim[:, j % 2, 0:W],
                             func=AF.Copy, scale=1.0 / 127.0).then_inc(fc, 1)

        @b.tensor
        def _(t):
            t.wait_ge(pg, 2)
            for j, (sv, dvt, rc, ncb, W) in enumerate(units):
                t.wait_ge(fc, j + 1)
                for gg in range(ncb // 4):
                    gi = groups_before[j] + gg
                    if gi >= 1:
                        t.wait_ge(fv, gi)
                    for q in range(4):
                        cb = 4 * gg + q
                        ii = t.transpose(ptb[:, q, :], wbb[:, j % 2, cb * 128:(cb + 1) * 128],
                                         idb[:])
                    ii.then_inc(fp, 1)

        @b.vector
        def _(v):
            for j, (sv, dvt, rc, ncb, W) in enumerate(units):
                for gg in range(ncb // 4):
                    gi = groups_before[j] + gg
                    v.wait_ge(fp, gi + 1)
                    if gi >= 1:
                        v.wait_ge(fsg, 64 * gi)
                    v.tensor_copy(out=tst[:], in_=ptb[:]).then_inc(fv, 1)
    es.close()


def ph_ffn(ctx, d):
    nc = ctx.nc
    tg = "f"
    es = contextlib.ExitStack()
    E = es.enter_context
    idb = E(nc.sbuf_tensor(tg + "idb", [128, 128], BF16))
    epst = E(nc.sbuf_tensor(tg + "epst", [128, 1], F32))
    wgs = E(nc.sbuf_tensor(tg + "wgs", [128, 8, 3072], BF16))
    wus = E(nc.sbuf_tensor(tg + "wus", [128, 8, 3072], BF16))
    wds = E(nc.sbuf_tensor(tg + "wds", [128, 24, 1024], BF16))
    xt = E(nc.sbuf_tensor(tg + "xt", [128, 4, DM], F32))
    bn6 = E(nc.sbuf_tensor(tg + "bn6", [128, 48], F32))
    bno = E(nc.sbuf_tensor(tg + "bno", [128, 8], F32))
    sd = E(nc.sbuf_tensor(tg + "sd", [128, 4], F32))
    rstd = E(nc.sbuf_tensor(tg + "rstd", [128, 4], F32))
    htok = E(nc.sbuf_tensor(tg + "htok", [128, 4, DM], BF16))
    hbf = E(nc.sbuf_tensor(tg + "hbf", [128, 8, 512], BF16))
    sg = E(nc.sbuf_tensor(tg + "sg", [128, 1, 512], F32))
    h3 = E(nc.sbuf_tensor(tg + "h3", [128, 24, 512], BF16))
    obsb = E(nc.sbuf_tensor(tg + "obsb", [128, 4, 512], BF16))
    pst = E(nc.psum_tensor(tg + "pst", [128, 512], BF16))
    pg_ = E(nc.psum_tensor(tg + "pg", [128, 512], F32))
    pu_ = E(nc.psum_tensor(tg + "pu", [128, 512], F32))
    pp2 = E(nc.psum_tensor(tg + "pp2", [128, 2, 512], F32))
    pepi = E(nc.psum_tensor(tg + "pepi", [128, 512], BF16))

    (pg, fv1, fa1, fv4, fp1, fv5, fpg, fpu, fas, fve, fpw, fae, fpe, fvp,
     fself) = ctx.sems(tg, "pg", "fv1", "fa1", "fv4", "fp1", "fv5", "fpg",
                       "fpu", "fas", "fve", "fpw", "fae", "fpe", "fvp", "fself")
    fw, = ctx.sems(tg, "fw")
    fx = ctx.sems(tg, "fx0", "fx1")
    fsx = ctx.sems(tg, "fsx0", "fsx1")

    with nc.Block(tg + "prep") as b:
        @b.gpsimd
        def _(g):
            g.memset(epst[:], 1.0e-5)
            g.memset(idb[:], 0.0).then_inc(pg, 1)
            g.wait_ge(pg, 1)
            g.affine_select(out=idb[:], in_=idb[:], compare_op=ALU.not_equal,
                            fill=1.0, base=0, pattern=[[-1, 128]],
                            channel_multiplier=1).then_inc(pg, 1)

        @b.sync
        def _(s):
            s.dma_start(out=wgs[:], in_=d["wgT"].rearrange("(c p) n -> p c n", p=128)).then_inc(fw, 16)
            s.dma_start(out=wus[:], in_=d["wuT"].rearrange("(c p) n -> p c n", p=128)).then_inc(fw, 16)
            s.dma_start(out=wds[:], in_=d["wdT"].rearrange("(c p) n -> p c n", p=128)).then_inc(fw, 16)

    NT = 8
    with nc.Block(tg + "main") as b:
        @b.sync
        def _(s):
            xr = d["x3"].rearrange("(n c p) dd -> n p c dd", p=128, c=4)
            for tt in range(NT):
                if tt >= 1:
                    s.wait_ge(fsx[(tt - 1) % 2], 16 * ((tt - 1) // 2 + 1))
                s.dma_start(out=xt[:], in_=xr[tt]).then_inc(fx[tt % 2], 16)

        @b.vector
        def _(v):
            ns = 0

            def selfw(ii):
                nonlocal ns
                ii.then_inc(fself, 1)
                ns += 1
                v.wait_ge(fself, ns)

            for tt in range(NT):
                v.wait_ge(fx[tt % 2], 16 * (tt // 2 + 1))
                if tt >= 1:
                    v.wait_ge(fvp, 8 * tt)
                for c in range(4):
                    ii = v.bn_stats(bn6[:, c * 12:c * 12 + 6], xt[:, c, 0:512])
                    ii = v.bn_stats(bn6[:, c * 12 + 6:c * 12 + 12], xt[:, c, 512:1024])
                selfw(ii)
                for c in range(4):
                    ii = v.bn_aggr(bno[:, 2 * c:2 * c + 2], bn6[:, c * 12:c * 12 + 12])
                ii.then_inc(fv1, 1)
                v.wait_ge(fa1, tt + 1)
                ii = v.reciprocal(rstd[:], sd[:])
                selfw(ii)
                if tt >= 1:
                    v.wait_ge(fp1, 8 * tt)
                for c in range(4):
                    ii = v.tensor_scalar(htok[:, c, :], xt[:, c, :],
                                         bno[:, 2 * c:2 * c + 1], rstd[:, c:c + 1],
                                         op0=ALU.subtract, op1=ALU.mult)
                ii.then_inc(fv4, 1)
                for k in range(8):
                    v.wait_ge(fp1, 8 * tt + k + 1)
                    v.tensor_copy(out=hbf[:, k, :], in_=pst[:]).then_inc(fv5, 1)
                for mh in range(24):
                    ix = 24 * tt + mh
                    v.wait_ge(fas, ix + 1)
                    v.wait_ge(fpu, ix + 1)
                    if tt >= 1:
                        v.wait_ge(fpw, 8 * tt)
                    v.tensor_tensor(out=h3[:, mh, :], in0=pu_[:], in1=sg[:, 0, :],
                                    op=ALU.mult).then_inc(fve, 1)
                for mo in range(8):
                    v.wait_ge(fpe, 8 * tt + mo + 1)
                    for c in range(4):
                        ii = v.tensor_tensor(out=xt[:, c, mo * 128:(mo + 1) * 128],
                                             in0=pepi[:, c * 128:(c + 1) * 128],
                                             in1=xt[:, c, mo * 128:(mo + 1) * 128],
                                             op=ALU.add)
                    ii.then_inc(fvp, 1)

        @b.scalar
        def _(s):
            vcol = bno[:].rearrange("p (c two) -> p c two", two=2)[:, :, 1:2].rearrange("p c o -> p (c o)")
            for tt in range(NT):
                s.wait_ge(fv1, tt + 1)
                s.activation(out=sd[:], in_=vcol, func=AF.Sqrt, bias=epst[:, 0:1]).then_inc(fa1, 1)
                for mh in range(24):
                    ix = 24 * tt + mh
                    s.wait_ge(fpg, ix + 1)
                    if ix >= 1:
                        s.wait_ge(fve, ix)
                    s.activation(out=sg[:, 0, :], in_=pg_[:],
                                 func=AF.Silu).then_inc(fas, 1)
                for mo in range(8):
                    j = 8 * tt + mo
                    s.wait_ge(fpw, j + 1)
                    if j >= 4:
                        s.wait_ge(fpe, j - 3)
                    s.activation(out=obsb[:, mo % 4, :], in_=pp2[:, mo % 2, :],
                                 func=AF.Copy).then_inc(fae, 1)

        @b.tensor
        def _(t):
            t.wait_ge(pg, 2)
            t.wait_ge(fw, 48)
            for tt in range(NT):
                t.wait_ge(fv4, tt + 1)
                for k in range(8):
                    if 8 * tt + k >= 1:
                        t.wait_ge(fv5, 8 * tt + k)
                    for c in range(4):
                        ii = t.transpose(pst[:, c * 128:(c + 1) * 128],
                                         htok[:, c, k * 128:(k + 1) * 128], idb[:])
                    ii.then_inc(fp1, 1)
                t.wait_ge(fv5, 8 * (tt + 1))
                for mh in range(24):
                    ix = 24 * tt + mh
                    if ix >= 1:
                        t.wait_ge(fas, ix)
                    for k in range(8):
                        ii = t.matmul(pg_[:], wgs[:, k, mh * 128:(mh + 1) * 128],
                                      hbf[:, k, :], start=(k == 0), stop=(k == 7))
                    ii.then_inc(fpg, 1)
                    if ix >= 1:
                        t.wait_ge(fve, ix)
                    for k in range(8):
                        ii = t.matmul(pu_[:], wus[:, k, mh * 128:(mh + 1) * 128],
                                      hbf[:, k, :], start=(k == 0), stop=(k == 7))
                    ii.then_inc(fpu, 1)
                t.wait_ge(fve, 24 * (tt + 1))
                for mo in range(9):
                    if mo < 8:
                        j = 8 * tt + mo
                        if j >= 2:
                            t.wait_ge(fae, j - 1)
                        for mh in range(24):
                            ii = t.matmul(pp2[:, mo % 2, :], wds[:, mh, mo * 128:(mo + 1) * 128],
                                          h3[:, mh, :], start=(mh == 0), stop=(mh == 23))
                        ii.then_inc(fpw, 1)
                    if mo >= 1:
                        mt = mo - 1
                        t.wait_ge(fae, 8 * tt + mt + 1)
                        if 8 * tt + mt >= 1:
                            t.wait_ge(fvp, 8 * tt + mt)
                        for c in range(4):
                            ii = t.transpose(pepi[:, c * 128:(c + 1) * 128],
                                             obsb[:, mt % 4, c * 128:(c + 1) * 128], idb[:])
                        ii.then_inc(fpe, 1)

        @b.gpsimd
        def _(g):
            yo = d["y"].rearrange("(n c p) dd -> n p c dd", p=128, c=4)
            for tt in range(NT):
                g.wait_ge(fvp, 8 * (tt + 1))
                g.dma_start(out=yo[tt], in_=xt[:]).then_inc(fsx[tt % 2], 16)
            g.wait_ge(fsx[0], 16 * (NT // 2))
            g.wait_ge(fsx[1], 16 * (NT // 2))
    es.close()

EMIT = {
    "qkv0": lambda c, d: ph_qkv(c, d, 0),
    "qkv1": lambda c, d: ph_qkv(c, d, 1),
    "attn0": lambda c, d: ph_attn(c, d, 0),
    "attn1": lambda c, d: ph_attn(c, d, 1),
    "oproj0": lambda c, d: ph_oproj(c, d, 0),
    "oproj1": lambda c, d: ph_oproj(c, d, 1),
    "moeprep": lambda c, d: ph_moeprep(c, d),
    "moe": lambda c, d: ph_moe(c, d),
    "ffnprep": lambda c, d: ph_ffnprep(c, d),
    "ffn": lambda c, d: ph_ffn(c, d),
}

GROUPS = [ALL_PHASES]


def build_group(phases):
    nc = bass.Bass("TRN2", target_bir_lowering=False)
    ctx = Ctx(nc)
    reads = set()
    writes = set()
    for p in phases:
        r, w = PHASE_IO[p]
        reads.update(x for x in r if x not in writes)
        writes.update(w)
    gi = [ALL_PHASES.index(p) for p in phases]
    later_reads = set()
    for p in ALL_PHASES:
        if ALL_PHASES.index(p) > max(gi) and p not in phases:
            later_reads.update(PHASE_IO[p][0])
    kinds = {}
    for name in set(reads) | set(writes):
        if name in writes:
            kinds[name] = ("ExternalOutput"
                           if (name in later_reads or TENSORS[name][2] == "out")
                           else "Internal")
        else:
            kinds[name] = "ExternalInput"
    d = {}
    for name in sorted(kinds):
        k = kinds[name]
        shape, dt, _ = TENSORS[name]
        d[name] = nc.dram_tensor(name, shape, dt, kind=k).ap()
    with ctx.stack:
        for p in phases:
            with nc.cleanup_on_exit():
                EMIT[p](ctx, d)
    ext_in = sorted(n for n, k in kinds.items() if k == "ExternalInput")
    ext_out = sorted(n for n, k in kinds.items() if k == "ExternalOutput")
    return nc, ext_in, ext_out


def prep_inputs(inputs):
    common = {
        "qkvw": np.ascontiguousarray(inputs["L0_attn_qkv_w"].reshape(DM, NH * 192)),
        "qkvb": np.ascontiguousarray(inputs["L0_attn_qkv_b"]),
        "outw": np.ascontiguousarray(inputs["L0_attn_out_w"].reshape(DM, DM)),
        "outb": np.ascontiguousarray(inputs["L0_attn_out_b"]),
        "rw": np.ascontiguousarray(inputs["L0_ffn_router_w"]),
        "eb": np.ascontiguousarray(inputs["L0_ffn_experts_bias"]),
        "w1c": np.ascontiguousarray(np.concatenate([inputs["L0_ffn_w1"], inputs["L0_ffn_sw1"]], axis=0)),
        "w3c": np.ascontiguousarray(np.concatenate([inputs["L0_ffn_w3"], inputs["L0_ffn_sw3"]], axis=0)),
        "w2c": np.ascontiguousarray(np.concatenate([inputs["L0_ffn_w2"], inputs["L0_ffn_sw2"]], axis=0)),
        "wq": np.ascontiguousarray(inputs["L1_attn_Wq"]),
        "wk": np.ascontiguousarray(inputs["L1_attn_Wk"]),
        "wv": np.ascontiguousarray(inputs["L1_attn_Wv"]),
        "wu": np.ascontiguousarray(inputs["L1_attn_Wu"]),
        "bu": np.ascontiguousarray(inputs["L1_attn_bu"]),
        "wg": np.ascontiguousarray(inputs["L1_ffn_w_gate"]),
        "wup": np.ascontiguousarray(inputs["L1_ffn_w_up"]),
        "wd": np.ascontiguousarray(inputs["L1_ffn_w_down"]),
    }
    pools = []
    for c in range(NCORE):
        p = dict(common)
        p["x0"] = np.ascontiguousarray(np.asarray(inputs["L0_attn_x"])[c])
        pools.append(p)
    return pools


_BUILD_CACHE = {}


def kernel(**inputs):
    pools = prep_inputs(inputs)
    for phases in GROUPS:
        key = tuple(phases)
        if key not in _BUILD_CACHE:
            _BUILD_CACHE[key] = build_group(phases)
        nc, ext_in, ext_out = _BUILD_CACHE[key]
        in_maps = [{n: pools[c][n] for n in ext_in} for c in range(NCORE)]
        res = run_bass_kernel_spmd(nc, in_maps, core_ids=list(range(NCORE)))
        for c in range(NCORE):
            for n in ext_out:
                pools[c][n] = res.results[c][n]
    return np.stack([pools[c]["y"] for c in range(NCORE)])


# version 09
# speedup vs baseline: 1.5371x; 1.1383x over previous
import contextlib
import numpy as np
import concourse.bass as bass
import concourse.mybir as mybir
from concourse.bass_utils import run_bass_kernel_spmd

F32 = mybir.dt.float32
BF16 = mybir.dt.bfloat16
I32 = mybir.dt.int32
AF = mybir.ActivationFunctionType
ALU = mybir.AluOpType
AX = mybir.AxisListType

T = 4096
DM = 1024
NH, HD = 16, 64
NCORE = 8
NEG = -1.0e38

TENSORS = {
    "x0":    ([T, DM], F32, "inp"),
    "qkvw":  ([DM, NH * 192], F32, "inp"),
    "qkvb":  ([NH, 192], F32, "inp"),
    "outw":  ([DM, DM], F32, "inp"),
    "outb":  ([DM], F32, "inp"),
    "rw":    ([8, DM], F32, "inp"),
    "eb":    ([8], F32, "inp"),
    "w1c":   ([9, DM, 2048], F32, "inp"),
    "w3c":   ([9, DM, 2048], F32, "inp"),
    "w2c":   ([9, 2048, DM], F32, "inp"),
    "wq":    ([HD, HD], F32, "inp"),
    "wk":    ([HD, HD], F32, "inp"),
    "wv":    ([HD, HD], F32, "inp"),
    "wu":    ([DM, DM], F32, "inp"),
    "bu":    ([DM], F32, "inp"),
    "wg":    ([3072, DM], I32, "inp"),
    "wup":   ([3072, DM], I32, "inp"),
    "wd":    ([DM, 3072], I32, "inp"),
    "qT":    ([NH, HD, T], BF16, "scr"),
    "kT":    ([NH, HD, T], BF16, "scr"),
    "vv":    ([NH, T, HD], BF16, "scr"),
    "attnT": ([DM, T], BF16, "scr"),
    "x1":    ([T, DM], F32, "scr"),
    "x2":    ([T, DM], F32, "scr"),
    "x3":    ([T, DM], F32, "scr"),
    "w1b":   ([9, DM, 2048], BF16, "scr"),
    "w3b":   ([9, DM, 2048], BF16, "scr"),
    "w2b":   ([9, 2048, DM], BF16, "scr"),
    "wgB":   ([3072, DM], BF16, "scr"),
    "wupB":  ([3072, DM], BF16, "scr"),
    "wdB":   ([DM, 3072], BF16, "scr"),
    "y":     ([T, DM], F32, "out"),
}

PHASE_IO = {
    "qkv0":    (["x0", "qkvw", "qkvb"], ["qT", "kT", "vv"]),
    "attn0":   (["qT", "kT", "vv", "w1c", "w3c", "w2c"], ["attnT", "w1b", "w3b", "w2b"]),
    "oproj0":  (["attnT", "outw", "outb", "x0"], ["x1"]),
    "moeprep": (["w1c", "w3c", "w2c"], ["w1b", "w3b", "w2b"]),
    "moe":     (["x1", "rw", "eb", "w1b", "w3b", "w2b"], ["x2"]),
    "qkv1":    (["x2", "wq", "wk", "wv"], ["qT", "kT", "vv"]),
    "attn1":   (["qT", "kT", "vv", "wg", "wup", "wd"], ["attnT", "wgB", "wupB", "wdB"]),
    "oproj1":  (["attnT", "wu", "bu", "x2"], ["x3"]),
    "ffnprep": (["wg", "wup", "wd"], ["wgB", "wupB", "wdB"]),
    "ffn":     (["x3", "wgB", "wupB", "wdB"], ["y"]),
}
ALL_PHASES = list(PHASE_IO)


class Ctx:
    def __init__(self, nc):
        self.nc = nc
        self.stack = contextlib.ExitStack()

    def sems(self, tag, *names):
        return [self.nc.alloc_semaphore(tag + "_" + n) for n in names]


def ph_qkv(ctx, d, layer):
    nc = ctx.nc
    tg = "q%d" % layer
    xin = d["x0"] if layer == 0 else d["x2"]
    es = contextlib.ExitStack()
    E = es.enter_context
    idf = E(nc.sbuf_tensor(tg + "idf", [128, 128], F32))
    epst = E(nc.sbuf_tensor(tg + "epst", [128, 1], F32))
    idb = E(nc.sbuf_tensor(tg + "idb", [128, 128], BF16))
    pbias = E(nc.psum_tensor(tg + "pbias", [128, 512], F32))
    if layer == 0:
        qkwb = E(nc.sbuf_tensor(tg + "qkwb", [128, 8, NH, 128], BF16))
        vwb = E(nc.sbuf_tensor(tg + "vwb", [128, 8, DM], BF16))
        qkb = E(nc.sbuf_tensor(tg + "qkb", [128, NH], F32))
        vb = E(nc.sbuf_tensor(tg + "vb", [128, DM], F32))
        wf = E(nc.sbuf_tensor(tg + "wf", [128, 2, 3072], F32))
        bsrc = E(nc.sbuf_tensor(tg + "bsrc", [16, 192], F32))
        vrow = E(nc.sbuf_tensor(tg + "vrow", [1, DM], F32))
        ones1 = E(nc.sbuf_tensor(tg + "ones1", [1, 128], F32))
    else:
        wsrc = E(nc.sbuf_tensor(tg + "wsrc", [64, 192], F32))
        wqkT = E(nc.sbuf_tensor(tg + "wqkT", [128, 128], BF16))
        wvT = E(nc.sbuf_tensor(tg + "wvT", [128, 64], BF16))
        vwb = E(nc.sbuf_tensor(tg + "vwb", [128, 8, DM], BF16))

    pl, pg, pa, pp, pbs, pvr, pdup, pvw = ctx.sems(tg, "pl", "pg", "pa", "pp", "pbs", "pvr", "pdup", "pvw")
    pw = ctx.sems(tg, "pw0", "pw1")

    with nc.Block(tg + "prep") as b:
        @b.gpsimd
        def _(g):
            g.memset(epst[:], 1.0e-5)
            g.memset(idf[:], 0.0).then_inc(pg, 1)
            g.wait_ge(pg, 1)
            g.affine_select(out=idf[:], in_=idf[:], compare_op=ALU.not_equal,
                            fill=1.0, base=0, pattern=[[-1, 128]],
                            channel_multiplier=1)
            g.memset(idb[:], 0.0).then_inc(pg, 1)
            g.wait_ge(pg, 2)
            g.affine_select(out=idb[:], in_=idb[:], compare_op=ALU.not_equal,
                            fill=1.0, base=0, pattern=[[-1, 128]],
                            channel_multiplier=1).then_inc(pg, 1)
            if layer == 0:
                g.memset(ones1[:], 1.0).then_inc(pg, 1)


        if layer == 0:
            @b.sync
            def _(s):
                s.dma_start(out=bsrc[:], in_=d["qkvb"]).then_inc(pbs, 16)
                s.dma_start(out=vrow[0:1, :].rearrange("o (h dd) -> o h dd", h=NH),
                            in_=d["qkvb"][None, :, 128:192]).then_inc(pvr, 16)
                qw = d["qkvw"].rearrange("(c p) n -> c p n", p=128)
                for k in range(8):
                    if k >= 2:
                        s.wait_ge(pa, k - 1)
                    s.dma_start(out=wf[:, k % 2, :], in_=qw[k]).then_inc(pw[k % 2], 16)

            @b.scalar
            def _(s):
                for k in range(8):
                    s.wait_ge(pw[k % 2], 16 * (k // 2 + 1))
                    wv_ = wf[:, k % 2, :].rearrange("p (h s) -> p h s", h=NH)
                    s.activation(out=qkwb[:, k, :, 0:64], in_=wv_[:, :, 0:64],
                                 func=AF.Copy, scale=0.125)
                    s.activation(out=qkwb[:, k, :, 64:128], in_=wv_[:, :, 64:128],
                                 func=AF.Copy)
                    s.activation(out=vwb[:, k, :].rearrange("p (h s) -> p h s", h=NH),
                                 in_=wv_[:, :, 128:192], func=AF.Copy).then_inc(pa, 1)
                s.wait_ge(pp, 1)
                s.activation(out=qkb[0:64, :], in_=pbias[0:64, 0:NH], func=AF.Copy,
                             scale=0.125)
                s.activation(out=qkb[64:128, :], in_=pbias[64:128, 0:NH],
                             func=AF.Copy).then_inc(pa, 1)

            @b.tensor
            def _(t):
                t.wait_ge(pbs, 16)
                t.wait_ge(pg, 3)
                t.transpose(pbias[:, 0:16], bsrc[:, 0:128], idf[0:16, 0:16]).then_inc(pp, 1)
        else:
            @b.sync
            def _(s):
                s.dma_start(out=wsrc[:, 0:64], in_=d["wq"]).then_inc(pl, 16)
                s.dma_start(out=wsrc[:, 64:128], in_=d["wk"]).then_inc(pl, 16)
                s.dma_start(out=wsrc[:, 128:192], in_=d["wv"]).then_inc(pl, 16)

            @b.tensor
            def _(t):
                t.wait_ge(pl, 48)
                t.wait_ge(pg, 3)
                t.transpose(pbias[0:64, 0:64], wsrc[:, 0:64], idf[0:64, 0:64])
                t.transpose(pbias[0:64, 64:128], wsrc[:, 64:128], idf[0:64, 0:64])
                t.transpose(pbias[0:64, 128:192], wsrc[:, 128:192], idf[0:64, 0:64]).then_inc(pp, 1)

            @b.scalar
            def _(s):
                s.wait_ge(pp, 1)
                sc = float(DM) ** (-0.25)
                s.activation(out=wqkT[0:64, 0:64], in_=pbias[0:64, 0:64], func=AF.Copy, scale=sc)
                s.activation(out=wqkT[0:64, 64:128], in_=pbias[0:64, 64:128], func=AF.Copy, scale=sc)
                s.activation(out=wvT[0:64, :], in_=pbias[0:64, 128:192], func=AF.Copy).then_inc(pa, 1)

            @b.gpsimd
            def _(g):
                g.memset(vwb[:], 0.0).then_inc(pvw, 1)
                g.wait_ge(pa, 1)
                g.dma_start(out=wqkT[64:128, :], in_=wqkT[0:64, :]).then_inc(pdup, 16)
                g.dma_start(out=wvT[64:128, :], in_=wvT[0:64, :]).then_inc(pdup, 16)

            @b.vector
            def _(v):
                v.wait_ge(pvw, 1)
                v.wait_ge(pdup, 32)
                for k in range(8):
                    v.tensor_copy(out=vwb[0:64, k, k * 128:k * 128 + 64],
                                  in_=wvT[0:64, :])
                    ii = v.tensor_copy(out=vwb[64:128, k, k * 128 + 64:k * 128 + 128],
                                       in_=wvT[64:128, :])
                ii.then_inc(pvw, 1)

    if layer == 0:
        with nc.Block(tg + "prep2") as b:
            pv2, = ctx.sems(tg, "pv2")

            @b.tensor
            def _(t):
                t.wait_ge(pvr, 16)
                t.matmul(pbias[:, 0:512], ones1[:], vrow[:, 0:512], start=True, stop=True).then_inc(pv2, 1)
                t.wait_ge(pv2, 2)
                t.matmul(pbias[:, 0:512], ones1[:], vrow[:, 512:1024], start=True, stop=True).then_inc(pv2, 1)

            @b.vector
            def _(v):
                v.wait_ge(pv2, 1)
                v.tensor_copy(out=vb[:, 0:512], in_=pbias[:, 0:512]).then_inc(pv2, 1)
                v.wait_ge(pv2, 3)
                v.tensor_copy(out=vb[:, 512:1024], in_=pbias[:, 0:512]).then_inc(pv2, 1)

    xt = E(nc.sbuf_tensor(tg + "xt", [128, 2, 4, DM], F32))
    bn6 = E(nc.sbuf_tensor(tg + "bn6", [128, 48], F32))
    bno = E(nc.sbuf_tensor(tg + "bno", [128, 8], F32))
    sd = E(nc.sbuf_tensor(tg + "sd", [128, 4], F32))
    rstd = E(nc.sbuf_tensor(tg + "rstd", [128, 4], F32))
    htok = E(nc.sbuf_tensor(tg + "htok", [128, 4, DM], BF16))
    hf = E(nc.sbuf_tensor(tg + "hf", [128, 8, 512], BF16))
    qksb = E(nc.sbuf_tensor(tg + "qksb", [128, 2, 512], BF16))
    vsb = E(nc.sbuf_tensor(tg + "vsb", [128, 2, NH, HD], BF16))
    pst = E(nc.psum_tensor(tg + "pst", [128, 512], BF16))
    pqk = E(nc.psum_tensor(tg + "pqk", [128, 2, 512], F32))
    pvv = E(nc.psum_tensor(tg + "pvv", [128, 2, 512], F32))

    sv1, sv2, sa1, sv3, sv4, sp1, sv5, sp2, sa2, sp3, sv6 = ctx.sems(
        tg, "sv1", "sv2", "sa1", "sv3", "sv4", "sp1", "sv5", "sp2", "sa2",
        "sp3", "sv6")
    sx = ctx.sems(tg, "sx0", "sx1")
    sqs = ctx.sems(tg, "sqs0", "sqs1")
    svs = ctx.sems(tg, "svs0", "svs1")

    NT = 8

    with nc.Block(tg + "main") as b:
        @b.sync
        def _(s):
            xr = xin.rearrange("(n c p) d -> n p c d", p=128, c=4)
            for i in range(NT):
                if i >= 2:
                    s.wait_ge(sv4, i - 1)
                s.dma_start(out=xt[:, i % 2], in_=xr[i]).then_inc(sx[i % 2], 16)

        @b.vector
        def _(v):
            for i in range(NT):
                v.wait_ge(sx[i % 2], 16 * (i // 2 + 1))
                for c in range(4):
                    ii = v.bn_stats(bn6[:, c * 12:c * 12 + 6], xt[:, i % 2, c, 0:512])
                    ii = v.bn_stats(bn6[:, c * 12 + 6:c * 12 + 12], xt[:, i % 2, c, 512:1024])
                ii.then_inc(sv1, 1)
                v.wait_ge(sv1, i + 1)
                for c in range(4):
                    ii = v.bn_aggr(bno[:, 2 * c:2 * c + 2], bn6[:, c * 12:c * 12 + 12])
                ii.then_inc(sv2, 1)
                v.wait_ge(sa1, i + 1)
                v.reciprocal(rstd[:], sd[:]).then_inc(sv3, 1)
                v.wait_ge(sv3, i + 1)
                for c in range(4):
                    ii = v.tensor_scalar(htok[:, c, :], xt[:, i % 2, c, :],
                                         bno[:, 2 * c:2 * c + 1], rstd[:, c:c + 1],
                                         op0=ALU.subtract, op1=ALU.mult)
                ii.then_inc(sv4, 1)
                for k in range(8):
                    v.wait_ge(sp1, 8 * i + k + 1)
                    v.tensor_copy(out=hf[:, k, :], in_=pst[:]).then_inc(sv5, 1)
                for h in range(NH):
                    v.wait_ge(sp2, 16 * i + h + 1)
                    if 16 * i + h >= 2:
                        v.wait_ge(sqs[h % 2], 32 * ((16 * i + h) // 2))
                    if layer == 0:
                        v.tensor_scalar(qksb[:, h % 2, :], pqk[:, h % 2, :],
                                        qkb[:, h:h + 1], None,
                                        op0=ALU.add).then_inc(sa2, 1)
                    else:
                        v.tensor_copy(out=qksb[:, h % 2, :],
                                      in_=pqk[:, h % 2, :]).then_inc(sa2, 1)
                for c in range(4):
                    if 4 * i + c >= 2:
                        v.wait_ge(svs[c % 2], 16 * ((4 * i + c) // 2))
                    for hh in range(2):
                        v.wait_ge(sp3, 8 * i + 2 * c + hh + 1)
                        dst = vsb[:, c % 2, hh * 8:(hh + 1) * 8, :].rearrange("p h dd -> p (h dd)")
                        if layer == 0:
                            ii = v.tensor_tensor(out=dst, in0=pvv[:, (2 * c + hh) % 2, :],
                                                 in1=vb[:, hh * 512:(hh + 1) * 512], op=ALU.add)
                        else:
                            ii = v.tensor_copy(out=dst, in_=pvv[:, (2 * c + hh) % 2, :])
                    ii.then_inc(sv6, 1)

        @b.scalar
        def _(s):
            vcol = bno[:].rearrange("p (c two) -> p c two", two=2)[:, :, 1:2].rearrange("p c o -> p (c o)")
            for i in range(NT):
                s.wait_ge(sv2, i + 1)
                s.activation(out=sd[:], in_=vcol, func=AF.Sqrt, bias=epst[:, 0:1]).then_inc(sa1, 1)

        @b.tensor
        def _(t):
            t.wait_ge(pa, 9 if layer == 0 else 1)
            if layer == 1:
                t.wait_ge(pdup, 32)
                t.wait_ge(pvw, 2)
            for i in range(NT):
                t.wait_ge(sv4, i + 1)
                for k in range(8):
                    if 8 * i + k >= 1:
                        t.wait_ge(sv5, 8 * i + k)
                    for c in range(4):
                        ii = t.transpose(pst[:, c * 128:(c + 1) * 128],
                                         htok[:, c, k * 128:(k + 1) * 128], idb[:])
                    ii.then_inc(sp1, 1)
                t.wait_ge(sv5, 8 * (i + 1))
                for h in range(NH):
                    if 16 * i + h >= 2:
                        t.wait_ge(sa2, 16 * i + h - 1)
                    if layer == 0:
                        for k in range(8):
                            ii = t.matmul(pqk[:, h % 2, :], qkwb[:, k, h, :], hf[:, k, :],
                                          start=(k == 0), stop=(k == 7))
                    else:
                        hr = (h % 2) * 64
                        ii = t.matmul(pqk[:, h % 2, :], wqkT[hr:hr + 64, :],
                                      hf[hr:hr + 64, h // 2, :], start=True, stop=True)
                    ii.then_inc(sp2, 1)
                for c in range(4):
                    for hh in range(2):
                        j = 8 * i + 2 * c + hh
                        if j >= 2:
                            t.wait_ge(sv6, 4 * i + c)
                        if layer == 0:
                            for k in range(8):
                                ii = t.matmul(pvv[:, (2 * c + hh) % 2, :],
                                              hf[:, k, c * 128:(c + 1) * 128],
                                              vwb[:, k, hh * 512:(hh + 1) * 512],
                                              start=(k == 0), stop=(k == 7))
                        else:
                            for k in range(8):
                                ii = t.matmul(pvv[:, (2 * c + hh) % 2, :],
                                              hf[:, k, c * 128:(c + 1) * 128],
                                              vwb[:, k, hh * 512:(hh + 1) * 512],
                                              start=(k == 0), stop=(k == 7))
                        ii.then_inc(sp3, 1)

        @b.gpsimd
        def _(g):
            vvr = d["vv"].rearrange("h (n p) dd -> n p h dd", p=128)
            for i in range(NT):
                for h in range(NH):
                    g.wait_ge(sa2, 16 * i + h + 1)
                    g.dma_start(out=d["qT"][h][:, i * 512:(i + 1) * 512],
                                in_=qksb[0:64, h % 2, :]).then_inc(sqs[h % 2], 16)
                    g.dma_start(out=d["kT"][h][:, i * 512:(i + 1) * 512],
                                in_=qksb[64:128, h % 2, :]).then_inc(sqs[h % 2], 16)
                for c in range(4):
                    g.wait_ge(sv6, 4 * i + c + 1)
                    g.dma_start(out=vvr[4 * i + c], in_=vsb[:, c % 2]).then_inc(svs[c % 2], 16)
            g.wait_ge(sqs[0], 32 * NT * NH // 2)
            g.wait_ge(sqs[1], 32 * NT * NH // 2)
            g.wait_ge(svs[0], 16 * 2 * NT)
            g.wait_ge(svs[1], 16 * 2 * NT)
    es.close()



def mk_moeprep_rider(ctx, d, E):
    nc = ctx.nc
    tg = "mr"
    wfm = E(nc.sbuf_tensor(tg + "wfm", [128, 3, 2048], F32))
    wbm = E(nc.sbuf_tensor(tg + "wbm", [128, 2, 2048], BF16))
    mc, = ctx.sems(tg, "mc")
    ml = ctx.sems(tg, "ml0", "ml1", "ml2")
    ms = ctx.sems(tg, "ms0", "ms1")
    units = []
    for name_s, name_d, nchunk, width in (("w1c", "w1b", 8, 2048),
                                          ("w3c", "w3b", 8, 2048),
                                          ("w2c", "w2b", 16, 1024)):
        for e in range(9):
            sv = d[name_s][e].rearrange("(c p) n -> c p n", p=128)
            dv = d[name_d][e].rearrange("(c p) n -> c p n", p=128)
            for rc in range(nchunk):
                units.append((sv[rc], dv[rc], width))
    NU = len(units)

    def scalar_gen(s):
        for j in range(2):
            s.dma_start(out=wfm[:, j % 3, 0:units[j][2]],
                        in_=units[j][0]).then_inc(ml[j % 3], 16)
        for j, (sv, dv, w) in enumerate(units):
            s.wait_ge(ml[j % 3], 16 * (j // 3 + 1))
            if j >= 2:
                s.wait_ge(ms[j % 2], 16 * (j // 2))
            s.activation(out=wbm[:, j % 2, 0:w], in_=wfm[:, j % 3, 0:w],
                         func=AF.Copy).then_inc(mc, 1)
            if j + 2 < NU:
                s.dma_start(out=wfm[:, (j + 2) % 3, 0:units[j + 2][2]],
                            in_=units[j + 2][0]).then_inc(ml[(j + 2) % 3], 16)
            yield

    def gp_gen(g):
        for j, (sv, dv, w) in enumerate(units):
            g.wait_ge(mc, j + 1)
            g.dma_start(out=dv, in_=wbm[:, j % 2, 0:w]).then_inc(ms[j % 2], 16)
            if j < NU - 1:
                yield
        g.wait_ge(ms[0], 16 * ((NU + 1) // 2))
        g.wait_ge(ms[1], 16 * (NU // 2))
        yield

    return {"scalar": scalar_gen, "gpsimd": gp_gen, "n": NU}


def mk_ffnprep_rider(ctx, d, E):
    nc = ctx.nc
    tg = "fr"
    wim = E(nc.sbuf_tensor(tg + "wim", [128, 3, 3072], I32))
    wbb = E(nc.sbuf_tensor(tg + "wbb", [128, 2, 3072], BF16))
    fc, = ctx.sems(tg, "fc")
    fl = ctx.sems(tg, "fl0", "fl1", "fl2")
    fs = ctx.sems(tg, "fs0", "fs1")
    units = []
    for name_s, name_d, nrc, W in (("wg", "wgB", 24, 1024),
                                   ("wup", "wupB", 24, 1024),
                                   ("wd", "wdB", 8, 3072)):
        sv = d[name_s].rearrange("(c p) n -> c p n", p=128)
        dv = d[name_d].rearrange("(c p) n -> c p n", p=128)
        for rc in range(nrc):
            units.append((sv[rc], dv[rc], W))
    NU = len(units)

    def scalar_gen(s):
        for j in range(2):
            s.dma_start(out=wim[:, j % 3, 0:units[j][2]],
                        in_=units[j][0]).then_inc(fl[j % 3], 16)
        for j, (sv, dv, w) in enumerate(units):
            s.wait_ge(fl[j % 3], 16 * (j // 3 + 1))
            if j >= 2:
                s.wait_ge(fs[j % 2], 16 * (j // 2))
            s.activation(out=wbb[:, j % 2, 0:w], in_=wim[:, j % 3, 0:w],
                         func=AF.Copy, scale=1.0 / 127.0).then_inc(fc, 1)
            if j + 2 < NU:
                s.dma_start(out=wim[:, (j + 2) % 3, 0:units[j + 2][2]],
                            in_=units[j + 2][0]).then_inc(fl[(j + 2) % 3], 16)
            yield

    def gp_gen(g):
        for j, (sv, dv, w) in enumerate(units):
            g.wait_ge(fc, j + 1)
            g.dma_start(out=dv, in_=wbb[:, j % 2, 0:w]).then_inc(fs[j % 2], 16)
            if j < NU - 1:
                yield
        g.wait_ge(fs[0], 16 * ((NU + 1) // 2))
        g.wait_ge(fs[1], 16 * (NU // 2))
        yield

    return {"scalar": scalar_gen, "gpsimd": gp_gen, "n": NU}


def ph_attn(ctx, d, layer, rider_mk=None):
    nc = ctx.nc
    tg = "a%d" % layer
    es = contextlib.ExitStack()
    E = es.enter_context
    rider = rider_mk(ctx, d, E) if rider_mk is not None else None

    def mk_pump(gen_factory, eng, total):
        if rider is None:
            return lambda frac: None
        gen = gen_factory(eng)
        state = [0]

        def pump(frac):
            target = min(total, int(frac * total) + 1)
            while state[0] < target:
                try:
                    next(gen)
                except StopIteration:
                    state[0] = total
                    return
                state[0] += 1
        return pump

    kbuf = E(nc.sbuf_tensor(tg + "kbuf", [65, 2, T], BF16))
    qbuf = E(nc.sbuf_tensor(tg + "qbuf", [65, 2, T], BF16))
    vab = E(nc.sbuf_tensor(tg + "vab", [128, 2, 32, 65], BF16))
    sq = E(nc.sbuf_tensor(tg + "sq", [64, T], BF16))
    nrm = E(nc.sbuf_tensor(tg + "nrm", [1, 2, T], F32))
    kn2 = E(nc.sbuf_tensor(tg + "kn2", [1, T], F32))
    km2 = E(nc.sbuf_tensor(tg + "km2", [1, 1], F32))
    kms = E(nc.sbuf_tensor(tg + "kms", [1, 1], F32))
    kmn = E(nc.sbuf_tensor(tg + "kmn", [1, 2, 1], F32))
    ones64 = E(nc.sbuf_tensor(tg + "ones64", [64, 1], BF16))
    PT = E(nc.sbuf_tensor(tg + "PT", [128, 32, 512], BF16))
    zrow = E(nc.sbuf_tensor(tg + "zrow", [1, 512], F32))
    zbsb = E(nc.sbuf_tensor(tg + "zbsb", [64, 512], F32))
    aosb = E(nc.sbuf_tensor(tg + "aosb", [64, 2, 512], BF16))
    binm = E(nc.sbuf_tensor(tg + "binm", [128, 4, 512], BF16))
    onesf = E(nc.sbuf_tensor(tg + "onesf", [1, 64], F32))
    ps = E(nc.psum_tensor(tg + "ps", [128, 3, 512], F32))
    po = E(nc.psum_tensor(tg + "po", [128, 2, 512], F32))
    pzb = E(nc.psum_tensor(tg + "pzb", [64, 512], F32))
    pn = E(nc.psum_tensor(tg + "pn", [1, 2, 512], F32))

    pg, = ctx.sems(tg, "pg")
    with nc.Block(tg + "prep") as b:
        @b.gpsimd
        def _(g):
            g.memset(onesf[:], 1.0)
            g.memset(ones64[:], 1.0)
            g.memset(kbuf[64:65, :, :], 1.0)
            g.memset(vab[:], 1.0)
            g.memset(binm[:], 1.0).then_inc(pg, 1)
            g.wait_ge(pg, 1)
            for j in range(4):
                g.affine_select(out=binm[:, j, :], in_=binm[:, j, :],
                                compare_op=ALU.is_ge, fill=0.0, base=-128 * j,
                                pattern=[[1, 512]], channel_multiplier=-1)

    units = [(h, g) for h in range(NH) for g in range(8)]
    NU = len(units)

    def K_of(g):
        return 4 * g + 4

    P1 = [0]
    for (h, g) in units:
        P1.append(P1[-1] + K_of(g))

    (an, pnm, vn, vq, mm, ae, vd, spo, vz, pz, vz2, va) = ctx.sems(
        tg, "an", "pnm", "vn", "vq", "mm", "ae", "vd", "spo", "vz", "pz",
        "vz2", "va")
    ld = ctx.sems(tg, "ld0", "ld1")
    gs = ctx.sems(tg, "gs0", "gs1")

    with nc.Block(tg + "main") as b:
        @b.sync
        def _(s):
            for h in range(NH):
                if h >= 2:
                    s.wait_ge(spo, 8 * (h - 1))
                s.dma_start(out=kbuf[0:64, h % 2, :], in_=d["kT"][h]).then_inc(ld[h % 2], 16)
                s.dma_start(out=qbuf[0:64, h % 2, :], in_=d["qT"][h]).then_inc(ld[h % 2], 16)
                s.dma_start(out=vab[:, h % 2, :, 0:64],
                            in_=d["vv"][h].rearrange("(kt p) dd -> p kt dd", p=128)).then_inc(ld[h % 2], 16)

        def norm_scalar_s1(s, h):
            s.wait_ge(ld[h % 2], 48 * (h // 2 + 1))
            if h >= 1:
                s.wait_ge(pnm, 16 * h)
            s.activation(out=sq[:], in_=kbuf[0:64, h % 2, :],
                         func=AF.Square).then_inc(an, 1)

        def norm_scalar_s23(s, h):
            s.wait_ge(vn, 18 * h + 9)
            s.activation(out=kms[:], in_=km2[:], func=AF.Sqrt).then_inc(an, 1)
            s.wait_ge(pnm, 16 * h + 8)
            s.activation(out=sq[:], in_=qbuf[0:64, h % 2, :],
                         func=AF.Square).then_inc(an, 1)

        def norm_scalar_s4(s, h):
            s.wait_ge(vn, 18 * h + 18)
            if h >= 2:
                s.wait_ge(vq, 8 * (h - 1))
            s.activation(out=nrm[0:1, h % 2, :], in_=kn2[0:1, :],
                         func=AF.Sqrt).then_inc(an, 1)

        def norm_pe_k(t, h):
            t.wait_ge(an, 4 * h + 1)
            for c in range(8):
                if c >= 2:
                    t.wait_ge(vn, 18 * h + c - 1)
                elif h >= 1:
                    t.wait_ge(vn, 18 * h)
                t.matmul(pn[0:1, c % 2, :], ones64[:],
                         sq[:, c * 512:(c + 1) * 512], start=True,
                         stop=True).then_inc(pnm, 1)

        def norm_pe_q(t, h):
            t.wait_ge(an, 4 * h + 3)
            for c in range(8):
                t.wait_ge(vn, 18 * h + c + 9 if c >= 2 else 18 * h + c + 7)
                t.matmul(pn[0:1, c % 2, :], ones64[:],
                         sq[:, c * 512:(c + 1) * 512], start=True,
                         stop=True).then_inc(pnm, 1)

        def norm_vector_k(v, h):
            if h >= 1:
                v.wait_ge(an, 4 * h)
            for c in range(8):
                v.wait_ge(pnm, 16 * h + c + 1)
                v.tensor_copy(out=kn2[0:1, c * 512:(c + 1) * 512],
                              in_=pn[0:1, c % 2, :]).then_inc(vn, 1)
            v.reduce_max(km2[:], kn2[:], axis=AX.X).then_inc(vn, 1)

        def norm_vector_kmn(v, h):
            v.wait_ge(an, 4 * h + 2)
            v.tensor_scalar(kmn[0:1, h % 2, :], kms[:], -1.0, None,
                            op0=ALU.mult).then_inc(vn, 1)

        def norm_vector_q(v, h):
            for c in range(8):
                v.wait_ge(pnm, 16 * h + 8 + c + 1)
                v.tensor_copy(out=kn2[0:1, c * 512:(c + 1) * 512],
                              in_=pn[0:1, c % 2, :]).then_inc(vn, 1)

        @b.tensor
        def _(t):
            norm_pe_k(t, 0)
            norm_pe_q(t, 0)
            for u, (h, g) in enumerate(units):
                K = K_of(g)
                t.wait_ge(vq, u + 1)
                for kt in range(K):
                    if P1[u] + kt >= 3:
                        t.wait_ge(ae, P1[u] + kt - 2)
                    t.matmul(ps[:, kt % 3, :], kbuf[0:65, h % 2, kt * 128:(kt + 1) * 128],
                             qbuf[0:65, h % 2, g * 512:(g + 1) * 512],
                             start=True, stop=True).then_inc(mm, 1)
                if u >= 1:
                    t.wait_ge(vz, u)
                    if u >= 2:
                        t.wait_ge(vz2, u - 1)
                    t.matmul(pzb[:], onesf[:], zrow[:], start=True, stop=True).then_inc(pz, 1)
                if u >= 2:
                    t.wait_ge(va, u - 1)
                for kt in range(K):
                    t.wait_ge(ae, P1[u] + kt + 1)
                    if kt >= K - 4:
                        t.wait_ge(vd, 4 * u + (kt - (K - 4)) + 1)
                    ii = t.matmul(po[0:65, u % 2, :], vab[:, h % 2, kt, :],
                                  PT[:, kt, :], start=(kt == 0), stop=(kt == K - 1))
                ii.then_inc(spo, 1)
                if g == 0 and h + 1 < NH:
                    norm_pe_k(t, h + 1)
                if g == 2 and h + 1 < NH:
                    norm_pe_q(t, h + 1)
            t.wait_ge(vz, NU)
            t.wait_ge(vz2, NU - 1)
            t.matmul(pzb[:], onesf[:], zrow[:], start=True, stop=True).then_inc(pz, 1)

        @b.vector
        def _(v):
            norm_vector_k(v, 0)
            norm_vector_kmn(v, 0)
            norm_vector_q(v, 0)

            def epilogue(u2):
                v.wait_ge(spo, u2 + 1)
                if u2 >= 1:
                    v.wait_ge(pz, u2)
                v.reciprocal(zrow[:], po[64:65, u2 % 2, :]).then_inc(vz, 1)
                v.wait_ge(pz, u2 + 1)
                v.tensor_copy(out=zbsb[:], in_=pzb[0:64, :]).then_inc(vz2, 1)
                if u2 >= 2:
                    v.wait_ge(gs[u2 % 2], 16 * (u2 // 2))
                v.tensor_tensor(out=aosb[:, u2 % 2, :], in0=po[0:64, u2 % 2, :], in1=zbsb[:],
                                op=ALU.mult).then_inc(va, 1)

            for u, (h, g) in enumerate(units):
                K = K_of(g)
                v.wait_ge(an, 4 * h + 4)
                if u >= 16:
                    v.wait_ge(spo, u - 15)
                v.tensor_scalar(qbuf[64:65, h % 2, g * 512:(g + 1) * 512],
                                nrm[0:1, h % 2, g * 512:(g + 1) * 512],
                                kmn[0:1, h % 2, 0:1], None,
                                op0=ALU.mult).then_inc(vq, 1)
                if u >= 1:
                    epilogue(u - 1)
                if h + 1 < NH:
                    if g == 0:
                        norm_vector_k(v, h + 1)
                    elif g == 1:
                        norm_vector_kmn(v, h + 1)
                    elif g == 2:
                        norm_vector_q(v, h + 1)
            epilogue(NU - 1)

        @b.scalar
        def _(s):
            cpump = mk_pump(rider["scalar"], s, rider["n"]) if rider else None
            norm_scalar_s1(s, 0)
            norm_scalar_s23(s, 0)
            norm_scalar_s4(s, 0)
            for u, (h, g) in enumerate(units):
                K = K_of(g)
                for kt in range(K):
                    s.wait_ge(mm, P1[u] + kt + 1)
                    s.activation(out=PT[:, kt, :], in_=ps[:, kt % 3, :],
                                 func=AF.Exp).then_inc(ae, 1)
                if cpump:
                    cpump((u + 1) / NU)
                if h + 1 < NH:
                    if g == 0:
                        norm_scalar_s1(s, h + 1)
                    elif g == 1:
                        norm_scalar_s23(s, h + 1)
                    elif g == 3:
                        norm_scalar_s4(s, h + 1)

        @b.gpsimd
        def _(g_):
            def store(u2):
                h2, g2 = units[u2]
                g_.wait_ge(va, u2 + 1)
                g_.dma_start(out=d["attnT"][h2 * 64:(h2 + 1) * 64, g2 * 512:(g2 + 1) * 512],
                             in_=aosb[:, u2 % 2, :]).then_inc(gs[u2 % 2], 16)
            gpump = mk_pump(rider["gpsimd"], g_, rider["n"]) if rider else None
            for u, (h, g) in enumerate(units):
                for j in range(4):
                    kt = K_of(g) - 4 + j
                    g_.wait_ge(ae, P1[u] + kt + 1)
                    g_.tensor_tensor(out=PT[:, kt, :], in0=PT[:, kt, :],
                                     in1=binm[:, j, :], op=ALU.mult).then_inc(vd, 1)
                if u >= 1:
                    store(u - 1)
                if gpump:
                    gpump((u + 1) / NU)
            store(NU - 1)
            g_.wait_ge(gs[0], 16 * NU // 2)
            g_.wait_ge(gs[1], 16 * NU // 2)
    es.close()


def ph_oproj(ctx, d, layer):
    nc = ctx.nc
    tg = "o%d" % layer
    es = contextlib.ExitStack()
    E = es.enter_context
    xin = d["x0"] if layer == 0 else d["x2"]
    xout = d["x1"] if layer == 0 else d["x3"]
    wsrc_name = "outw" if layer == 0 else "wu"
    bsrc_name = "outb" if layer == 0 else "bu"

    idb = E(nc.sbuf_tensor(tg + "idb", [128, 128], BF16))
    idf = E(nc.sbuf_tensor(tg + "idf", [128, 128], F32))
    owb = E(nc.sbuf_tensor(tg + "owb", [128, 8, DM], BF16))
    obt = E(nc.sbuf_tensor(tg + "obt", [128, 8], F32))
    wfo = E(nc.sbuf_tensor(tg + "wfo", [128, 2, DM], F32))
    bro = E(nc.sbuf_tensor(tg + "bro", [8, 128], F32))
    pprep = E(nc.psum_tensor(tg + "pprep", [128, 512], F32))
    pprepb = E(nc.psum_tensor(tg + "pprepb", [128, 512], BF16))

    pl, pg, pa, pp, pv, pp2, pv2, pbs = ctx.sems(tg, "pl", "pg", "pa", "pp", "pv", "pp2", "pv2", "pbs")
    pw = ctx.sems(tg, "pw0", "pw1")

    with nc.Block(tg + "prep") as b:
        @b.gpsimd
        def _(g):
            g.memset(idb[:], 0.0)
            g.memset(idf[:], 0.0).then_inc(pg, 1)
            g.wait_ge(pg, 1)
            g.affine_select(out=idb[:], in_=idb[:], compare_op=ALU.not_equal,
                            fill=1.0, base=0, pattern=[[-1, 128]],
                            channel_multiplier=1)
            g.affine_select(out=idf[:], in_=idf[:], compare_op=ALU.not_equal,
                            fill=1.0, base=0, pattern=[[-1, 128]],
                            channel_multiplier=1).then_inc(pg, 1)

        @b.sync
        def _(s):
            s.dma_start(out=bro[:], in_=d[bsrc_name].rearrange("(c k) -> c k", k=128)).then_inc(pbs, 16)
            wr = d[wsrc_name].rearrange("(c p) n -> c p n", p=128)
            for k in range(8):
                if k >= 2:
                    s.wait_ge(pa, k - 1)
                s.dma_start(out=wfo[:, k % 2, :], in_=wr[k]).then_inc(pw[k % 2], 16)

        if layer == 0:
            @b.scalar
            def _(s):
                for k in range(8):
                    s.wait_ge(pw[k % 2], 16 * (k // 2 + 1))
                    s.activation(out=owb[:, k, :], in_=wfo[:, k % 2, :],
                                 func=AF.Copy).then_inc(pa, 1)
        else:
            wtb = E(nc.sbuf_tensor(tg + "wtb", [128, 2, DM], BF16))

            @b.scalar
            def _(s):
                for k in range(8):
                    s.wait_ge(pw[k % 2], 16 * (k // 2 + 1))
                    if k >= 2:
                        s.wait_ge(pp, 2 * (k - 2) + 2)
                    s.activation(out=wtb[:, k % 2, :], in_=wfo[:, k % 2, :],
                                 func=AF.Copy).then_inc(pa, 1)

            @b.tensor
            def _(t):
                t.wait_ge(pg, 2)
                for k in range(8):
                    t.wait_ge(pa, k + 1)
                    for j in range(2):
                        gi = 2 * k + j
                        if gi >= 1:
                            t.wait_ge(pv, gi)
                        for c4 in range(4):
                            ii = t.transpose(pprepb[:, c4 * 128:(c4 + 1) * 128],
                                             wtb[:, k % 2, (4 * j + c4) * 128:(4 * j + c4 + 1) * 128],
                                             idb[:])
                        ii.then_inc(pp, 1)

            @b.vector
            def _(v):
                pr = pprepb[:].rearrange("p (c f) -> p c f", f=128)
                for k in range(8):
                    for j in range(2):
                        gi = 2 * k + j
                        v.wait_ge(pp, gi + 1)
                        v.tensor_copy(out=owb[:, 4 * j:4 * j + 4, k * 128:(k + 1) * 128],
                                      in_=pr).then_inc(pv, 1)

        @b.tensor
        def _(t):
            t.wait_ge(pbs, 16)
            t.wait_ge(pg, 2)
            t.transpose(pprep[:, 0:8], bro[:], idf[0:8, 0:8]).then_inc(pp2, 1)

        @b.vector
        def _(v):
            v.wait_ge(pp2, 1)
            v.tensor_copy(out=obt[:], in_=pprep[:, 0:8]).then_inc(pv2, 1)

    at = E(nc.sbuf_tensor(tg + "at", [128, 2, 8, 512], BF16))
    xti = E(nc.sbuf_tensor(tg + "xti", [128, 2, 4, DM], F32))
    xto = E(nc.sbuf_tensor(tg + "xto", [128, 2, 4, DM], F32))
    obsb = E(nc.sbuf_tensor(tg + "obsb", [128, 8, 512], BF16))
    ppj = E(nc.psum_tensor(tg + "ppj", [128, 2, 512], F32))
    ptr = E(nc.psum_tensor(tg + "ptr", [128, 512], BF16))

    pm, ve, pt2, vadd = ctx.sems(tg, "pm", "ve", "pt2", "vadd")
    lx = ctx.sems(tg, "lx0", "lx1")
    gso = ctx.sems(tg, "gso0", "gso1")
    NT = 8

    with nc.Block(tg + "main") as b:
        @b.sync
        def _(s):
            atr = d["attnT"].rearrange("(k p) t -> p k t", p=128)
            xr = xin.rearrange("(n c p) dd -> n p c dd", p=128, c=4)
            for i in range(NT):
                if i >= 2:
                    s.wait_ge(pm, 8 * (i - 1))
                s.dma_start(out=at[:, i % 2], in_=atr[:, :, i * 512:(i + 1) * 512]).then_inc(lx[i % 2], 16)
                if i >= 2:
                    s.wait_ge(vadd, 8 * (i - 1))
                s.dma_start(out=xti[:, i % 2], in_=xr[i]).then_inc(lx[i % 2], 16)

        @b.tensor
        def _(t):
            t.wait_ge(pa, 8 if layer == 0 else 0)
            if layer == 1:
                t.wait_ge(pv, 16)
            for i in range(NT):
                t.wait_ge(lx[i % 2], 32 * (i // 2 + 1))
                for mo in range(8):
                    j = 8 * i + mo
                    if j >= 2:
                        t.wait_ge(ve, j - 1)
                    for k in range(8):
                        ii = t.matmul(ppj[:, mo % 2, :], owb[:, k, mo * 128:(mo + 1) * 128],
                                      at[:, i % 2, k, :], start=(k == 0), stop=(k == 7))
                    ii.then_inc(pm, 1)
                for mo in range(8):
                    j = 8 * i + mo
                    t.wait_ge(ve, j + 1)
                    if j >= 1:
                        t.wait_ge(vadd, j)
                    for c in range(4):
                        ii = t.transpose(ptr[:, c * 128:(c + 1) * 128],
                                         obsb[:, mo, c * 128:(c + 1) * 128], idb[:])
                    ii.then_inc(pt2, 1)

        @b.vector
        def _(v):
            v.wait_ge(pv2, 1)
            for i in range(NT):
                for mo in range(8):
                    j = 8 * i + mo
                    v.wait_ge(pm, j + 1)
                    if i >= 1:
                        v.wait_ge(pt2, 8 * (i - 1) + mo + 1)
                    v.tensor_scalar(obsb[:, mo, :], ppj[:, mo % 2, :],
                                    obt[:, mo:mo + 1], None, op0=ALU.add).then_inc(ve, 1)
                v.wait_ge(lx[i % 2], 32 * (i // 2 + 1))
                if i >= 2:
                    v.wait_ge(gso[i % 2], 16 * (i // 2))
                for mo in range(8):
                    j = 8 * i + mo
                    v.wait_ge(pt2, j + 1)
                    for c in range(4):
                        ii = v.tensor_tensor(out=xto[:, i % 2, c, mo * 128:(mo + 1) * 128],
                                             in0=ptr[:, c * 128:(c + 1) * 128],
                                             in1=xti[:, i % 2, c, mo * 128:(mo + 1) * 128],
                                             op=ALU.add)
                    ii.then_inc(vadd, 1)

        @b.gpsimd
        def _(g):
            xo = xout.rearrange("(n c p) dd -> n p c dd", p=128, c=4)
            for i in range(NT):
                g.wait_ge(vadd, 8 * (i + 1))
                g.dma_start(out=xo[i], in_=xto[:, i % 2]).then_inc(gso[i % 2], 16)
            g.wait_ge(gso[0], 16 * NT // 2)
            g.wait_ge(gso[1], 16 * NT // 2)
    es.close()


def ph_moeprep(ctx, d):
    nc = ctx.nc
    tg = "mp"
    es = contextlib.ExitStack()
    E = es.enter_context
    wfm = E(nc.sbuf_tensor(tg + "wfm", [128, 3, 2048], F32))
    wbm = E(nc.sbuf_tensor(tg + "wbm", [128, 2, 2048], BF16))
    mc, = ctx.sems(tg, "mc")
    ml = ctx.sems(tg, "ml0", "ml1", "ml2")
    ms = ctx.sems(tg, "ms0", "ms1")

    units = []
    for name_s, name_d, nchunk, width in (("w1c", "w1b", 8, 2048),
                                          ("w3c", "w3b", 8, 2048),
                                          ("w2c", "w2b", 16, 1024)):
        for e in range(9):
            sv = d[name_s][e].rearrange("(c p) n -> c p n", p=128)
            dv = d[name_d][e].rearrange("(c p) n -> c p n", p=128)
            for rc in range(nchunk):
                units.append((sv[rc], dv[rc], width))
    NU = len(units)

    with nc.Block(tg + "main") as b:
        @b.sync
        def _(s):
            for j, (sv, dv, w) in enumerate(units):
                if j >= 2:
                    s.wait_ge(mc, j - 1)
                s.dma_start(out=wfm[:, j % 2, 0:w], in_=sv).then_inc(ml[j % 2], 16)

        @b.scalar
        def _(s):
            for j, (sv, dv, w) in enumerate(units):
                s.wait_ge(ml[j % 2], 16 * (j // 2 + 1))
                if j >= 2:
                    s.wait_ge(ms[j % 2], 16 * (j // 2))
                s.activation(out=wbm[:, j % 2, 0:w], in_=wfm[:, j % 2, 0:w],
                             func=AF.Copy).then_inc(mc, 1)

        @b.gpsimd
        def _(g):
            for j, (sv, dv, w) in enumerate(units):
                g.wait_ge(mc, j + 1)
                g.dma_start(out=dv, in_=wbm[:, j % 2, 0:w]).then_inc(ms[j % 2], 16)
            g.wait_ge(ms[0], 16 * ((NU + 1) // 2))
            g.wait_ge(ms[1], 16 * (NU // 2))
    es.close()


def ph_moe(ctx, d):
    nc = ctx.nc
    tg = "m"
    es = contextlib.ExitStack()
    E = es.enter_context
    idf = E(nc.sbuf_tensor(tg + "idf", [128, 128], F32))
    idb = E(nc.sbuf_tensor(tg + "idb", [128, 128], BF16))
    epst = E(nc.sbuf_tensor(tg + "epst", [128, 1], F32))
    ones1 = E(nc.sbuf_tensor(tg + "ones1", [1, 128], F32))
    rsrc = E(nc.sbuf_tensor(tg + "rsrc", [8, DM], F32))
    ebr = E(nc.sbuf_tensor(tg + "ebr", [1, 8], F32))
    rwT = E(nc.sbuf_tensor(tg + "rwT", [128, 8, 8], F32))
    ebb = E(nc.sbuf_tensor(tg + "ebb", [128, 8], F32))
    xt = E(nc.sbuf_tensor(tg + "xt", [128, 2, 4, DM], F32))
    bn6 = E(nc.sbuf_tensor(tg + "bn6", [128, 48], F32))
    bno = E(nc.sbuf_tensor(tg + "bno", [128, 8], F32))
    sd = E(nc.sbuf_tensor(tg + "sd", [128, 4], F32))
    rstd = E(nc.sbuf_tensor(tg + "rstd", [128, 4], F32))
    htf = E(nc.sbuf_tensor(tg + "htf", [128, 4, DM], F32))
    hf32 = E(nc.sbuf_tensor(tg + "hf32", [128, 8, 512], F32))
    hbf = E(nc.sbuf_tensor(tg + "hbf", [128, 8, 512], BF16))
    lgs = E(nc.sbuf_tensor(tg + "lgs", [128, 32], F32))
    m1 = E(nc.sbuf_tensor(tg + "m1", [128, 4], F32))
    m1n = E(nc.sbuf_tensor(tg + "m1n", [128, 4], F32))
    m2 = E(nc.sbuf_tensor(tg + "m2", [128, 4], F32))
    sel = E(nc.sbuf_tensor(tg + "sel", [128, 32], F32))
    msk = E(nc.sbuf_tensor(tg + "msk", [128, 32], F32))
    e1t = E(nc.sbuf_tensor(tg + "e1t", [128, 32], F32))
    wun = E(nc.sbuf_tensor(tg + "wun", [128, 32], F32))
    zz = E(nc.sbuf_tensor(tg + "zz", [128, 4], F32))
    zrr = E(nc.sbuf_tensor(tg + "zrr", [128, 4], F32))
    cwT = E(nc.sbuf_tensor(tg + "cwT", [128, 4, 8], F32))
    cwr = E(nc.sbuf_tensor(tg + "cwr", [8, 512], F32))
    cw0 = E(nc.sbuf_tensor(tg + "cw0", [1, 2, 512], F32))
    cwb = E(nc.sbuf_tensor(tg + "cwb", [128, 8, 512], BF16))
    w1s = E(nc.sbuf_tensor(tg + "w1s", [128, 2, 8, 512], BF16))
    w3s = E(nc.sbuf_tensor(tg + "w3s", [128, 2, 8, 512], BF16))
    w2s = E(nc.sbuf_tensor(tg + "w2s", [128, 16, 1024], BF16))
    sg = E(nc.sbuf_tensor(tg + "sg", [128, 2, 512], F32))
    eo1 = E(nc.sbuf_tensor(tg + "eo1", [128, 2, 512], BF16))
    eo = E(nc.sbuf_tensor(tg + "eo", [128, 16, 512], BF16))
    acc = E(nc.sbuf_tensor(tg + "acc", [128, 8, 512], F32))
    accb = E(nc.sbuf_tensor(tg + "accb", [128, 8, 512], BF16))
    pmisc = E(nc.psum_tensor(tg + "pmisc", [128, 512], F32))
    pg_ = E(nc.psum_tensor(tg + "pg", [128, 512], F32))
    pu_ = E(nc.psum_tensor(tg + "pu", [128, 512], F32))
    pp2 = E(nc.psum_tensor(tg + "pp2", [128, 2, 512], F32))
    pcwt = E(nc.psum_tensor(tg + "pcwt", [8, 512], F32))
    pcb = E(nc.psum_tensor(tg + "pcb", [128, 512], F32))
    pepi = E(nc.psum_tensor(tg + "pepi", [128, 512], BF16))

    EO = [8, 0, 1, 2, 3, 4, 5, 6, 7]

    (pg, mrw, meb, mpp, mvv, mv1, ma1, mv4, mp1, mv5, mhb, mpr, mvm, mae,
     mcw1, mpc, mvr, mpb, macw, mpg, mpu, mas, mve, mps, mpw, mva, mab,
     mpe, mvp, mself) = ctx.sems(
        tg, "pg", "mrw", "meb", "mpp", "mvv", "mv1", "ma1", "mv4", "mp1",
        "mv5", "mhb", "mpr", "mvm", "mae", "mcw1", "mpc", "mvr", "mpb",
        "macw", "mpg", "mpu", "mas", "mve", "mps", "mpw", "mva", "mab",
        "mpe", "mvp", "mself")
    mx = ctx.sems(tg, "mx0", "mx1")
    mw1 = ctx.sems(tg, "mw10", "mw11")
    mw3 = ctx.sems(tg, "mw30", "mw31")
    mw2, = ctx.sems(tg, "mw2")
    msx = ctx.sems(tg, "msx0", "msx1")
    mcw = ctx.sems(tg, "mcw0", "mcw1s")

    with nc.Block(tg + "prep") as b:
        @b.gpsimd
        def _(g):
            g.memset(epst[:], 1.0e-5)
            g.memset(ones1[:], 1.0)
            g.memset(idf[:], 0.0)
            g.memset(idb[:], 0.0).then_inc(pg, 1)
            g.wait_ge(pg, 1)
            g.affine_select(out=idf[:], in_=idf[:], compare_op=ALU.not_equal,
                            fill=1.0, base=0, pattern=[[-1, 128]],
                            channel_multiplier=1)
            g.affine_select(out=idb[:], in_=idb[:], compare_op=ALU.not_equal,
                            fill=1.0, base=0, pattern=[[-1, 128]],
                            channel_multiplier=1).then_inc(pg, 1)

        @b.sync
        def _(s):
            s.dma_start(out=rsrc[:], in_=d["rw"]).then_inc(mrw, 16)
            s.dma_start(out=ebr[:], in_=d["eb"][None, :]).then_inc(meb, 16)

        @b.tensor
        def _(t):
            t.wait_ge(mrw, 16)
            t.wait_ge(pg, 2)
            for k in range(8):
                if k >= 1:
                    t.wait_ge(mvv, k)
                t.transpose(pmisc[:, 0:8], rsrc[:, k * 128:(k + 1) * 128],
                            idf[0:8, 0:8]).then_inc(mpp, 1)
            t.wait_ge(meb, 16)
            t.wait_ge(mvv, 8)
            t.matmul(pmisc[:, 8:16], ones1[:], ebr[:], start=True,
                     stop=True).then_inc(mpp, 1)

        @b.vector
        def _(v):
            for k in range(8):
                v.wait_ge(mpp, k + 1)
                v.tensor_copy(out=rwT[:, k, :], in_=pmisc[:, 0:8]).then_inc(mvv, 1)
            v.wait_ge(mpp, 9)
            v.tensor_copy(out=ebb[:], in_=pmisc[:, 8:16]).then_inc(mvv, 1)

    NT = 8

    def IDX(tt, ei, mh):
        return 144 * tt + 16 * ei + mh

    def WIDX(tt, ei, mo):
        return 72 * tt + 8 * ei + mo

    with nc.Block(tg + "main") as b:
        @b.sync
        def _(s):
            xr = d["x1"].rearrange("(n c p) dd -> n p c dd", p=128, c=4)
            for tt in range(NT):
                if tt >= 2:
                    s.wait_ge(msx[tt % 2], 16 * (tt // 2))
                s.dma_start(out=xt[:, tt % 2], in_=xr[tt]).then_inc(mx[tt % 2], 16)
                for ei, e in enumerate(EO):
                    w1v = d["w1b"][e].rearrange("(c p) n -> p c n", p=128)
                    w3v = d["w3b"][e].rearrange("(c p) n -> p c n", p=128)
                    w2v = d["w2b"][e].rearrange("(c p) n -> p c n", p=128)
                    for q in range(4):
                        gate = 144 * tt + 16 * ei + 4 * q - 4
                        if gate >= 1:
                            s.wait_ge(mpg, gate)
                        s.dma_start(out=w1s[:, q % 2],
                                    in_=w1v[:, :, q * 512:(q + 1) * 512]).then_inc(mw1[q % 2], 16)
                        if gate >= 1:
                            s.wait_ge(mpu, gate)
                        s.dma_start(out=w3s[:, q % 2],
                                    in_=w3v[:, :, q * 512:(q + 1) * 512]).then_inc(mw3[q % 2], 16)
                    if 9 * tt + ei >= 1:
                        s.wait_ge(mpw, 72 * tt + 8 * ei)
                    s.dma_start(out=w2s[:], in_=w2v).then_inc(mw2, 16)

        @b.vector
        def _(v):
            ns = 0

            def selfw(ii):
                nonlocal ns
                ii.then_inc(mself, 1)
                ns += 1
                v.wait_ge(mself, ns)

            for tt in range(NT):
                v.wait_ge(mx[tt % 2], 16 * (tt // 2 + 1))
                for c in range(4):
                    ii = v.bn_stats(bn6[:, c * 12:c * 12 + 6], xt[:, tt % 2, c, 0:512])
                    ii = v.bn_stats(bn6[:, c * 12 + 6:c * 12 + 12], xt[:, tt % 2, c, 512:1024])
                selfw(ii)
                for c in range(4):
                    ii = v.bn_aggr(bno[:, 2 * c:2 * c + 2], bn6[:, c * 12:c * 12 + 12])
                ii.then_inc(mv1, 1)
                v.wait_ge(ma1, tt + 1)
                ii = v.reciprocal(rstd[:], sd[:])
                selfw(ii)
                if tt >= 1:
                    v.wait_ge(mp1, 8 * tt)
                for c in range(4):
                    ii = v.tensor_scalar(htf[:, c, :], xt[:, tt % 2, c, :],
                                         bno[:, 2 * c:2 * c + 1], rstd[:, c:c + 1],
                                         op0=ALU.subtract, op1=ALU.mult)
                ii.then_inc(mv4, 1)
                for k in range(8):
                    v.wait_ge(mp1, 8 * tt + k + 1)
                    v.tensor_copy(out=hf32[:, k, :], in_=pmisc[:]).then_inc(mv5, 1)
                for c in range(4):
                    v.wait_ge(mpr, 4 * tt + 4)
                    c8 = slice(c * 8, (c + 1) * 8)
                    ii = v.tensor_tensor(out=lgs[:, c8], in0=pmisc[:, 448 + c * 8:448 + (c + 1) * 8],
                                         in1=ebb[:], op=ALU.add)
                    selfw(ii)
                    ii = v.reduce_max(m1[:, c:c + 1], lgs[:, c8], axis=AX.X)
                    selfw(ii)
                    ii = v.tensor_scalar(sel[:, c8], lgs[:, c8], m1[:, c:c + 1], None,
                                         op0=ALU.is_ge)
                    selfw(ii)
                    ii = v.scalar_tensor_tensor(msk[:, c8], sel[:, c8], -1.0e30,
                                                lgs[:, c8], op0=ALU.mult, op1=ALU.add)
                    selfw(ii)
                    ii = v.reduce_max(m2[:, c:c + 1], msk[:, c8], axis=AX.X)
                    selfw(ii)
                    ii = v.tensor_scalar(sel[:, c8], lgs[:, c8], m2[:, c:c + 1], None,
                                         op0=ALU.is_ge)
                    selfw(ii)
                    v.tensor_scalar(m1n[:, c:c + 1], m1[:, c:c + 1], -1.0, None,
                                    op0=ALU.mult).then_inc(mvm, 1)
                    v.wait_ge(mae, 4 * tt + c + 1)
                    ii = v.tensor_tensor(out=wun[:, c8], in0=sel[:, c8], in1=e1t[:, c8],
                                         op=ALU.mult)
                    selfw(ii)
                    ii = v.reduce_sum(zz[:, c:c + 1], wun[:, c8], axis=AX.X)
                    selfw(ii)
                    ii = v.reciprocal(zrr[:, c:c + 1], zz[:, c:c + 1])
                    selfw(ii)
                    v.tensor_scalar(cwT[:, c, :], wun[:, c8], zrr[:, c:c + 1], None,
                                    op0=ALU.mult).then_inc(mcw1, 1)
                v.wait_ge(mpc, tt + 1)
                v.tensor_copy(out=cwr[:], in_=pcwt[:]).then_inc(mvr, 1)
                for ei, e in enumerate(EO):
                    for mh in range(16):
                        ix = IDX(tt, ei, mh)
                        if ei == 0:
                            v.wait_ge(mas, ix + 1)
                            v.wait_ge(mpu, ix + 1)
                            if tt >= 1:
                                v.wait_ge(mpw, 72 * tt)
                            v.tensor_tensor(out=eo[:, mh, :], in0=pu_[:],
                                            in1=sg[:, mh % 2, :], op=ALU.mult).then_inc(mve, 1)
                        else:
                            v.wait_ge(mas, ix + 1)
                            v.wait_ge(mpu, ix + 1)
                            six = 128 * tt + 16 * (ei - 1) + mh
                            if six >= 2:
                                v.wait_ge(mps, six - 1)
                            v.tensor_tensor(out=eo1[:, mh % 2, :], in0=pu_[:],
                                            in1=sg[:, mh % 2, :], op=ALU.mult).then_inc(mve, 1)
                    for mo in range(8):
                        wx = WIDX(tt, ei, mo)
                        v.wait_ge(mpw, wx + 1)
                        if ei == 0:
                            if tt >= 1:
                                v.wait_ge(mab, tt)
                            v.tensor_copy(out=acc[:, mo, :], in_=pp2[:, mo % 2, :]).then_inc(mva, 1)
                        else:
                            v.wait_ge(mva, WIDX(tt, ei - 1, mo) + 1)
                            v.tensor_tensor(out=acc[:, mo, :], in0=pp2[:, mo % 2, :],
                                            in1=acc[:, mo, :], op=ALU.add).then_inc(mva, 1)
                for mo in range(8):
                    v.wait_ge(mpe, 8 * tt + mo + 1)
                    for c in range(4):
                        ii = v.tensor_tensor(out=xt[:, tt % 2, c, mo * 128:(mo + 1) * 128],
                                             in0=pepi[:, c * 128:(c + 1) * 128],
                                             in1=xt[:, tt % 2, c, mo * 128:(mo + 1) * 128],
                                             op=ALU.add)
                    ii.then_inc(mvp, 1)

        @b.scalar
        def _(s):
            vcol = bno[:].rearrange("p (c two) -> p c two", two=2)[:, :, 1:2].rearrange("p c o -> p (c o)")
            for tt in range(NT):
                s.wait_ge(mv1, tt + 1)
                s.activation(out=sd[:], in_=vcol, func=AF.Sqrt, bias=epst[:, 0:1]).then_inc(ma1, 1)
                s.wait_ge(mv5, 8 * (tt + 1))
                s.activation(out=hbf[:], in_=hf32[:], func=AF.Copy).then_inc(mhb, 1)
                for c in range(4):
                    s.wait_ge(mvm, 4 * tt + c + 1)
                    s.activation(out=e1t[:, c * 8:(c + 1) * 8], in_=lgs[:, c * 8:(c + 1) * 8],
                                 func=AF.Exp, bias=m1n[:, c:c + 1]).then_inc(mae, 1)
                for e in range(8):
                    s.wait_ge(mpb, 8 * tt + e + 1)
                    if tt >= 1:
                        s.wait_ge(mps, 128 * (tt - 1) + 16 * (e + 1))
                    s.activation(out=cwb[:, e, :], in_=pcb[:], func=AF.Copy).then_inc(macw, 1)
                for ei, e in enumerate(EO):
                    for mh in range(16):
                        ix = IDX(tt, ei, mh)
                        s.wait_ge(mpg, ix + 1)
                        if ix >= 2:
                            s.wait_ge(mve, ix - 1)
                        s.activation(out=sg[:, mh % 2, :], in_=pg_[:],
                                     func=AF.Silu).then_inc(mas, 1)
                s.wait_ge(mva, 72 * (tt + 1))
                if tt >= 1:
                    s.wait_ge(mpe, 8 * tt)
                s.activation(out=accb[:], in_=acc[:], func=AF.Copy).then_inc(mab, 1)

        @b.tensor
        def _(t):
            t.wait_ge(mvv, 9)
            t.wait_ge(pg, 2)
            for tt in range(NT):
                t.wait_ge(mv4, tt + 1)
                if tt >= 1:
                    t.wait_ge(mcw1, 4 * tt)
                for k in range(8):
                    if 8 * tt + k >= 1:
                        t.wait_ge(mv5, 8 * tt + k)
                    for c in range(4):
                        ii = t.transpose(pmisc[:, c * 128:(c + 1) * 128],
                                         htf[:, c, k * 128:(k + 1) * 128], idf[:])
                    ii.then_inc(mp1, 1)
                t.wait_ge(mv5, 8 * (tt + 1))
                for c in range(4):
                    for k in range(8):
                        ii = t.matmul(pmisc[:, 448 + c * 8:448 + (c + 1) * 8],
                                      hf32[:, k, c * 128:(c + 1) * 128], rwT[:, k, :],
                                      start=(k == 0), stop=(k == 7))
                    ii.then_inc(mpr, 1)
                t.wait_ge(mcw1, 4 * (tt + 1))
                if tt >= 1:
                    t.wait_ge(mvr, tt)
                for c in range(4):
                    ii = t.transpose(pcwt[:, c * 128:(c + 1) * 128], cwT[:, c, :],
                                     idf[0:128, 0:128])
                ii.then_inc(mpc, 1)
                for e in range(8):
                    t.wait_ge(mcw[e % 2], 16 * ((8 * tt + e) // 2 + 1))
                    if 8 * tt + e >= 1:
                        t.wait_ge(macw, 8 * tt + e)
                    t.matmul(pcb[:], ones1[:], cw0[0:1, e % 2, :], start=True,
                             stop=True).then_inc(mpb, 1)
                t.wait_ge(mhb, tt + 1)
                for ei, e in enumerate(EO):
                    for q in range(4):
                        t.wait_ge(mw1[q % 2], 16 * (2 * (9 * tt + ei) + q // 2 + 1))
                        t.wait_ge(mw3[q % 2], 16 * (2 * (9 * tt + ei) + q // 2 + 1))
                        for mhl in range(4):
                            mh = 4 * q + mhl
                            ix = IDX(tt, ei, mh)
                            if ix >= 1:
                                t.wait_ge(mas, ix)
                            for k in range(8):
                                ii = t.matmul(pg_[:], w1s[:, q % 2, k, mhl * 128:(mhl + 1) * 128],
                                              hbf[:, k, :], start=(k == 0), stop=(k == 7))
                            ii.then_inc(mpg, 1)
                            if ix >= 1:
                                t.wait_ge(mve, ix)
                            for k in range(8):
                                ii = t.matmul(pu_[:], w3s[:, q % 2, k, mhl * 128:(mhl + 1) * 128],
                                              hbf[:, k, :], start=(k == 0), stop=(k == 7))
                            ii.then_inc(mpu, 1)
                    t.wait_ge(mw2, 16 * (9 * tt + ei + 1))
                    if ei == 0:
                        t.wait_ge(mve, 144 * tt + 16)
                    else:
                        t.wait_ge(mps, 128 * tt + 16 * ei)
                    for mo in range(8):
                        wx = WIDX(tt, ei, mo)
                        if wx >= 2:
                            t.wait_ge(mva, wx - 1)
                        for mh in range(16):
                            ii = t.matmul(pp2[:, mo % 2, :], w2s[:, mh, mo * 128:(mo + 1) * 128],
                                          eo[:, mh, :], start=(mh == 0), stop=(mh == 15))
                        ii.then_inc(mpw, 1)
                t.wait_ge(mab, tt + 1)
                for mo in range(8):
                    if 8 * tt + mo >= 1:
                        t.wait_ge(mvp, 8 * tt + mo)
                    for c in range(4):
                        ii = t.transpose(pepi[:, c * 128:(c + 1) * 128],
                                         accb[:, mo, c * 128:(c + 1) * 128], idb[:])
                    ii.then_inc(mpe, 1)

        @b.gpsimd
        def _(g):
            xo = d["x2"].rearrange("(n c p) dd -> n p c dd", p=128, c=4)
            for tt in range(NT):
                g.wait_ge(mvr, tt + 1)
                for e in range(8):
                    if 8 * tt + e >= 2:
                        g.wait_ge(mpb, 8 * tt + e - 1)
                    g.dma_start(out=cw0[0:1, e % 2, :], in_=cwr[e:e + 1, :]).then_inc(mcw[e % 2], 16)
                for ei, e in enumerate(EO):
                    if ei == 0:
                        continue
                    for mh in range(16):
                        ix = IDX(tt, ei, mh)
                        six = 128 * tt + 16 * (ei - 1) + mh
                        g.wait_ge(mve, ix + 1)
                        g.wait_ge(macw, 8 * tt + ei)
                        g.wait_ge(mpw, 72 * tt + 8 * ei)
                        g.tensor_tensor(out=eo[:, mh, :], in0=eo1[:, mh % 2, :],
                                        in1=cwb[:, e, :], op=ALU.mult).then_inc(mps, 1)
                g.wait_ge(mvp, 8 * (tt + 1))
                g.dma_start(out=xo[tt], in_=xt[:, tt % 2]).then_inc(msx[tt % 2], 16)
            g.wait_ge(msx[0], 16 * (NT // 2))
            g.wait_ge(msx[1], 16 * (NT // 2))
    es.close()


def ph_ffnprep(ctx, d):
    nc = ctx.nc
    tg = "fp"
    es = contextlib.ExitStack()
    E = es.enter_context
    wim = E(nc.sbuf_tensor(tg + "wim", [128, 2, 3072], I32))
    wbb = E(nc.sbuf_tensor(tg + "wbb", [128, 2, 3072], BF16))
    fc, = ctx.sems(tg, "fc")
    fl = ctx.sems(tg, "fl0", "fl1")
    fs = ctx.sems(tg, "fs0", "fs1")

    units = []
    for name_s, name_d, nrc, W in (("wg", "wgB", 24, 1024),
                                   ("wup", "wupB", 24, 1024),
                                   ("wd", "wdB", 8, 3072)):
        sv = d[name_s].rearrange("(c p) n -> c p n", p=128)
        dv = d[name_d].rearrange("(c p) n -> c p n", p=128)
        for rc in range(nrc):
            units.append((sv[rc], dv[rc], W))
    NU = len(units)

    with nc.Block(tg + "main") as b:
        @b.sync
        def _(s):
            for j, (sv, dv, W) in enumerate(units):
                if j >= 2:
                    s.wait_ge(fc, j - 1)
                s.dma_start(out=wim[:, j % 2, 0:W], in_=sv).then_inc(fl[j % 2], 16)

        @b.scalar
        def _(s):
            for j, (sv, dv, W) in enumerate(units):
                s.wait_ge(fl[j % 2], 16 * (j // 2 + 1))
                if j >= 2:
                    s.wait_ge(fs[j % 2], 16 * (j // 2))
                s.activation(out=wbb[:, j % 2, 0:W], in_=wim[:, j % 2, 0:W],
                             func=AF.Copy, scale=1.0 / 127.0).then_inc(fc, 1)

        @b.gpsimd
        def _(g):
            for j, (sv, dv, W) in enumerate(units):
                g.wait_ge(fc, j + 1)
                g.dma_start(out=dv, in_=wbb[:, j % 2, 0:W]).then_inc(fs[j % 2], 16)
            g.wait_ge(fs[0], 16 * ((NU + 1) // 2))
            g.wait_ge(fs[1], 16 * (NU // 2))
    es.close()


def ph_ffn(ctx, d):
    nc = ctx.nc
    tg = "f"
    es = contextlib.ExitStack()
    E = es.enter_context
    idb = E(nc.sbuf_tensor(tg + "idb", [128, 128], BF16))
    epst = E(nc.sbuf_tensor(tg + "epst", [128, 1], F32))
    wgs = E(nc.sbuf_tensor(tg + "wgs", [128, 8, 3072], BF16))
    wus = E(nc.sbuf_tensor(tg + "wus", [128, 8, 3072], BF16))
    wds = E(nc.sbuf_tensor(tg + "wds", [128, 24, 1024], BF16))
    xt = E(nc.sbuf_tensor(tg + "xt", [128, 4, DM], F32))
    bn6 = E(nc.sbuf_tensor(tg + "bn6", [128, 48], F32))
    bno = E(nc.sbuf_tensor(tg + "bno", [128, 8], F32))
    sd = E(nc.sbuf_tensor(tg + "sd", [128, 4], F32))
    rstd = E(nc.sbuf_tensor(tg + "rstd", [128, 4], F32))
    htok = E(nc.sbuf_tensor(tg + "htok", [128, 4, DM], BF16))
    hbf = E(nc.sbuf_tensor(tg + "hbf", [128, 8, 512], BF16))
    sg = E(nc.sbuf_tensor(tg + "sg", [128, 1, 512], F32))
    h3 = E(nc.sbuf_tensor(tg + "h3", [128, 24, 512], BF16))
    obsb = E(nc.sbuf_tensor(tg + "obsb", [128, 4, 512], BF16))
    pst = E(nc.psum_tensor(tg + "pst", [128, 512], BF16))
    pg_ = E(nc.psum_tensor(tg + "pg", [128, 512], F32))
    pu_ = E(nc.psum_tensor(tg + "pu", [128, 512], F32))
    pp2 = E(nc.psum_tensor(tg + "pp2", [128, 2, 512], F32))
    pepi = E(nc.psum_tensor(tg + "pepi", [128, 512], BF16))

    (pg, fv1, fa1, fv4, fp1, fv5, fpg, fpu, fas, fve, fpw, fae, fpe, fvp,
     fself) = ctx.sems(tg, "pg", "fv1", "fa1", "fv4", "fp1", "fv5", "fpg",
                       "fpu", "fas", "fve", "fpw", "fae", "fpe", "fvp", "fself")
    fw, = ctx.sems(tg, "fw")
    fx = ctx.sems(tg, "fx0", "fx1")
    fsx = ctx.sems(tg, "fsx0", "fsx1")

    with nc.Block(tg + "prep") as b:
        @b.gpsimd
        def _(g):
            g.memset(epst[:], 1.0e-5)
            g.memset(idb[:], 0.0).then_inc(pg, 1)
            g.wait_ge(pg, 1)
            g.affine_select(out=idb[:], in_=idb[:], compare_op=ALU.not_equal,
                            fill=1.0, base=0, pattern=[[-1, 128]],
                            channel_multiplier=1).then_inc(pg, 1)

        @b.sync
        def _(s):
            for c in range(8):
                s.dma_start(out=wgs[:, c, :], in_=d["wgB"][:, c * 128:(c + 1) * 128],
                            transpose=True).then_inc(fw, 16)
                s.dma_start(out=wus[:, c, :], in_=d["wupB"][:, c * 128:(c + 1) * 128],
                            transpose=True).then_inc(fw, 16)
            for m in range(24):
                s.dma_start(out=wds[:, m, :], in_=d["wdB"][:, m * 128:(m + 1) * 128],
                            transpose=True).then_inc(fw, 16)

    NT = 8
    with nc.Block(tg + "main") as b:
        @b.sync
        def _(s):
            xr = d["x3"].rearrange("(n c p) dd -> n p c dd", p=128, c=4)
            for tt in range(NT):
                if tt >= 1:
                    s.wait_ge(fsx[(tt - 1) % 2], 16 * ((tt - 1) // 2 + 1))
                s.dma_start(out=xt[:], in_=xr[tt]).then_inc(fx[tt % 2], 16)

        @b.vector
        def _(v):
            ns = 0

            def selfw(ii):
                nonlocal ns
                ii.then_inc(fself, 1)
                ns += 1
                v.wait_ge(fself, ns)

            for tt in range(NT):
                v.wait_ge(fx[tt % 2], 16 * (tt // 2 + 1))
                if tt >= 1:
                    v.wait_ge(fvp, 8 * tt)
                for c in range(4):
                    ii = v.bn_stats(bn6[:, c * 12:c * 12 + 6], xt[:, c, 0:512])
                    ii = v.bn_stats(bn6[:, c * 12 + 6:c * 12 + 12], xt[:, c, 512:1024])
                selfw(ii)
                for c in range(4):
                    ii = v.bn_aggr(bno[:, 2 * c:2 * c + 2], bn6[:, c * 12:c * 12 + 12])
                ii.then_inc(fv1, 1)
                v.wait_ge(fa1, tt + 1)
                ii = v.reciprocal(rstd[:], sd[:])
                selfw(ii)
                if tt >= 1:
                    v.wait_ge(fp1, 8 * tt)
                for c in range(4):
                    ii = v.tensor_scalar(htok[:, c, :], xt[:, c, :],
                                         bno[:, 2 * c:2 * c + 1], rstd[:, c:c + 1],
                                         op0=ALU.subtract, op1=ALU.mult)
                ii.then_inc(fv4, 1)
                for k in range(8):
                    v.wait_ge(fp1, 8 * tt + k + 1)
                    v.tensor_copy(out=hbf[:, k, :], in_=pst[:]).then_inc(fv5, 1)
                for mh in range(24):
                    ix = 24 * tt + mh
                    v.wait_ge(fas, ix + 1)
                    v.wait_ge(fpu, ix + 1)
                    if tt >= 1:
                        v.wait_ge(fpw, 8 * tt)
                    v.tensor_tensor(out=h3[:, mh, :], in0=pu_[:], in1=sg[:, 0, :],
                                    op=ALU.mult).then_inc(fve, 1)
                for mo in range(8):
                    v.wait_ge(fpe, 8 * tt + mo + 1)
                    for c in range(4):
                        ii = v.tensor_tensor(out=xt[:, c, mo * 128:(mo + 1) * 128],
                                             in0=pepi[:, c * 128:(c + 1) * 128],
                                             in1=xt[:, c, mo * 128:(mo + 1) * 128],
                                             op=ALU.add)
                    ii.then_inc(fvp, 1)

        @b.scalar
        def _(s):
            vcol = bno[:].rearrange("p (c two) -> p c two", two=2)[:, :, 1:2].rearrange("p c o -> p (c o)")
            for tt in range(NT):
                s.wait_ge(fv1, tt + 1)
                s.activation(out=sd[:], in_=vcol, func=AF.Sqrt, bias=epst[:, 0:1]).then_inc(fa1, 1)
                for mh in range(24):
                    ix = 24 * tt + mh
                    s.wait_ge(fpg, ix + 1)
                    if ix >= 1:
                        s.wait_ge(fve, ix)
                    s.activation(out=sg[:, 0, :], in_=pg_[:],
                                 func=AF.Silu).then_inc(fas, 1)
                for mo in range(8):
                    j = 8 * tt + mo
                    s.wait_ge(fpw, j + 1)
                    if j >= 4:
                        s.wait_ge(fpe, j - 3)
                    s.activation(out=obsb[:, mo % 4, :], in_=pp2[:, mo % 2, :],
                                 func=AF.Copy).then_inc(fae, 1)

        @b.tensor
        def _(t):
            t.wait_ge(pg, 2)
            t.wait_ge(fw, 16 * 40)
            for tt in range(NT):
                t.wait_ge(fv4, tt + 1)
                for k in range(8):
                    if 8 * tt + k >= 1:
                        t.wait_ge(fv5, 8 * tt + k)
                    for c in range(4):
                        ii = t.transpose(pst[:, c * 128:(c + 1) * 128],
                                         htok[:, c, k * 128:(k + 1) * 128], idb[:])
                    ii.then_inc(fp1, 1)
                t.wait_ge(fv5, 8 * (tt + 1))
                for mh in range(24):
                    ix = 24 * tt + mh
                    if ix >= 1:
                        t.wait_ge(fas, ix)
                    for k in range(8):
                        ii = t.matmul(pg_[:], wgs[:, k, mh * 128:(mh + 1) * 128],
                                      hbf[:, k, :], start=(k == 0), stop=(k == 7))
                    ii.then_inc(fpg, 1)
                    if ix >= 1:
                        t.wait_ge(fve, ix)
                    for k in range(8):
                        ii = t.matmul(pu_[:], wus[:, k, mh * 128:(mh + 1) * 128],
                                      hbf[:, k, :], start=(k == 0), stop=(k == 7))
                    ii.then_inc(fpu, 1)
                t.wait_ge(fve, 24 * (tt + 1))
                for mo in range(9):
                    if mo < 8:
                        j = 8 * tt + mo
                        if j >= 2:
                            t.wait_ge(fae, j - 1)
                        for mh in range(24):
                            ii = t.matmul(pp2[:, mo % 2, :], wds[:, mh, mo * 128:(mo + 1) * 128],
                                          h3[:, mh, :], start=(mh == 0), stop=(mh == 23))
                        ii.then_inc(fpw, 1)
                    if mo >= 1:
                        mt = mo - 1
                        t.wait_ge(fae, 8 * tt + mt + 1)
                        if 8 * tt + mt >= 1:
                            t.wait_ge(fvp, 8 * tt + mt)
                        for c in range(4):
                            ii = t.transpose(pepi[:, c * 128:(c + 1) * 128],
                                             obsb[:, mt % 4, c * 128:(c + 1) * 128], idb[:])
                        ii.then_inc(fpe, 1)

        @b.gpsimd
        def _(g):
            yo = d["y"].rearrange("(n c p) dd -> n p c dd", p=128, c=4)
            for tt in range(NT):
                g.wait_ge(fvp, 8 * (tt + 1))
                g.dma_start(out=yo[tt], in_=xt[:]).then_inc(fsx[tt % 2], 16)
            g.wait_ge(fsx[0], 16 * (NT // 2))
            g.wait_ge(fsx[1], 16 * (NT // 2))
    es.close()

EMIT = {
    "qkv0": lambda c, d: ph_qkv(c, d, 0),
    "qkv1": lambda c, d: ph_qkv(c, d, 1),
    "attn0": lambda c, d: ph_attn(c, d, 0, rider_mk=mk_moeprep_rider),
    "attn1": lambda c, d: ph_attn(c, d, 1, rider_mk=mk_ffnprep_rider),
    "oproj0": lambda c, d: ph_oproj(c, d, 0),
    "oproj1": lambda c, d: ph_oproj(c, d, 1),
    "moeprep": lambda c, d: ph_moeprep(c, d),
    "moe": lambda c, d: ph_moe(c, d),
    "ffnprep": lambda c, d: ph_ffnprep(c, d),
    "ffn": lambda c, d: ph_ffn(c, d),
}

GROUPS = [["qkv0", "attn0", "oproj0", "moe", "qkv1", "attn1", "oproj1", "ffn"]]


def build_group(phases):
    nc = bass.Bass("TRN2", target_bir_lowering=False)
    ctx = Ctx(nc)
    reads = set()
    writes = set()
    for p in phases:
        r, w = PHASE_IO[p]
        reads.update(x for x in r if x not in writes)
        writes.update(w)
    gi = [ALL_PHASES.index(p) for p in phases]
    later_reads = set()
    for p in ALL_PHASES:
        if ALL_PHASES.index(p) > max(gi) and p not in phases:
            later_reads.update(PHASE_IO[p][0])
    kinds = {}
    for name in set(reads) | set(writes):
        if name in writes:
            kinds[name] = ("ExternalOutput"
                           if (name in later_reads or TENSORS[name][2] == "out")
                           else "Internal")
        else:
            kinds[name] = "ExternalInput"
    d = {}
    for name in sorted(kinds):
        k = kinds[name]
        shape, dt, _ = TENSORS[name]
        d[name] = nc.dram_tensor(name, shape, dt, kind=k).ap()
    with ctx.stack:
        for p in phases:
            with nc.cleanup_on_exit():
                EMIT[p](ctx, d)
    ext_in = sorted(n for n, k in kinds.items() if k == "ExternalInput")
    ext_out = sorted(n for n, k in kinds.items() if k == "ExternalOutput")
    return nc, ext_in, ext_out


def prep_inputs(inputs):
    common = {
        "qkvw": np.ascontiguousarray(inputs["L0_attn_qkv_w"].reshape(DM, NH * 192)),
        "qkvb": np.ascontiguousarray(inputs["L0_attn_qkv_b"]),
        "outw": np.ascontiguousarray(inputs["L0_attn_out_w"].reshape(DM, DM)),
        "outb": np.ascontiguousarray(inputs["L0_attn_out_b"]),
        "rw": np.ascontiguousarray(inputs["L0_ffn_router_w"]),
        "eb": np.ascontiguousarray(inputs["L0_ffn_experts_bias"]),
        "w1c": np.ascontiguousarray(np.concatenate([inputs["L0_ffn_w1"], inputs["L0_ffn_sw1"]], axis=0)),
        "w3c": np.ascontiguousarray(np.concatenate([inputs["L0_ffn_w3"], inputs["L0_ffn_sw3"]], axis=0)),
        "w2c": np.ascontiguousarray(np.concatenate([inputs["L0_ffn_w2"], inputs["L0_ffn_sw2"]], axis=0)),
        "wq": np.ascontiguousarray(inputs["L1_attn_Wq"]),
        "wk": np.ascontiguousarray(inputs["L1_attn_Wk"]),
        "wv": np.ascontiguousarray(inputs["L1_attn_Wv"]),
        "wu": np.ascontiguousarray(inputs["L1_attn_Wu"]),
        "bu": np.ascontiguousarray(inputs["L1_attn_bu"]),
        "wg": np.ascontiguousarray(inputs["L1_ffn_w_gate"]),
        "wup": np.ascontiguousarray(inputs["L1_ffn_w_up"]),
        "wd": np.ascontiguousarray(inputs["L1_ffn_w_down"]),
    }
    pools = []
    for c in range(NCORE):
        p = dict(common)
        p["x0"] = np.ascontiguousarray(np.asarray(inputs["L0_attn_x"])[c])
        pools.append(p)
    return pools


_BUILD_CACHE = {}


def kernel(**inputs):
    pools = prep_inputs(inputs)
    for phases in GROUPS:
        key = tuple(phases)
        if key not in _BUILD_CACHE:
            _BUILD_CACHE[key] = build_group(phases)
        nc, ext_in, ext_out = _BUILD_CACHE[key]
        in_maps = [{n: pools[c][n] for n in ext_in} for c in range(NCORE)]
        res = run_bass_kernel_spmd(nc, in_maps, core_ids=list(range(NCORE)))
        for c in range(NCORE):
            for n in ext_out:
                pools[c][n] = res.results[c][n]
    return np.stack([pools[c]["y"] for c in range(NCORE)])
